# version 06
# baseline (speedup 1.0000x reference)
import contextlib
import numpy as np
import ml_dtypes
import concourse.bass as bass
import concourse.mybir as mybir
from concourse.bass_utils import run_bass_kernel_spmd

F32 = mybir.dt.float32
F32R = mybir.dt.float32r
SOLVE_DT = F32
BF16 = mybir.dt.bfloat16
I32 = mybir.dt.int32
AF = mybir.ActivationFunctionType
ALU = mybir.AluOpType
AX = mybir.AxisListType
BF = ml_dtypes.bfloat16

NCORES = 8
SEQ = 2
L = 2048
D = 1024
DI = 2048
KD = 768
VD = 1536
H = 4
DK = 192
DV = 384
C = 128
NG = 8
TPG = 256
CPG = TPG // C
NTAU = SEQ * CPG
EPS_LN = 1e-5
EPS_L2 = 1e-6
EPS_RMS = 1e-5
LN2C = float(np.log(2.0))
MAGIC = 12582912.0
STOP_PHASE = None
SKIP = set()
POW2MASK = float(-8388608)

ENGINES = ("sync", "scalar", "tensor", "vector", "gpsimd")


class B:

    def __init__(self):
        self.nc = bass.Bass("TRN2", target_bir_lowering=False)
        self.steps = {e: [] for e in ENGINES}
        self.counts = {}
        self.bank_free = [[] for _ in range(8)]

    def now(self, sem):
        return self.counts.get(sem, 0)

    def _step(self, eng, fn, waits, inc, amt):
        waits = [w for w in waits if w[1] > 0]
        val = None
        if inc is not None:
            self.counts[inc] = self.counts.get(inc, 0) + amt
            val = self.counts[inc]
        self.steps[eng].append(dict(fn=fn, waits=waits, inc=inc, amt=amt, val=val))
        return val

    def dma(self, queue, out_ap, in_ap, sem="dw", waits=(), slow=False):
        return self._step(queue,
                          lambda e, o=out_ap, i=in_ap, sl=slow: e.dma_start(
                              out=o, in_=i, allow_slow_non_contiguous=sl),
                          waits, sem, 16)

    def T(self, fn, waits=()):
        return self._step("tensor", fn, waits, "ct", 1)

    def V(self, fn, waits=()):
        return self._step("vector", fn, waits, "cv", 1)

    def S(self, fn, waits=()):
        return self._step("scalar", fn, waits, "cs", 1)

    def G(self, fn, waits=()):
        return self._step("gpsimd", fn, waits, "cg", 1)

    def snap(self, *sems):
        return [(s, self.now(s)) for s in sems]

    def all_compute(self):
        return self.snap("ct", "cv", "cs", "cg")

    def bank(self, i, extra=()):
        w = self.bank_free[i] + list(extra)
        self.bank_free[i] = []
        return w

    def bank_done(self, i, sem, val):
        self.bank_free[i].append((sem, val))

    def build(self):
        nc = self.nc
        self.dram()
        self.consts()
        self.es = contextlib.ExitStack()
        with self.es:
            self.alloc()
            self.prep()
            self.emit_ln1(0)
            for g in range(NG):
                self.group(g)
            self._step("sync", lambda e: e.nop(),
                       [("dst0", self.now("dst0")), ("dst1", self.now("dst1"))],
                       None, 0)
            compute_sems = ("ct", "cv", "cs", "cg")
            used = {cs_: set() for cs_ in compute_sems}
            for ename in ENGINES:
                for st in self.steps[ename]:
                    for (sname, val) in st["waits"]:
                        if sname in used:
                            used[sname].add(val)
            ranks = {cs_: {v: i + 1 for i, v in enumerate(sorted(used[cs_]))}
                     for cs_ in compute_sems}

            def remap_wait(sname, val):
                if sname not in ranks:
                    return val
                return ranks[sname][val]

            blk = self.es.enter_context(nc.Block())
            for ename in ENGINES:
                steps = self.steps[ename]
                sems = self.sems

                def emitter(eng, steps=steps, sems=sems):
                    for st in steps:
                        for (sname, val) in st["waits"]:
                            eng.wait_ge(sems[sname], remap_wait(sname, val))
                        ins = st["fn"](eng)
                        inc = st["inc"]
                        if inc is not None:
                            if inc in used:
                                if st["val"] in used[inc]:
                                    ins.then_inc(sems[inc], 1)
                            else:
                                ins.then_inc(sems[inc], st["amt"])

                getattr(blk, ename)(emitter)
        return nc

    def dram(self):
        nc = self.nc
        t = lambda n, s: nc.dram_tensor(n, s, F32, kind="ExternalInput").ap()
        self.dx = t("x", [SEQ, L, D])
        self.dwin = t("w_in", [D, DI])
        self.dconvw = t("conv_w", [2 * D, 4])
        self.dconvb = t("conv_b", [1, 2 * D])
        self.dwq = t("w_q", [D, KD])
        self.dwk = t("w_k", [D, KD])
        self.dwv = t("w_v", [D, VD])
        self.dwg = t("w_g", [D, VD])
        self.dwgkb = t("w_gkb", [D, 8])
        self.dsmallin = t("small_in", [1, 16])
        self.dgnw = t("g_norm_w", [1, DV])
        self.dwso = t("w_so", [VD, D])
        self.dwout = t("w_out", [DI, D])
        self.dfw = t("ffn_w", [D, D])
        self.dfb = t("ffn_b", [1, D])
        self.dfdw = t("ffn_dw", [D, 32])
        self.dout = nc.dram_tensor("out", [SEQ, L, D], F32, kind="ExternalOutput").ap()
        self.dwqt = nc.dram_tensor("wqt_scratch", [8, 128, D], BF16, kind="Internal").ap()

    def consts(self):
        nc = self.nc
        self.i_idf32 = nc.inline_tensor(np.eye(128, dtype=np.float32), "c_idf32")
        self.i_idbf = nc.inline_tensor(np.eye(128, dtype=np.float32).astype(BF), "c_idbf")
        self.i_i32rep = nc.inline_tensor(
            np.tile(np.eye(32, dtype=np.float32), (4, 1)).astype(BF), "c_i32rep")
        self.i_maskSN = nc.inline_tensor(
            -np.triu(np.ones((128, 128), dtype=np.float32), 1), "c_maskSN")
        self.i_maskI = nc.inline_tensor(
            np.triu(np.ones((128, 128), dtype=np.float32)), "c_maskI")
        self.i_i4bf = nc.inline_tensor(
            np.tile(np.eye(128, dtype=np.float32), (1, 4)).astype(BF), "c_i4bf")
        sel = np.zeros((4, H * 128), np.float32)
        for h in range(H):
            sel[h, h * 128:(h + 1) * 128] = 1.0
        self.i_sel4 = nc.inline_tensor(sel, "c_sel4")
        self.i_ones1f = nc.inline_tensor(np.ones((1, 128), np.float32), "c_ones1f")
        self.i_ones1b = nc.inline_tensor(np.ones((1, 128), np.float32).astype(BF),
                                         "c_ones1b")
        self.i_pow2 = nc.inline_tensor(np.full((128, 1), -8388608, np.int32), "c_pow2")

    def alloc(self):
        nc, es = self.nc, self.es
        sb = lambda n, s, d: es.enter_context(nc.sbuf_tensor(n, s, d))
        ps = lambda n: es.enter_context(nc.psum_tensor(n, [128, 512], F32))
        self.c_idf32 = sb("s_idf32", [128, 128], F32)
        self.c_idbf = sb("s_idbf", [128, 128], BF16)
        self.c_i32rep = sb("s_i32rep", [128, 32], BF16)
        self.c_maskSN = sb("s_maskSN", [128, 128], F32)
        self.c_maskI = sb("s_maskI", [128, 128], F32)
        self.c_i4bf = sb("s_i4bf", [128, 512], BF16)
        self.c_sel4 = sb("s_sel4", [4, H * 128], F32)
        self.c_ones1f = sb("s_ones1f", [1, 128], F32)
        self.c_ones1b = sb("s_ones1b", [1, 128], BF16)
        self.c_pow2 = sb("s_pow2", [128, 1], I32)
        self.c_idf32r = sb("s_idf32r", [128, 128], SOLVE_DT)
        self.d_convdiag = sb("d_convdiag", [128, 16 * 4 * 32], BF16)
        self.d_cb = sb("d_cb", [128, 16], F32)
        self.d_bb = sb("d_bb", [128, 4], F32)
        self.d_dtb = sb("d_dtb", [128, 4], F32)
        self.d_negAexp = sb("d_negAexp", [128, 4], F32)
        self.d_gnw = sb("d_gnw", [128, DV], F32)
        self.d_fbias = sb("d_fbias", [1, D], BF16)
        self.d_dactp = sb("d_dactp", [128, 1], F32)
        self.d_small = sb("d_small", [1, 64], F32)
        self.wslot = [sb(f"wslot{i}", [128, 4096], BF16) for i in range(2)]
        self.a_x = [sb(f"a_x{i}", [128, D], F32) for i in range(2)]
        self.a_xn = sb("a_xn", [128, 8 * 520], BF16)
        self.a_big = sb("a_big", [128, 8320], BF16)
        self.a_xs = sb("a_xs", [128, 6144], BF16)
        self.a_zs = sb("a_zs", [128, 4096], BF16)
        self.a_q = sb("a_q", [128, NTAU * KD], BF16)
        self.a_k = sb("a_k", [128, NTAU * KD], BF16)
        self.a_v = sb("a_v", [128, NTAU * VD], BF16)
        self.a_gs = sb("a_gs", [128, NTAU * VD], BF16)
        self.st1 = sb("st1", [128, SEQ * H * DV], BF16)
        self.st2 = sb("st2", [64, SEQ * H * DV], BF16)
        self.w_pre = sb("w_pre", [128, 128], F32)
        self.w_t1 = sb("w_t1", [128, 128], F32)
        self.w_sv = [sb(f"w_sv_{s}", [128, H * 384], SOLVE_DT) for s in range(SEQ)]
        self.w_DkT = [sb(f"w_DkT_{s}", [128, 512], SOLVE_DT) for s in range(SEQ)]
        self.w_Zh = [sb(f"w_Zh_{s}", [128, DV], SOLVE_DT) for s in range(SEQ)]
        self.w_Pb = [sb(f"w_Pb_{s}", [128, 512], BF16) for s in range(SEQ)]
        self.w_qfm = [sb(f"w_qfm_{s}", [128, H * 256], BF16) for s in range(SEQ)]
        self.w_kfm = [sb(f"w_kfm_{s}", [128, H * 256], BF16) for s in range(SEQ)]
        self.w_zw = sb("w_zw", [128, 6144], BF16)
        self.w_Z = [self.w_zw[:, s * (H * DV):(s + 1) * (H * DV)] for s in range(SEQ)]
        self.w_W = [self.w_zw[:, 3072 + s * (H * DV):3072 + (s + 1) * (H * DV)]
                    for s in range(SEQ)]
        self.r_q = self.w_zw[:, 0:NTAU * KD]
        self.r_k = self.w_zw[:, 3072:3072 + NTAU * KD]
        self.w_O = [sb(f"w_O_{s}", [128, DV], F32) for s in range(SEQ)]
        self.w_osq = [sb(f"w_osq_{s}", [128, DV], F32) for s in range(SEQ)]
        self.w_ksc = [sb(f"w_ksc_{s}", [128, DK], BF16) for s in range(SEQ)]
        self.w_gI = [sb(f"w_gI_{s}", [128, 128], BF16) for s in range(SEQ)]
        self.b_beta = sb("b_beta", [128, NTAU * H], F32)
        self.b_G = sb("b_G", [128, NTAU * H], F32)
        self.b_Lam = sb("b_Lam", [128, NTAU * H], F32)
        self.b_negLam = sb("b_negLam", [128, NTAU * H], F32)
        self.b_bias = sb("b_bias", [128, NTAU * H], F32)
        self.b_Growt = sb("b_Growt", [4, NTAU * 128], F32)
        self.b_lamb4 = [sb(f"b_lamb4_{s}", [128, H], F32) for s in range(SEQ)]
        self.b_gam4 = [sb(f"b_gam4_{s}", [128, H], F32) for s in range(SEQ)]
        self.b_rs = sb("b_rs", [128, H], F32)
        self.b_ssq = [sb(f"b_ssq_{s}", [128, 2], F32) for s in range(SEQ)]
        self.b_stat = sb("b_stat", [128, 8], F32)
        self.r_gkb = sb("r_gkb", [128, NTAU * 8], F32)
        self.p2t = sb("p2t", [128, 24], F32)
        self.carry = sb("carry", [128, 2 * 16], BF16)
        self.qt0 = sb("qt0", [128, D], F32)
        self.r_sq = self.qt0[:, 0:KD]
        self.qs = sb("qs", [128, 32], F32)
        self.qsi = sb("qsi", [128, 32], F32)
        self.qm = sb("qm", [128, 32], I32)
        self.qw = sb("qw", [128, 32], F32)
        xsf = self.a_xs.ap().bitcast(F32)
        self.qt1 = xsf[:, 0:D]
        self.qt2 = xsf[:, D:2 * D]
        self.a_xnt = self.a_xs[:, 4096:4096 + D]
        self.a_xq = self.a_gs
        bigf = self.a_big.ap().bitcast(F32)
        self.x2 = bigf[:, 0:NTAU * D]
        self.fb_stage = bigf[0:1, 0:D]
        self.fq = self.a_big[:, 4096:4096 + D]
        self.fdw = bigf[:, 3100:3132]
        self.d_cw = bigf[:, 3200:3264]
        xnf = self.a_xn.ap().bitcast(F32)
        self.a_out = [xnf[:, 0:D], xnf[:, D:2 * D]]
        self.pb = [ps(f"pb{i}") for i in range(8)]
        self.out_store = [0, 0]
        self.sems = {}
        for sname in ("dx", "dw0", "dw1", "dmisc", "dst0", "dst1", "ct", "cv", "cs", "cg"):
            self.sems[sname] = es.enter_context(nc.semaphore(sname))

    def prep(self):
        for dst, src in ((self.c_idf32, self.i_idf32), (self.c_idbf, self.i_idbf),
                         (self.c_i32rep, self.i_i32rep),
                         (self.c_maskSN, self.i_maskSN), (self.c_maskI, self.i_maskI),
                         (self.c_i4bf, self.i_i4bf), (self.c_sel4, self.i_sel4),
                         (self.c_ones1f, self.i_ones1f), (self.c_ones1b, self.i_ones1b),
                         (self.c_pow2, self.i_pow2)):
            self.dma("sync", dst[:], src.ap(), "dmisc")
        self.dma("sync", self.d_small[0:1, 0:16], self.dsmallin, "dmisc")
        self.dma("sync", self.d_cw.rearrange("p (c j) -> p c j", j=4),
                 self.dconvw.rearrange("(c p) j -> p c j", p=128), "dmisc", slow=True)
        self.dma("sync", self.d_cb[:].rearrange("p (c o) -> p c o", o=1),
                 self.dconvb.rearrange("o (c p) -> p c o", p=128), "dmisc", slow=True)
        self.dma("sync", self.d_gnw[0:1, :], self.dgnw, "dmisc")
        self.dma("sync", self.fb_stage, self.dfb, "dmisc")
        WM = [("dmisc", self.now("dmisc"))]
        pb = self.pb

        def bcast(bank_i, dst_ap, src_row, n, extraw=()):
            t = self.T(lambda e, b=pb[bank_i], s=src_row, n2=n: e.matmul(
                b[:, 0:n2], self.c_ones1f[:, :], s, start=True, stop=True),
                waits=WM + list(extraw) + self.bank(bank_i))
            v = self.V(lambda e, b=pb[bank_i], d2=dst_ap, n2=n: e.tensor_copy(
                d2, b[:, 0:n2]), waits=[("ct", t)])
            self.bank_done(bank_i, "cv", v)
            return v

        bcast(0, self.d_bb[:, :], self.d_small[0:1, 0:4], 4)
        bcast(1, self.d_dtb[:, :], self.d_small[0:1, 8:12], 4)
        sA = self.S(lambda e: e.activation(self.d_small[0:1, 16:20],
                                           self.d_small[0:1, 4:8], AF.Exp), waits=WM)
        tA = self.T(lambda e: e.matmul(pb[2][:, 0:4], self.c_ones1f[:, :],
                                       self.d_small[0:1, 16:20], start=True, stop=True),
                    waits=[("cs", sA)] + self.bank(2))
        vA = self.V(lambda e: e.tensor_scalar(self.d_negAexp[:, :], pb[2][:, 0:4],
                                              -1.0, None, ALU.mult), waits=[("ct", tA)])
        self.bank_done(2, "cv", vA)
        bcast(3, self.d_gnw[:, :], self.d_gnw[0:1, :], DV)
        self.S(lambda e: e.copy(self.c_idf32r[:, :], self.c_idf32[:, :]), waits=WM)
        self.S(lambda e: e.copy(self.d_fbias[:, :], self.fb_stage), waits=WM)
        sD = self.S(lambda e: e.activation(self.d_small[0:1, 20:21],
                                           self.d_small[0:1, 12:13], AF.Tanh,
                                           scale=np.float32(4.0)), waits=WM)
        sD = self.S(lambda e: e.activation(self.d_small[0:1, 21:22],
                                           self.d_small[0:1, 20:21], AF.Exp,
                                           scale=np.float32(0.5 * LN2C)), waits=[("cs", sD)])
        tD = self.T(lambda e: e.matmul(pb[4][:, 0:1], self.c_ones1f[:, :],
                                       self.d_small[0:1, 21:22], start=True, stop=True),
                    waits=[("cs", sD)] + self.bank(4))
        vD = self.V(lambda e: e.tensor_copy(self.d_dactp[:, :], pb[4][:, 0:1]),
                    waits=[("ct", tD)])
        self.bank_done(4, "cv", vD)
        for ch in range(16):
            for j in range(4):
                off = (ch * 4 + j) * 32
                self.S(lambda e, o=off, c2=ch, j2=j: e.activation(
                    self.d_convdiag[:, o:o + 32], self.c_i32rep[:, :], AF.Identity,
                    scale=self.d_cw[:, c2 * 4 + j2:c2 * 4 + j2 + 1]), waits=WM)
        for kc in range(8):
            self.quant_w_chunk(kc)
        self.G(lambda e: e.memset(self.st1[:, :], 0.0))
        self.G(lambda e: e.memset(self.st2[:, :], 0.0))
        self.G(lambda e: e.memset(self.carry[:, :], 0.0))

    def quant_w_chunk(self, kc):
        qt0, qs, qsi, qm = self.qt0, self.qs, self.qsi, self.qm
        self.dma("sync", qt0[:], self.dfw[kc * 128:(kc + 1) * 128, :], "dmisc",
                 waits=self.snap("cv", "cs"))
        self.dma("sync", self.fdw, self.dfdw[kc * 128:(kc + 1) * 128, :], "dmisc")
        WD = [("dmisc", self.now("dmisc")), ("cv", self.now("cv")),
              ("cs", self.now("cs"))]
        v = self.V(lambda e: e.reduce_max(qs[:].unsqueeze(-1),
                                          qt0[:].rearrange("p (b k) -> p b k", k=32),
                                          axis=AX.X, apply_absolute_value=True),
                   waits=WD)
        v = self.V(lambda e: e.tensor_scalar(qsi[:], qs[:], 1e-30, None, ALU.max),
                   waits=[("cv", v)])
        v = self.V(lambda e: e.tensor_scalar(qsi[:].bitcast(I32), qsi[:].bitcast(I32),
                                             self.c_pow2[:, 0:1], None, ALU.bitwise_and),
                   waits=[("cv", v)])
        v = self.V(lambda e: e.tensor_single_scalar(qm[:], qs[:], 0.0, ALU.is_gt),
                   waits=[("cv", v)])
        v = self.V(lambda e: e.tensor_scalar(self.qw[:], qsi[:], 0.25, None, ALU.mult),
                   waits=[("cv", v)])
        v = self.V(lambda e: e.memset(qsi[:], 1.0), waits=[("cv", v)])
        v = self.V(lambda e: e.select(qs[:], qm[:], self.qw[:], qsi[:], add_drain=True), waits=[("cv", v)])
        s = self.S(lambda e: e.activation(qsi[:], self.fdw, AF.Tanh, scale=np.float32(4.0)),
                   waits=WD + [("cv", v)])
        s = self.S(lambda e: e.activation(qsi[:], qsi[:], AF.Exp, scale=np.float32(0.5 * LN2C)),
                   waits=[("cs", s)])
        v = self.V(lambda e: e.tensor_tensor(qs[:], qs[:], qsi[:], ALU.mult),
                   waits=[("cs", s), ("cv", v)])
        v = self.V(lambda e: e.reciprocal(qsi[:], qs[:]), waits=[("cv", v)])
        vq = self.emit_quant_core(qt0[:], self.qt1, self.qt2, qs[:], qsi[:],
                                  self.fq[:, 0:D], [("cv", v)])
        for tt in range(8):
            bi = 5 + (tt % 2)
            t = self.T(lambda e, t2=tt, b=self.pb[bi]: e.transpose(
                b.ap().bitcast(BF16)[0:128, 0:128],
                self.fq[:, t2 * 128:(t2 + 1) * 128], self.c_idbf[:, :]),
                waits=[("cv", vq)] + self.bank(bi)
                + ([("dmisc", self.now("dmisc"))] if tt == 0 else []))
            vv = self.V(lambda e, t2=tt, b=self.pb[bi]: e.tensor_copy(
                self.a_big[:, 5120 + t2 * 128: 5120 + (t2 + 1) * 128],
                b.ap().bitcast(BF16)[0:128, 0:128]), waits=[("ct", t)])
            self.bank_done(bi, "cv", vv)
        for tt in range(8):
            self.dma("sync", self.dwqt[tt, :, kc * 128:(kc + 1) * 128],
                     self.a_big[:, 5120 + tt * 128: 5120 + (tt + 1) * 128], "dmisc",
                     waits=[("cv", self.now("cv"))] if tt == 0 else ())

    def emit_quant_core(self, x_in, tmp_sgn, tmp_ph, scale, inv_scale, out_bf, waits):
        V, S = self.V, self.S
        r32 = lambda ap: ap.rearrange("p (b k) -> p b k", k=32)
        bc = lambda ap: ap.unsqueeze(-1).broadcast_to((128, 32, 32))
        v = V(lambda e: e.tensor_tensor(r32(x_in), r32(x_in), bc(inv_scale),
                                        ALU.mult), waits=waits)
        v = V(lambda e: e.tensor_copy(tmp_sgn, x_in), waits=[("cv", v)])
        s = S(lambda e: e.activation(x_in, x_in, AF.Abs),
              waits=[("cv", v)] + self.snap("cs"))
        v = V(lambda e: e.tensor_scalar(x_in, x_in, 6.0, 1e-30, ALU.min, ALU.max),
              waits=[("cs", s)])
        v = V(lambda e: e.tensor_scalar(tmp_ph.bitcast(I32), x_in.bitcast(I32),
                                        self.c_pow2[:, 0:1], None, ALU.bitwise_and),
              waits=[("cv", v)])
        v = V(lambda e: e.tensor_scalar(tmp_ph, tmp_ph, 1.0, 4.0, ALU.max, ALU.min),
              waits=[("cv", v)])
        v = V(lambda e: e.reciprocal(tmp_ph, tmp_ph), waits=[("cv", v)])
        v = V(lambda e: e.scalar_tensor_tensor(x_in, x_in, 2.0, tmp_ph,
                                               ALU.mult, ALU.mult), waits=[("cv", v)])
        v = V(lambda e: e.tensor_scalar(x_in, x_in, MAGIC, MAGIC,
                                        ALU.add, ALU.subtract), waits=[("cv", v)])
        v = V(lambda e: e.reciprocal(tmp_ph, tmp_ph), waits=[("cv", v)])
        v = V(lambda e: e.scalar_tensor_tensor(x_in, tmp_ph, 0.5, x_in,
                                               ALU.mult, ALU.mult), waits=[("cv", v)])
        s = S(lambda e: e.sign(tmp_ph, tmp_sgn), waits=[("cv", v)] + self.snap("cs"))
        v = V(lambda e: e.tensor_tensor(x_in, x_in, tmp_ph, ALU.mult),
              waits=[("cs", s)])
        v = V(lambda e: e.tensor_tensor(r32(out_bf), r32(x_in), bc(scale),
                                        ALU.mult), waits=[("cv", v)])
        return v

    def emit_ln1(self, g):
        t0 = g * TPG
        last = (g == NG - 1)
        W0 = self.all_compute() + [("dmisc", self.now("dmisc")),
                                   ("dst0", self.now("dst0")),
                                   ("dst1", self.now("dst1"))]
        cnt = 0
        for s in range(SEQ):
            tiles = [(0, 128), (128, 128)] + ([] if last else [(256, 2)])
            for (toff, tn) in tiles:
                buf = self.a_x[cnt % 2]
                cnt += 1
                self.dma("sync", buf[0:tn, :],
                         self.dx[s, t0 + toff:t0 + toff + tn, :], "dx",
                         waits=self.snap("cv", "cs"))
                WX = [("dx", self.now("dx"))]
                v = self.V(lambda e, b=buf, n=tn: e.reduce_sum(
                    self.b_stat[0:n, 0:1], b[0:n, :], axis=AX.X, negate=True),
                    waits=WX + (W0 if cnt == 1 else []))
                v = self.V(lambda e, n=tn: e.tensor_scalar(
                    self.b_stat[0:n, 1:2], self.b_stat[0:n, 0:1], 1.0 / D, None,
                    ALU.mult), waits=[("cv", v)])
                s1 = self.S(lambda e, b=buf, n=tn: e.activation(
                    b[0:n, :], b[0:n, :], AF.Identity, bias=self.b_stat[0:n, 1:2]),
                    waits=[("cv", v)] + (W0 if cnt == 1 else []))
                s1 = self.S(lambda e, b=buf, n=tn: e.activation(
                    self.qt0[0:n, :], b[0:n, :], AF.Square,
                    accum_out=self.b_stat[0:n, 2:3]), waits=[("cs", s1)])
                s1 = self.S(lambda e, n=tn: e.activation(
                    self.b_stat[0:n, 3:4], self.b_stat[0:n, 2:3], AF.Ln,
                    scale=np.float32(1.0 / D), bias=np.float32(EPS_LN)), waits=[("cs", s1)])
                s1 = self.S(lambda e, n=tn: e.activation(
                    self.b_stat[0:n, 3:4], self.b_stat[0:n, 3:4], AF.Exp, scale=np.float32(-0.5)),
                    waits=[("cs", s1)])
                s1 = self.S(lambda e, b=buf, n=tn: e.activation(
                    self.a_xnt[0:n, :], b[0:n, :], AF.Identity,
                    scale=self.b_stat[0:n, 3:4]), waits=[("cs", s1)])
                for kc in range(8):
                    bi = 6 + (kc % 2)
                    t = self.T(lambda e, k2=kc, b=self.pb[bi], n=tn: e.transpose(
                        b.ap().bitcast(BF16)[0:128, 0:n],
                        self.a_xnt[0:n, k2 * 128:(k2 + 1) * 128],
                        self.c_idbf[0:n, 0:n]),
                        waits=[("cs", s1)] + self.bank(bi))
                    vv = self.V(lambda e, k2=kc, b=self.pb[bi], n=tn, s2=s, to=toff:
                                e.tensor_copy(
                        self.a_xn[:, k2 * 520 + s2 * 260 + to:
                                  k2 * 520 + s2 * 260 + to + n],
                        b.ap().bitcast(BF16)[0:128, 0:n]), waits=[("ct", t)])
                    self.bank_done(bi, "cv", vv)

    def load_piece(self, slot, specs, twait=None):
        w = [("ct", twait if twait is not None else self.now("ct"))]
        sem = f"dw{slot}"
        for (coff, ncols, src) in specs:
            self.dma("gpsimd", self.wslot[slot][:, coff:coff + ncols], src, sem,
                     waits=w)
            w = []
        return (sem, self.now(sem))

    def group(self, g):
        t0 = g * TPG
        last = (g == NG - 1)
        ncol = TPG if last else TPG + 2
        pb = self.pb

        xn_ready = self.snap("cv")
        g0_extra = [("dmisc", self.now("dmisc"))] if g == 0 else []
        ev_xz = None
        for piece in range(4):
            slot = piece % 2
            specs = [(kc * 512, 512,
                      self.dwin[kc * 128:(kc + 1) * 128, piece * 512:(piece + 1) * 512])
                     for kc in range(8)]
            ld = self.load_piece(slot, specs)
            for dsub in range(4):
                dchunk = piece * 4 + dsub
                for s in range(SEQ):
                    bi = (dsub * 2 + s) % 4
                    mm = None
                    for kc in range(8):
                        lhs = self.wslot[slot][:, kc * 512 + dsub * 128:
                                               kc * 512 + dsub * 128 + 128]
                        rhs = self.a_xn[:, kc * 520 + s * 260:
                                        kc * 520 + s * 260 + ncol]
                        w = ([ld] + xn_ready + self.bank(bi)) if kc == 0 else []
                        mm = self.T(lambda e, l=lhs, r=rhs, b=pb[bi], k2=kc, n=ncol:
                                    e.matmul(b[:, 0:n], l, r, start=(k2 == 0),
                                             stop=(k2 == 7)), waits=w)
                    ev_xz = self.S(lambda e, b=pb[bi], d2=dchunk, s2=s, n=ncol: e.copy(
                        self.a_big[:, d2 * 520 + s2 * 260 + 1:
                                   d2 * 520 + s2 * 260 + 1 + n], b[:, 0:n]),
                        waits=[("ct", mm)] + (g0_extra if dchunk == 0 else []))
                    self.bank_done(bi, "cs", ev_xz)
        cw = [("cs", ev_xz), ("cg", self.now("cg"))]
        big3 = self.a_big[:].rearrange("p (ch q) -> p ch q", q=520)
        for s in range(SEQ):
            self.V(lambda e, s2=s: e.tensor_copy(
                big3[:, :, s2 * 260: s2 * 260 + 1],
                self.carry[:, s2 * 16: s2 * 16 + 16].rearrange("p (c o) -> p c o", o=1)),
                waits=cw)
            if last:
                self.V(lambda e, s2=s: e.memset(
                    big3[:, :, s2 * 260 + 257: s2 * 260 + 260], 0.0), waits=cw)
        xz_ready_cv = self.now("cv")

        for dchunk in range(16):
            half = dchunk // 8
            ch = dchunk % 8
            dst = self.a_xs if half == 0 else self.a_zs
            for s in range(SEQ):
                bi = 4 + (dchunk * 2 + s) % 2
                mm = None
                for blk in range(4):
                    for j in range(4):
                        diag = self.d_convdiag[32 * blk:32 * blk + 32,
                                               (dchunk * 4 + j) * 32:
                                               (dchunk * 4 + j) * 32 + 32]
                        rhs = self.a_big[32 * blk:32 * blk + 32,
                                         dchunk * 520 + s * 260 + j:
                                         dchunk * 520 + s * 260 + j + TPG]
                        w = ([("cv", xz_ready_cv)] + self.bank(bi)) \
                            if (j == 0 and blk == 0) else []
                        mm = self.T(lambda e, d2=diag, r=rhs, b=pb[bi], j2=j, bl=blk:
                                    e.matmul(b[32 * bl:32 * bl + 32, 0:TPG], d2, r,
                                             start=(j2 == 0), stop=(j2 == 3),
                                             tile_position=(32 * bl, 32 * bl)),
                                    waits=w)
                sc = self.S(lambda e, b=pb[bi], c2=ch, s2=s, h2=half, d2=dst:
                            e.activation(
                    d2[:, c2 * 512 + s2 * 256: c2 * 512 + s2 * 256 + TPG],
                    b[:, 0:TPG], AF.Silu,
                    bias=self.d_cb[:, h2 * 8 + c2: h2 * 8 + c2 + 1]),
                    waits=[("ct", mm)])
                self.bank_done(bi, "cs", sc)
        conv_ct = self.now("ct")
        if not last:
            for s in range(SEQ):
                self.V(lambda e, s2=s: e.tensor_copy(
                    self.carry[:, s2 * 16: s2 * 16 + 16]
                    .rearrange("p (c o) -> p c o", o=1),
                    big3[:, :, s2 * 260 + 256: s2 * 260 + 257]),
                    waits=[("ct", conv_ct)])

        if STOP_PHASE == "p1":
            return
        self.p2(g)

        so_specs = []
        for piece in range(4):
            khalf, mhalf = divmod(piece, 2)
            so_specs.append([(kc6 * 512, 512,
                              self.dwso[(khalf * 6 + kc6) * 128:
                                        (khalf * 6 + kc6 + 1) * 128,
                                        mhalf * 512:(mhalf + 1) * 512])
                             for kc6 in range(6)])
        ld_so = [None] * 4
        ld_so[0] = self.load_piece(0, so_specs[0])
        ld_so[2] = self.load_piece(1, so_specs[2])

        if STOP_PHASE == "p2":
            return
        for c in range(CPG):
            for s in range(SEQ):
                self.scan_chunk(g, s, c)
        p3_cv, p3_cs = self.now("cv"), self.now("cs")

        if STOP_PHASE == "p3":
            return
        for tau in range(NTAU):
            s, c = divmod(tau, CPG)
            for kk in range(12):
                bi = 6 + (kk % 2)
                t = self.T(lambda e, t2=tau, k2=kk, b=pb[bi]: e.transpose(
                    b.ap().bitcast(BF16)[0:128, 0:128],
                    self.a_big[:, t2 * VD + k2 * 128: t2 * VD + (k2 + 1) * 128],
                    self.c_idbf[:, :]),
                    waits=(([("cv", p3_cv), ("cs", p3_cs)] if (tau == 0 and kk == 0)
                            else []) + self.bank(bi)))
                vv = self.V(lambda e, k2=kk, b=pb[bi], s2=s, c2=c: e.tensor_copy(
                    self.a_xs[:, k2 * 512 + s2 * 256 + c2 * 128:
                              k2 * 512 + s2 * 256 + c2 * 128 + 128],
                    b.ap().bitcast(BF16)[0:128, 0:128]), waits=[("ct", t)])
                self.bank_done(bi, "cv", vv)
        p4a_cv = self.now("cv")

        p4b_cs = None
        for mhalf in range(2):
            if mhalf == 1:
                ld_so[1] = self.load_piece(0, so_specs[1])
                ld_so[3] = self.load_piece(1, so_specs[3])
            for msub in range(4):
                m = mhalf * 4 + msub
                bi = msub % 4
                mm = None
                for kc in range(12):
                    khalf, kc6 = divmod(kc, 6)
                    piece = khalf * 2 + mhalf
                    slot = 0 if khalf == 0 else 1
                    w = []
                    if kc == 0:
                        w = [ld_so[piece], ("cv", p4a_cv)] + self.bank(bi)
                    elif kc6 == 0:
                        w = [ld_so[piece]]
                    mm = self.T(lambda e, k6=kc6, sl=slot, m2=msub, b=pb[bi], k2=kc:
                                e.matmul(
                        b[:, 0:512],
                        self.wslot[sl][:, k6 * 512 + m2 * 128:
                                       k6 * 512 + (m2 + 1) * 128],
                        self.a_xs[:, k2 * 512: k2 * 512 + 512],
                        start=(k2 == 0), stop=(k2 == 11)), waits=w)
                p4b_cs = self.S(lambda e, m2=m, b=pb[bi]: e.activation(
                    self.a_v[:, m2 * 512: m2 * 512 + 512], b[:, 0:512], AF.Copy),
                    waits=[("ct", mm)])
                self.bank_done(bi, "cs", p4b_cs)

        for half in range(2):
            lds = []
            for khalf in range(2):
                specs = [(kc * 512, 512,
                          self.dwout[(khalf * 8 + kc) * 128:(khalf * 8 + kc + 1) * 128,
                                     half * 512:(half + 1) * 512])
                         for kc in range(8)]
                lds.append(self.load_piece(khalf, specs))
            for tau in range(NTAU):
                s, c = divmod(tau, CPG)
                xbuf = self.a_x[tau % 2]
                self.dma("sync", xbuf[:, 0:512],
                         self.dx[s, t0 + c * 128: t0 + c * 128 + 128,
                                 half * 512:(half + 1) * 512],
                         "dx", waits=self.snap("cv"))
                WX = [("dx", self.now("dx"))]
                bi = 4 + (tau % 2)
                mm = None
                for khalf in range(2):
                    src = self.a_v if khalf == 0 else self.a_zs
                    for kc in range(8):
                        lhs = src[:, kc * 512 + s * 256 + c * 128:
                                  kc * 512 + s * 256 + c * 128 + 128]
                        w = ([lds[khalf], ("cs", p4b_cs)] + self.bank(bi)) \
                            if kc == 0 else []
                        mm = self.T(lambda e, l=lhs, b=pb[bi], k2=kc, kh=khalf:
                                    e.matmul(
                            b[:, 0:512], l,
                            self.wslot[kh][:, k2 * 512:(k2 + 1) * 512],
                            start=(kh == 0 and k2 == 0), stop=(kh == 1 and k2 == 7)),
                            waits=w)
                vv = self.V(lambda e, b=pb[bi], t2=tau, h2=half, bx=xbuf:
                            e.tensor_tensor(
                    self.x2[:, t2 * D + h2 * 512: t2 * D + h2 * 512 + 512],
                    bx[:, 0:512], b[:, 0:512], ALU.add),
                    waits=[("ct", mm)] + WX)
                self.bank_done(bi, "cv", vv)

        if STOP_PHASE == "p4":
            return
        self.ld_wq = []
        for half in range(2):
            specs = [(tt * 1024, 1024, self.dwqt[half * 4 + tt, :, :])
                     for tt in range(4)]
            self.ld_wq.append(self.load_piece(half, specs))
        for tau in range(NTAU):
            self.p5_tile(g, tau)

        if not last:
            self.emit_ln1(g + 1)

    def p2(self, g):
        pb = self.pb

        def xs_lhs(kc, tau):
            s, c = divmod(tau, CPG)
            return self.a_xs[:, kc * 512 + s * 256 + c * 128:
                             kc * 512 + s * 256 + c * 128 + 128]

        def project(dten, outcols, evac_fn):
            npieces = (outcols + 511) // 512
            for piece in range(npieces):
                pcols = min(512, outcols - piece * 512)
                slot = piece % 2
                specs = [(kc * pcols, pcols,
                          dten[kc * 128:(kc + 1) * 128,
                               piece * 512: piece * 512 + pcols])
                         for kc in range(8)]
                ld = self.load_piece(slot, specs)
                for tau in range(NTAU):
                    bi = tau % 4
                    mm = None
                    for kc in range(8):
                        w = ([ld] + self.bank(bi)) if kc == 0 else []
                        mm = self.T(lambda e, k2=kc, b=pb[bi], t2=tau, sl=slot,
                                    pc=pcols: e.matmul(
                            b[:, 0:pc], xs_lhs(k2, t2),
                            self.wslot[sl][:, k2 * pc: k2 * pc + pc],
                            start=(k2 == 0), stop=(k2 == 7)), waits=w)
                    sem, ev = evac_fn(piece, pcols, tau, bi, mm)
                    self.bank_done(bi, sem, ev)

        def mk_evac(dst, width, func):
            def f(piece, pcols, tau, bi, mm):
                e1 = self.S(lambda e, b=pb[bi], t2=tau, p2=piece, pc=pcols:
                            e.activation(
                    dst[:, t2 * width + p2 * 512: t2 * width + p2 * 512 + pc],
                    b[:, 0:pc], func), waits=[("ct", mm)])
                return ("cs", e1)
            return f

        project(self.dwv, VD, mk_evac(self.a_v, VD, AF.Copy))
        project(self.dwg, VD, mk_evac(self.a_gs, VD, AF.Silu))
        project(self.dwq, KD, mk_evac(self.r_q, KD, AF.Copy))
        project(self.dwk, KD, mk_evac(self.r_k, KD, AF.Copy))

        def gkb_evac(piece, pcols, tau, bi, mm):
            v = self.V(lambda e, b=pb[bi], t2=tau: e.tensor_copy(
                self.r_gkb[:, t2 * 8: t2 * 8 + 8], b[:, 0:8]), waits=[("ct", mm)])
            return ("cv", v)

        project(self.dwgkb, 8, gkb_evac)

        p2t = self.p2t
        for tau in range(NTAU):
            v = self.V(lambda e, t2=tau: e.tensor_tensor(
                p2t[:, 0:4], self.r_gkb[:, t2 * 8 + 4: t2 * 8 + 8],
                self.d_bb[:, :], ALU.add), waits=self.snap("cv"))
            s = self.S(lambda e: e.activation(p2t[:, 4:8], p2t[:, 0:4],
                                              AF.Exp, scale=np.float32(-1.0)),
                       waits=[("cv", v)] + self.snap("cs"))
            v = self.V(lambda e: e.tensor_scalar(p2t[:, 4:8], p2t[:, 4:8],
                                                 1.0, None, ALU.add), waits=[("cs", s)])
            v = self.V(lambda e, t2=tau: e.reciprocal(
                self.b_beta[:, t2 * H: t2 * H + H], p2t[:, 4:8]), waits=[("cv", v)])
            v2 = self.V(lambda e, t2=tau: e.tensor_tensor(
                p2t[:, 8:12], self.r_gkb[:, t2 * 8: t2 * 8 + 4],
                self.d_dtb[:, :], ALU.add), waits=[("cv", v)])
            s = self.S(lambda e: e.activation(p2t[:, 12:16], p2t[:, 8:12], AF.Abs),
                       waits=[("cv", v2)])
            s = self.S(lambda e: e.activation(p2t[:, 12:16], p2t[:, 12:16],
                                              AF.Exp, scale=np.float32(-1.0)), waits=[("cs", s)])
            s = self.S(lambda e: e.activation(p2t[:, 12:16], p2t[:, 12:16],
                                              AF.Ln, bias=np.float32(1.0)), waits=[("cs", s)])
            s = self.S(lambda e: e.activation(p2t[:, 16:20], p2t[:, 8:12], AF.Relu),
                       waits=[("cs", s)])
            v2 = self.V(lambda e: e.tensor_tensor(p2t[:, 12:16], p2t[:, 12:16],
                                                  p2t[:, 16:20], ALU.add),
                        waits=[("cs", s)])
            v2 = self.V(lambda e: e.tensor_tensor(p2t[:, 12:16], p2t[:, 12:16],
                                                  self.d_negAexp[:, :], ALU.mult),
                        waits=[("cv", v2)])
            bi = 4 + (tau % 2)
            t = self.T(lambda e, b=pb[bi]: e.matmul(
                b[:, 0:4], self.c_maskI[:, :], p2t[:, 12:16], start=True, stop=True),
                waits=[("cv", v2)] + self.bank(bi))
            v3 = self.V(lambda e, b=pb[bi], t2=tau: e.tensor_copy(
                self.b_G[:, t2 * H: t2 * H + H], b[:, 0:4]), waits=[("ct", t)])
            t2i = self.T(lambda e, b=pb[bi], t2=tau: e.transpose(
                b[0:4, 128:256], self.b_G[:, t2 * H: t2 * H + H], self.c_idf32[:, :]),
                waits=[("cv", v3)])
            v3 = self.V(lambda e, b=pb[bi], t2=tau: e.tensor_copy(
                self.b_Growt[0:4, t2 * 128:(t2 + 1) * 128], b[0:4, 128:256]),
                waits=[("ct", t2i)])
            self.bank_done(bi, "cv", v3)
        bw = self.snap("cv")
        s = self.S(lambda e: e.activation(self.b_Lam[:, :], self.b_G[:, :], AF.Exp),
                   waits=bw + self.snap("cs"))
        self.V(lambda e: e.tensor_scalar(self.b_negLam[:, :], self.b_Lam[:, :],
                                         -1.0, None, ALU.mult), waits=[("cs", s)])
        s = self.S(lambda e: e.activation(self.b_bias[:, :], self.b_beta[:, :], AF.Ln),
                   waits=bw)
        self.V(lambda e: e.tensor_tensor(self.b_bias[:, :], self.b_bias[:, :],
                                         self.b_G[:, :], ALU.subtract),
               waits=[("cs", s)])
        for (raw, dst) in ((self.r_q, self.a_q), (self.r_k, self.a_k)):
            for tau in range(NTAU):
                s = self.S(lambda e, r2=raw, t2=tau: e.activation(
                    self.r_sq, r2[:, t2 * KD:(t2 + 1) * KD], AF.Square),
                    waits=self.snap("cs"))
                v = self.V(lambda e: e.reduce_sum(
                    self.b_rs[:, :].unsqueeze(-1),
                    self.r_sq.rearrange("p (h d) -> p h d", d=DK), axis=AX.X),
                    waits=[("cs", s)] + self.snap("cv"))
                s = self.S(lambda e: e.activation(self.b_rs[:, :], self.b_rs[:, :],
                                                  AF.Ln, bias=np.float32(EPS_L2)),
                           waits=[("cv", v)])
                s = self.S(lambda e: e.activation(self.b_rs[:, :], self.b_rs[:, :],
                                                  AF.Exp, scale=np.float32(-0.5)),
                           waits=[("cs", s)])
                for h in range(H):
                    s = self.S(lambda e, r2=raw, d2=dst, t2=tau, h2=h: e.activation(
                        d2[:, t2 * KD + h2 * DK: t2 * KD + (h2 + 1) * DK],
                        r2[:, t2 * KD + h2 * DK: t2 * KD + (h2 + 1) * DK],
                        AF.Identity, scale=self.b_rs[:, h2:h2 + 1]), waits=[("cs", s)])

    def scan_chunk(self, g, s, c):
        pb = self.pb
        tau = s * CPG + c
        qk_ready = self.snap("cs", "cv")
        bA, bB = s * 2, s * 2 + 1
        bE = 4 + s
        bT = 6 + s

        for (src, dstfm) in ((self.a_q, self.w_qfm[s]), (self.a_k, self.w_kfm[s])):
            tl = None
            for h in range(H):
                for (p0, fn, oo) in ((0, 128, 0), (128, 64, 128)):
                    tl = self.T(lambda e, h2=h, p2=p0, f2=fn, o2=oo, b=pb[bT], s2=src:
                                e.transpose(
                        b.ap().bitcast(BF16)[0:f2, h2 * 256 + o2: h2 * 256 + o2 + 128],
                        s2[:, tau * KD + h2 * DK + p2: tau * KD + h2 * DK + p2 + f2],
                        self.c_idbf[:, :]),
                        waits=(qk_ready + self.bank(bT)) if (h == 0 and p0 == 0) else [])
            pv = pb[bT].ap().bitcast(BF16).rearrange("p (h q) -> p h q", q=256)
            dv = dstfm[:].rearrange("p (h q) -> p h q", q=256)
            self.V(lambda e, pv2=pv, dv2=dv: e.tensor_copy(
                dv2[0:128, :, 0:128], pv2[0:128, :, 0:128]), waits=[("ct", tl)])
            vv = self.V(lambda e, pv2=pv, dv2=dv: e.tensor_copy(
                dv2[0:64, :, 128:256], pv2[0:64, :, 128:256]))
            self.bank_done(bT, "cv", vv)
        kfm, qfm = self.w_kfm[s], self.w_qfm[s]
        fm_cv = self.now("cv")
        if STOP_PHASE == "sA":
            return

        def fmp(buf, h, piece):
            if piece == 0:
                return buf[:, h * 256: h * 256 + 128]
            return buf[0:64, h * 256 + 128: h * 256 + 256]

        mmA = mmB = None
        for h in range(H):
            w = ([("cv", fm_cv)] + self.bank(bA)) if h == 0 else []
            self.T(lambda e, h2=h, b=pb[bA]: e.matmul(
                b[:, h2 * 128:(h2 + 1) * 128], fmp(kfm, h2, 0), fmp(kfm, h2, 0),
                start=True, stop=False), waits=w)
            mmA = self.T(lambda e, h2=h, b=pb[bA]: e.matmul(
                b[:, h2 * 128:(h2 + 1) * 128], fmp(kfm, h2, 1), fmp(kfm, h2, 1),
                start=False, stop=True))
        for h in range(H):
            w = self.bank(bB) if h == 0 else []
            self.T(lambda e, h2=h, b=pb[bB]: e.matmul(
                b[:, h2 * 128:(h2 + 1) * 128], fmp(kfm, h2, 0), fmp(qfm, h2, 0),
                start=True, stop=False), waits=w)
            mmB = self.T(lambda e, h2=h, b=pb[bB]: e.matmul(
                b[:, h2 * 128:(h2 + 1) * 128], fmp(kfm, h2, 1), fmp(qfm, h2, 1),
                start=False, stop=True))

        if STOP_PHASE == "sB":
            return
        prev_readers = []
        for h in range(H):
            col = tau * H + h
            w = (self.bank(bE) if h == 0 else []) + prev_readers
            t = self.T(lambda e, h2=h, b=pb[bE], t2=tau: e.matmul(
                b[:, 0:128], self.c_sel4[:, h2 * 128:(h2 + 1) * 128],
                self.b_Growt[0:4, t2 * 128:(t2 + 1) * 128],
                start=True, stop=True), waits=w + ([("ct", mmB)] if h == 0 else []))
            v = self.V(lambda e, b=pb[bE], c2=col: e.tensor_scalar(
                self.w_t1[:, 0:1], b[:, 127:128], self.b_G[:, c2:c2 + 1], None,
                ALU.subtract), waits=[("ct", t)] + self.snap("cv"))
            sx = self.S(lambda e, h2=h: e.activation(
                self.b_lamb4[s][:, h2:h2 + 1], self.w_t1[:, 0:1], AF.Exp),
                waits=[("cv", v)] + self.snap("cs"))
            sx2 = self.S(lambda e, b=pb[bE], h2=h: e.activation(
                self.b_gam4[s][:, h2:h2 + 1], b[:, 127:128], AF.Exp),
                waits=[("ct", t)])
            v = self.V(lambda e, h2=h, c2=col: e.tensor_tensor(
                self.b_lamb4[s][:, h2:h2 + 1], self.b_lamb4[s][:, h2:h2 + 1],
                self.b_beta[:, c2:c2 + 1], ALU.mult), waits=[("cs", sx)])
            vpre = self.V(lambda e, b=pb[bE], c2=col: e.tensor_scalar(
                self.w_pre[:, :], b[:, 0:128], self.b_bias[:, c2:c2 + 1], 0.0,
                ALU.add, ALU.min), waits=[("ct", t)] + self.snap("cv"))
            prev_readers = [("cv", vpre), ("cs", sx2)]
            se = self.S(lambda e: e.activation(self.w_pre[:, :], self.w_pre[:, :],
                                               AF.Exp), waits=[("cv", vpre), ("cs", sx2)])
            v = self.V(lambda e: e.tensor_tensor(self.w_t1[:, :], self.w_pre[:, :],
                                                 self.c_maskSN[:, :], ALU.mult),
                       waits=[("cs", se)])
            vD = self.V(lambda e, h2=h, b=pb[bA]: e.tensor_tensor(
                self.w_sv[s][:, h2 * 384: h2 * 384 + 128], self.w_t1[:, :],
                b[:, h2 * 128:(h2 + 1) * 128], ALU.mult),
                waits=[("cv", v), ("ct", mmA)])
            self.V(lambda e, h2=h: e.tensor_copy(
                self.w_sv[s][:, h2 * 384 + 128: h2 * 384 + 256], self.c_idf32[:, :]))
            v = self.V(lambda e: e.tensor_tensor(self.w_t1[:, :], self.w_pre[:, :],
                                                 self.c_maskI[:, :], ALU.mult),
                       waits=[("cv", vD)])
            vP = self.V(lambda e, h2=h, b=pb[bB]: e.tensor_tensor(
                self.w_Pb[s][:, h2 * 128:(h2 + 1) * 128], self.w_t1[:, :],
                b[:, h2 * 128:(h2 + 1) * 128], ALU.mult), waits=[("cv", v)])
            if h == H - 1:
                self.bank_done(bA, "cv", vD)
                self.bank_done(bB, "cv", vP)
                self.bank_done(bE, "cv", vP)
                self.bank_done(bE, "cs", sx2)
        d0_cv = self.now("cv")

        if STOP_PHASE == "sC":
            return
        d0_cv2 = self.now("cv")
        tl = None
        for h in range(H):
            tl = self.T(lambda e, h2=h, b=pb[bT]: e.transpose(
                b.ap().bitcast(SOLVE_DT)[0:128, h2 * 128:(h2 + 1) * 128],
                self.w_sv[s][:, h2 * 384: h2 * 384 + 128], self.c_idf32r[:, :]),
                waits=([("cv", d0_cv2)] + self.bank(bT)) if h == 0 else [])
        dkt_ev = self.V(lambda e, b=pb[bT]: e.tensor_copy(
            self.w_DkT[s][:, :], b.ap().bitcast(SOLVE_DT)[0:128, 0:512]),
            waits=[("ct", tl)])
        self.bank_done(bT, "cv", dkt_ev)
        x_ready = self.now("cv")
        if STOP_PHASE == "sD0":
            return
        for r in range(1, 8):
            cur_a = (r % 2 == 1)
            roff = 0 if cur_a else 128
            dk_psoff = 0 if cur_a else 128
            x_psoff = 128 - dk_psoff
            dk_dst = 256 if cur_a else 0
            mq = None
            wts = [("cv", dkt_ev), ("cv", x_ready)]
            for h in range(H):
                bi2 = bA if h < 2 else bB
                w2 = (wts + self.bank(bA) + self.bank(bB)) if h == 0 else []
                mq = self.T(lambda e, h2=h, b=pb[bi2], ro=roff: e.matmul(
                    b[:, (h2 % 2) * 256: (h2 % 2) * 256 + 256],
                    self.w_DkT[s][:, h2 * 128:(h2 + 1) * 128],
                    self.w_sv[s][:, h2 * 384 + ro: h2 * 384 + ro + 256],
                    start=True, stop=True), waits=w2)
            sv3 = self.w_sv[s][:].rearrange("p (h q) -> p h q", q=384)
            ev1 = None
            if r < 7 and "ev1" not in SKIP:
                for half in range(2):
                    b2 = pb[bA] if half == 0 else pb[bB]
                    pv = b2.ap().rearrange("p (h q) -> p h q", q=256)
                    ev1 = self.V(lambda e, pv2=pv, hf=half, dd=dk_dst, dp=dk_psoff:
                                 e.tensor_copy(
                        sv3[:, 2 * hf: 2 * hf + 2, dd: dd + 128],
                        pv2[:, 0:2, dp: dp + 128]),
                        waits=([("ct", mq)] + self.snap("cv")) if half == 0
                        else [])
            for half in range(2 if "ev2" not in SKIP else 0):
                b2 = pb[bA] if half == 0 else pb[bB]
                pv = b2.ap().rearrange("p (h q) -> p h q", q=256)
                ev2 = self.V(lambda e, pv2=pv, hf=half, xp=x_psoff: e.tensor_tensor(
                    sv3[:, 2 * hf: 2 * hf + 2, 128:256],
                    sv3[:, 2 * hf: 2 * hf + 2, 128:256],
                    pv2[:, 0:2, xp: xp + 128], ALU.add),
                    waits=([("ct", mq)] + self.snap("cv")) if half == 0 else [])
            x_ready = ev2
            self.bank_done(bA, "cv", ev2)
            self.bank_done(bB, "cv", ev2)
            if ev1 is not None:
                self.bank_done(bA, "cv", ev1)
                self.bank_done(bB, "cv", ev1)
            if r < 7 and "tr" not in SKIP:
                tl = None
                for h in range(H):
                    tl = self.T(lambda e, h2=h, b=pb[bT], dd=dk_dst: e.transpose(
                        b.ap().bitcast(SOLVE_DT)[0:128, h2 * 128:(h2 + 1) * 128],
                        self.w_sv[s][:, h2 * 384 + dd: h2 * 384 + dd + 128],
                        self.c_idf32r[:, :]),
                        waits=([("cv", ev1)] + self.bank(bT)) if h == 0 else [])
                dkt_ev = self.V(lambda e, b=pb[bT]: e.tensor_copy(
                    self.w_DkT[s][:, :], b.ap().bitcast(SOLVE_DT)[0:128, 0:512]),
                    waits=[("ct", tl)] + self.snap("cv"))
                self.bank_done(bT, "cv", dkt_ev)
            if STOP_PHASE == "sD1" and r == 1:
                return

        if STOP_PHASE == "sD":
            return
        for h in range(H):
            col = tau * H + h
            soff = (s * H + h) * DV
            sg = self.S(lambda e, h2=h: e.activation(
                self.w_gI[s][:, :], self.c_idbf[:, :], AF.Identity,
                scale=self.b_gam4[s][:, h2:h2 + 1]),
                waits=self.snap("cs") + self.snap("ct"))
            sk = self.S(lambda e, h2=h: e.activation(
                self.w_ksc[s][:, :],
                self.a_k[:, tau * KD + h2 * DK: tau * KD + (h2 + 1) * DK],
                AF.Identity, scale=self.b_lamb4[s][:, h2:h2 + 1]), waits=[("cs", sg)])
            w = self.bank(bA) + ([("cv", x_ready)] if h == 0 else [])
            self.T(lambda e, h2=h, b=pb[bA], so=soff: e.matmul(
                b[:, 0:DV], fmp(kfm, h2, 0), self.st1[:, so:so + DV],
                start=True, stop=False), waits=w)
            mmz = self.T(lambda e, h2=h, b=pb[bA], so=soff: e.matmul(
                b[:, 0:DV], fmp(kfm, h2, 1), self.st2[0:64, so:so + DV],
                start=False, stop=True))
            vz = self.V(lambda e, b=pb[bA], h2=h, c2=col: e.scalar_tensor_tensor(
                self.w_Zh[s][:, :], b[:, 0:DV],
                self.b_negLam[:, c2:c2 + 1],
                self.a_v[:, tau * VD + h2 * DV: tau * VD + (h2 + 1) * DV],
                ALU.mult, ALU.add),
                waits=[("ct", mmz)] + self.snap("cv") + self.snap("ct"))
            self.bank_done(bA, "cv", vz)
            wW = self.bank(bE)
            mmw = self.T(lambda e, h2=h, b=pb[bE]: e.matmul(
                b[:, 0:DV],
                self.w_sv[s][:, h2 * 384 + 128: h2 * 384 + 256],
                self.w_Zh[s][:, :], start=True, stop=True),
                waits=wW + [("cv", vz)])
            sw = self.S(lambda e, b=pb[bE], h2=h: e.activation(
                self.w_W[s][:, h2 * DV:(h2 + 1) * DV], b[:, 0:DV], AF.Copy),
                waits=[("ct", mmw)] + self.snap("cs"))
            self.bank_done(bE, "cs", sw)
            wq = self.bank(bB)
            self.T(lambda e, h2=h, b=pb[bB], so=soff: e.matmul(
                b[:, 0:DV], fmp(qfm, h2, 0), self.st1[:, so:so + DV],
                start=True, stop=False), waits=wq)
            mmq = self.T(lambda e, h2=h, b=pb[bB], so=soff: e.matmul(
                b[:, 0:DV], fmp(qfm, h2, 1), self.st2[0:64, so:so + DV],
                start=False, stop=True))
            so1 = self.S(lambda e, b=pb[bB], c2=col: e.activation(
                self.w_O[s][:, :], b[:, 0:DV], AF.Identity,
                scale=self.b_Lam[:, c2:c2 + 1]), waits=[("ct", mmq)])
            self.bank_done(bB, "cs", so1)
            wo = self.bank(bB)
            mmo = self.T(lambda e, h2=h, b=pb[bB]: e.matmul(
                b[:, 0:DV], self.w_Pb[s][:, h2 * 128:(h2 + 1) * 128],
                self.w_W[s][:, h2 * DV:(h2 + 1) * DV], start=True, stop=True),
                waits=wo + [("cs", sw)])
            vo = self.V(lambda e, b=pb[bB]: e.tensor_tensor(
                self.w_O[s][:, :], self.w_O[s][:, :], b[:, 0:DV], ALU.add),
                waits=[("ct", mmo), ("cs", so1)])
            self.bank_done(bB, "cv", vo)
            wt = self.bank(bT)
            self.T(lambda e, b=pb[bT], so=soff: e.matmul(
                b[:, 0:DV], self.w_gI[s][:, :], self.st1[:, so:so + DV],
                start=True, stop=False), waits=wt + [("cs", sk)])
            mm1 = self.T(lambda e, b=pb[bT], h2=h: e.matmul(
                b[:, 0:DV], self.w_ksc[s][:, 0:128],
                self.w_W[s][:, h2 * DV:(h2 + 1) * DV], start=False, stop=True))
            bT2 = 6 + (1 - s)
            wt2 = self.bank(bT2)
            self.T(lambda e, b=pb[bT2], so=soff: e.matmul(
                b[0:64, 0:DV], self.w_gI[s][0:64, 0:64], self.st2[0:64, so:so + DV],
                start=True, stop=False), waits=wt2)
            mm2 = self.T(lambda e, b=pb[bT2], h2=h: e.matmul(
                b[0:64, 0:DV], self.w_ksc[s][:, 128:192],
                self.w_W[s][:, h2 * DV:(h2 + 1) * DV], start=False, stop=True))
            v1 = self.V(lambda e, b=pb[bT], so=soff: e.tensor_copy(
                self.st1[:, so:so + DV], b[:, 0:DV]), waits=[("ct", mm1)])
            self.bank_done(bT, "cv", v1)
            v2 = self.V(lambda e, b=pb[bT2], so=soff: e.tensor_copy(
                self.st2[0:64, so:so + DV], b[0:64, 0:DV]), waits=[("ct", mm2)])
            self.bank_done(bT2, "cv", v2)
            sq = self.S(lambda e: e.activation(
                self.w_osq[s][:, :], self.w_O[s][:, :], AF.Square,
                accum_out=self.b_ssq[s][:, 0:1]), waits=[("cv", vo)] + self.snap("cs"))
            sq = self.S(lambda e: e.activation(
                self.b_ssq[s][:, 1:2], self.b_ssq[s][:, 0:1], AF.Ln,
                scale=np.float32(1.0 / DV), bias=np.float32(EPS_RMS)), waits=[("cs", sq)])
            sq = self.S(lambda e: e.activation(
                self.b_ssq[s][:, 1:2], self.b_ssq[s][:, 1:2], AF.Exp, scale=np.float32(-0.5)),
                waits=[("cs", sq)])
            vg2 = self.V(lambda e: e.scalar_tensor_tensor(
                self.w_osq[s][:, :], self.w_O[s][:, :], self.b_ssq[s][:, 1:2],
                self.d_gnw[:, :], ALU.mult, ALU.mult),
                waits=[("cs", sq)] + self.snap("cv"))
            self.V(lambda e, h2=h, t2=tau: e.tensor_tensor(
                self.a_big[:, t2 * VD + h2 * DV: t2 * VD + (h2 + 1) * DV],
                self.w_osq[s][:, :],
                self.a_gs[:, t2 * VD + h2 * DV: t2 * VD + (h2 + 1) * DV], ALU.mult),
                waits=[("cv", vg2)])

    def p5_tile(self, g, tau):
        pb = self.pb
        s, c = divmod(tau, CPG)
        t0 = g * TPG
        xt = self.x2[:, tau * D: tau * D + D]
        qt0 = self.qt0
        W0 = self.snap("cv", "cs")
        v = self.V(lambda e: e.reduce_sum(self.b_stat[:, 4:5], xt, axis=AX.X,
                                          negate=True), waits=W0)
        v = self.V(lambda e: e.tensor_scalar(self.b_stat[:, 5:6], self.b_stat[:, 4:5],
                                             1.0 / D, None, ALU.mult),
                   waits=[("cv", v)])
        s1 = self.S(lambda e: e.activation(qt0[:, :], xt, AF.Identity,
                                           bias=self.b_stat[:, 5:6]),
                    waits=[("cv", v)] + self.snap("cs"))
        s1 = self.S(lambda e: e.activation(self.qt1, qt0[:, :], AF.Square,
                                           accum_out=self.b_stat[:, 6:7]),
                    waits=[("cs", s1)])
        s1 = self.S(lambda e: e.activation(self.b_stat[:, 7:8], self.b_stat[:, 6:7],
                                           AF.Ln, scale=np.float32(1.0 / D), bias=np.float32(EPS_LN)),
                    waits=[("cs", s1)])
        s1 = self.S(lambda e: e.activation(self.b_stat[:, 7:8], self.b_stat[:, 7:8],
                                           AF.Exp, scale=np.float32(-0.5)), waits=[("cs", s1)])
        s1 = self.S(lambda e: e.activation(qt0[:, :], qt0[:, :], AF.Identity,
                                           scale=self.b_stat[:, 7:8]),
                    waits=[("cs", s1)])
        v = self.V(lambda e: e.reduce_max(self.qs[:].unsqueeze(-1),
                                          qt0[:].rearrange("p (b k) -> p b k", k=32),
                                          axis=AX.X, apply_absolute_value=True),
                   waits=[("cs", s1)] + self.snap("cv"))
        v = self.V(lambda e: e.tensor_scalar(self.qsi[:], self.qs[:], 1e-30, None,
                                             ALU.max), waits=[("cv", v)])
        v = self.V(lambda e: e.tensor_scalar(self.qsi[:].bitcast(I32),
                                             self.qsi[:].bitcast(I32),
                                             self.c_pow2[:, 0:1], None, ALU.bitwise_and),
                   waits=[("cv", v)])
        v = self.V(lambda e: e.tensor_single_scalar(self.qm[:], self.qs[:], 0.0,
                                                    ALU.is_gt), waits=[("cv", v)])
        v = self.V(lambda e: e.tensor_scalar(self.qw[:], self.qsi[:], 0.25, None,
                                             ALU.mult), waits=[("cv", v)])
        v = self.V(lambda e: e.memset(self.qsi[:], 1.0), waits=[("cv", v)])
        v = self.V(lambda e: e.select(self.qs[:], self.qm[:], self.qw[:], self.qsi[:], add_drain=True),
                   waits=[("cv", v)])
        v = self.V(lambda e: e.tensor_scalar(self.qs[:], self.qs[:],
                                             self.d_dactp[:, 0:1], None, ALU.mult),
                   waits=[("cv", v)])
        v = self.V(lambda e: e.reciprocal(self.qsi[:], self.qs[:]), waits=[("cv", v)])
        vq = self.emit_quant_core(qt0[:], self.qt1, self.qt2, self.qs[:], self.qsi[:],
                                  self.a_xq[:, tau * D: tau * D + D], [("cv", v)])
        for kk in range(8):
            bi = (tau % 2) * 2 + (kk % 2)
            t = self.T(lambda e, k2=kk, b=pb[bi], t2=tau: e.transpose(
                b.ap().bitcast(BF16)[0:128, 0:128],
                self.a_xq[:, t2 * D + k2 * 128: t2 * D + (k2 + 1) * 128],
                self.c_idbf[:, :]), waits=[("cv", vq)] + self.bank(bi))
            vv = self.V(lambda e, k2=kk, b=pb[bi], s2=s, c2=c: e.tensor_copy(
                self.a_zs[:, k2 * 512 + s2 * 256 + c2 * 128:
                          k2 * 512 + s2 * 256 + c2 * 128 + 128],
                b.ap().bitcast(BF16)[0:128, 0:128]), waits=[("ct", t)])
            self.bank_done(bi, "cv", vv)
        xqfm_cv = self.now("cv")
        obuf = self.a_out[tau % 2]
        for half in range(2):
            bi = 4 + (tau % 2) * 2 + half
            mm = None
            for kc in range(8):
                lhs = self.a_zs[:, kc * 512 + s * 256 + c * 128:
                                kc * 512 + s * 256 + c * 128 + 128]
                w = ([("cv", xqfm_cv), self.ld_wq[0]] + self.bank(bi)) \
                    if kc == 0 else ([self.ld_wq[1]] if kc == 4 else [])
                mm = self.T(lambda e, l=lhs, b=pb[bi], k2=kc, h2=half: e.matmul(
                    b[:, 0:512], l,
                    self.wslot[k2 // 4][:, (k2 % 4) * 1024 + h2 * 512:
                                        (k2 % 4) * 1024 + h2 * 512 + 512],
                    start=(k2 == 0), stop=False), waits=w)
            mm = self.T(lambda e, b=pb[bi], h2=half: e.matmul(
                b[:, 0:512], self.c_ones1b[:, :],
                self.d_fbias[0:1, h2 * 512: h2 * 512 + 512],
                start=False, stop=True))
            vv = self.V(lambda e, b=pb[bi], h2=half, ob=obuf: e.tensor_tensor(
                ob[:, h2 * 512: h2 * 512 + 512],
                self.x2[:, tau * D + h2 * 512: tau * D + h2 * 512 + 512],
                b[:, 0:512], ALU.add),
                waits=[("ct", mm), (f"dst{tau % 2}", self.out_store[tau % 2])])
            self.bank_done(bi, "cv", vv)
        self.dma("sync", self.dout[s, t0 + c * 128: t0 + c * 128 + 128, :],
                 obuf, f"dst{tau % 2}", waits=self.snap("cv"))
        self.out_store[tau % 2] = self.now(f"dst{tau % 2}")


def kernel(**inputs):
    x = np.ascontiguousarray(inputs["L0_attn_x"], dtype=np.float32)
    convw = np.concatenate([np.ascontiguousarray(inputs["L0_attn_conv_x_w"][:, 0, :]),
                            np.ascontiguousarray(inputs["L0_attn_conv_z_w"][:, 0, :])],
                           axis=0)
    convb = np.concatenate([np.ascontiguousarray(inputs["L0_attn_conv_x_b"]),
                            np.ascontiguousarray(inputs["L0_attn_conv_z_b"])])[None, :]
    small = np.zeros((1, 16), np.float32)
    small[0, 0:4] = inputs["L0_attn_b_b"]
    small[0, 4:8] = inputs["L0_attn_A_log"]
    small[0, 8:12] = inputs["L0_attn_dt_bias"]
    small[0, 12:13] = inputs["L0_ffn_delta_act"]
    base = {
        "w_in": np.ascontiguousarray(inputs["L0_attn_W_in"]),
        "conv_w": convw,
        "conv_b": convb,
        "w_q": np.ascontiguousarray(inputs["L0_attn_W_q"]),
        "w_k": np.ascontiguousarray(inputs["L0_attn_W_k"]),
        "w_v": np.ascontiguousarray(inputs["L0_attn_W_v"]),
        "w_g": np.ascontiguousarray(inputs["L0_attn_W_g"]),
        "w_gkb": np.concatenate([np.ascontiguousarray(inputs["L0_attn_W_gk"]),
                                 np.ascontiguousarray(inputs["L0_attn_W_b"])], axis=1),
        "small_in": small,
        "g_norm_w": np.ascontiguousarray(inputs["L0_attn_g_norm_w"][None, :]),
        "w_so": np.ascontiguousarray(inputs["L0_attn_W_so"]),
        "w_out": np.ascontiguousarray(inputs["L0_attn_W_out"]),
        "ffn_w": np.ascontiguousarray(inputs["L0_ffn_weight"]),
        "ffn_b": np.ascontiguousarray(inputs["L0_ffn_bias"][None, :]),
        "ffn_dw": np.ascontiguousarray(inputs["L0_ffn_delta_w"]),
    }
    nc = B().build()
    in_maps = []
    for core in range(NCORES):
        m = dict(base)
        m["x"] = np.ascontiguousarray(x[core * SEQ:(core + 1) * SEQ])
        in_maps.append(m)
    res = run_bass_kernel_spmd(nc, in_maps, core_ids=list(range(NCORES)))
    return np.concatenate([r["out"] for r in res.results], axis=0)


# version 07
# speedup vs baseline: 1.1351x; 1.1351x over previous
import contextlib
import numpy as np
import ml_dtypes
import concourse.bass as bass
import concourse.mybir as mybir
from concourse.bass_utils import run_bass_kernel_spmd

F32 = mybir.dt.float32
F32R = mybir.dt.float32r
SOLVE_DT = F32R
BF16 = mybir.dt.bfloat16
I32 = mybir.dt.int32
AF = mybir.ActivationFunctionType
ALU = mybir.AluOpType
AX = mybir.AxisListType
BF = ml_dtypes.bfloat16

NCORES = 8
SEQ = 2
L = 2048
D = 1024
DI = 2048
KD = 768
VD = 1536
H = 4
DK = 192
DV = 384
C = 128
NG = 8
TPG = 256
CPG = TPG // C
NTAU = SEQ * CPG
EPS_LN = 1e-5
EPS_L2 = 1e-6
EPS_RMS = 1e-5
LN2C = float(np.log(2.0))
MAGIC = 12582912.0
STOP_PHASE = None
SKIP = set()
POW2MASK = float(-8388608)

ENGINES = ("sync", "scalar", "tensor", "vector", "gpsimd")


class B:

    def __init__(self):
        self.nc = bass.Bass("TRN2", target_bir_lowering=False)
        self.steps = {e: [] for e in ENGINES}
        self.counts = {}
        self.bank_free = [[] for _ in range(8)]

    def now(self, sem):
        return self.counts.get(sem, 0)

    def _step(self, eng, fn, waits, inc, amt):
        waits = [w for w in waits if w[1] > 0]
        val = None
        if inc is not None:
            self.counts[inc] = self.counts.get(inc, 0) + amt
            val = self.counts[inc]
        self.steps[eng].append(dict(fn=fn, waits=waits, inc=inc, amt=amt, val=val))
        return val

    def dma(self, queue, out_ap, in_ap, sem="dw", waits=(), slow=False):
        return self._step(queue,
                          lambda e, o=out_ap, i=in_ap, sl=slow: e.dma_start(
                              out=o, in_=i, allow_slow_non_contiguous=sl),
                          waits, sem, 16)

    def T(self, fn, waits=()):
        return self._step("tensor", fn, waits, "ct", 1)

    def V(self, fn, waits=()):
        return self._step("vector", fn, waits, "cv", 1)

    def S(self, fn, waits=()):
        return self._step("scalar", fn, waits, "cs", 1)

    def G(self, fn, waits=()):
        return self._step("gpsimd", fn, waits, "cg", 1)

    def snap(self, *sems):
        return [(s, self.now(s)) for s in sems]

    def all_compute(self):
        return self.snap("ct", "cv", "cs", "cg")

    def bank(self, i, extra=()):
        w = self.bank_free[i] + list(extra)
        self.bank_free[i] = []
        return w

    def bank_done(self, i, sem, val):
        self.bank_free[i].append((sem, val))

    def build(self):
        nc = self.nc
        self.dram()
        self.consts()
        self.es = contextlib.ExitStack()
        with self.es:
            self.alloc()
            self.prep()
            self.emit_ln1(0)
            for g in range(NG):
                self.group(g)
            self._step("sync", lambda e: e.nop(),
                       [("dst0", self.now("dst0")), ("dst1", self.now("dst1"))],
                       None, 0)
            compute_sems = ("ct", "cv", "cs", "cg")
            used = {cs_: set() for cs_ in compute_sems}
            for ename in ENGINES:
                for st in self.steps[ename]:
                    for (sname, val) in st["waits"]:
                        if sname in used:
                            used[sname].add(val)
            ranks = {cs_: {v: i + 1 for i, v in enumerate(sorted(used[cs_]))}
                     for cs_ in compute_sems}

            def remap_wait(sname, val):
                if sname not in ranks:
                    return val
                return ranks[sname][val]

            blk = self.es.enter_context(nc.Block())
            for ename in ENGINES:
                steps = self.steps[ename]
                sems = self.sems

                def emitter(eng, steps=steps, sems=sems):
                    for st in steps:
                        for (sname, val) in st["waits"]:
                            eng.wait_ge(sems[sname], remap_wait(sname, val))
                        ins = st["fn"](eng)
                        inc = st["inc"]
                        if inc is not None:
                            if inc in used:
                                if st["val"] in used[inc]:
                                    ins.then_inc(sems[inc], 1)
                            else:
                                ins.then_inc(sems[inc], st["amt"])

                getattr(blk, ename)(emitter)
        return nc

    def dram(self):
        nc = self.nc
        t = lambda n, s: nc.dram_tensor(n, s, F32, kind="ExternalInput").ap()
        self.dx = t("x", [SEQ, L, D])
        self.dwin = t("w_in", [D, DI])
        self.dconvw = t("conv_w", [2 * D, 4])
        self.dconvb = t("conv_b", [1, 2 * D])
        self.dwq = t("w_q", [D, KD])
        self.dwk = t("w_k", [D, KD])
        self.dwv = t("w_v", [D, VD])
        self.dwg = t("w_g", [D, VD])
        self.dwgkb = t("w_gkb", [D, 8])
        self.dsmallin = t("small_in", [1, 16])
        self.dgnw = t("g_norm_w", [1, DV])
        self.dwso = t("w_so", [VD, D])
        self.dwout = t("w_out", [DI, D])
        self.dfw = t("ffn_w", [D, D])
        self.dfb = t("ffn_b", [1, D])
        self.dfdw = t("ffn_dw", [D, 32])
        self.dout = nc.dram_tensor("out", [SEQ, L, D], F32, kind="ExternalOutput").ap()
        self.dwqt = nc.dram_tensor("wqt_scratch", [8, 128, D], BF16, kind="Internal").ap()

    def consts(self):
        nc = self.nc
        self.i_idf32 = nc.inline_tensor(np.eye(128, dtype=np.float32), "c_idf32")
        self.i_idbf = nc.inline_tensor(np.eye(128, dtype=np.float32).astype(BF), "c_idbf")
        self.i_i32rep = nc.inline_tensor(
            np.tile(np.eye(32, dtype=np.float32), (4, 1)).astype(BF), "c_i32rep")
        self.i_maskSN = nc.inline_tensor(
            -np.triu(np.ones((128, 128), dtype=np.float32), 1), "c_maskSN")
        self.i_maskI = nc.inline_tensor(
            np.triu(np.ones((128, 128), dtype=np.float32)), "c_maskI")
        self.i_i4bf = nc.inline_tensor(
            np.tile(np.eye(128, dtype=np.float32), (1, 4)).astype(BF), "c_i4bf")
        sel = np.zeros((4, H * 128), np.float32)
        for h in range(H):
            sel[h, h * 128:(h + 1) * 128] = 1.0
        self.i_sel4 = nc.inline_tensor(sel, "c_sel4")
        self.i_ones1f = nc.inline_tensor(np.ones((1, 128), np.float32), "c_ones1f")
        self.i_ones1b = nc.inline_tensor(np.ones((1, 128), np.float32).astype(BF),
                                         "c_ones1b")
        self.i_pow2 = nc.inline_tensor(np.full((128, 1), -8388608, np.int32), "c_pow2")

    def alloc(self):
        nc, es = self.nc, self.es
        sb = lambda n, s, d: es.enter_context(nc.sbuf_tensor(n, s, d))
        ps = lambda n: es.enter_context(nc.psum_tensor(n, [128, 512], F32))
        self.c_idf32 = sb("s_idf32", [128, 128], F32)
        self.c_idbf = sb("s_idbf", [128, 128], BF16)
        self.c_i32rep = sb("s_i32rep", [128, 32], BF16)
        self.c_maskSN = sb("s_maskSN", [128, 128], F32)
        self.c_maskI = sb("s_maskI", [128, 128], F32)
        self.c_i4bf = sb("s_i4bf", [128, 512], BF16)
        self.c_sel4 = sb("s_sel4", [4, H * 128], F32)
        self.c_ones1f = sb("s_ones1f", [1, 128], F32)
        self.c_ones1b = sb("s_ones1b", [1, 128], BF16)
        self.c_pow2 = sb("s_pow2", [128, 1], I32)
        self.c_idf32r = sb("s_idf32r", [128, 128], SOLVE_DT)
        self.d_convdiag = sb("d_convdiag", [128, 16 * 4 * 32], BF16)
        self.d_cb = sb("d_cb", [128, 16], F32)
        self.d_bb = sb("d_bb", [128, 4], F32)
        self.d_dtb = sb("d_dtb", [128, 4], F32)
        self.d_negAexp = sb("d_negAexp", [128, 4], F32)
        self.d_gnw = sb("d_gnw", [128, DV], F32)
        self.d_fbias = sb("d_fbias", [1, D], BF16)
        self.d_dactp = sb("d_dactp", [128, 1], F32)
        self.d_small = sb("d_small", [1, 64], F32)
        self.wslot = [sb(f"wslot{i}", [128, 4096], BF16) for i in range(2)]
        self.a_x = [sb(f"a_x{i}", [128, D], F32) for i in range(2)]
        self.a_xn = sb("a_xn", [128, 8 * 520], BF16)
        self.a_big = sb("a_big", [128, 8320], BF16)
        self.a_xs = sb("a_xs", [128, 6144], BF16)
        self.a_zs = sb("a_zs", [128, 4096], BF16)
        self.a_q = sb("a_q", [128, NTAU * KD], BF16)
        self.a_k = sb("a_k", [128, NTAU * KD], BF16)
        self.a_v = sb("a_v", [128, NTAU * VD], BF16)
        self.a_gs = sb("a_gs", [128, NTAU * VD], BF16)
        self.st1 = sb("st1", [128, SEQ * H * DV], BF16)
        self.st2 = sb("st2", [64, SEQ * H * DV], BF16)
        self.w_pre = sb("w_pre", [128, 128], F32)
        self.w_t1 = sb("w_t1", [128, 128], F32)
        self.w_sv = [sb(f"w_sv_{s}", [128, H * 384], SOLVE_DT) for s in range(SEQ)]
        self.w_DkT = [sb(f"w_DkT_{s}", [128, 512], SOLVE_DT) for s in range(SEQ)]
        self.w_Zh = [sb(f"w_Zh_{s}", [128, DV], SOLVE_DT) for s in range(SEQ)]
        self.w_Pb = [sb(f"w_Pb_{s}", [128, 512], BF16) for s in range(SEQ)]
        self.w_qfm = [sb(f"w_qfm_{s}", [128, H * 256], BF16) for s in range(SEQ)]
        self.w_kfm = [sb(f"w_kfm_{s}", [128, H * 256], BF16) for s in range(SEQ)]
        self.w_zw = sb("w_zw", [128, 6144], BF16)
        self.w_Z = [self.w_zw[:, s * (H * DV):(s + 1) * (H * DV)] for s in range(SEQ)]
        self.w_W = [self.w_zw[:, 3072 + s * (H * DV):3072 + (s + 1) * (H * DV)]
                    for s in range(SEQ)]
        self.r_q = self.w_zw[:, 0:NTAU * KD]
        self.r_k = self.w_zw[:, 3072:3072 + NTAU * KD]
        self.w_O = [sb(f"w_O_{s}", [128, DV], F32) for s in range(SEQ)]
        self.w_osq = [sb(f"w_osq_{s}", [128, DV], F32) for s in range(SEQ)]
        self.w_ksc = [sb(f"w_ksc_{s}", [128, DK], BF16) for s in range(SEQ)]
        self.w_gI = [sb(f"w_gI_{s}", [128, 128], BF16) for s in range(SEQ)]
        self.b_beta = sb("b_beta", [128, NTAU * H], F32)
        self.b_G = sb("b_G", [128, NTAU * H], F32)
        self.b_Lam = sb("b_Lam", [128, NTAU * H], F32)
        self.b_negLam = sb("b_negLam", [128, NTAU * H], F32)
        self.b_bias = sb("b_bias", [128, NTAU * H], F32)
        self.b_Growt = sb("b_Growt", [4, NTAU * 128], F32)
        self.b_lamb4 = [sb(f"b_lamb4_{s}", [128, H], F32) for s in range(SEQ)]
        self.b_gam4 = [sb(f"b_gam4_{s}", [128, H], F32) for s in range(SEQ)]
        self.b_rs = sb("b_rs", [128, H], F32)
        self.b_ssq = [sb(f"b_ssq_{s}", [128, 2], F32) for s in range(SEQ)]
        self.b_stat = sb("b_stat", [128, 8], F32)
        self.r_gkb = sb("r_gkb", [128, NTAU * 8], F32)
        self.p2t = sb("p2t", [128, 24], F32)
        self.carry = sb("carry", [128, 2 * 16], BF16)
        self.qt0 = sb("qt0", [128, D], F32)
        self.r_sq = self.qt0[:, 0:KD]
        self.qs = sb("qs", [128, 32], F32)
        self.qsi = sb("qsi", [128, 32], F32)
        self.qm = sb("qm", [128, 32], I32)
        self.qw = sb("qw", [128, 32], F32)
        xsf = self.a_xs.ap().bitcast(F32)
        self.qt1 = xsf[:, 0:D]
        self.qt2 = xsf[:, D:2 * D]
        self.a_xnt = self.a_xs[:, 4096:4096 + D]
        self.a_xq = self.a_gs
        bigf = self.a_big.ap().bitcast(F32)
        self.x2 = bigf[:, 0:NTAU * D]
        self.fb_stage = bigf[0:1, 0:D]
        self.fq = self.a_big[:, 4096:4096 + D]
        self.fdw = bigf[:, 3100:3132]
        self.d_cw = bigf[:, 3200:3264]
        xnf = self.a_xn.ap().bitcast(F32)
        self.a_out = [xnf[:, 0:D], xnf[:, D:2 * D]]
        self.pb = [ps(f"pb{i}") for i in range(8)]
        self.out_store = [0, 0]
        self.sems = {}
        for sname in ("dx", "dw0", "dw1", "dmisc", "dst0", "dst1", "ct", "cv", "cs", "cg"):
            self.sems[sname] = es.enter_context(nc.semaphore(sname))

    def prep(self):
        for dst, src in ((self.c_idf32, self.i_idf32), (self.c_idbf, self.i_idbf),
                         (self.c_i32rep, self.i_i32rep),
                         (self.c_maskSN, self.i_maskSN), (self.c_maskI, self.i_maskI),
                         (self.c_i4bf, self.i_i4bf), (self.c_sel4, self.i_sel4),
                         (self.c_ones1f, self.i_ones1f), (self.c_ones1b, self.i_ones1b),
                         (self.c_pow2, self.i_pow2)):
            self.dma("sync", dst[:], src.ap(), "dmisc")
        self.dma("sync", self.d_small[0:1, 0:16], self.dsmallin, "dmisc")
        self.dma("sync", self.d_cw.rearrange("p (c j) -> p c j", j=4),
                 self.dconvw.rearrange("(c p) j -> p c j", p=128), "dmisc", slow=True)
        self.dma("sync", self.d_cb[:].rearrange("p (c o) -> p c o", o=1),
                 self.dconvb.rearrange("o (c p) -> p c o", p=128), "dmisc", slow=True)
        self.dma("sync", self.d_gnw[0:1, :], self.dgnw, "dmisc")
        self.dma("sync", self.fb_stage, self.dfb, "dmisc")
        WM = [("dmisc", self.now("dmisc"))]
        pb = self.pb

        def bcast(bank_i, dst_ap, src_row, n, extraw=()):
            t = self.T(lambda e, b=pb[bank_i], s=src_row, n2=n: e.matmul(
                b[:, 0:n2], self.c_ones1f[:, :], s, start=True, stop=True),
                waits=WM + list(extraw) + self.bank(bank_i))
            v = self.V(lambda e, b=pb[bank_i], d2=dst_ap, n2=n: e.tensor_copy(
                d2, b[:, 0:n2]), waits=[("ct", t)])
            self.bank_done(bank_i, "cv", v)
            return v

        bcast(0, self.d_bb[:, :], self.d_small[0:1, 0:4], 4)
        bcast(1, self.d_dtb[:, :], self.d_small[0:1, 8:12], 4)
        sA = self.S(lambda e: e.activation(self.d_small[0:1, 16:20],
                                           self.d_small[0:1, 4:8], AF.Exp), waits=WM)
        tA = self.T(lambda e: e.matmul(pb[2][:, 0:4], self.c_ones1f[:, :],
                                       self.d_small[0:1, 16:20], start=True, stop=True),
                    waits=[("cs", sA)] + self.bank(2))
        vA = self.V(lambda e: e.tensor_scalar(self.d_negAexp[:, :], pb[2][:, 0:4],
                                              -1.0, None, ALU.mult), waits=[("ct", tA)])
        self.bank_done(2, "cv", vA)
        bcast(3, self.d_gnw[:, :], self.d_gnw[0:1, :], DV)
        self.S(lambda e: e.copy(self.c_idf32r[:, :], self.c_idf32[:, :]), waits=WM)
        self.S(lambda e: e.copy(self.d_fbias[:, :], self.fb_stage), waits=WM)
        sD = self.S(lambda e: e.activation(self.d_small[0:1, 20:21],
                                           self.d_small[0:1, 12:13], AF.Tanh,
                                           scale=np.float32(4.0)), waits=WM)
        sD = self.S(lambda e: e.activation(self.d_small[0:1, 21:22],
                                           self.d_small[0:1, 20:21], AF.Exp,
                                           scale=np.float32(0.5 * LN2C)), waits=[("cs", sD)])
        tD = self.T(lambda e: e.matmul(pb[4][:, 0:1], self.c_ones1f[:, :],
                                       self.d_small[0:1, 21:22], start=True, stop=True),
                    waits=[("cs", sD)] + self.bank(4))
        vD = self.V(lambda e: e.tensor_copy(self.d_dactp[:, :], pb[4][:, 0:1]),
                    waits=[("ct", tD)])
        self.bank_done(4, "cv", vD)
        for ch in range(16):
            for j in range(4):
                off = (ch * 4 + j) * 32
                self.S(lambda e, o=off, c2=ch, j2=j: e.activation(
                    self.d_convdiag[:, o:o + 32], self.c_i32rep[:, :], AF.Identity,
                    scale=self.d_cw[:, c2 * 4 + j2:c2 * 4 + j2 + 1]), waits=WM)
        for kc in range(8):
            self.quant_w_chunk(kc)
        self.G(lambda e: e.memset(self.st1[:, :], 0.0))
        self.G(lambda e: e.memset(self.st2[:, :], 0.0))
        self.G(lambda e: e.memset(self.carry[:, :], 0.0))

    def quant_w_chunk(self, kc):
        qt0, qs, qsi, qm = self.qt0, self.qs, self.qsi, self.qm
        self.dma("sync", qt0[:], self.dfw[kc * 128:(kc + 1) * 128, :], "dmisc",
                 waits=self.snap("cv", "cs"))
        self.dma("sync", self.fdw, self.dfdw[kc * 128:(kc + 1) * 128, :], "dmisc")
        WD = [("dmisc", self.now("dmisc")), ("cv", self.now("cv")),
              ("cs", self.now("cs"))]
        v = self.V(lambda e: e.reduce_max(qs[:].unsqueeze(-1),
                                          qt0[:].rearrange("p (b k) -> p b k", k=32),
                                          axis=AX.X, apply_absolute_value=True),
                   waits=WD)
        v = self.V(lambda e: e.tensor_scalar(qsi[:], qs[:], 1e-30, None, ALU.max),
                   waits=[("cv", v)])
        v = self.V(lambda e: e.tensor_scalar(qsi[:].bitcast(I32), qsi[:].bitcast(I32),
                                             self.c_pow2[:, 0:1], None, ALU.bitwise_and),
                   waits=[("cv", v)])
        v = self.V(lambda e: e.tensor_single_scalar(qm[:], qs[:], 0.0, ALU.is_gt),
                   waits=[("cv", v)])
        v = self.V(lambda e: e.tensor_scalar(self.qw[:], qsi[:], 0.25, None, ALU.mult),
                   waits=[("cv", v)])
        v = self.V(lambda e: e.memset(qsi[:], 1.0), waits=[("cv", v)])
        v = self.V(lambda e: e.select(qs[:], qm[:], self.qw[:], qsi[:], add_drain=True), waits=[("cv", v)])
        s = self.S(lambda e: e.activation(qsi[:], self.fdw, AF.Tanh, scale=np.float32(4.0)),
                   waits=WD + [("cv", v)])
        s = self.S(lambda e: e.activation(qsi[:], qsi[:], AF.Exp, scale=np.float32(0.5 * LN2C)),
                   waits=[("cs", s)])
        v = self.V(lambda e: e.tensor_tensor(qs[:], qs[:], qsi[:], ALU.mult),
                   waits=[("cs", s), ("cv", v)])
        v = self.V(lambda e: e.reciprocal(qsi[:], qs[:]), waits=[("cv", v)])
        vq = self.emit_quant_core(qt0[:], self.qt1, self.qt2, qs[:], qsi[:],
                                  self.fq[:, 0:D], [("cv", v)])
        for tt in range(8):
            bi = 5 + (tt % 2)
            t = self.T(lambda e, t2=tt, b=self.pb[bi]: e.transpose(
                b.ap().bitcast(BF16)[0:128, 0:128],
                self.fq[:, t2 * 128:(t2 + 1) * 128], self.c_idbf[:, :]),
                waits=[("cv", vq)] + self.bank(bi)
                + ([("dmisc", self.now("dmisc"))] if tt == 0 else []))
            vv = self.V(lambda e, t2=tt, b=self.pb[bi]: e.tensor_copy(
                self.a_big[:, 5120 + t2 * 128: 5120 + (t2 + 1) * 128],
                b.ap().bitcast(BF16)[0:128, 0:128]), waits=[("ct", t)])
            self.bank_done(bi, "cv", vv)
        for tt in range(8):
            self.dma("sync", self.dwqt[tt, :, kc * 128:(kc + 1) * 128],
                     self.a_big[:, 5120 + tt * 128: 5120 + (tt + 1) * 128], "dmisc",
                     waits=[("cv", self.now("cv"))] if tt == 0 else ())

    def emit_quant_core(self, x_in, tmp_sgn, tmp_ph, scale, inv_scale, out_bf, waits):
        V, S = self.V, self.S
        r32 = lambda ap: ap.rearrange("p (b k) -> p b k", k=32)
        bc = lambda ap: ap.unsqueeze(-1).broadcast_to((128, 32, 32))
        v = V(lambda e: e.tensor_tensor(r32(x_in), r32(x_in), bc(inv_scale),
                                        ALU.mult), waits=waits)
        v = V(lambda e: e.tensor_copy(tmp_sgn, x_in), waits=[("cv", v)])
        s = S(lambda e: e.activation(x_in, x_in, AF.Abs),
              waits=[("cv", v)] + self.snap("cs"))
        v = V(lambda e: e.tensor_scalar(x_in, x_in, 6.0, 1e-30, ALU.min, ALU.max),
              waits=[("cs", s)])
        v = V(lambda e: e.tensor_scalar(tmp_ph.bitcast(I32), x_in.bitcast(I32),
                                        self.c_pow2[:, 0:1], None, ALU.bitwise_and),
              waits=[("cv", v)])
        v = V(lambda e: e.tensor_scalar(tmp_ph, tmp_ph, 1.0, 4.0, ALU.max, ALU.min),
              waits=[("cv", v)])
        v = V(lambda e: e.tensor_scalar(tmp_ph, tmp_ph, MAGIC * 0.5, None,
                                        ALU.mult), waits=[("cv", v)])
        v = V(lambda e: e.tensor_tensor(x_in, x_in, tmp_ph, ALU.add),
              waits=[("cv", v)])
        v = V(lambda e: e.tensor_tensor(x_in, x_in, tmp_ph, ALU.subtract),
              waits=[("cv", v)])
        s = S(lambda e: e.sign(tmp_ph, tmp_sgn), waits=[("cv", v)] + self.snap("cs"))
        v = V(lambda e: e.tensor_tensor(x_in, x_in, tmp_ph, ALU.mult),
              waits=[("cs", s)])
        v = V(lambda e: e.tensor_tensor(r32(out_bf), r32(x_in), bc(scale),
                                        ALU.mult), waits=[("cv", v)])
        return v

    def emit_ln1(self, g):
        t0 = g * TPG
        last = (g == NG - 1)
        W0 = self.all_compute() + [("dmisc", self.now("dmisc")),
                                   ("dst0", self.now("dst0")),
                                   ("dst1", self.now("dst1"))]
        cnt = 0
        for s in range(SEQ):
            tiles = [(0, 128), (128, 128)] + ([] if last else [(256, 2)])
            for (toff, tn) in tiles:
                buf = self.a_x[cnt % 2]
                cnt += 1
                self.dma("sync", buf[0:tn, :],
                         self.dx[s, t0 + toff:t0 + toff + tn, :], "dx",
                         waits=self.snap("cv", "cs"))
                WX = [("dx", self.now("dx"))]
                v = self.V(lambda e, b=buf, n=tn: e.reduce_sum(
                    self.b_stat[0:n, 0:1], b[0:n, :], axis=AX.X, negate=True),
                    waits=WX + (W0 if cnt == 1 else []))
                v = self.V(lambda e, n=tn: e.tensor_scalar(
                    self.b_stat[0:n, 1:2], self.b_stat[0:n, 0:1], 1.0 / D, None,
                    ALU.mult), waits=[("cv", v)])
                s1 = self.S(lambda e, b=buf, n=tn: e.activation(
                    b[0:n, :], b[0:n, :], AF.Identity, bias=self.b_stat[0:n, 1:2]),
                    waits=[("cv", v)] + (W0 if cnt == 1 else []))
                s1 = self.S(lambda e, b=buf, n=tn: e.activation(
                    self.qt0[0:n, :], b[0:n, :], AF.Square,
                    accum_out=self.b_stat[0:n, 2:3]), waits=[("cs", s1)])
                s1 = self.S(lambda e, n=tn: e.activation(
                    self.b_stat[0:n, 3:4], self.b_stat[0:n, 2:3], AF.Ln,
                    scale=np.float32(1.0 / D), bias=np.float32(EPS_LN)), waits=[("cs", s1)])
                s1 = self.S(lambda e, n=tn: e.activation(
                    self.b_stat[0:n, 3:4], self.b_stat[0:n, 3:4], AF.Exp, scale=np.float32(-0.5)),
                    waits=[("cs", s1)])
                s1 = self.S(lambda e, b=buf, n=tn: e.activation(
                    self.a_xnt[0:n, :], b[0:n, :], AF.Identity,
                    scale=self.b_stat[0:n, 3:4]), waits=[("cs", s1)])
                for kc in range(8):
                    bi = 6 + (kc % 2)
                    t = self.T(lambda e, k2=kc, b=self.pb[bi], n=tn: e.transpose(
                        b.ap().bitcast(BF16)[0:128, 0:n],
                        self.a_xnt[0:n, k2 * 128:(k2 + 1) * 128],
                        self.c_idbf[0:n, 0:n]),
                        waits=[("cs", s1)] + self.bank(bi))
                    vv = self.V(lambda e, k2=kc, b=self.pb[bi], n=tn, s2=s, to=toff:
                                e.tensor_copy(
                        self.a_xn[:, k2 * 520 + s2 * 260 + to:
                                  k2 * 520 + s2 * 260 + to + n],
                        b.ap().bitcast(BF16)[0:128, 0:n]), waits=[("ct", t)])
                    self.bank_done(bi, "cv", vv)

    def load_piece(self, slot, specs, twait=None):
        w = [("ct", twait if twait is not None else self.now("ct"))]
        sem = f"dw{slot}"
        for (coff, ncols, src) in specs:
            self.dma("gpsimd", self.wslot[slot][:, coff:coff + ncols], src, sem,
                     waits=w)
            w = []
        return (sem, self.now(sem))

    def group(self, g):
        t0 = g * TPG
        last = (g == NG - 1)
        ncol = TPG if last else TPG + 2
        pb = self.pb

        xn_ready = self.snap("cv")
        g0_extra = [("dmisc", self.now("dmisc"))] if g == 0 else []
        ev_xz = None
        for piece in range(4):
            slot = piece % 2
            specs = [(kc * 512, 512,
                      self.dwin[kc * 128:(kc + 1) * 128, piece * 512:(piece + 1) * 512])
                     for kc in range(8)]
            ld = self.load_piece(slot, specs)
            for dsub in range(4):
                dchunk = piece * 4 + dsub
                for s in range(SEQ):
                    bi = (dsub * 2 + s) % 4
                    mm = None
                    for kc in range(8):
                        lhs = self.wslot[slot][:, kc * 512 + dsub * 128:
                                               kc * 512 + dsub * 128 + 128]
                        rhs = self.a_xn[:, kc * 520 + s * 260:
                                        kc * 520 + s * 260 + ncol]
                        w = ([ld] + xn_ready + self.bank(bi)) if kc == 0 else []
                        mm = self.T(lambda e, l=lhs, r=rhs, b=pb[bi], k2=kc, n=ncol:
                                    e.matmul(b[:, 0:n], l, r, start=(k2 == 0),
                                             stop=(k2 == 7)), waits=w)
                    ev_xz = self.S(lambda e, b=pb[bi], d2=dchunk, s2=s, n=ncol: e.copy(
                        self.a_big[:, d2 * 520 + s2 * 260 + 1:
                                   d2 * 520 + s2 * 260 + 1 + n], b[:, 0:n]),
                        waits=[("ct", mm)] + (g0_extra if dchunk == 0 else []))
                    self.bank_done(bi, "cs", ev_xz)
        cw = [("cs", ev_xz), ("cg", self.now("cg"))]
        big3 = self.a_big[:].rearrange("p (ch q) -> p ch q", q=520)
        for s in range(SEQ):
            self.V(lambda e, s2=s: e.tensor_copy(
                big3[:, :, s2 * 260: s2 * 260 + 1],
                self.carry[:, s2 * 16: s2 * 16 + 16].rearrange("p (c o) -> p c o", o=1)),
                waits=cw)
            if last:
                self.V(lambda e, s2=s: e.memset(
                    big3[:, :, s2 * 260 + 257: s2 * 260 + 260], 0.0), waits=cw)
        xz_ready_cv = self.now("cv")

        for dchunk in range(16):
            half = dchunk // 8
            ch = dchunk % 8
            dst = self.a_xs if half == 0 else self.a_zs
            for s in range(SEQ):
                bi = 4 + (dchunk * 2 + s) % 2
                mm = None
                for blk in range(4):
                    for j in range(4):
                        diag = self.d_convdiag[32 * blk:32 * blk + 32,
                                               (dchunk * 4 + j) * 32:
                                               (dchunk * 4 + j) * 32 + 32]
                        rhs = self.a_big[32 * blk:32 * blk + 32,
                                         dchunk * 520 + s * 260 + j:
                                         dchunk * 520 + s * 260 + j + TPG]
                        w = ([("cv", xz_ready_cv)] + self.bank(bi)) \
                            if (j == 0 and blk == 0) else []
                        mm = self.T(lambda e, d2=diag, r=rhs, b=pb[bi], j2=j, bl=blk:
                                    e.matmul(b[32 * bl:32 * bl + 32, 0:TPG], d2, r,
                                             start=(j2 == 0), stop=(j2 == 3),
                                             tile_position=(32 * bl, 32 * bl)),
                                    waits=w)
                sc = self.S(lambda e, b=pb[bi], c2=ch, s2=s, h2=half, d2=dst:
                            e.activation(
                    d2[:, c2 * 512 + s2 * 256: c2 * 512 + s2 * 256 + TPG],
                    b[:, 0:TPG], AF.Silu,
                    bias=self.d_cb[:, h2 * 8 + c2: h2 * 8 + c2 + 1]),
                    waits=[("ct", mm)])
                self.bank_done(bi, "cs", sc)
        conv_ct = self.now("ct")
        if not last:
            for s in range(SEQ):
                self.V(lambda e, s2=s: e.tensor_copy(
                    self.carry[:, s2 * 16: s2 * 16 + 16]
                    .rearrange("p (c o) -> p c o", o=1),
                    big3[:, :, s2 * 260 + 256: s2 * 260 + 257]),
                    waits=[("ct", conv_ct)])

        if STOP_PHASE == "p1":
            return
        self.p2(g)

        so_specs = []
        for piece in range(4):
            khalf, mhalf = divmod(piece, 2)
            so_specs.append([(kc6 * 512, 512,
                              self.dwso[(khalf * 6 + kc6) * 128:
                                        (khalf * 6 + kc6 + 1) * 128,
                                        mhalf * 512:(mhalf + 1) * 512])
                             for kc6 in range(6)])
        ld_so = [None] * 4
        ld_so[0] = self.load_piece(0, so_specs[0])
        ld_so[2] = self.load_piece(1, so_specs[2])

        if STOP_PHASE == "p2":
            return
        for c in range(CPG):
            for s in range(SEQ):
                self.scan_chunk(g, s, c)
        p3_cv, p3_cs = self.now("cv"), self.now("cs")

        if STOP_PHASE == "p3":
            return
        for tau in range(NTAU):
            s, c = divmod(tau, CPG)
            for kk in range(12):
                bi = 6 + (kk % 2)
                t = self.T(lambda e, t2=tau, k2=kk, b=pb[bi]: e.transpose(
                    b.ap().bitcast(BF16)[0:128, 0:128],
                    self.a_big[:, t2 * VD + k2 * 128: t2 * VD + (k2 + 1) * 128],
                    self.c_idbf[:, :]),
                    waits=(([("cv", p3_cv), ("cs", p3_cs)] if (tau == 0 and kk == 0)
                            else []) + self.bank(bi)))
                vv = self.V(lambda e, k2=kk, b=pb[bi], s2=s, c2=c: e.tensor_copy(
                    self.a_xs[:, k2 * 512 + s2 * 256 + c2 * 128:
                              k2 * 512 + s2 * 256 + c2 * 128 + 128],
                    b.ap().bitcast(BF16)[0:128, 0:128]), waits=[("ct", t)])
                self.bank_done(bi, "cv", vv)
        p4a_cv = self.now("cv")

        p4b_cs = None
        for mhalf in range(2):
            if mhalf == 1:
                ld_so[1] = self.load_piece(0, so_specs[1])
                ld_so[3] = self.load_piece(1, so_specs[3])
            for msub in range(4):
                m = mhalf * 4 + msub
                bi = msub % 4
                mm = None
                for kc in range(12):
                    khalf, kc6 = divmod(kc, 6)
                    piece = khalf * 2 + mhalf
                    slot = 0 if khalf == 0 else 1
                    w = []
                    if kc == 0:
                        w = [ld_so[piece], ("cv", p4a_cv)] + self.bank(bi)
                    elif kc6 == 0:
                        w = [ld_so[piece]]
                    mm = self.T(lambda e, k6=kc6, sl=slot, m2=msub, b=pb[bi], k2=kc:
                                e.matmul(
                        b[:, 0:512],
                        self.wslot[sl][:, k6 * 512 + m2 * 128:
                                       k6 * 512 + (m2 + 1) * 128],
                        self.a_xs[:, k2 * 512: k2 * 512 + 512],
                        start=(k2 == 0), stop=(k2 == 11)), waits=w)
                p4b_cs = self.S(lambda e, m2=m, b=pb[bi]: e.activation(
                    self.a_v[:, m2 * 512: m2 * 512 + 512], b[:, 0:512], AF.Copy),
                    waits=[("ct", mm)])
                self.bank_done(bi, "cs", p4b_cs)

        for half in range(2):
            lds = []
            for khalf in range(2):
                specs = [(kc * 512, 512,
                          self.dwout[(khalf * 8 + kc) * 128:(khalf * 8 + kc + 1) * 128,
                                     half * 512:(half + 1) * 512])
                         for kc in range(8)]
                lds.append(self.load_piece(khalf, specs))
            for tau in range(NTAU):
                s, c = divmod(tau, CPG)
                xbuf = self.a_x[tau % 2]
                self.dma("sync", xbuf[:, 0:512],
                         self.dx[s, t0 + c * 128: t0 + c * 128 + 128,
                                 half * 512:(half + 1) * 512],
                         "dx", waits=self.snap("cv"))
                WX = [("dx", self.now("dx"))]
                bi = 4 + (tau % 2)
                mm = None
                for khalf in range(2):
                    src = self.a_v if khalf == 0 else self.a_zs
                    for kc in range(8):
                        lhs = src[:, kc * 512 + s * 256 + c * 128:
                                  kc * 512 + s * 256 + c * 128 + 128]
                        w = ([lds[khalf], ("cs", p4b_cs)] + self.bank(bi)) \
                            if kc == 0 else []
                        mm = self.T(lambda e, l=lhs, b=pb[bi], k2=kc, kh=khalf:
                                    e.matmul(
                            b[:, 0:512], l,
                            self.wslot[kh][:, k2 * 512:(k2 + 1) * 512],
                            start=(kh == 0 and k2 == 0), stop=(kh == 1 and k2 == 7)),
                            waits=w)
                vv = self.V(lambda e, b=pb[bi], t2=tau, h2=half, bx=xbuf:
                            e.tensor_tensor(
                    self.x2[:, t2 * D + h2 * 512: t2 * D + h2 * 512 + 512],
                    bx[:, 0:512], b[:, 0:512], ALU.add),
                    waits=[("ct", mm)] + WX)
                self.bank_done(bi, "cv", vv)

        if STOP_PHASE == "p4":
            return
        self.ld_wq = []
        for half in range(2):
            specs = [(tt * 1024, 1024, self.dwqt[half * 4 + tt, :, :])
                     for tt in range(4)]
            self.ld_wq.append(self.load_piece(half, specs))
        for tau in range(NTAU):
            self.p5_tile(g, tau)

        if not last:
            self.emit_ln1(g + 1)

    def p2(self, g):
        pb = self.pb

        def xs_lhs(kc, tau):
            s, c = divmod(tau, CPG)
            return self.a_xs[:, kc * 512 + s * 256 + c * 128:
                             kc * 512 + s * 256 + c * 128 + 128]

        def project(dten, outcols, evac_fn):
            npieces = (outcols + 511) // 512
            for piece in range(npieces):
                pcols = min(512, outcols - piece * 512)
                slot = piece % 2
                specs = [(kc * pcols, pcols,
                          dten[kc * 128:(kc + 1) * 128,
                               piece * 512: piece * 512 + pcols])
                         for kc in range(8)]
                ld = self.load_piece(slot, specs)
                for tau in range(NTAU):
                    bi = tau % 4
                    mm = None
                    for kc in range(8):
                        w = ([ld] + self.bank(bi)) if kc == 0 else []
                        mm = self.T(lambda e, k2=kc, b=pb[bi], t2=tau, sl=slot,
                                    pc=pcols: e.matmul(
                            b[:, 0:pc], xs_lhs(k2, t2),
                            self.wslot[sl][:, k2 * pc: k2 * pc + pc],
                            start=(k2 == 0), stop=(k2 == 7)), waits=w)
                    sem, ev = evac_fn(piece, pcols, tau, bi, mm)
                    self.bank_done(bi, sem, ev)

        def mk_evac(dst, width, func):
            def f(piece, pcols, tau, bi, mm):
                e1 = self.S(lambda e, b=pb[bi], t2=tau, p2=piece, pc=pcols:
                            e.activation(
                    dst[:, t2 * width + p2 * 512: t2 * width + p2 * 512 + pc],
                    b[:, 0:pc], func), waits=[("ct", mm)])
                return ("cs", e1)
            return f

        project(self.dwv, VD, mk_evac(self.a_v, VD, AF.Copy))
        project(self.dwg, VD, mk_evac(self.a_gs, VD, AF.Silu))
        project(self.dwq, KD, mk_evac(self.r_q, KD, AF.Copy))
        project(self.dwk, KD, mk_evac(self.r_k, KD, AF.Copy))

        def gkb_evac(piece, pcols, tau, bi, mm):
            v = self.V(lambda e, b=pb[bi], t2=tau: e.tensor_copy(
                self.r_gkb[:, t2 * 8: t2 * 8 + 8], b[:, 0:8]), waits=[("ct", mm)])
            return ("cv", v)

        project(self.dwgkb, 8, gkb_evac)

        p2t = self.p2t
        for tau in range(NTAU):
            v = self.V(lambda e, t2=tau: e.tensor_tensor(
                p2t[:, 0:4], self.r_gkb[:, t2 * 8 + 4: t2 * 8 + 8],
                self.d_bb[:, :], ALU.add), waits=self.snap("cv"))
            s = self.S(lambda e: e.activation(p2t[:, 4:8], p2t[:, 0:4],
                                              AF.Exp, scale=np.float32(-1.0)),
                       waits=[("cv", v)] + self.snap("cs"))
            v = self.V(lambda e: e.tensor_scalar(p2t[:, 4:8], p2t[:, 4:8],
                                                 1.0, None, ALU.add), waits=[("cs", s)])
            v = self.V(lambda e, t2=tau: e.reciprocal(
                self.b_beta[:, t2 * H: t2 * H + H], p2t[:, 4:8]), waits=[("cv", v)])
            v2 = self.V(lambda e, t2=tau: e.tensor_tensor(
                p2t[:, 8:12], self.r_gkb[:, t2 * 8: t2 * 8 + 4],
                self.d_dtb[:, :], ALU.add), waits=[("cv", v)])
            s = self.S(lambda e: e.activation(p2t[:, 12:16], p2t[:, 8:12], AF.Abs),
                       waits=[("cv", v2)])
            s = self.S(lambda e: e.activation(p2t[:, 12:16], p2t[:, 12:16],
                                              AF.Exp, scale=np.float32(-1.0)), waits=[("cs", s)])
            s = self.S(lambda e: e.activation(p2t[:, 12:16], p2t[:, 12:16],
                                              AF.Ln, bias=np.float32(1.0)), waits=[("cs", s)])
            s = self.S(lambda e: e.activation(p2t[:, 16:20], p2t[:, 8:12], AF.Relu),
                       waits=[("cs", s)])
            v2 = self.V(lambda e: e.tensor_tensor(p2t[:, 12:16], p2t[:, 12:16],
                                                  p2t[:, 16:20], ALU.add),
                        waits=[("cs", s)])
            v2 = self.V(lambda e: e.tensor_tensor(p2t[:, 12:16], p2t[:, 12:16],
                                                  self.d_negAexp[:, :], ALU.mult),
                        waits=[("cv", v2)])
            bi = 4 + (tau % 2)
            t = self.T(lambda e, b=pb[bi]: e.matmul(
                b[:, 0:4], self.c_maskI[:, :], p2t[:, 12:16], start=True, stop=True),
                waits=[("cv", v2)] + self.bank(bi))
            v3 = self.V(lambda e, b=pb[bi], t2=tau: e.tensor_copy(
                self.b_G[:, t2 * H: t2 * H + H], b[:, 0:4]), waits=[("ct", t)])
            t2i = self.T(lambda e, b=pb[bi], t2=tau: e.transpose(
                b[0:4, 128:256], self.b_G[:, t2 * H: t2 * H + H], self.c_idf32[:, :]),
                waits=[("cv", v3)])
            v3 = self.V(lambda e, b=pb[bi], t2=tau: e.tensor_copy(
                self.b_Growt[0:4, t2 * 128:(t2 + 1) * 128], b[0:4, 128:256]),
                waits=[("ct", t2i)])
            self.bank_done(bi, "cv", v3)
        bw = self.snap("cv")
        s = self.S(lambda e: e.activation(self.b_Lam[:, :], self.b_G[:, :], AF.Exp),
                   waits=bw + self.snap("cs"))
        self.V(lambda e: e.tensor_scalar(self.b_negLam[:, :], self.b_Lam[:, :],
                                         -1.0, None, ALU.mult), waits=[("cs", s)])
        s = self.S(lambda e: e.activation(self.b_bias[:, :], self.b_beta[:, :], AF.Ln),
                   waits=bw)
        self.V(lambda e: e.tensor_tensor(self.b_bias[:, :], self.b_bias[:, :],
                                         self.b_G[:, :], ALU.subtract),
               waits=[("cs", s)])
        for (raw, dst) in ((self.r_q, self.a_q), (self.r_k, self.a_k)):
            for tau in range(NTAU):
                s = self.S(lambda e, r2=raw, t2=tau: e.activation(
                    self.r_sq, r2[:, t2 * KD:(t2 + 1) * KD], AF.Square),
                    waits=self.snap("cs"))
                v = self.V(lambda e: e.reduce_sum(
                    self.b_rs[:, :].unsqueeze(-1),
                    self.r_sq.rearrange("p (h d) -> p h d", d=DK), axis=AX.X),
                    waits=[("cs", s)] + self.snap("cv"))
                s = self.S(lambda e: e.activation(self.b_rs[:, :], self.b_rs[:, :],
                                                  AF.Ln, bias=np.float32(EPS_L2)),
                           waits=[("cv", v)])
                s = self.S(lambda e: e.activation(self.b_rs[:, :], self.b_rs[:, :],
                                                  AF.Exp, scale=np.float32(-0.5)),
                           waits=[("cs", s)])
                for h in range(H):
                    s = self.S(lambda e, r2=raw, d2=dst, t2=tau, h2=h: e.activation(
                        d2[:, t2 * KD + h2 * DK: t2 * KD + (h2 + 1) * DK],
                        r2[:, t2 * KD + h2 * DK: t2 * KD + (h2 + 1) * DK],
                        AF.Identity, scale=self.b_rs[:, h2:h2 + 1]), waits=[("cs", s)])

    def scan_chunk(self, g, s, c):
        pb = self.pb
        tau = s * CPG + c
        qk_ready = self.snap("cs", "cv")
        bA, bB = s * 2, s * 2 + 1
        bE = 4 + s
        bT = 6 + s

        for (src, dstfm) in ((self.a_q, self.w_qfm[s]), (self.a_k, self.w_kfm[s])):
            tl = None
            for h in range(H):
                for (p0, fn, oo) in ((0, 128, 0), (128, 64, 128)):
                    tl = self.T(lambda e, h2=h, p2=p0, f2=fn, o2=oo, b=pb[bT], s2=src:
                                e.transpose(
                        b.ap().bitcast(BF16)[0:f2, h2 * 256 + o2: h2 * 256 + o2 + 128],
                        s2[:, tau * KD + h2 * DK + p2: tau * KD + h2 * DK + p2 + f2],
                        self.c_idbf[:, :]),
                        waits=(qk_ready + self.bank(bT)) if (h == 0 and p0 == 0) else [])
            pv = pb[bT].ap().bitcast(BF16).rearrange("p (h q) -> p h q", q=256)
            dv = dstfm[:].rearrange("p (h q) -> p h q", q=256)
            self.V(lambda e, pv2=pv, dv2=dv: e.tensor_copy(
                dv2[0:128, :, 0:128], pv2[0:128, :, 0:128]), waits=[("ct", tl)])
            vv = self.V(lambda e, pv2=pv, dv2=dv: e.tensor_copy(
                dv2[0:64, :, 128:256], pv2[0:64, :, 128:256]))
            self.bank_done(bT, "cv", vv)
        kfm, qfm = self.w_kfm[s], self.w_qfm[s]
        fm_cv = self.now("cv")
        if STOP_PHASE == "sA":
            return

        def fmp(buf, h, piece):
            if piece == 0:
                return buf[:, h * 256: h * 256 + 128]
            return buf[0:64, h * 256 + 128: h * 256 + 256]

        mmA = mmB = None
        for h in range(H):
            w = ([("cv", fm_cv)] + self.bank(bA)) if h == 0 else []
            self.T(lambda e, h2=h, b=pb[bA]: e.matmul(
                b[:, h2 * 128:(h2 + 1) * 128], fmp(kfm, h2, 0), fmp(kfm, h2, 0),
                start=True, stop=False), waits=w)
            mmA = self.T(lambda e, h2=h, b=pb[bA]: e.matmul(
                b[:, h2 * 128:(h2 + 1) * 128], fmp(kfm, h2, 1), fmp(kfm, h2, 1),
                start=False, stop=True))
        for h in range(H):
            w = self.bank(bB) if h == 0 else []
            self.T(lambda e, h2=h, b=pb[bB]: e.matmul(
                b[:, h2 * 128:(h2 + 1) * 128], fmp(kfm, h2, 0), fmp(qfm, h2, 0),
                start=True, stop=False), waits=w)
            mmB = self.T(lambda e, h2=h, b=pb[bB]: e.matmul(
                b[:, h2 * 128:(h2 + 1) * 128], fmp(kfm, h2, 1), fmp(qfm, h2, 1),
                start=False, stop=True))

        if STOP_PHASE == "sB":
            return
        prev_readers = []
        for h in range(H):
            col = tau * H + h
            w = (self.bank(bE) if h == 0 else []) + prev_readers
            t = self.T(lambda e, h2=h, b=pb[bE], t2=tau: e.matmul(
                b[:, 0:128], self.c_sel4[:, h2 * 128:(h2 + 1) * 128],
                self.b_Growt[0:4, t2 * 128:(t2 + 1) * 128],
                start=True, stop=True), waits=w + ([("ct", mmB)] if h == 0 else []))
            v = self.V(lambda e, b=pb[bE], c2=col: e.tensor_scalar(
                self.w_t1[:, 0:1], b[:, 127:128], self.b_G[:, c2:c2 + 1], None,
                ALU.subtract), waits=[("ct", t)] + self.snap("cv"))
            sx = self.S(lambda e, h2=h: e.activation(
                self.b_lamb4[s][:, h2:h2 + 1], self.w_t1[:, 0:1], AF.Exp),
                waits=[("cv", v)] + self.snap("cs"))
            sx2 = self.S(lambda e, b=pb[bE], h2=h: e.activation(
                self.b_gam4[s][:, h2:h2 + 1], b[:, 127:128], AF.Exp),
                waits=[("ct", t)])
            v = self.V(lambda e, h2=h, c2=col: e.tensor_tensor(
                self.b_lamb4[s][:, h2:h2 + 1], self.b_lamb4[s][:, h2:h2 + 1],
                self.b_beta[:, c2:c2 + 1], ALU.mult), waits=[("cs", sx)])
            vpre = self.V(lambda e, b=pb[bE], c2=col: e.tensor_scalar(
                self.w_pre[:, :], b[:, 0:128], self.b_bias[:, c2:c2 + 1], 0.0,
                ALU.add, ALU.min), waits=[("ct", t)] + self.snap("cv"))
            prev_readers = [("cv", vpre), ("cs", sx2)]
            se = self.S(lambda e: e.activation(self.w_pre[:, :], self.w_pre[:, :],
                                               AF.Exp), waits=[("cv", vpre), ("cs", sx2)])
            v = self.V(lambda e: e.tensor_tensor(self.w_t1[:, :], self.w_pre[:, :],
                                                 self.c_maskSN[:, :], ALU.mult),
                       waits=[("cs", se)])
            vD = self.V(lambda e, h2=h, b=pb[bA]: e.tensor_tensor(
                self.w_sv[s][:, h2 * 384: h2 * 384 + 128], self.w_t1[:, :],
                b[:, h2 * 128:(h2 + 1) * 128], ALU.mult),
                waits=[("cv", v), ("ct", mmA)])
            self.V(lambda e, h2=h: e.tensor_copy(
                self.w_sv[s][:, h2 * 384 + 128: h2 * 384 + 256], self.c_idf32[:, :]))
            v = self.V(lambda e: e.tensor_tensor(self.w_t1[:, :], self.w_pre[:, :],
                                                 self.c_maskI[:, :], ALU.mult),
                       waits=[("cv", vD)])
            vP = self.V(lambda e, h2=h, b=pb[bB]: e.tensor_tensor(
                self.w_Pb[s][:, h2 * 128:(h2 + 1) * 128], self.w_t1[:, :],
                b[:, h2 * 128:(h2 + 1) * 128], ALU.mult), waits=[("cv", v)])
            if h == H - 1:
                self.bank_done(bA, "cv", vD)
                self.bank_done(bB, "cv", vP)
                self.bank_done(bE, "cv", vP)
                self.bank_done(bE, "cs", sx2)
        d0_cv = self.now("cv")

        if STOP_PHASE == "sC":
            return
        d0_cv2 = self.now("cv")
        tl = None
        for h in range(H):
            tl = self.T(lambda e, h2=h, b=pb[bT]: e.transpose(
                b.ap().bitcast(SOLVE_DT)[0:128, h2 * 128:(h2 + 1) * 128],
                self.w_sv[s][:, h2 * 384: h2 * 384 + 128], self.c_idf32r[:, :]),
                waits=([("cv", d0_cv2)] + self.bank(bT)) if h == 0 else [])
        dkt_ev = self.V(lambda e, b=pb[bT]: e.tensor_copy(
            self.w_DkT[s][:, :], b.ap().bitcast(SOLVE_DT)[0:128, 0:512]),
            waits=[("ct", tl)])
        self.bank_done(bT, "cv", dkt_ev)
        x_ready = self.now("cv")
        if STOP_PHASE == "sD0":
            return
        for r in range(1, 8):
            cur_a = (r % 2 == 1)
            roff = 0 if cur_a else 128
            dk_psoff = 0 if cur_a else 128
            x_psoff = 128 - dk_psoff
            dk_dst = 256 if cur_a else 0
            mq = None
            wts = [("cv", dkt_ev), ("cv", x_ready)]
            for h in range(H):
                bi2 = bA if h < 2 else bB
                w2 = (wts + self.bank(bA) + self.bank(bB)) if h == 0 else []
                mq = self.T(lambda e, h2=h, b=pb[bi2], ro=roff: e.matmul(
                    b[:, (h2 % 2) * 256: (h2 % 2) * 256 + 256],
                    self.w_DkT[s][:, h2 * 128:(h2 + 1) * 128],
                    self.w_sv[s][:, h2 * 384 + ro: h2 * 384 + ro + 256],
                    start=True, stop=True), waits=w2)
            sv3 = self.w_sv[s][:].rearrange("p (h q) -> p h q", q=384)
            ev1 = None
            if r < 7 and "ev1" not in SKIP:
                for half in range(2):
                    b2 = pb[bA] if half == 0 else pb[bB]
                    pv = b2.ap().rearrange("p (h q) -> p h q", q=256)
                    ev1 = self.V(lambda e, pv2=pv, hf=half, dd=dk_dst, dp=dk_psoff:
                                 e.tensor_copy(
                        sv3[:, 2 * hf: 2 * hf + 2, dd: dd + 128],
                        pv2[:, 0:2, dp: dp + 128]),
                        waits=([("ct", mq)] + self.snap("cv")) if half == 0
                        else [])
            for half in range(2 if "ev2" not in SKIP else 0):
                b2 = pb[bA] if half == 0 else pb[bB]
                pv = b2.ap().rearrange("p (h q) -> p h q", q=256)
                ev2 = self.V(lambda e, pv2=pv, hf=half, xp=x_psoff: e.tensor_tensor(
                    sv3[:, 2 * hf: 2 * hf + 2, 128:256],
                    sv3[:, 2 * hf: 2 * hf + 2, 128:256],
                    pv2[:, 0:2, xp: xp + 128], ALU.add),
                    waits=([("ct", mq)] + self.snap("cv")) if half == 0 else [])
            x_ready = ev2
            self.bank_done(bA, "cv", ev2)
            self.bank_done(bB, "cv", ev2)
            if ev1 is not None:
                self.bank_done(bA, "cv", ev1)
                self.bank_done(bB, "cv", ev1)
            if r < 7 and "tr" not in SKIP:
                tl = None
                for h in range(H):
                    tl = self.T(lambda e, h2=h, b=pb[bT], dd=dk_dst: e.transpose(
                        b.ap().bitcast(SOLVE_DT)[0:128, h2 * 128:(h2 + 1) * 128],
                        self.w_sv[s][:, h2 * 384 + dd: h2 * 384 + dd + 128],
                        self.c_idf32r[:, :]),
                        waits=([("cv", ev1)] + self.bank(bT)) if h == 0 else [])
                dkt_ev = self.V(lambda e, b=pb[bT]: e.tensor_copy(
                    self.w_DkT[s][:, :], b.ap().bitcast(SOLVE_DT)[0:128, 0:512]),
                    waits=[("ct", tl)] + self.snap("cv"))
                self.bank_done(bT, "cv", dkt_ev)
            if STOP_PHASE == "sD1" and r == 1:
                return

        if STOP_PHASE == "sD":
            return
        for h in range(H):
            col = tau * H + h
            soff = (s * H + h) * DV
            sg = self.S(lambda e, h2=h: e.activation(
                self.w_gI[s][:, :], self.c_idbf[:, :], AF.Identity,
                scale=self.b_gam4[s][:, h2:h2 + 1]),
                waits=self.snap("cs") + self.snap("ct"))
            sk = self.S(lambda e, h2=h: e.activation(
                self.w_ksc[s][:, :],
                self.a_k[:, tau * KD + h2 * DK: tau * KD + (h2 + 1) * DK],
                AF.Identity, scale=self.b_lamb4[s][:, h2:h2 + 1]), waits=[("cs", sg)])
            w = self.bank(bA) + ([("cv", x_ready)] if h == 0 else [])
            self.T(lambda e, h2=h, b=pb[bA], so=soff: e.matmul(
                b[:, 0:DV], fmp(kfm, h2, 0), self.st1[:, so:so + DV],
                start=True, stop=False), waits=w)
            mmz = self.T(lambda e, h2=h, b=pb[bA], so=soff: e.matmul(
                b[:, 0:DV], fmp(kfm, h2, 1), self.st2[0:64, so:so + DV],
                start=False, stop=True))
            vz = self.V(lambda e, b=pb[bA], h2=h, c2=col: e.scalar_tensor_tensor(
                self.w_Zh[s][:, :], b[:, 0:DV],
                self.b_negLam[:, c2:c2 + 1],
                self.a_v[:, tau * VD + h2 * DV: tau * VD + (h2 + 1) * DV],
                ALU.mult, ALU.add),
                waits=[("ct", mmz)] + self.snap("cv") + self.snap("ct"))
            self.bank_done(bA, "cv", vz)
            wW = self.bank(bE)
            mmw = self.T(lambda e, h2=h, b=pb[bE]: e.matmul(
                b[:, 0:DV],
                self.w_sv[s][:, h2 * 384 + 128: h2 * 384 + 256],
                self.w_Zh[s][:, :], start=True, stop=True),
                waits=wW + [("cv", vz)])
            sw = self.S(lambda e, b=pb[bE], h2=h: e.activation(
                self.w_W[s][:, h2 * DV:(h2 + 1) * DV], b[:, 0:DV], AF.Copy),
                waits=[("ct", mmw)] + self.snap("cs"))
            self.bank_done(bE, "cs", sw)
            wq = self.bank(bB)
            self.T(lambda e, h2=h, b=pb[bB], so=soff: e.matmul(
                b[:, 0:DV], fmp(qfm, h2, 0), self.st1[:, so:so + DV],
                start=True, stop=False), waits=wq)
            mmq = self.T(lambda e, h2=h, b=pb[bB], so=soff: e.matmul(
                b[:, 0:DV], fmp(qfm, h2, 1), self.st2[0:64, so:so + DV],
                start=False, stop=True))
            so1 = self.S(lambda e, b=pb[bB], c2=col: e.activation(
                self.w_O[s][:, :], b[:, 0:DV], AF.Identity,
                scale=self.b_Lam[:, c2:c2 + 1]), waits=[("ct", mmq)])
            self.bank_done(bB, "cs", so1)
            wo = self.bank(bB)
            mmo = self.T(lambda e, h2=h, b=pb[bB]: e.matmul(
                b[:, 0:DV], self.w_Pb[s][:, h2 * 128:(h2 + 1) * 128],
                self.w_W[s][:, h2 * DV:(h2 + 1) * DV], start=True, stop=True),
                waits=wo + [("cs", sw)])
            vo = self.V(lambda e, b=pb[bB]: e.tensor_tensor(
                self.w_O[s][:, :], self.w_O[s][:, :], b[:, 0:DV], ALU.add),
                waits=[("ct", mmo), ("cs", so1)])
            self.bank_done(bB, "cv", vo)
            wt = self.bank(bT)
            self.T(lambda e, b=pb[bT], so=soff: e.matmul(
                b[:, 0:DV], self.w_gI[s][:, :], self.st1[:, so:so + DV],
                start=True, stop=False), waits=wt + [("cs", sk)])
            mm1 = self.T(lambda e, b=pb[bT], h2=h: e.matmul(
                b[:, 0:DV], self.w_ksc[s][:, 0:128],
                self.w_W[s][:, h2 * DV:(h2 + 1) * DV], start=False, stop=True))
            bT2 = 6 + (1 - s)
            wt2 = self.bank(bT2)
            self.T(lambda e, b=pb[bT2], so=soff: e.matmul(
                b[0:64, 0:DV], self.w_gI[s][0:64, 0:64], self.st2[0:64, so:so + DV],
                start=True, stop=False), waits=wt2)
            mm2 = self.T(lambda e, b=pb[bT2], h2=h: e.matmul(
                b[0:64, 0:DV], self.w_ksc[s][:, 128:192],
                self.w_W[s][:, h2 * DV:(h2 + 1) * DV], start=False, stop=True))
            v1 = self.V(lambda e, b=pb[bT], so=soff: e.tensor_copy(
                self.st1[:, so:so + DV], b[:, 0:DV]), waits=[("ct", mm1)])
            self.bank_done(bT, "cv", v1)
            v2 = self.V(lambda e, b=pb[bT2], so=soff: e.tensor_copy(
                self.st2[0:64, so:so + DV], b[0:64, 0:DV]), waits=[("ct", mm2)])
            self.bank_done(bT2, "cv", v2)
            sq = self.S(lambda e: e.activation(
                self.w_osq[s][:, :], self.w_O[s][:, :], AF.Square,
                accum_out=self.b_ssq[s][:, 0:1]), waits=[("cv", vo)] + self.snap("cs"))
            sq = self.S(lambda e: e.activation(
                self.b_ssq[s][:, 1:2], self.b_ssq[s][:, 0:1], AF.Ln,
                scale=np.float32(1.0 / DV), bias=np.float32(EPS_RMS)), waits=[("cs", sq)])
            sq = self.S(lambda e: e.activation(
                self.b_ssq[s][:, 1:2], self.b_ssq[s][:, 1:2], AF.Exp, scale=np.float32(-0.5)),
                waits=[("cs", sq)])
            vg2 = self.V(lambda e: e.scalar_tensor_tensor(
                self.w_osq[s][:, :], self.w_O[s][:, :], self.b_ssq[s][:, 1:2],
                self.d_gnw[:, :], ALU.mult, ALU.mult),
                waits=[("cs", sq)] + self.snap("cv"))
            self.V(lambda e, h2=h, t2=tau: e.tensor_tensor(
                self.a_big[:, t2 * VD + h2 * DV: t2 * VD + (h2 + 1) * DV],
                self.w_osq[s][:, :],
                self.a_gs[:, t2 * VD + h2 * DV: t2 * VD + (h2 + 1) * DV], ALU.mult),
                waits=[("cv", vg2)])

    def p5_tile(self, g, tau):
        pb = self.pb
        s, c = divmod(tau, CPG)
        t0 = g * TPG
        xt = self.x2[:, tau * D: tau * D + D]
        qt0 = self.qt0
        W0 = self.snap("cv", "cs")
        v = self.V(lambda e: e.reduce_sum(self.b_stat[:, 4:5], xt, axis=AX.X,
                                          negate=True), waits=W0)
        v = self.V(lambda e: e.tensor_scalar(self.b_stat[:, 5:6], self.b_stat[:, 4:5],
                                             1.0 / D, None, ALU.mult),
                   waits=[("cv", v)])
        s1 = self.S(lambda e: e.activation(qt0[:, :], xt, AF.Identity,
                                           bias=self.b_stat[:, 5:6]),
                    waits=[("cv", v)] + self.snap("cs"))
        s1 = self.S(lambda e: e.activation(self.qt1, qt0[:, :], AF.Square,
                                           accum_out=self.b_stat[:, 6:7]),
                    waits=[("cs", s1)])
        s1 = self.S(lambda e: e.activation(self.b_stat[:, 7:8], self.b_stat[:, 6:7],
                                           AF.Ln, scale=np.float32(1.0 / D), bias=np.float32(EPS_LN)),
                    waits=[("cs", s1)])
        s1 = self.S(lambda e: e.activation(self.b_stat[:, 7:8], self.b_stat[:, 7:8],
                                           AF.Exp, scale=np.float32(-0.5)), waits=[("cs", s1)])
        s1 = self.S(lambda e: e.activation(qt0[:, :], qt0[:, :], AF.Identity,
                                           scale=self.b_stat[:, 7:8]),
                    waits=[("cs", s1)])
        v = self.V(lambda e: e.reduce_max(self.qs[:].unsqueeze(-1),
                                          qt0[:].rearrange("p (b k) -> p b k", k=32),
                                          axis=AX.X, apply_absolute_value=True),
                   waits=[("cs", s1)] + self.snap("cv"))
        v = self.V(lambda e: e.tensor_scalar(self.qsi[:], self.qs[:], 1e-30, None,
                                             ALU.max), waits=[("cv", v)])
        v = self.V(lambda e: e.tensor_scalar(self.qsi[:].bitcast(I32),
                                             self.qsi[:].bitcast(I32),
                                             self.c_pow2[:, 0:1], None, ALU.bitwise_and),
                   waits=[("cv", v)])
        v = self.V(lambda e: e.tensor_single_scalar(self.qm[:], self.qs[:], 0.0,
                                                    ALU.is_gt), waits=[("cv", v)])
        v = self.V(lambda e: e.tensor_scalar(self.qw[:], self.qsi[:], 0.25, None,
                                             ALU.mult), waits=[("cv", v)])
        v = self.V(lambda e: e.memset(self.qsi[:], 1.0), waits=[("cv", v)])
        v = self.V(lambda e: e.select(self.qs[:], self.qm[:], self.qw[:], self.qsi[:], add_drain=True),
                   waits=[("cv", v)])
        v = self.V(lambda e: e.tensor_scalar(self.qs[:], self.qs[:],
                                             self.d_dactp[:, 0:1], None, ALU.mult),
                   waits=[("cv", v)])
        v = self.V(lambda e: e.reciprocal(self.qsi[:], self.qs[:]), waits=[("cv", v)])
        vq = self.emit_quant_core(qt0[:], self.qt1, self.qt2, self.qs[:], self.qsi[:],
                                  self.a_xq[:, tau * D: tau * D + D], [("cv", v)])
        for kk in range(8):
            bi = (tau % 2) * 2 + (kk % 2)
            t = self.T(lambda e, k2=kk, b=pb[bi], t2=tau: e.transpose(
                b.ap().bitcast(BF16)[0:128, 0:128],
                self.a_xq[:, t2 * D + k2 * 128: t2 * D + (k2 + 1) * 128],
                self.c_idbf[:, :]), waits=[("cv", vq)] + self.bank(bi))
            vv = self.V(lambda e, k2=kk, b=pb[bi], s2=s, c2=c: e.tensor_copy(
                self.a_zs[:, k2 * 512 + s2 * 256 + c2 * 128:
                          k2 * 512 + s2 * 256 + c2 * 128 + 128],
                b.ap().bitcast(BF16)[0:128, 0:128]), waits=[("ct", t)])
            self.bank_done(bi, "cv", vv)
        xqfm_cv = self.now("cv")
        obuf = self.a_out[tau % 2]
        for half in range(2):
            bi = 4 + (tau % 2) * 2 + half
            mm = None
            for kc in range(8):
                lhs = self.a_zs[:, kc * 512 + s * 256 + c * 128:
                                kc * 512 + s * 256 + c * 128 + 128]
                w = ([("cv", xqfm_cv), self.ld_wq[0]] + self.bank(bi)) \
                    if kc == 0 else ([self.ld_wq[1]] if kc == 4 else [])
                mm = self.T(lambda e, l=lhs, b=pb[bi], k2=kc, h2=half: e.matmul(
                    b[:, 0:512], l,
                    self.wslot[k2 // 4][:, (k2 % 4) * 1024 + h2 * 512:
                                        (k2 % 4) * 1024 + h2 * 512 + 512],
                    start=(k2 == 0), stop=False), waits=w)
            mm = self.T(lambda e, b=pb[bi], h2=half: e.matmul(
                b[:, 0:512], self.c_ones1b[:, :],
                self.d_fbias[0:1, h2 * 512: h2 * 512 + 512],
                start=False, stop=True))
            vv = self.V(lambda e, b=pb[bi], h2=half, ob=obuf: e.tensor_tensor(
                ob[:, h2 * 512: h2 * 512 + 512],
                self.x2[:, tau * D + h2 * 512: tau * D + h2 * 512 + 512],
                b[:, 0:512], ALU.add),
                waits=[("ct", mm), (f"dst{tau % 2}", self.out_store[tau % 2])])
            self.bank_done(bi, "cv", vv)
        self.dma("sync", self.dout[s, t0 + c * 128: t0 + c * 128 + 128, :],
                 obuf, f"dst{tau % 2}", waits=self.snap("cv"))
        self.out_store[tau % 2] = self.now(f"dst{tau % 2}")


def kernel(**inputs):
    x = np.ascontiguousarray(inputs["L0_attn_x"], dtype=np.float32)
    convw = np.concatenate([np.ascontiguousarray(inputs["L0_attn_conv_x_w"][:, 0, :]),
                            np.ascontiguousarray(inputs["L0_attn_conv_z_w"][:, 0, :])],
                           axis=0)
    convb = np.concatenate([np.ascontiguousarray(inputs["L0_attn_conv_x_b"]),
                            np.ascontiguousarray(inputs["L0_attn_conv_z_b"])])[None, :]
    small = np.zeros((1, 16), np.float32)
    small[0, 0:4] = inputs["L0_attn_b_b"]
    small[0, 4:8] = inputs["L0_attn_A_log"]
    small[0, 8:12] = inputs["L0_attn_dt_bias"]
    small[0, 12:13] = inputs["L0_ffn_delta_act"]
    base = {
        "w_in": np.ascontiguousarray(inputs["L0_attn_W_in"]),
        "conv_w": convw,
        "conv_b": convb,
        "w_q": np.ascontiguousarray(inputs["L0_attn_W_q"]),
        "w_k": np.ascontiguousarray(inputs["L0_attn_W_k"]),
        "w_v": np.ascontiguousarray(inputs["L0_attn_W_v"]),
        "w_g": np.ascontiguousarray(inputs["L0_attn_W_g"]),
        "w_gkb": np.concatenate([np.ascontiguousarray(inputs["L0_attn_W_gk"]),
                                 np.ascontiguousarray(inputs["L0_attn_W_b"])], axis=1),
        "small_in": small,
        "g_norm_w": np.ascontiguousarray(inputs["L0_attn_g_norm_w"][None, :]),
        "w_so": np.ascontiguousarray(inputs["L0_attn_W_so"]),
        "w_out": np.ascontiguousarray(inputs["L0_attn_W_out"]),
        "ffn_w": np.ascontiguousarray(inputs["L0_ffn_weight"]),
        "ffn_b": np.ascontiguousarray(inputs["L0_ffn_bias"][None, :]),
        "ffn_dw": np.ascontiguousarray(inputs["L0_ffn_delta_w"]),
    }
    nc = B().build()
    in_maps = []
    for core in range(NCORES):
        m = dict(base)
        m["x"] = np.ascontiguousarray(x[core * SEQ:(core + 1) * SEQ])
        in_maps.append(m)
    res = run_bass_kernel_spmd(nc, in_maps, core_ids=list(range(NCORES)))
    return np.concatenate([r["out"] for r in res.results], axis=0)


# version 08
# speedup vs baseline: 1.3946x; 1.2287x over previous
import contextlib
import numpy as np
import ml_dtypes
import concourse.bass as bass
import concourse.mybir as mybir
from concourse.bass_utils import run_bass_kernel_spmd

F32 = mybir.dt.float32
F32R = mybir.dt.float32r
SOLVE_DT = F32R
BF16 = mybir.dt.bfloat16
I32 = mybir.dt.int32
AF = mybir.ActivationFunctionType
ALU = mybir.AluOpType
AX = mybir.AxisListType
BF = ml_dtypes.bfloat16

NCORES = 8
SEQ = 2
L = 2048
D = 1024
DI = 2048
KD = 768
VD = 1536
H = 4
DK = 192
DV = 384
C = 128
NG = 8
TPG = 256
CPG = TPG // C
NTAU = SEQ * CPG
EPS_LN = 1e-5
EPS_L2 = 1e-6
EPS_RMS = 1e-5
LN2C = float(np.log(2.0))
MAGIC = 12582912.0
STOP_PHASE = None
SKIP = set()
POW2MASK = float(-8388608)

ENGINES = ("sync", "scalar", "tensor", "vector", "gpsimd")


class B:

    def __init__(self):
        self.nc = bass.Bass("TRN2", target_bir_lowering=False)
        self.steps = {e: [] for e in ENGINES}
        self.counts = {}
        self.bank_free = [[] for _ in range(8)]

    def now(self, sem):
        return self.counts.get(sem, 0)

    def _step(self, eng, fn, waits, inc, amt):
        waits = [w for w in waits if w[1] > 0]
        val = None
        if inc is not None:
            self.counts[inc] = self.counts.get(inc, 0) + amt
            val = self.counts[inc]
        self.steps[eng].append(dict(fn=fn, waits=waits, inc=inc, amt=amt, val=val))
        return val

    def dma(self, queue, out_ap, in_ap, sem="dw", waits=(), slow=False):
        return self._step(queue,
                          lambda e, o=out_ap, i=in_ap, sl=slow: e.dma_start(
                              out=o, in_=i, allow_slow_non_contiguous=sl),
                          waits, sem, 16)

    def T(self, fn, waits=()):
        return self._step("tensor", fn, waits, "ct", 1)

    def V(self, fn, waits=()):
        return self._step("vector", fn, waits, "cv", 1)

    def S(self, fn, waits=()):
        return self._step("scalar", fn, waits, "cs", 1)

    def G(self, fn, waits=()):
        return self._step("gpsimd", fn, waits, "cg", 1)

    def snap(self, *sems):
        return [(s, self.now(s)) for s in sems]

    def all_compute(self):
        return self.snap("ct", "cv", "cs", "cg")

    def bank(self, i, extra=()):
        w = self.bank_free[i] + list(extra)
        self.bank_free[i] = []
        return w

    def bank_done(self, i, sem, val):
        self.bank_free[i].append((sem, val))

    def build(self):
        nc = self.nc
        self.dram()
        self.consts()
        self.es = contextlib.ExitStack()
        with self.es:
            self.alloc()
            self.prep()
            self.pump()
            self.pump()
            self.emit_ln1(0)
            for g in range(NG):
                self.group(g)
            self._step("sync", lambda e: e.nop(),
                       [("dst0", self.now("dst0")), ("dst1", self.now("dst1"))],
                       None, 0)
            compute_sems = ("ct", "cv", "cs", "cg")
            used = {cs_: set() for cs_ in compute_sems}
            for ename in ENGINES:
                for st in self.steps[ename]:
                    for (sname, val) in st["waits"]:
                        if sname in used:
                            used[sname].add(val)
            ranks = {cs_: {v: i + 1 for i, v in enumerate(sorted(used[cs_]))}
                     for cs_ in compute_sems}

            def remap_wait(sname, val):
                if sname not in ranks:
                    return val
                return ranks[sname][val]

            blk = self.es.enter_context(nc.Block())
            for ename in ENGINES:
                steps = self.steps[ename]
                sems = self.sems

                def emitter(eng, steps=steps, sems=sems):
                    for st in steps:
                        for (sname, val) in st["waits"]:
                            eng.wait_ge(sems[sname], remap_wait(sname, val))
                        ins = st["fn"](eng)
                        inc = st["inc"]
                        if inc is not None:
                            if inc in used:
                                if st["val"] in used[inc]:
                                    ins.then_inc(sems[inc], 1)
                            else:
                                ins.then_inc(sems[inc], st["amt"])

                getattr(blk, ename)(emitter)
        return nc

    def dram(self):
        nc = self.nc
        t = lambda n, s: nc.dram_tensor(n, s, F32, kind="ExternalInput").ap()
        self.dx = t("x", [SEQ, L, D])
        self.dwin = t("w_in", [D, DI])
        self.dconvw = t("conv_w", [2 * D, 4])
        self.dconvb = t("conv_b", [1, 2 * D])
        self.dwq = t("w_q", [D, KD])
        self.dwk = t("w_k", [D, KD])
        self.dwv = t("w_v", [D, VD])
        self.dwg = t("w_g", [D, VD])
        self.dwgkb = t("w_gkb", [D, 8])
        self.dsmallin = t("small_in", [1, 16])
        self.dgnw = t("g_norm_w", [1, DV])
        self.dwso = t("w_so", [VD, D])
        self.dwout = t("w_out", [DI, D])
        self.dfw = t("ffn_w", [D, D])
        self.dfb = t("ffn_b", [1, D])
        self.dfdw = t("ffn_dw", [D, 32])
        self.dout = nc.dram_tensor("out", [SEQ, L, D], F32, kind="ExternalOutput").ap()
        self.dwqt = nc.dram_tensor("wqt_scratch", [8, 128, D], BF16, kind="Internal").ap()
        self.dwbf = nc.dram_tensor("wbf_scratch", [23, 128, 4096], BF16,
                                   kind="Internal").ap()
        self.mk_pieces()

    def mk_pieces(self):
        P = []

        def add(subs):
            P.append(dict(w=max(c + n for c, n, _ in subs), subs=subs))

        for p in range(4):
            add([(kc * 512, 512,
                  self.dwin[kc * 128:(kc + 1) * 128, p * 512:(p + 1) * 512])
                 for kc in range(8)])

        def proj(d, outcols):
            for p in range((outcols + 511) // 512):
                pc = min(512, outcols - p * 512)
                add([(kc * pc, pc,
                      d[kc * 128:(kc + 1) * 128, p * 512:p * 512 + pc])
                     for kc in range(8)])

        proj(self.dwv, VD)
        proj(self.dwg, VD)
        proj(self.dwq, KD)
        proj(self.dwk, KD)
        add([(kc * 8, 8, self.dwgkb[kc * 128:(kc + 1) * 128, :]) for kc in range(8)])
        for mhalf in range(2):
            for khalf in range(2):
                add([(kc6 * 512, 512,
                      self.dwso[(khalf * 6 + kc6) * 128:(khalf * 6 + kc6 + 1) * 128,
                                mhalf * 512:(mhalf + 1) * 512])
                     for kc6 in range(6)])
        for half in range(2):
            for khalf in range(2):
                add([(kc * 512, 512,
                      self.dwout[(khalf * 8 + kc) * 128:(khalf * 8 + kc + 1) * 128,
                                 half * 512:(half + 1) * 512])
                     for kc in range(8)])
        self.pieces = P
        self.wsched = [(g, pi) for g in range(NG) for pi in range(len(P))]

    def pump(self):
        if self.wnext >= len(self.wsched):
            return
        g, pi = self.wsched[self.wnext]
        slot = self.wnext % 2
        self.wnext += 1
        pc = self.pieces[pi]
        sem = f"dw{slot}"
        if g == 0:
            w = [("ct", self.slot_free[slot])]
            if self.wb_last_slot[slot] is not None:
                w.append(("dwb", self.wb_last_slot[slot]))
            for (coff, ncols, src) in pc["subs"]:
                self.dma("gpsimd", self.wslot[slot][:, coff:coff + ncols], src,
                         sem, waits=w)
                w = []
            ldv = self.now(sem)
            self.dma("sync", self.dwbf[pi, :, 0:pc["w"]],
                     self.wslot[slot][:, 0:pc["w"]], "dwb",
                     waits=[(sem, ldv), ("dwb", self.now("dwb"))])
            self.wb_val[pi] = self.now("dwb")
            self.wb_last_slot[slot] = self.wb_val[pi]
        else:
            sem = f"ds{slot}"
            wbw = max(self.wb_val[pi], self.wb_last_slot[slot] or 0)
            self.dma("sync", self.wslot[slot][:, 0:pc["w"]],
                     self.dwbf[pi, :, 0:pc["w"]], sem,
                     waits=[("ct", self.slot_free[slot]), ("dwb", wbw)])
            ldv = self.now(sem)
        self.wrecs.append(dict(slot=slot, ld=(sem, ldv)))

    def use(self):
        r = self.wrecs[self.wuse]
        self.wuse += 1
        return r

    def release(self, rec, ctval):
        self.slot_free[rec["slot"]] = ctval
        self.pump()

    def consts(self):
        nc = self.nc
        self.i_idf32 = nc.inline_tensor(np.eye(128, dtype=np.float32), "c_idf32")
        self.i_idbf = nc.inline_tensor(np.eye(128, dtype=np.float32).astype(BF), "c_idbf")
        self.i_i32rep = nc.inline_tensor(
            np.tile(np.eye(32, dtype=np.float32), (4, 1)).astype(BF), "c_i32rep")
        self.i_maskSN = nc.inline_tensor(
            -np.triu(np.ones((128, 128), dtype=np.float32), 1), "c_maskSN")
        self.i_maskI = nc.inline_tensor(
            np.triu(np.ones((128, 128), dtype=np.float32)), "c_maskI")
        self.i_i4bf = nc.inline_tensor(
            np.tile(np.eye(128, dtype=np.float32), (1, 4)).astype(BF), "c_i4bf")
        sel = np.zeros((4, H * 128), np.float32)
        for h in range(H):
            sel[h, h * 128:(h + 1) * 128] = 1.0
        self.i_sel4 = nc.inline_tensor(sel, "c_sel4")
        self.i_ones1f = nc.inline_tensor(np.ones((1, 128), np.float32), "c_ones1f")
        self.i_ones1b = nc.inline_tensor(np.ones((1, 128), np.float32).astype(BF),
                                         "c_ones1b")
        self.i_pow2 = nc.inline_tensor(np.full((128, 1), -8388608, np.int32), "c_pow2")

    def alloc(self):
        nc, es = self.nc, self.es
        sb = lambda n, s, d: es.enter_context(nc.sbuf_tensor(n, s, d))
        ps = lambda n: es.enter_context(nc.psum_tensor(n, [128, 512], F32))
        self.c_idf32 = sb("s_idf32", [128, 128], F32)
        self.c_idbf = sb("s_idbf", [128, 128], BF16)
        self.c_i32rep = sb("s_i32rep", [128, 32], BF16)
        self.c_maskSN = sb("s_maskSN", [128, 128], F32)
        self.c_maskI = sb("s_maskI", [128, 128], F32)
        self.c_i4bf = sb("s_i4bf", [128, 512], BF16)
        self.c_sel4 = sb("s_sel4", [4, H * 128], F32)
        self.c_ones1f = sb("s_ones1f", [1, 128], F32)
        self.c_ones1b = sb("s_ones1b", [1, 128], BF16)
        self.c_pow2 = sb("s_pow2", [128, 1], I32)
        self.c_idf32r = sb("s_idf32r", [128, 128], SOLVE_DT)
        self.d_convdiag = sb("d_convdiag", [128, 16 * 4 * 32], BF16)
        self.d_cb = sb("d_cb", [128, 16], F32)
        self.d_bb = sb("d_bb", [128, 4], F32)
        self.d_dtb = sb("d_dtb", [128, 4], F32)
        self.d_negAexp = sb("d_negAexp", [128, 4], F32)
        self.d_gnw = sb("d_gnw", [128, DV], F32)
        self.d_fbias = sb("d_fbias", [1, D], BF16)
        self.d_dactp = sb("d_dactp", [128, 1], F32)
        self.d_small = sb("d_small", [1, 64], F32)
        self.wslot = [sb(f"wslot{i}", [128, 4096], BF16) for i in range(2)]
        self.wqslot = [sb(f"wqslot{i}", [128, 4096], BF16) for i in range(2)]
        self.a_x = [sb(f"a_x{i}", [128, D], F32) for i in range(2)]
        self.a_xn = sb("a_xn", [128, 8 * 520], BF16)
        self.a_big = sb("a_big", [128, 8320], BF16)
        self.a_xs = sb("a_xs", [128, 6144], BF16)
        self.a_zs = sb("a_zs", [128, 4096], BF16)
        self.a_q = sb("a_q", [128, NTAU * KD], BF16)
        self.a_k = sb("a_k", [128, NTAU * KD], BF16)
        self.a_v = sb("a_v", [128, NTAU * VD], BF16)
        self.a_gs = sb("a_gs", [128, NTAU * VD], BF16)
        self.st1 = sb("st1", [128, SEQ * H * DV], BF16)
        self.st2 = sb("st2", [64, SEQ * H * DV], BF16)
        self.w_pre = sb("w_pre", [128, 128], F32)
        self.w_t1 = sb("w_t1", [128, 128], F32)
        self.w_sv = [sb(f"w_sv_{s}", [128, H * 384], SOLVE_DT) for s in range(SEQ)]
        self.w_DkT = [sb(f"w_DkT_{s}", [128, 512], SOLVE_DT) for s in range(SEQ)]
        self.w_Zh = [sb(f"w_Zh_{s}", [128, DV], SOLVE_DT) for s in range(SEQ)]
        self.w_Pb = [sb(f"w_Pb_{s}", [128, 512], BF16) for s in range(SEQ)]
        self.w_qfm = [sb(f"w_qfm_{s}", [128, H * 256], BF16) for s in range(SEQ)]
        self.w_kfm = [sb(f"w_kfm_{s}", [128, H * 256], BF16) for s in range(SEQ)]
        self.w_zw = sb("w_zw", [128, 6144], BF16)
        self.w_Z = [self.w_zw[:, s * (H * DV):(s + 1) * (H * DV)] for s in range(SEQ)]
        self.w_W = [self.w_zw[:, 3072 + s * (H * DV):3072 + (s + 1) * (H * DV)]
                    for s in range(SEQ)]
        self.r_q = self.w_zw[:, 0:NTAU * KD]
        self.r_k = self.w_zw[:, 3072:3072 + NTAU * KD]
        self.w_O = [sb(f"w_O_{s}", [128, DV], F32) for s in range(SEQ)]
        self.w_osq = self.w_Zh
        self.w_ksc = [sb(f"w_ksc_{s}", [128, DK], BF16) for s in range(SEQ)]
        self.w_gI = [sb(f"w_gI_{s}", [128, 128], BF16) for s in range(SEQ)]
        self.b_beta = sb("b_beta", [128, NTAU * H], F32)
        self.b_G = sb("b_G", [128, NTAU * H], F32)
        self.b_Lam = sb("b_Lam", [128, NTAU * H], F32)
        self.b_negLam = sb("b_negLam", [128, NTAU * H], F32)
        self.b_bias = sb("b_bias", [128, NTAU * H], F32)
        self.b_Growt = sb("b_Growt", [4, NTAU * 128], F32)
        self.b_lamb4 = [sb(f"b_lamb4_{s}", [128, H], F32) for s in range(SEQ)]
        self.b_gam4 = [sb(f"b_gam4_{s}", [128, H], F32) for s in range(SEQ)]
        self.b_rs = sb("b_rs", [128, H], F32)
        self.b_ssq = [sb(f"b_ssq_{s}", [128, 2], F32) for s in range(SEQ)]
        self.b_stat = sb("b_stat", [128, 8], F32)
        self.r_gkb = sb("r_gkb", [128, NTAU * 8], F32)
        self.p2t = sb("p2t", [128, 24], F32)
        self.carry = sb("carry", [128, 2 * 16], BF16)
        self.qt0 = sb("qt0", [128, D], F32)
        self.r_sq = self.qt0[:, 0:KD]
        self.qs = sb("qs", [128, 32], F32)
        self.qsi = sb("qsi", [128, 32], F32)
        self.qm = sb("qm", [128, 32], I32)
        self.qw = sb("qw", [128, 32], F32)
        xsf = self.a_xs.ap().bitcast(F32)
        self.qt1 = xsf[:, 0:D]
        self.qt2 = xsf[:, D:2 * D]
        self.a_xnt = self.a_xs[:, 4096:4096 + D]
        self.a_xq = self.a_gs
        bigf = self.a_big.ap().bitcast(F32)
        self.x2 = bigf[:, 0:NTAU * D]
        self.fb_stage = bigf[0:1, 0:D]
        self.fq = self.a_big[:, 4096:4096 + D]
        self.fdw = bigf[:, 3100:3132]
        self.d_cw = bigf[:, 3200:3264]
        xnf = self.a_xn.ap().bitcast(F32)
        self.a_out = [xnf[:, 0:D], xnf[:, D:2 * D]]
        self.pb = [ps(f"pb{i}") for i in range(8)]
        self.out_store = [0, 0]
        self.sems = {}
        for sname in ("dx", "dw0", "dw1", "ds0", "ds1", "dwb", "dq0", "dq1",
                      "dmisc", "dst0", "dst1", "ct", "cv", "cs", "cg"):
            self.sems[sname] = es.enter_context(nc.semaphore(sname))
        self.slot_free = [0, 0]
        self.wb_val = [None] * len(self.pieces)
        self.wb_last_slot = [None, None]
        self.wrecs = []
        self.wuse = 0
        self.wnext = 0
        self.wq_free = 0
        self.ld_wq = [None, None]
        self.ax_free = [("cv", 0)] * 4

    def prep(self):
        for dst, src in ((self.c_idf32, self.i_idf32), (self.c_idbf, self.i_idbf),
                         (self.c_i32rep, self.i_i32rep),
                         (self.c_maskSN, self.i_maskSN), (self.c_maskI, self.i_maskI),
                         (self.c_i4bf, self.i_i4bf), (self.c_sel4, self.i_sel4),
                         (self.c_ones1f, self.i_ones1f), (self.c_ones1b, self.i_ones1b),
                         (self.c_pow2, self.i_pow2)):
            self.dma("sync", dst[:], src.ap(), "dmisc")
        self.dma("sync", self.d_small[0:1, 0:16], self.dsmallin, "dmisc")
        self.dma("sync", self.d_cw.rearrange("p (c j) -> p c j", j=4),
                 self.dconvw.rearrange("(c p) j -> p c j", p=128), "dmisc", slow=True)
        self.dma("sync", self.d_cb[:].rearrange("p (c o) -> p c o", o=1),
                 self.dconvb.rearrange("o (c p) -> p c o", p=128), "dmisc", slow=True)
        self.dma("sync", self.d_gnw[0:1, :], self.dgnw, "dmisc")
        self.dma("sync", self.fb_stage, self.dfb, "dmisc")
        WM = [("dmisc", self.now("dmisc"))]
        pb = self.pb

        def bcast(bank_i, dst_ap, src_row, n, extraw=()):
            t = self.T(lambda e, b=pb[bank_i], s=src_row, n2=n: e.matmul(
                b[:, 0:n2], self.c_ones1f[:, :], s, start=True, stop=True),
                waits=WM + list(extraw) + self.bank(bank_i))
            v = self.V(lambda e, b=pb[bank_i], d2=dst_ap, n2=n: e.tensor_copy(
                d2, b[:, 0:n2]), waits=[("ct", t)])
            self.bank_done(bank_i, "cv", v)
            return v

        bcast(0, self.d_bb[:, :], self.d_small[0:1, 0:4], 4)
        bcast(1, self.d_dtb[:, :], self.d_small[0:1, 8:12], 4)
        sA = self.S(lambda e: e.activation(self.d_small[0:1, 16:20],
                                           self.d_small[0:1, 4:8], AF.Exp), waits=WM)
        tA = self.T(lambda e: e.matmul(pb[2][:, 0:4], self.c_ones1f[:, :],
                                       self.d_small[0:1, 16:20], start=True, stop=True),
                    waits=[("cs", sA)] + self.bank(2))
        vA = self.V(lambda e: e.tensor_scalar(self.d_negAexp[:, :], pb[2][:, 0:4],
                                              -1.0, None, ALU.mult), waits=[("ct", tA)])
        self.bank_done(2, "cv", vA)
        bcast(3, self.d_gnw[:, :], self.d_gnw[0:1, :], DV)
        self.S(lambda e: e.copy(self.c_idf32r[:, :], self.c_idf32[:, :]), waits=WM)
        self.S(lambda e: e.copy(self.d_fbias[:, :], self.fb_stage), waits=WM)
        sD = self.S(lambda e: e.activation(self.d_small[0:1, 20:21],
                                           self.d_small[0:1, 12:13], AF.Tanh,
                                           scale=np.float32(4.0)), waits=WM)
        sD = self.S(lambda e: e.activation(self.d_small[0:1, 21:22],
                                           self.d_small[0:1, 20:21], AF.Exp,
                                           scale=np.float32(0.5 * LN2C)), waits=[("cs", sD)])
        tD = self.T(lambda e: e.matmul(pb[4][:, 0:1], self.c_ones1f[:, :],
                                       self.d_small[0:1, 21:22], start=True, stop=True),
                    waits=[("cs", sD)] + self.bank(4))
        vD = self.V(lambda e: e.tensor_copy(self.d_dactp[:, :], pb[4][:, 0:1]),
                    waits=[("ct", tD)])
        self.bank_done(4, "cv", vD)
        for ch in range(16):
            for j in range(4):
                off = (ch * 4 + j) * 32
                self.S(lambda e, o=off, c2=ch, j2=j: e.activation(
                    self.d_convdiag[:, o:o + 32], self.c_i32rep[:, :], AF.Identity,
                    scale=self.d_cw[:, c2 * 4 + j2:c2 * 4 + j2 + 1]), waits=WM)
        for kc in range(8):
            self.quant_w_chunk(kc)
        self.G(lambda e: e.memset(self.st1[:, :], 0.0))
        self.G(lambda e: e.memset(self.st2[:, :], 0.0))
        self.G(lambda e: e.memset(self.carry[:, :], 0.0))

    def quant_w_chunk(self, kc):
        qt0, qs, qsi, qm = self.qt0, self.qs, self.qsi, self.qm
        self.dma("sync", qt0[:], self.dfw[kc * 128:(kc + 1) * 128, :], "dmisc",
                 waits=self.snap("cv", "cs"))
        self.dma("sync", self.fdw, self.dfdw[kc * 128:(kc + 1) * 128, :], "dmisc")
        WD = [("dmisc", self.now("dmisc")), ("cv", self.now("cv")),
              ("cs", self.now("cs"))]
        v = self.V(lambda e: e.reduce_max(qs[:].unsqueeze(-1),
                                          qt0[:].rearrange("p (b k) -> p b k", k=32),
                                          axis=AX.X, apply_absolute_value=True),
                   waits=WD)
        v = self.V(lambda e: e.tensor_scalar(qsi[:], qs[:], 1e-30, None, ALU.max),
                   waits=[("cv", v)])
        v = self.V(lambda e: e.tensor_scalar(qsi[:].bitcast(I32), qsi[:].bitcast(I32),
                                             self.c_pow2[:, 0:1], None, ALU.bitwise_and),
                   waits=[("cv", v)])
        v = self.V(lambda e: e.tensor_single_scalar(qm[:], qs[:], 0.0, ALU.is_gt),
                   waits=[("cv", v)])
        v = self.V(lambda e: e.tensor_scalar(self.qw[:], qsi[:], 0.25, None, ALU.mult),
                   waits=[("cv", v)])
        v = self.V(lambda e: e.memset(qsi[:], 1.0), waits=[("cv", v)])
        v = self.V(lambda e: e.select(qs[:], qm[:], self.qw[:], qsi[:], add_drain=True), waits=[("cv", v)])
        s = self.S(lambda e: e.activation(qsi[:], self.fdw, AF.Tanh, scale=np.float32(4.0)),
                   waits=WD + [("cv", v)])
        s = self.S(lambda e: e.activation(qsi[:], qsi[:], AF.Exp, scale=np.float32(0.5 * LN2C)),
                   waits=[("cs", s)])
        v = self.V(lambda e: e.tensor_tensor(qs[:], qs[:], qsi[:], ALU.mult),
                   waits=[("cs", s), ("cv", v)])
        v = self.V(lambda e: e.reciprocal(qsi[:], qs[:]), waits=[("cv", v)])
        vq = self.emit_quant_core(qt0[:], self.qt1, self.qt2, qs[:], qsi[:],
                                  self.fq[:, 0:D], [("cv", v)])
        for tt in range(8):
            bi = 5 + (tt % 2)
            t = self.T(lambda e, t2=tt, b=self.pb[bi]: e.transpose(
                b.ap().bitcast(BF16)[0:128, 0:128],
                self.fq[:, t2 * 128:(t2 + 1) * 128], self.c_idbf[:, :]),
                waits=[("cv", vq)] + self.bank(bi)
                + ([("dmisc", self.now("dmisc"))] if tt == 0 else []))
            vv = self.V(lambda e, t2=tt, b=self.pb[bi]: e.tensor_copy(
                self.a_big[:, 5120 + t2 * 128: 5120 + (t2 + 1) * 128],
                b.ap().bitcast(BF16)[0:128, 0:128]), waits=[("ct", t)])
            self.bank_done(bi, "cv", vv)
        for tt in range(8):
            self.dma("sync", self.dwqt[tt, :, kc * 128:(kc + 1) * 128],
                     self.a_big[:, 5120 + tt * 128: 5120 + (tt + 1) * 128], "dmisc",
                     waits=[("cv", self.now("cv"))] if tt == 0 else ())

    def emit_quant_core(self, x_in, tmp_sgn, tmp_ph, scale, inv_scale, out_bf, waits):
        V, S = self.V, self.S
        r32 = lambda ap: ap.rearrange("p (b k) -> p b k", k=32)
        bc = lambda ap: ap.unsqueeze(-1).broadcast_to((128, 32, 32))
        v = V(lambda e: e.tensor_tensor(r32(x_in), r32(x_in), bc(inv_scale),
                                        ALU.mult), waits=waits)
        v = V(lambda e: e.tensor_copy(tmp_sgn, x_in), waits=[("cv", v)])
        s = S(lambda e: e.activation(x_in, x_in, AF.Abs),
              waits=[("cv", v)] + self.snap("cs"))
        v = V(lambda e: e.tensor_scalar(x_in, x_in, 6.0, 1e-30, ALU.min, ALU.max),
              waits=[("cs", s)])
        v = V(lambda e: e.tensor_scalar(tmp_ph.bitcast(I32), x_in.bitcast(I32),
                                        self.c_pow2[:, 0:1], None, ALU.bitwise_and),
              waits=[("cv", v)])
        v = V(lambda e: e.tensor_scalar(tmp_ph, tmp_ph, 1.0, 4.0, ALU.max, ALU.min),
              waits=[("cv", v)])
        v = V(lambda e: e.tensor_scalar(tmp_ph, tmp_ph, MAGIC * 0.5, None,
                                        ALU.mult), waits=[("cv", v)])
        v = V(lambda e: e.tensor_tensor(x_in, x_in, tmp_ph, ALU.add),
              waits=[("cv", v)])
        v = V(lambda e: e.tensor_tensor(x_in, x_in, tmp_ph, ALU.subtract),
              waits=[("cv", v)])
        s = S(lambda e: e.sign(tmp_ph, tmp_sgn), waits=[("cv", v)] + self.snap("cs"))
        v = V(lambda e: e.tensor_tensor(x_in, x_in, tmp_ph, ALU.mult),
              waits=[("cs", s)])
        v = V(lambda e: e.tensor_tensor(r32(out_bf), r32(x_in), bc(scale),
                                        ALU.mult), waits=[("cv", v)])
        return v

    def emit_ln1(self, g):
        t0 = g * TPG
        last = (g == NG - 1)
        W0 = self.all_compute() + [("dmisc", self.now("dmisc")),
                                   ("dst0", self.now("dst0")),
                                   ("dst1", self.now("dst1"))]
        cnt = 0
        for s in range(SEQ):
            tiles = [(0, 128), (128, 128)] + ([] if last else [(256, 2)])
            for (toff, tn) in tiles:
                bidx = cnt % 2
                buf = self.a_x[bidx]
                cnt += 1
                self.dma("sync", buf[0:tn, :],
                         self.dx[s, t0 + toff:t0 + toff + tn, :], "dx",
                         waits=[self.ax_free[bidx], self.ax_free[bidx + 2],
                                ("dx", self.now("dx"))])
                WX = [("dx", self.now("dx"))]
                v = self.V(lambda e, b=buf, n=tn: e.reduce_sum(
                    self.b_stat[0:n, 0:1], b[0:n, :], axis=AX.X, negate=True),
                    waits=WX + (W0 if cnt == 1 else []))
                v = self.V(lambda e, n=tn: e.tensor_scalar(
                    self.b_stat[0:n, 1:2], self.b_stat[0:n, 0:1], 1.0 / D, None,
                    ALU.mult), waits=[("cv", v)])
                s1 = self.S(lambda e, b=buf, n=tn: e.activation(
                    b[0:n, :], b[0:n, :], AF.Identity, bias=self.b_stat[0:n, 1:2]),
                    waits=[("cv", v)] + (W0 if cnt == 1 else []))
                s1 = self.S(lambda e, b=buf, n=tn: e.activation(
                    self.qt0[0:n, :], b[0:n, :], AF.Square,
                    accum_out=self.b_stat[0:n, 2:3]), waits=[("cs", s1)])
                s1 = self.S(lambda e, n=tn: e.activation(
                    self.b_stat[0:n, 3:4], self.b_stat[0:n, 2:3], AF.Ln,
                    scale=np.float32(1.0 / D), bias=np.float32(EPS_LN)), waits=[("cs", s1)])
                s1 = self.S(lambda e, n=tn: e.activation(
                    self.b_stat[0:n, 3:4], self.b_stat[0:n, 3:4], AF.Exp, scale=np.float32(-0.5)),
                    waits=[("cs", s1)])
                s1 = self.S(lambda e, b=buf, n=tn: e.activation(
                    self.a_xnt[0:n, :], b[0:n, :], AF.Identity,
                    scale=self.b_stat[0:n, 3:4]), waits=[("cs", s1)])
                self.ax_free[bidx] = ("cs", s1)
                self.ax_free[bidx + 2] = ("cs", s1)
                for kc in range(8):
                    bi = 6 + (kc % 2)
                    t = self.T(lambda e, k2=kc, b=self.pb[bi], n=tn: e.transpose(
                        b.ap().bitcast(BF16)[0:128, 0:n],
                        self.a_xnt[0:n, k2 * 128:(k2 + 1) * 128],
                        self.c_idbf[0:n, 0:n]),
                        waits=[("cs", s1)] + self.bank(bi))
                    vv = self.V(lambda e, k2=kc, b=self.pb[bi], n=tn, s2=s, to=toff:
                                e.tensor_copy(
                        self.a_xn[:, k2 * 520 + s2 * 260 + to:
                                  k2 * 520 + s2 * 260 + to + n],
                        b.ap().bitcast(BF16)[0:128, 0:n]), waits=[("ct", t)])
                    self.bank_done(bi, "cv", vv)

    def group(self, g):
        t0 = g * TPG
        last = (g == NG - 1)
        ncol = TPG if last else TPG + 2
        pb = self.pb

        xn_ready = self.snap("cv")
        g0_extra = [("dmisc", self.now("dmisc"))] if g == 0 else []
        ev_xz = None
        for piece in range(4):
            rec = self.use()
            slot = rec["slot"]
            for dsub in range(4):
                dchunk = piece * 4 + dsub
                for s in range(SEQ):
                    bi = (dsub * 2 + s) % 4
                    mm = None
                    for kc in range(8):
                        lhs = self.wslot[slot][:, kc * 512 + dsub * 128:
                                               kc * 512 + dsub * 128 + 128]
                        rhs = self.a_xn[:, kc * 520 + s * 260:
                                        kc * 520 + s * 260 + ncol]
                        w = ([rec["ld"]] + xn_ready + self.bank(bi)) if kc == 0 else []
                        mm = self.T(lambda e, l=lhs, r=rhs, b=pb[bi], k2=kc, n=ncol:
                                    e.matmul(b[:, 0:n], l, r, start=(k2 == 0),
                                             stop=(k2 == 7)), waits=w)
                    ev_xz = self.S(lambda e, b=pb[bi], d2=dchunk, s2=s, n=ncol: e.copy(
                        self.a_big[:, d2 * 520 + s2 * 260 + 1:
                                   d2 * 520 + s2 * 260 + 1 + n], b[:, 0:n]),
                        waits=[("ct", mm)] + (g0_extra if dchunk == 0 else []))
                    self.bank_done(bi, "cs", ev_xz)
            self.release(rec, mm)
        cw = [("cs", ev_xz), ("cg", self.now("cg"))]
        big3 = self.a_big[:].rearrange("p (ch q) -> p ch q", q=520)
        for s in range(SEQ):
            self.V(lambda e, s2=s: e.tensor_copy(
                big3[:, :, s2 * 260: s2 * 260 + 1],
                self.carry[:, s2 * 16: s2 * 16 + 16].rearrange("p (c o) -> p c o", o=1)),
                waits=cw)
            if last:
                self.V(lambda e, s2=s: e.memset(
                    big3[:, :, s2 * 260 + 257: s2 * 260 + 260], 0.0), waits=cw)
        xz_ready_cv = self.now("cv")

        for dchunk in range(16):
            half = dchunk // 8
            ch = dchunk % 8
            dst = self.a_xs if half == 0 else self.a_zs
            for s in range(SEQ):
                bi = 4 + (dchunk * 2 + s) % 2
                mm = None
                for blk in range(4):
                    for j in range(4):
                        diag = self.d_convdiag[32 * blk:32 * blk + 32,
                                               (dchunk * 4 + j) * 32:
                                               (dchunk * 4 + j) * 32 + 32]
                        rhs = self.a_big[32 * blk:32 * blk + 32,
                                         dchunk * 520 + s * 260 + j:
                                         dchunk * 520 + s * 260 + j + TPG]
                        w = ([("cv", xz_ready_cv)] + self.bank(bi)) \
                            if (j == 0 and blk == 0) else []
                        mm = self.T(lambda e, d2=diag, r=rhs, b=pb[bi], j2=j, bl=blk:
                                    e.matmul(b[32 * bl:32 * bl + 32, 0:TPG], d2, r,
                                             start=(j2 == 0), stop=(j2 == 3),
                                             tile_position=(32 * bl, 32 * bl)),
                                    waits=w)
                sc = self.S(lambda e, b=pb[bi], c2=ch, s2=s, h2=half, d2=dst:
                            e.activation(
                    d2[:, c2 * 512 + s2 * 256: c2 * 512 + s2 * 256 + TPG],
                    b[:, 0:TPG], AF.Silu,
                    bias=self.d_cb[:, h2 * 8 + c2: h2 * 8 + c2 + 1]),
                    waits=[("ct", mm)])
                self.bank_done(bi, "cs", sc)
        conv_ct = self.now("ct")
        if not last:
            for s in range(SEQ):
                self.V(lambda e, s2=s: e.tensor_copy(
                    self.carry[:, s2 * 16: s2 * 16 + 16]
                    .rearrange("p (c o) -> p c o", o=1),
                    big3[:, :, s2 * 260 + 256: s2 * 260 + 257]),
                    waits=[("ct", conv_ct)])

        if STOP_PHASE == "p1":
            return
        self.p2(g)

        if STOP_PHASE == "p2":
            return
        for c in range(CPG):
            for s in range(SEQ):
                self.scan_chunk(g, s, c)
        p3_cv, p3_cs = self.now("cv"), self.now("cs")

        if STOP_PHASE == "p3":
            return
        for tau in range(NTAU):
            s, c = divmod(tau, CPG)
            for kk in range(12):
                bi = 6 + (kk % 2)
                t = self.T(lambda e, t2=tau, k2=kk, b=pb[bi]: e.transpose(
                    b.ap().bitcast(BF16)[0:128, 0:128],
                    self.a_big[:, t2 * VD + k2 * 128: t2 * VD + (k2 + 1) * 128],
                    self.c_idbf[:, :]),
                    waits=(([("cv", p3_cv), ("cs", p3_cs)] if (tau == 0 and kk == 0)
                            else []) + self.bank(bi)))
                vv = self.V(lambda e, k2=kk, b=pb[bi], s2=s, c2=c: e.tensor_copy(
                    self.a_xs[:, k2 * 512 + s2 * 256 + c2 * 128:
                              k2 * 512 + s2 * 256 + c2 * 128 + 128],
                    b.ap().bitcast(BF16)[0:128, 0:128]), waits=[("ct", t)])
                self.bank_done(bi, "cv", vv)
        p4a_cv = self.now("cv")

        p4b_cs = None
        for mhalf in range(2):
            recs = [self.use(), self.use()]
            for msub in range(4):
                m = mhalf * 4 + msub
                bi = msub % 4
                mm = None
                for kc in range(12):
                    khalf, kc6 = divmod(kc, 6)
                    slot = recs[khalf]["slot"]
                    w = []
                    if kc == 0:
                        w = [recs[0]["ld"], ("cv", p4a_cv)] + self.bank(bi)
                    elif kc6 == 0:
                        w = [recs[khalf]["ld"]]
                    mm = self.T(lambda e, k6=kc6, sl=slot, m2=msub, b=pb[bi], k2=kc:
                                e.matmul(
                        b[:, 0:512],
                        self.wslot[sl][:, k6 * 512 + m2 * 128:
                                       k6 * 512 + (m2 + 1) * 128],
                        self.a_xs[:, k2 * 512: k2 * 512 + 512],
                        start=(k2 == 0), stop=(k2 == 11)), waits=w)
                p4b_cs = self.S(lambda e, m2=m, b=pb[bi]: e.activation(
                    self.a_v[:, m2 * 512: m2 * 512 + 512], b[:, 0:512], AF.Copy),
                    waits=[("ct", mm)])
                self.bank_done(bi, "cs", p4b_cs)
            self.release(recs[0], mm)
            self.release(recs[1], mm)

        for qh in range(2):
            w = [("ct", self.wq_free)]
            if g == 0:
                w.append(("dmisc", self.now("dmisc")))
            for tt in range(4):
                self.dma("sync", self.wqslot[qh][:, tt * 1024:(tt + 1) * 1024],
                         self.dwqt[qh * 4 + tt, :, :], f"dq{qh}", waits=w)
                w = []
            self.ld_wq[qh] = (f"dq{qh}", self.now(f"dq{qh}"))

        for half in range(2):
            recs = [self.use(), self.use()]
            for tau in range(NTAU):
                s, c = divmod(tau, CPG)
                xq = self.a_x[tau % 2][:, (tau // 2) * 512:(tau // 2) * 512 + 512]
                self.dma("sync", xq,
                         self.dx[s, t0 + c * 128: t0 + c * 128 + 128,
                                 half * 512:(half + 1) * 512],
                         "dx", waits=[self.ax_free[tau], ("dx", self.now("dx"))])
                WX = [("dx", self.now("dx"))]
                bi = 4 + (tau % 2)
                mm = None
                for khalf in range(2):
                    src = self.a_v if khalf == 0 else self.a_zs
                    sl = recs[khalf]["slot"]
                    for kc in range(8):
                        lhs = src[:, kc * 512 + s * 256 + c * 128:
                                  kc * 512 + s * 256 + c * 128 + 128]
                        w = ([recs[khalf]["ld"]] +
                             ([("cs", p4b_cs)] + self.bank(bi) if khalf == 0 else [])) \
                            if kc == 0 else []
                        mm = self.T(lambda e, l=lhs, b=pb[bi], k2=kc, kh=khalf, s2=sl:
                                    e.matmul(
                            b[:, 0:512], l,
                            self.wslot[s2][:, k2 * 512:(k2 + 1) * 512],
                            start=(kh == 0 and k2 == 0), stop=(kh == 1 and k2 == 7)),
                            waits=w)
                vv = self.V(lambda e, b=pb[bi], t2=tau, h2=half, bx=xq:
                            e.tensor_tensor(
                    self.x2[:, t2 * D + h2 * 512: t2 * D + h2 * 512 + 512],
                    bx, b[:, 0:512], ALU.add),
                    waits=[("ct", mm)] + WX)
                self.bank_done(bi, "cv", vv)
                self.ax_free[tau] = ("cv", vv)
            self.release(recs[0], mm)
            self.release(recs[1], mm)

        if STOP_PHASE == "p4":
            return
        for tau in range(NTAU):
            self.p5_tile(g, tau)

        if not last:
            self.emit_ln1(g + 1)

    def p2(self, g):
        pb = self.pb

        def xs_lhs(kc, tau):
            s, c = divmod(tau, CPG)
            return self.a_xs[:, kc * 512 + s * 256 + c * 128:
                             kc * 512 + s * 256 + c * 128 + 128]

        def project(dten, outcols, evac_fn):
            npieces = (outcols + 511) // 512
            for piece in range(npieces):
                pcols = min(512, outcols - piece * 512)
                rec = self.use()
                slot = rec["slot"]
                for tau in range(NTAU):
                    bi = tau % 4
                    mm = None
                    for kc in range(8):
                        w = ([rec["ld"]] + self.bank(bi)) if kc == 0 else []
                        mm = self.T(lambda e, k2=kc, b=pb[bi], t2=tau, sl=slot,
                                    pc=pcols: e.matmul(
                            b[:, 0:pc], xs_lhs(k2, t2),
                            self.wslot[sl][:, k2 * pc: k2 * pc + pc],
                            start=(k2 == 0), stop=(k2 == 7)), waits=w)
                    sem, ev = evac_fn(piece, pcols, tau, bi, mm)
                    self.bank_done(bi, sem, ev)
                self.release(rec, mm)

        def mk_evac(dst, width, func):
            def f(piece, pcols, tau, bi, mm):
                e1 = self.S(lambda e, b=pb[bi], t2=tau, p2=piece, pc=pcols:
                            e.activation(
                    dst[:, t2 * width + p2 * 512: t2 * width + p2 * 512 + pc],
                    b[:, 0:pc], func), waits=[("ct", mm)])
                return ("cs", e1)
            return f

        project(self.dwv, VD, mk_evac(self.a_v, VD, AF.Copy))
        project(self.dwg, VD, mk_evac(self.a_gs, VD, AF.Silu))
        project(self.dwq, KD, mk_evac(self.r_q, KD, AF.Copy))
        project(self.dwk, KD, mk_evac(self.r_k, KD, AF.Copy))

        def gkb_evac(piece, pcols, tau, bi, mm):
            v = self.V(lambda e, b=pb[bi], t2=tau: e.tensor_copy(
                self.r_gkb[:, t2 * 8: t2 * 8 + 8], b[:, 0:8]), waits=[("ct", mm)])
            return ("cv", v)

        project(self.dwgkb, 8, gkb_evac)

        p2t = self.p2t
        for tau in range(NTAU):
            v = self.V(lambda e, t2=tau: e.tensor_tensor(
                p2t[:, 0:4], self.r_gkb[:, t2 * 8 + 4: t2 * 8 + 8],
                self.d_bb[:, :], ALU.add), waits=self.snap("cv"))
            s = self.S(lambda e: e.activation(p2t[:, 4:8], p2t[:, 0:4],
                                              AF.Exp, scale=np.float32(-1.0)),
                       waits=[("cv", v)] + self.snap("cs"))
            v = self.V(lambda e: e.tensor_scalar(p2t[:, 4:8], p2t[:, 4:8],
                                                 1.0, None, ALU.add), waits=[("cs", s)])
            v = self.V(lambda e, t2=tau: e.reciprocal(
                self.b_beta[:, t2 * H: t2 * H + H], p2t[:, 4:8]), waits=[("cv", v)])
            v2 = self.V(lambda e, t2=tau: e.tensor_tensor(
                p2t[:, 8:12], self.r_gkb[:, t2 * 8: t2 * 8 + 4],
                self.d_dtb[:, :], ALU.add), waits=[("cv", v)])
            s = self.S(lambda e: e.activation(p2t[:, 12:16], p2t[:, 8:12], AF.Abs),
                       waits=[("cv", v2)])
            s = self.S(lambda e: e.activation(p2t[:, 12:16], p2t[:, 12:16],
                                              AF.Exp, scale=np.float32(-1.0)), waits=[("cs", s)])
            s = self.S(lambda e: e.activation(p2t[:, 12:16], p2t[:, 12:16],
                                              AF.Ln, bias=np.float32(1.0)), waits=[("cs", s)])
            s = self.S(lambda e: e.activation(p2t[:, 16:20], p2t[:, 8:12], AF.Relu),
                       waits=[("cs", s)])
            v2 = self.V(lambda e: e.tensor_tensor(p2t[:, 12:16], p2t[:, 12:16],
                                                  p2t[:, 16:20], ALU.add),
                        waits=[("cs", s)])
            v2 = self.V(lambda e: e.tensor_tensor(p2t[:, 12:16], p2t[:, 12:16],
                                                  self.d_negAexp[:, :], ALU.mult),
                        waits=[("cv", v2)])
            bi = 4 + (tau % 2)
            t = self.T(lambda e, b=pb[bi]: e.matmul(
                b[:, 0:4], self.c_maskI[:, :], p2t[:, 12:16], start=True, stop=True),
                waits=[("cv", v2)] + self.bank(bi))
            v3 = self.V(lambda e, b=pb[bi], t2=tau: e.tensor_copy(
                self.b_G[:, t2 * H: t2 * H + H], b[:, 0:4]), waits=[("ct", t)])
            t2i = self.T(lambda e, b=pb[bi], t2=tau: e.transpose(
                b[0:4, 128:256], self.b_G[:, t2 * H: t2 * H + H], self.c_idf32[:, :]),
                waits=[("cv", v3)])
            v3 = self.V(lambda e, b=pb[bi], t2=tau: e.tensor_copy(
                self.b_Growt[0:4, t2 * 128:(t2 + 1) * 128], b[0:4, 128:256]),
                waits=[("ct", t2i)])
            self.bank_done(bi, "cv", v3)
        bw = self.snap("cv")
        s = self.S(lambda e: e.activation(self.b_Lam[:, :], self.b_G[:, :], AF.Exp),
                   waits=bw + self.snap("cs"))
        self.V(lambda e: e.tensor_scalar(self.b_negLam[:, :], self.b_Lam[:, :],
                                         -1.0, None, ALU.mult), waits=[("cs", s)])
        s = self.S(lambda e: e.activation(self.b_bias[:, :], self.b_beta[:, :], AF.Ln),
                   waits=bw)
        self.V(lambda e: e.tensor_tensor(self.b_bias[:, :], self.b_bias[:, :],
                                         self.b_G[:, :], ALU.subtract),
               waits=[("cs", s)])
        for (raw, dst) in ((self.r_q, self.a_q), (self.r_k, self.a_k)):
            for tau in range(NTAU):
                s = self.S(lambda e, r2=raw, t2=tau: e.activation(
                    self.r_sq, r2[:, t2 * KD:(t2 + 1) * KD], AF.Square),
                    waits=self.snap("cs"))
                v = self.V(lambda e: e.reduce_sum(
                    self.b_rs[:, :].unsqueeze(-1),
                    self.r_sq.rearrange("p (h d) -> p h d", d=DK), axis=AX.X),
                    waits=[("cs", s)] + self.snap("cv"))
                s = self.S(lambda e: e.activation(self.b_rs[:, :], self.b_rs[:, :],
                                                  AF.Ln, bias=np.float32(EPS_L2)),
                           waits=[("cv", v)])
                s = self.S(lambda e: e.activation(self.b_rs[:, :], self.b_rs[:, :],
                                                  AF.Exp, scale=np.float32(-0.5)),
                           waits=[("cs", s)])
                for h in range(H):
                    s = self.S(lambda e, r2=raw, d2=dst, t2=tau, h2=h: e.activation(
                        d2[:, t2 * KD + h2 * DK: t2 * KD + (h2 + 1) * DK],
                        r2[:, t2 * KD + h2 * DK: t2 * KD + (h2 + 1) * DK],
                        AF.Identity, scale=self.b_rs[:, h2:h2 + 1]), waits=[("cs", s)])

    def scan_chunk(self, g, s, c):
        pb = self.pb
        tau = s * CPG + c
        qk_ready = self.snap("cs", "cv")
        bA, bB = s * 2, s * 2 + 1
        bE = 4 + s
        bT = 6 + s

        for (src, dstfm) in ((self.a_q, self.w_qfm[s]), (self.a_k, self.w_kfm[s])):
            tl = None
            for h in range(H):
                for (p0, fn, oo) in ((0, 128, 0), (128, 64, 128)):
                    tl = self.T(lambda e, h2=h, p2=p0, f2=fn, o2=oo, b=pb[bT], s2=src:
                                e.transpose(
                        b.ap().bitcast(BF16)[0:f2, h2 * 256 + o2: h2 * 256 + o2 + 128],
                        s2[:, tau * KD + h2 * DK + p2: tau * KD + h2 * DK + p2 + f2],
                        self.c_idbf[:, :]),
                        waits=(qk_ready + self.bank(bT)) if (h == 0 and p0 == 0) else [])
            pv = pb[bT].ap().bitcast(BF16).rearrange("p (h q) -> p h q", q=256)
            dv = dstfm[:].rearrange("p (h q) -> p h q", q=256)
            self.V(lambda e, pv2=pv, dv2=dv: e.tensor_copy(
                dv2[0:128, :, 0:128], pv2[0:128, :, 0:128]), waits=[("ct", tl)])
            vv = self.V(lambda e, pv2=pv, dv2=dv: e.tensor_copy(
                dv2[0:64, :, 128:256], pv2[0:64, :, 128:256]))
            self.bank_done(bT, "cv", vv)
        kfm, qfm = self.w_kfm[s], self.w_qfm[s]
        fm_cv = self.now("cv")
        if STOP_PHASE == "sA":
            return

        def fmp(buf, h, piece):
            if piece == 0:
                return buf[:, h * 256: h * 256 + 128]
            return buf[0:64, h * 256 + 128: h * 256 + 256]

        mmA = mmB = None
        for h in range(H):
            w = ([("cv", fm_cv)] + self.bank(bA)) if h == 0 else []
            self.T(lambda e, h2=h, b=pb[bA]: e.matmul(
                b[:, h2 * 128:(h2 + 1) * 128], fmp(kfm, h2, 0), fmp(kfm, h2, 0),
                start=True, stop=False), waits=w)
            mmA = self.T(lambda e, h2=h, b=pb[bA]: e.matmul(
                b[:, h2 * 128:(h2 + 1) * 128], fmp(kfm, h2, 1), fmp(kfm, h2, 1),
                start=False, stop=True))
        for h in range(H):
            w = self.bank(bB) if h == 0 else []
            self.T(lambda e, h2=h, b=pb[bB]: e.matmul(
                b[:, h2 * 128:(h2 + 1) * 128], fmp(kfm, h2, 0), fmp(qfm, h2, 0),
                start=True, stop=False), waits=w)
            mmB = self.T(lambda e, h2=h, b=pb[bB]: e.matmul(
                b[:, h2 * 128:(h2 + 1) * 128], fmp(kfm, h2, 1), fmp(qfm, h2, 1),
                start=False, stop=True))

        if STOP_PHASE == "sB":
            return
        prev_readers = []
        for h in range(H):
            col = tau * H + h
            w = (self.bank(bE) if h == 0 else []) + prev_readers
            t = self.T(lambda e, h2=h, b=pb[bE], t2=tau: e.matmul(
                b[:, 0:128], self.c_sel4[:, h2 * 128:(h2 + 1) * 128],
                self.b_Growt[0:4, t2 * 128:(t2 + 1) * 128],
                start=True, stop=True), waits=w + ([("ct", mmB)] if h == 0 else []))
            v = self.V(lambda e, b=pb[bE], c2=col: e.tensor_scalar(
                self.w_t1[:, 0:1], b[:, 127:128], self.b_G[:, c2:c2 + 1], None,
                ALU.subtract), waits=[("ct", t)] + self.snap("cv"))
            sx = self.S(lambda e, h2=h: e.activation(
                self.b_lamb4[s][:, h2:h2 + 1], self.w_t1[:, 0:1], AF.Exp),
                waits=[("cv", v)] + self.snap("cs"))
            sx2 = self.S(lambda e, b=pb[bE], h2=h: e.activation(
                self.b_gam4[s][:, h2:h2 + 1], b[:, 127:128], AF.Exp),
                waits=[("ct", t)])
            v = self.V(lambda e, h2=h, c2=col: e.tensor_tensor(
                self.b_lamb4[s][:, h2:h2 + 1], self.b_lamb4[s][:, h2:h2 + 1],
                self.b_beta[:, c2:c2 + 1], ALU.mult), waits=[("cs", sx)])
            vpre = self.V(lambda e, b=pb[bE], c2=col: e.tensor_scalar(
                self.w_pre[:, :], b[:, 0:128], self.b_bias[:, c2:c2 + 1], 0.0,
                ALU.add, ALU.min), waits=[("ct", t)] + self.snap("cv"))
            prev_readers = [("cv", vpre), ("cs", sx2)]
            se = self.S(lambda e: e.activation(self.w_pre[:, :], self.w_pre[:, :],
                                               AF.Exp), waits=[("cv", vpre), ("cs", sx2)])
            v = self.V(lambda e: e.tensor_tensor(self.w_t1[:, :], self.w_pre[:, :],
                                                 self.c_maskSN[:, :], ALU.mult),
                       waits=[("cs", se)])
            vD = self.V(lambda e, h2=h, b=pb[bA]: e.tensor_tensor(
                self.w_sv[s][:, h2 * 384: h2 * 384 + 128], self.w_t1[:, :],
                b[:, h2 * 128:(h2 + 1) * 128], ALU.mult),
                waits=[("cv", v), ("ct", mmA)])
            self.V(lambda e, h2=h: e.tensor_copy(
                self.w_sv[s][:, h2 * 384 + 128: h2 * 384 + 256], self.c_idf32[:, :]))
            v = self.V(lambda e: e.tensor_tensor(self.w_t1[:, :], self.w_pre[:, :],
                                                 self.c_maskI[:, :], ALU.mult),
                       waits=[("cv", vD)])
            vP = self.V(lambda e, h2=h, b=pb[bB]: e.tensor_tensor(
                self.w_Pb[s][:, h2 * 128:(h2 + 1) * 128], self.w_t1[:, :],
                b[:, h2 * 128:(h2 + 1) * 128], ALU.mult), waits=[("cv", v)])
            if h == H - 1:
                self.bank_done(bA, "cv", vD)
                self.bank_done(bB, "cv", vP)
                self.bank_done(bE, "cv", vP)
                self.bank_done(bE, "cs", sx2)
        d0_cv = self.now("cv")

        if STOP_PHASE == "sC":
            return
        d0_cv2 = self.now("cv")
        tl = None
        for h in range(H):
            tl = self.T(lambda e, h2=h, b=pb[bT]: e.transpose(
                b.ap().bitcast(SOLVE_DT)[0:128, h2 * 128:(h2 + 1) * 128],
                self.w_sv[s][:, h2 * 384: h2 * 384 + 128], self.c_idf32r[:, :]),
                waits=([("cv", d0_cv2)] + self.bank(bT)) if h == 0 else [])
        dkt_ev = self.V(lambda e, b=pb[bT]: e.tensor_copy(
            self.w_DkT[s][:, :], b.ap().bitcast(SOLVE_DT)[0:128, 0:512]),
            waits=[("ct", tl)])
        self.bank_done(bT, "cv", dkt_ev)
        x_ready = self.now("cv")
        if STOP_PHASE == "sD0":
            return
        for r in range(1, 8):
            cur_a = (r % 2 == 1)
            roff = 0 if cur_a else 128
            dk_psoff = 0 if cur_a else 128
            x_psoff = 128 - dk_psoff
            dk_dst = 256 if cur_a else 0
            mq = None
            wts = [("cv", dkt_ev), ("cv", x_ready)]
            for h in range(H):
                bi2 = bA if h < 2 else bB
                w2 = (wts + self.bank(bA) + self.bank(bB)) if h == 0 else []
                mq = self.T(lambda e, h2=h, b=pb[bi2], ro=roff: e.matmul(
                    b[:, (h2 % 2) * 256: (h2 % 2) * 256 + 256],
                    self.w_DkT[s][:, h2 * 128:(h2 + 1) * 128],
                    self.w_sv[s][:, h2 * 384 + ro: h2 * 384 + ro + 256],
                    start=True, stop=True), waits=w2)
            sv3 = self.w_sv[s][:].rearrange("p (h q) -> p h q", q=384)
            ev1 = None
            if r < 7 and "ev1" not in SKIP:
                for half in range(2):
                    b2 = pb[bA] if half == 0 else pb[bB]
                    pv = b2.ap().rearrange("p (h q) -> p h q", q=256)
                    ev1 = self.V(lambda e, pv2=pv, hf=half, dd=dk_dst, dp=dk_psoff:
                                 e.tensor_copy(
                        sv3[:, 2 * hf: 2 * hf + 2, dd: dd + 128],
                        pv2[:, 0:2, dp: dp + 128]),
                        waits=([("ct", mq)] + self.snap("cv")) if half == 0
                        else [])
            for half in range(2 if "ev2" not in SKIP else 0):
                b2 = pb[bA] if half == 0 else pb[bB]
                pv = b2.ap().rearrange("p (h q) -> p h q", q=256)
                ev2 = self.V(lambda e, pv2=pv, hf=half, xp=x_psoff: e.tensor_tensor(
                    sv3[:, 2 * hf: 2 * hf + 2, 128:256],
                    sv3[:, 2 * hf: 2 * hf + 2, 128:256],
                    pv2[:, 0:2, xp: xp + 128], ALU.add),
                    waits=([("ct", mq)] + self.snap("cv")) if half == 0 else [])
            x_ready = ev2
            self.bank_done(bA, "cv", ev2)
            self.bank_done(bB, "cv", ev2)
            if ev1 is not None:
                self.bank_done(bA, "cv", ev1)
                self.bank_done(bB, "cv", ev1)
            if r < 7 and "tr" not in SKIP:
                tl = None
                for h in range(H):
                    tl = self.T(lambda e, h2=h, b=pb[bT], dd=dk_dst: e.transpose(
                        b.ap().bitcast(SOLVE_DT)[0:128, h2 * 128:(h2 + 1) * 128],
                        self.w_sv[s][:, h2 * 384 + dd: h2 * 384 + dd + 128],
                        self.c_idf32r[:, :]),
                        waits=([("cv", ev1)] + self.bank(bT)) if h == 0 else [])
                dkt_ev = self.V(lambda e, b=pb[bT]: e.tensor_copy(
                    self.w_DkT[s][:, :], b.ap().bitcast(SOLVE_DT)[0:128, 0:512]),
                    waits=[("ct", tl)] + self.snap("cv"))
                self.bank_done(bT, "cv", dkt_ev)
            if STOP_PHASE == "sD1" and r == 1:
                return

        if STOP_PHASE == "sD":
            return
        for h in range(H):
            col = tau * H + h
            soff = (s * H + h) * DV
            sg = self.S(lambda e, h2=h: e.activation(
                self.w_gI[s][:, :], self.c_idbf[:, :], AF.Identity,
                scale=self.b_gam4[s][:, h2:h2 + 1]),
                waits=self.snap("cs") + self.snap("ct"))
            sk = self.S(lambda e, h2=h: e.activation(
                self.w_ksc[s][:, :],
                self.a_k[:, tau * KD + h2 * DK: tau * KD + (h2 + 1) * DK],
                AF.Identity, scale=self.b_lamb4[s][:, h2:h2 + 1]), waits=[("cs", sg)])
            w = self.bank(bA) + ([("cv", x_ready)] if h == 0 else [])
            self.T(lambda e, h2=h, b=pb[bA], so=soff: e.matmul(
                b[:, 0:DV], fmp(kfm, h2, 0), self.st1[:, so:so + DV],
                start=True, stop=False), waits=w)
            mmz = self.T(lambda e, h2=h, b=pb[bA], so=soff: e.matmul(
                b[:, 0:DV], fmp(kfm, h2, 1), self.st2[0:64, so:so + DV],
                start=False, stop=True))
            vz = self.V(lambda e, b=pb[bA], h2=h, c2=col: e.scalar_tensor_tensor(
                self.w_Zh[s][:, :], b[:, 0:DV],
                self.b_negLam[:, c2:c2 + 1],
                self.a_v[:, tau * VD + h2 * DV: tau * VD + (h2 + 1) * DV],
                ALU.mult, ALU.add),
                waits=[("ct", mmz)] + self.snap("cv") + self.snap("ct")
                + self.snap("cs"))
            self.bank_done(bA, "cv", vz)
            wW = self.bank(bE)
            mmw = self.T(lambda e, h2=h, b=pb[bE]: e.matmul(
                b[:, 0:DV],
                self.w_sv[s][:, h2 * 384 + 128: h2 * 384 + 256],
                self.w_Zh[s][:, :], start=True, stop=True),
                waits=wW + [("cv", vz)])
            sw = self.S(lambda e, b=pb[bE], h2=h: e.activation(
                self.w_W[s][:, h2 * DV:(h2 + 1) * DV], b[:, 0:DV], AF.Copy),
                waits=[("ct", mmw)] + self.snap("cs"))
            self.bank_done(bE, "cs", sw)
            wq = self.bank(bB)
            self.T(lambda e, h2=h, b=pb[bB], so=soff: e.matmul(
                b[:, 0:DV], fmp(qfm, h2, 0), self.st1[:, so:so + DV],
                start=True, stop=False), waits=wq)
            mmq = self.T(lambda e, h2=h, b=pb[bB], so=soff: e.matmul(
                b[:, 0:DV], fmp(qfm, h2, 1), self.st2[0:64, so:so + DV],
                start=False, stop=True))
            so1 = self.S(lambda e, b=pb[bB], c2=col: e.activation(
                self.w_O[s][:, :], b[:, 0:DV], AF.Identity,
                scale=self.b_Lam[:, c2:c2 + 1]), waits=[("ct", mmq)])
            self.bank_done(bB, "cs", so1)
            wo = self.bank(bB)
            mmo = self.T(lambda e, h2=h, b=pb[bB]: e.matmul(
                b[:, 0:DV], self.w_Pb[s][:, h2 * 128:(h2 + 1) * 128],
                self.w_W[s][:, h2 * DV:(h2 + 1) * DV], start=True, stop=True),
                waits=wo + [("cs", sw)])
            vo = self.V(lambda e, b=pb[bB]: e.tensor_tensor(
                self.w_O[s][:, :], self.w_O[s][:, :], b[:, 0:DV], ALU.add),
                waits=[("ct", mmo), ("cs", so1)])
            self.bank_done(bB, "cv", vo)
            wt = self.bank(bT)
            self.T(lambda e, b=pb[bT], so=soff: e.matmul(
                b[:, 0:DV], self.w_gI[s][:, :], self.st1[:, so:so + DV],
                start=True, stop=False), waits=wt + [("cs", sk)])
            mm1 = self.T(lambda e, b=pb[bT], h2=h: e.matmul(
                b[:, 0:DV], self.w_ksc[s][:, 0:128],
                self.w_W[s][:, h2 * DV:(h2 + 1) * DV], start=False, stop=True))
            bT2 = 6 + (1 - s)
            wt2 = self.bank(bT2)
            self.T(lambda e, b=pb[bT2], so=soff: e.matmul(
                b[0:64, 0:DV], self.w_gI[s][0:64, 0:64], self.st2[0:64, so:so + DV],
                start=True, stop=False), waits=wt2)
            mm2 = self.T(lambda e, b=pb[bT2], h2=h: e.matmul(
                b[0:64, 0:DV], self.w_ksc[s][:, 128:192],
                self.w_W[s][:, h2 * DV:(h2 + 1) * DV], start=False, stop=True))
            v1 = self.V(lambda e, b=pb[bT], so=soff: e.tensor_copy(
                self.st1[:, so:so + DV], b[:, 0:DV]), waits=[("ct", mm1)])
            self.bank_done(bT, "cv", v1)
            v2 = self.V(lambda e, b=pb[bT2], so=soff: e.tensor_copy(
                self.st2[0:64, so:so + DV], b[0:64, 0:DV]), waits=[("ct", mm2)])
            self.bank_done(bT2, "cv", v2)
            sq = self.S(lambda e: e.activation(
                self.w_osq[s][:, :], self.w_O[s][:, :], AF.Square,
                accum_out=self.b_ssq[s][:, 0:1]), waits=[("cv", vo)] + self.snap("cs"))
            sq = self.S(lambda e: e.activation(
                self.b_ssq[s][:, 1:2], self.b_ssq[s][:, 0:1], AF.Ln,
                scale=np.float32(1.0 / DV), bias=np.float32(EPS_RMS)), waits=[("cs", sq)])
            sq = self.S(lambda e: e.activation(
                self.b_ssq[s][:, 1:2], self.b_ssq[s][:, 1:2], AF.Exp, scale=np.float32(-0.5)),
                waits=[("cs", sq)])
            vg2 = self.V(lambda e: e.scalar_tensor_tensor(
                self.w_osq[s][:, :], self.w_O[s][:, :], self.b_ssq[s][:, 1:2],
                self.d_gnw[:, :], ALU.mult, ALU.mult),
                waits=[("cs", sq)] + self.snap("cv"))
            self.V(lambda e, h2=h, t2=tau: e.tensor_tensor(
                self.a_big[:, t2 * VD + h2 * DV: t2 * VD + (h2 + 1) * DV],
                self.w_osq[s][:, :],
                self.a_gs[:, t2 * VD + h2 * DV: t2 * VD + (h2 + 1) * DV], ALU.mult),
                waits=[("cv", vg2)])

    def p5_tile(self, g, tau):
        pb = self.pb
        s, c = divmod(tau, CPG)
        t0 = g * TPG
        xt = self.x2[:, tau * D: tau * D + D]
        qt0 = self.qt0
        W0 = self.snap("cv", "cs")
        v = self.V(lambda e: e.reduce_sum(self.b_stat[:, 4:5], xt, axis=AX.X,
                                          negate=True), waits=W0)
        v = self.V(lambda e: e.tensor_scalar(self.b_stat[:, 5:6], self.b_stat[:, 4:5],
                                             1.0 / D, None, ALU.mult),
                   waits=[("cv", v)])
        s1 = self.S(lambda e: e.activation(qt0[:, :], xt, AF.Identity,
                                           bias=self.b_stat[:, 5:6]),
                    waits=[("cv", v)] + self.snap("cs"))
        s1 = self.S(lambda e: e.activation(self.qt1, qt0[:, :], AF.Square,
                                           accum_out=self.b_stat[:, 6:7]),
                    waits=[("cs", s1)])
        s1 = self.S(lambda e: e.activation(self.b_stat[:, 7:8], self.b_stat[:, 6:7],
                                           AF.Ln, scale=np.float32(1.0 / D), bias=np.float32(EPS_LN)),
                    waits=[("cs", s1)])
        s1 = self.S(lambda e: e.activation(self.b_stat[:, 7:8], self.b_stat[:, 7:8],
                                           AF.Exp, scale=np.float32(-0.5)), waits=[("cs", s1)])
        s1 = self.S(lambda e: e.activation(qt0[:, :], qt0[:, :], AF.Identity,
                                           scale=self.b_stat[:, 7:8]),
                    waits=[("cs", s1)])
        v = self.V(lambda e: e.reduce_max(self.qs[:].unsqueeze(-1),
                                          qt0[:].rearrange("p (b k) -> p b k", k=32),
                                          axis=AX.X, apply_absolute_value=True),
                   waits=[("cs", s1)] + self.snap("cv"))
        v = self.V(lambda e: e.tensor_scalar(self.qsi[:], self.qs[:], 1e-30, None,
                                             ALU.max), waits=[("cv", v)])
        v = self.V(lambda e: e.tensor_scalar(self.qsi[:].bitcast(I32),
                                             self.qsi[:].bitcast(I32),
                                             self.c_pow2[:, 0:1], None, ALU.bitwise_and),
                   waits=[("cv", v)])
        v = self.V(lambda e: e.tensor_single_scalar(self.qm[:], self.qs[:], 0.0,
                                                    ALU.is_gt), waits=[("cv", v)])
        v = self.V(lambda e: e.tensor_scalar(self.qw[:], self.qsi[:], 0.25, None,
                                             ALU.mult), waits=[("cv", v)])
        v = self.V(lambda e: e.memset(self.qsi[:], 1.0), waits=[("cv", v)])
        v = self.V(lambda e: e.select(self.qs[:], self.qm[:], self.qw[:], self.qsi[:], add_drain=True),
                   waits=[("cv", v)])
        v = self.V(lambda e: e.tensor_scalar(self.qs[:], self.qs[:],
                                             self.d_dactp[:, 0:1], None, ALU.mult),
                   waits=[("cv", v)])
        v = self.V(lambda e: e.reciprocal(self.qsi[:], self.qs[:]), waits=[("cv", v)])
        vq = self.emit_quant_core(qt0[:], self.qt1, self.qt2, self.qs[:], self.qsi[:],
                                  self.a_xq[:, tau * D: tau * D + D], [("cv", v)])
        for kk in range(8):
            bi = (tau % 2) * 2 + (kk % 2)
            t = self.T(lambda e, k2=kk, b=pb[bi], t2=tau: e.transpose(
                b.ap().bitcast(BF16)[0:128, 0:128],
                self.a_xq[:, t2 * D + k2 * 128: t2 * D + (k2 + 1) * 128],
                self.c_idbf[:, :]), waits=[("cv", vq)] + self.bank(bi))
            vv = self.V(lambda e, k2=kk, b=pb[bi], s2=s, c2=c: e.tensor_copy(
                self.a_zs[:, k2 * 512 + s2 * 256 + c2 * 128:
                          k2 * 512 + s2 * 256 + c2 * 128 + 128],
                b.ap().bitcast(BF16)[0:128, 0:128]), waits=[("ct", t)])
            self.bank_done(bi, "cv", vv)
        xqfm_cv = self.now("cv")
        obuf = self.a_out[tau % 2]
        for half in range(2):
            bi = 4 + (tau % 2) * 2 + half
            mm = None
            for kc in range(8):
                lhs = self.a_zs[:, kc * 512 + s * 256 + c * 128:
                                kc * 512 + s * 256 + c * 128 + 128]
                w = ([("cv", xqfm_cv), self.ld_wq[0]] + self.bank(bi)) \
                    if kc == 0 else ([self.ld_wq[1]] if kc == 4 else [])
                mm = self.T(lambda e, l=lhs, b=pb[bi], k2=kc, h2=half: e.matmul(
                    b[:, 0:512], l,
                    self.wqslot[k2 // 4][:, (k2 % 4) * 1024 + h2 * 512:
                                         (k2 % 4) * 1024 + h2 * 512 + 512],
                    start=(k2 == 0), stop=False), waits=w)
            mm = self.T(lambda e, b=pb[bi], h2=half: e.matmul(
                b[:, 0:512], self.c_ones1b[:, :],
                self.d_fbias[0:1, h2 * 512: h2 * 512 + 512],
                start=False, stop=True))
            self.wq_free = mm
            vv = self.V(lambda e, b=pb[bi], h2=half, ob=obuf: e.tensor_tensor(
                ob[:, h2 * 512: h2 * 512 + 512],
                self.x2[:, tau * D + h2 * 512: tau * D + h2 * 512 + 512],
                b[:, 0:512], ALU.add),
                waits=[("ct", mm), (f"dst{tau % 2}", self.out_store[tau % 2])])
            self.bank_done(bi, "cv", vv)
        self.dma("gpsimd", self.dout[s, t0 + c * 128: t0 + c * 128 + 128, :],
                 obuf, f"dst{tau % 2}", waits=self.snap("cv"))
        self.out_store[tau % 2] = self.now(f"dst{tau % 2}")


def kernel(**inputs):
    x = np.ascontiguousarray(inputs["L0_attn_x"], dtype=np.float32)
    convw = np.concatenate([np.ascontiguousarray(inputs["L0_attn_conv_x_w"][:, 0, :]),
                            np.ascontiguousarray(inputs["L0_attn_conv_z_w"][:, 0, :])],
                           axis=0)
    convb = np.concatenate([np.ascontiguousarray(inputs["L0_attn_conv_x_b"]),
                            np.ascontiguousarray(inputs["L0_attn_conv_z_b"])])[None, :]
    small = np.zeros((1, 16), np.float32)
    small[0, 0:4] = inputs["L0_attn_b_b"]
    small[0, 4:8] = inputs["L0_attn_A_log"]
    small[0, 8:12] = inputs["L0_attn_dt_bias"]
    small[0, 12:13] = inputs["L0_ffn_delta_act"]
    base = {
        "w_in": np.ascontiguousarray(inputs["L0_attn_W_in"]),
        "conv_w": convw,
        "conv_b": convb,
        "w_q": np.ascontiguousarray(inputs["L0_attn_W_q"]),
        "w_k": np.ascontiguousarray(inputs["L0_attn_W_k"]),
        "w_v": np.ascontiguousarray(inputs["L0_attn_W_v"]),
        "w_g": np.ascontiguousarray(inputs["L0_attn_W_g"]),
        "w_gkb": np.concatenate([np.ascontiguousarray(inputs["L0_attn_W_gk"]),
                                 np.ascontiguousarray(inputs["L0_attn_W_b"])], axis=1),
        "small_in": small,
        "g_norm_w": np.ascontiguousarray(inputs["L0_attn_g_norm_w"][None, :]),
        "w_so": np.ascontiguousarray(inputs["L0_attn_W_so"]),
        "w_out": np.ascontiguousarray(inputs["L0_attn_W_out"]),
        "ffn_w": np.ascontiguousarray(inputs["L0_ffn_weight"]),
        "ffn_b": np.ascontiguousarray(inputs["L0_ffn_bias"][None, :]),
        "ffn_dw": np.ascontiguousarray(inputs["L0_ffn_delta_w"]),
    }
    nc = B().build()
    in_maps = []
    for core in range(NCORES):
        m = dict(base)
        m["x"] = np.ascontiguousarray(x[core * SEQ:(core + 1) * SEQ])
        in_maps.append(m)
    res = run_bass_kernel_spmd(nc, in_maps, core_ids=list(range(NCORES)))
    return np.concatenate([r["out"] for r in res.results], axis=0)


# version 09
# speedup vs baseline: 1.4146x; 1.0143x over previous
import contextlib
import numpy as np
import ml_dtypes
import concourse.bass as bass
import concourse.mybir as mybir
from concourse.bass_utils import run_bass_kernel_spmd

F32 = mybir.dt.float32
F32R = mybir.dt.float32r
SOLVE_DT = F32R
BF16 = mybir.dt.bfloat16
I32 = mybir.dt.int32
AF = mybir.ActivationFunctionType
ALU = mybir.AluOpType
AX = mybir.AxisListType
BF = ml_dtypes.bfloat16

NCORES = 8
SEQ = 2
L = 2048
D = 1024
DI = 2048
KD = 768
VD = 1536
H = 4
DK = 192
DV = 384
C = 128
NG = 8
TPG = 256
CPG = TPG // C
NTAU = SEQ * CPG
EPS_LN = 1e-5
EPS_L2 = 1e-6
EPS_RMS = 1e-5
LN2C = float(np.log(2.0))
MAGIC = 12582912.0
STOP_PHASE = None
SKIP = set()
POW2MASK = float(-8388608)

ENGINES = ("sync", "scalar", "tensor", "vector", "gpsimd")


class B:

    def __init__(self):
        self.nc = bass.Bass("TRN2", target_bir_lowering=False)
        self.steps = {e: [] for e in ENGINES}
        self.counts = {}
        self.bank_free = [[] for _ in range(8)]

    def now(self, sem):
        return self.counts.get(sem, 0)

    def _step(self, eng, fn, waits, inc, amt):
        waits = [w for w in waits if w[1] > 0]
        val = None
        if inc is not None:
            self.counts[inc] = self.counts.get(inc, 0) + amt
            val = self.counts[inc]
        self.steps[eng].append(dict(fn=fn, waits=waits, inc=inc, amt=amt, val=val))
        return val

    def dma(self, queue, out_ap, in_ap, sem="dw", waits=(), slow=False):
        return self._step(queue,
                          lambda e, o=out_ap, i=in_ap, sl=slow: e.dma_start(
                              out=o, in_=i, allow_slow_non_contiguous=sl),
                          waits, sem, 16)

    def T(self, fn, waits=()):
        return self._step("tensor", fn, waits, "ct", 1)

    def V(self, fn, waits=()):
        return self._step("vector", fn, waits, "cv", 1)

    def S(self, fn, waits=()):
        return self._step("scalar", fn, waits, "cs", 1)

    def G(self, fn, waits=()):
        return self._step("gpsimd", fn, waits, "cg", 1)

    def snap(self, *sems):
        return [(s, self.now(s)) for s in sems]

    def all_compute(self):
        return self.snap("ct", "cv", "cs", "cg")

    def bank(self, i, extra=()):
        w = self.bank_free[i] + list(extra)
        self.bank_free[i] = []
        return w

    def bank_done(self, i, sem, val):
        self.bank_free[i].append((sem, val))

    def build(self):
        nc = self.nc
        self.dram()
        self.consts()
        self.es = contextlib.ExitStack()
        with self.es:
            self.alloc()
            self.prep()
            self.pump()
            self.pump()
            self.emit_ln1(0)
            for g in range(NG):
                self.group(g)
            self._step("sync", lambda e: e.nop(),
                       [("dst0", self.now("dst0")), ("dst1", self.now("dst1"))],
                       None, 0)
            compute_sems = ("ct", "cv", "cs", "cg")
            used = {cs_: set() for cs_ in compute_sems}
            for ename in ENGINES:
                for st in self.steps[ename]:
                    for (sname, val) in st["waits"]:
                        if sname in used:
                            used[sname].add(val)
            ranks = {cs_: {v: i + 1 for i, v in enumerate(sorted(used[cs_]))}
                     for cs_ in compute_sems}

            def remap_wait(sname, val):
                if sname not in ranks:
                    return val
                return ranks[sname][val]

            blk = self.es.enter_context(nc.Block())
            for ename in ENGINES:
                steps = self.steps[ename]
                sems = self.sems

                def emitter(eng, steps=steps, sems=sems):
                    for st in steps:
                        for (sname, val) in st["waits"]:
                            eng.wait_ge(sems[sname], remap_wait(sname, val))
                        ins = st["fn"](eng)
                        inc = st["inc"]
                        if inc is not None:
                            if inc in used:
                                if st["val"] in used[inc]:
                                    ins.then_inc(sems[inc], 1)
                            else:
                                ins.then_inc(sems[inc], st["amt"])

                getattr(blk, ename)(emitter)
        return nc

    def dram(self):
        nc = self.nc
        t = lambda n, s: nc.dram_tensor(n, s, F32, kind="ExternalInput").ap()
        self.dx = t("x", [SEQ, L, D])
        self.dwin = t("w_in", [D, DI])
        self.dconvw = t("conv_w", [2 * D, 4])
        self.dconvb = t("conv_b", [1, 2 * D])
        self.dwq = t("w_q", [D, KD])
        self.dwk = t("w_k", [D, KD])
        self.dwv = t("w_v", [D, VD])
        self.dwg = t("w_g", [D, VD])
        self.dwgkb = t("w_gkb", [D, 8])
        self.dsmallin = t("small_in", [1, 16])
        self.dgnw = t("g_norm_w", [1, DV])
        self.dwso = t("w_so", [VD, D])
        self.dwout = t("w_out", [DI, D])
        self.dfw = t("ffn_w", [D, D])
        self.dfb = t("ffn_b", [1, D])
        self.dfdw = t("ffn_dw", [D, 32])
        self.dout = nc.dram_tensor("out", [SEQ, L, D], F32, kind="ExternalOutput").ap()
        self.dwqt = nc.dram_tensor("wqt_scratch", [8, 128, D], BF16, kind="Internal").ap()
        self.dwbf = nc.dram_tensor("wbf_scratch", [23, 128, 4096], BF16,
                                   kind="Internal").ap()
        self.mk_pieces()

    def mk_pieces(self):
        P = []

        def add(subs):
            P.append(dict(w=max(c + n for c, n, _ in subs), subs=subs))

        for p in range(4):
            add([(kc * 512, 512,
                  self.dwin[kc * 128:(kc + 1) * 128, p * 512:(p + 1) * 512])
                 for kc in range(8)])

        def proj(d, outcols):
            for p in range((outcols + 511) // 512):
                pc = min(512, outcols - p * 512)
                add([(kc * pc, pc,
                      d[kc * 128:(kc + 1) * 128, p * 512:p * 512 + pc])
                     for kc in range(8)])

        proj(self.dwq, KD)
        proj(self.dwk, KD)
        add([(kc * 8, 8, self.dwgkb[kc * 128:(kc + 1) * 128, :]) for kc in range(8)])
        proj(self.dwv, VD)
        proj(self.dwg, VD)
        for mhalf in range(2):
            for khalf in range(2):
                add([(kc6 * 512, 512,
                      self.dwso[(khalf * 6 + kc6) * 128:(khalf * 6 + kc6 + 1) * 128,
                                mhalf * 512:(mhalf + 1) * 512])
                     for kc6 in range(6)])
        for half in range(2):
            for khalf in range(2):
                add([(kc * 512, 512,
                      self.dwout[(khalf * 8 + kc) * 128:(khalf * 8 + kc + 1) * 128,
                                 half * 512:(half + 1) * 512])
                     for kc in range(8)])
        self.pieces = P
        self.wsched = [(g, pi) for g in range(NG) for pi in range(len(P))]

    def pump(self):
        if self.wnext >= len(self.wsched):
            return
        g, pi = self.wsched[self.wnext]
        slot = self.wnext % 2
        self.wnext += 1
        pc = self.pieces[pi]
        sem = f"dw{slot}"
        if g == 0:
            w = [("ct", self.slot_free[slot])]
            if self.wb_last_slot[slot] is not None:
                w.append(("dwb", self.wb_last_slot[slot]))
            for (coff, ncols, src) in pc["subs"]:
                self.dma("gpsimd", self.wslot[slot][:, coff:coff + ncols], src,
                         sem, waits=w)
                w = []
            ldv = self.now(sem)
            self.dma("sync", self.dwbf[pi, :, 0:pc["w"]],
                     self.wslot[slot][:, 0:pc["w"]], "dwb",
                     waits=[(sem, ldv), ("dwb", self.now("dwb"))])
            self.wb_val[pi] = self.now("dwb")
            self.wb_last_slot[slot] = self.wb_val[pi]
        else:
            sem = f"ds{slot}"
            wbw = max(self.wb_val[pi], self.wb_last_slot[slot] or 0)
            self.dma("sync", self.wslot[slot][:, 0:pc["w"]],
                     self.dwbf[pi, :, 0:pc["w"]], sem,
                     waits=[("ct", self.slot_free[slot]), ("dwb", wbw)])
            ldv = self.now(sem)
        self.wrecs.append(dict(slot=slot, ld=(sem, ldv)))

    def use(self):
        r = self.wrecs[self.wuse]
        self.wuse += 1
        return r

    def release(self, rec, ctval):
        self.slot_free[rec["slot"]] = ctval
        self.pump()

    def consts(self):
        nc = self.nc
        self.i_idf32 = nc.inline_tensor(np.eye(128, dtype=np.float32), "c_idf32")
        self.i_idbf = nc.inline_tensor(np.eye(128, dtype=np.float32).astype(BF), "c_idbf")
        self.i_i32rep = nc.inline_tensor(
            np.tile(np.eye(32, dtype=np.float32), (4, 1)).astype(BF), "c_i32rep")
        self.i_maskSN = nc.inline_tensor(
            -np.triu(np.ones((128, 128), dtype=np.float32), 1), "c_maskSN")
        self.i_maskI = nc.inline_tensor(
            np.triu(np.ones((128, 128), dtype=np.float32)), "c_maskI")
        self.i_i4bf = nc.inline_tensor(
            np.tile(np.eye(128, dtype=np.float32), (1, 4)).astype(BF), "c_i4bf")
        sel = np.zeros((4, H * 128), np.float32)
        for h in range(H):
            sel[h, h * 128:(h + 1) * 128] = 1.0
        self.i_sel4 = nc.inline_tensor(sel, "c_sel4")
        self.i_ones1f = nc.inline_tensor(np.ones((1, 128), np.float32), "c_ones1f")
        self.i_ones1b = nc.inline_tensor(np.ones((1, 128), np.float32).astype(BF),
                                         "c_ones1b")
        self.i_pow2 = nc.inline_tensor(np.full((128, 1), -8388608, np.int32), "c_pow2")

    def alloc(self):
        nc, es = self.nc, self.es
        sb = lambda n, s, d: es.enter_context(nc.sbuf_tensor(n, s, d))
        ps = lambda n: es.enter_context(nc.psum_tensor(n, [128, 512], F32))
        self.c_idf32 = sb("s_idf32", [128, 128], F32)
        self.c_idbf = sb("s_idbf", [128, 128], BF16)
        self.c_i32rep = sb("s_i32rep", [128, 32], BF16)
        self.c_maskSN = sb("s_maskSN", [128, 128], F32)
        self.c_maskI = sb("s_maskI", [128, 128], F32)
        self.c_i4bf = sb("s_i4bf", [128, 512], BF16)
        self.c_sel4 = sb("s_sel4", [4, H * 128], F32)
        self.c_ones1f = sb("s_ones1f", [1, 128], F32)
        self.c_ones1b = sb("s_ones1b", [1, 128], BF16)
        self.c_pow2 = sb("s_pow2", [128, 1], I32)
        self.c_idf32r = sb("s_idf32r", [128, 128], SOLVE_DT)
        self.d_convdiag = sb("d_convdiag", [128, 16 * 4 * 32], BF16)
        self.d_cb = sb("d_cb", [128, 16], F32)
        self.d_bb = sb("d_bb", [128, 4], F32)
        self.d_dtb = sb("d_dtb", [128, 4], F32)
        self.d_negAexp = sb("d_negAexp", [128, 4], F32)
        self.d_gnw = sb("d_gnw", [128, DV], F32)
        self.d_fbias = sb("d_fbias", [1, D], BF16)
        self.d_dactp = sb("d_dactp", [128, 1], F32)
        self.d_small = sb("d_small", [1, 64], F32)
        self.wslot = [sb(f"wslot{i}", [128, 4096], BF16) for i in range(2)]
        self.wqslot = [sb(f"wqslot{i}", [128, 4096], BF16) for i in range(2)]
        self.a_x = [sb(f"a_x{i}", [128, D], F32) for i in range(2)]
        self.a_xn = sb("a_xn", [128, 8 * 520], BF16)
        self.a_big = sb("a_big", [128, 8320], BF16)
        self.a_xs = sb("a_xs", [128, 6144], BF16)
        self.a_zs = sb("a_zs", [128, 4096], BF16)
        self.a_q = sb("a_q", [128, NTAU * KD], BF16)
        self.a_k = sb("a_k", [128, NTAU * KD], BF16)
        self.a_v = sb("a_v", [128, NTAU * VD], BF16)
        self.a_gs = sb("a_gs", [128, NTAU * VD], BF16)
        self.st1 = sb("st1", [128, SEQ * H * DV], BF16)
        self.st2 = sb("st2", [64, SEQ * H * DV], BF16)
        self.w_pre = sb("w_pre", [128, 128], F32)
        self.w_t1 = sb("w_t1", [128, 128], F32)
        self.w_sv = [sb(f"w_sv_{s}", [128, H * 384], SOLVE_DT) for s in range(SEQ)]
        self.w_DkT = [sb(f"w_DkT_{s}", [128, 512], SOLVE_DT) for s in range(SEQ)]
        self.w_Zh = [sb(f"w_Zh_{s}", [128, DV], SOLVE_DT) for s in range(SEQ)]
        self.w_Pb = [sb(f"w_Pb_{s}", [128, 512], BF16) for s in range(SEQ)]
        self.w_qfm = [sb(f"w_qfm_{s}", [128, H * 256], BF16) for s in range(SEQ)]
        self.w_kfm = [sb(f"w_kfm_{s}", [128, H * 256], BF16) for s in range(SEQ)]
        self.w_zw = sb("w_zw", [128, 6144], BF16)
        self.w_Z = [self.w_zw[:, s * (H * DV):(s + 1) * (H * DV)] for s in range(SEQ)]
        self.w_W = [self.w_zw[:, 3072 + s * (H * DV):3072 + (s + 1) * (H * DV)]
                    for s in range(SEQ)]
        self.r_q = self.w_zw[:, 0:NTAU * KD]
        self.r_k = self.w_zw[:, 3072:3072 + NTAU * KD]
        self.w_O = [sb(f"w_O_{s}", [128, DV], F32) for s in range(SEQ)]
        self.w_osq = self.w_Zh
        self.w_ksc = [sb(f"w_ksc_{s}", [128, DK], BF16) for s in range(SEQ)]
        self.w_gI = [sb(f"w_gI_{s}", [128, 128], BF16) for s in range(SEQ)]
        self.b_beta = sb("b_beta", [128, NTAU * H], F32)
        self.b_G = sb("b_G", [128, NTAU * H], F32)
        self.b_Lam = sb("b_Lam", [128, NTAU * H], F32)
        self.b_negLam = sb("b_negLam", [128, NTAU * H], F32)
        self.b_bias = sb("b_bias", [128, NTAU * H], F32)
        self.b_Growt = sb("b_Growt", [4, NTAU * 128], F32)
        self.b_lamb4 = [sb(f"b_lamb4_{s}", [128, H], F32) for s in range(SEQ)]
        self.b_gam4 = [sb(f"b_gam4_{s}", [128, H], F32) for s in range(SEQ)]
        self.b_rs = sb("b_rs", [128, H], F32)
        self.b_ssq = [sb(f"b_ssq_{s}", [128, 2], F32) for s in range(SEQ)]
        self.b_stat = sb("b_stat", [128, 16], F32)
        self.r_gkb = sb("r_gkb", [128, NTAU * 8], F32)
        self.p2t = sb("p2t", [128, 24], F32)
        self.carry = sb("carry", [128, 2 * 16], BF16)
        self.qt0p = [sb(f"qt0_{i}", [128, D], F32) for i in range(2)]
        self.qsp = [[sb(f"qs_{i}", [128, 32], F32), sb(f"qsi_{i}", [128, 32], F32),
                     sb(f"qm_{i}", [128, 32], I32), sb(f"qw_{i}", [128, 32], F32)]
                    for i in range(2)]
        self.r_sq = self.qt0p[0][:, 0:KD]
        xsf = self.a_xs.ap().bitcast(F32)
        self.qt1 = xsf[:, 0:D]
        self.qt2 = xsf[:, D:2 * D]
        wzf = self.w_zw.ap().bitcast(F32)
        self.qt1b = wzf[:, 0:D]
        self.qt2b = wzf[:, D:2 * D]
        self.ln1sq = wzf[:, 2 * D:3 * D]
        self.a_xnt_p = [self.a_xs[:, 4096:4096 + D],
                        self.a_xs[:, 5120:5120 + D]]
        self.a_xq = self.a_gs
        bigf = self.a_big.ap().bitcast(F32)
        self.x2 = bigf[:, 0:NTAU * D]
        self.fb_stage = bigf[0:1, 0:D]
        self.d_cw = bigf[:, 3200:3264]
        aqf = self.a_q.ap().bitcast(F32)
        self.fq = self.a_q[:, 0:D]
        self.fdw = aqf[:, 1024:1056]
        self.stgap = self.a_q
        self.stg = 1024
        self.pb = [ps(f"pb{i}") for i in range(8)]
        self.out_store = [0, 0]
        self.sems = {}
        for sname in ("dx", "dw0", "dw1", "ds0", "ds1", "dwb", "dq0", "dq1",
                      "dmisc", "dst0", "dst1", "ct", "cv", "cs", "cg"):
            self.sems[sname] = es.enter_context(nc.semaphore(sname))
        self.slot_free = [0, 0]
        self.wb_val = [None] * len(self.pieces)
        self.wb_last_slot = [None, None]
        self.wrecs = []
        self.wuse = 0
        self.wnext = 0
        self.wq_free = 0
        self.ld_wq = [None, None]
        self.ax_free = [("cv", 0)] * 4
        self.x2_ready = [("cv", 0)] * NTAU
        self.x2_store = [(f"dst{t % 2}", 0) for t in range(NTAU)]
        self.qscr_free = [("cv", 0), ("cv", 0)]
        self.xnt_free = [("ct", 0), ("ct", 0)]
        self.qdmisc = 0
        self.gs_cs = 0
        self.conv_cs = 0

    def prep(self):
        for dst, src in ((self.c_idf32, self.i_idf32), (self.c_idbf, self.i_idbf),
                         (self.c_i32rep, self.i_i32rep),
                         (self.c_maskSN, self.i_maskSN), (self.c_maskI, self.i_maskI),
                         (self.c_i4bf, self.i_i4bf), (self.c_sel4, self.i_sel4),
                         (self.c_ones1f, self.i_ones1f), (self.c_ones1b, self.i_ones1b),
                         (self.c_pow2, self.i_pow2)):
            self.dma("sync", dst[:], src.ap(), "dmisc")
        self.dma("sync", self.d_small[0:1, 0:16], self.dsmallin, "dmisc")
        self.dma("sync", self.d_cw.rearrange("p (c j) -> p c j", j=4),
                 self.dconvw.rearrange("(c p) j -> p c j", p=128), "dmisc", slow=True)
        self.dma("sync", self.d_cb[:].rearrange("p (c o) -> p c o", o=1),
                 self.dconvb.rearrange("o (c p) -> p c o", p=128), "dmisc", slow=True)
        self.dma("sync", self.d_gnw[0:1, :], self.dgnw, "dmisc")
        self.dma("sync", self.fb_stage, self.dfb, "dmisc")
        WM = [("dmisc", self.now("dmisc"))]
        pb = self.pb

        def bcast(bank_i, dst_ap, src_row, n, extraw=()):
            t = self.T(lambda e, b=pb[bank_i], s=src_row, n2=n: e.matmul(
                b[:, 0:n2], self.c_ones1f[:, :], s, start=True, stop=True),
                waits=WM + list(extraw) + self.bank(bank_i))
            v = self.V(lambda e, b=pb[bank_i], d2=dst_ap, n2=n: e.tensor_copy(
                d2, b[:, 0:n2]), waits=[("ct", t)])
            self.bank_done(bank_i, "cv", v)
            return v

        bcast(0, self.d_bb[:, :], self.d_small[0:1, 0:4], 4)
        bcast(1, self.d_dtb[:, :], self.d_small[0:1, 8:12], 4)
        sA = self.S(lambda e: e.activation(self.d_small[0:1, 16:20],
                                           self.d_small[0:1, 4:8], AF.Exp), waits=WM)
        tA = self.T(lambda e: e.matmul(pb[2][:, 0:4], self.c_ones1f[:, :],
                                       self.d_small[0:1, 16:20], start=True, stop=True),
                    waits=[("cs", sA)] + self.bank(2))
        vA = self.V(lambda e: e.tensor_scalar(self.d_negAexp[:, :], pb[2][:, 0:4],
                                              -1.0, None, ALU.mult), waits=[("ct", tA)])
        self.bank_done(2, "cv", vA)
        bcast(3, self.d_gnw[:, :], self.d_gnw[0:1, :], DV)
        self.S(lambda e: e.copy(self.c_idf32r[:, :], self.c_idf32[:, :]), waits=WM)
        self.S(lambda e: e.copy(self.d_fbias[:, :], self.fb_stage), waits=WM)
        sD = self.S(lambda e: e.activation(self.d_small[0:1, 20:21],
                                           self.d_small[0:1, 12:13], AF.Tanh,
                                           scale=np.float32(4.0)), waits=WM)
        sD = self.S(lambda e: e.activation(self.d_small[0:1, 21:22],
                                           self.d_small[0:1, 20:21], AF.Exp,
                                           scale=np.float32(0.5 * LN2C)), waits=[("cs", sD)])
        tD = self.T(lambda e: e.matmul(pb[4][:, 0:1], self.c_ones1f[:, :],
                                       self.d_small[0:1, 21:22], start=True, stop=True),
                    waits=[("cs", sD)] + self.bank(4))
        vD = self.V(lambda e: e.tensor_copy(self.d_dactp[:, :], pb[4][:, 0:1]),
                    waits=[("ct", tD)])
        self.bank_done(4, "cv", vD)
        for ch in range(16):
            for j in range(4):
                off = (ch * 4 + j) * 32
                self.S(lambda e, o=off, c2=ch, j2=j: e.activation(
                    self.d_convdiag[:, o:o + 32], self.c_i32rep[:, :], AF.Identity,
                    scale=self.d_cw[:, c2 * 4 + j2:c2 * 4 + j2 + 1]), waits=WM)
        self.G(lambda e: e.memset(self.st1[:, :], 0.0))
        self.G(lambda e: e.memset(self.st2[:, :], 0.0))
        self.G(lambda e: e.memset(self.carry[:, :], 0.0))

    def quant_w_chunk(self, kc):
        par = kc % 2
        qt0 = self.qt0p[par]
        qs, qsi, qm, qw = self.qsp[par]
        qt1, qt2 = (self.qt1, self.qt2) if par == 0 else (self.qt1b, self.qt2b)
        self.dma("sync", qt0[:], self.dfw[kc * 128:(kc + 1) * 128, :], "dmisc",
                 waits=self.snap("cv", "cs"))
        self.dma("sync", self.fdw, self.dfdw[kc * 128:(kc + 1) * 128, :], "dmisc")
        WD = [("dmisc", self.now("dmisc")), ("cv", self.now("cv")),
              ("cs", self.now("cs"))]
        v = self.V(lambda e: e.reduce_max(qs[:].unsqueeze(-1),
                                          qt0[:].rearrange("p (b k) -> p b k", k=32),
                                          axis=AX.X, apply_absolute_value=True),
                   waits=WD)
        v = self.V(lambda e: e.tensor_scalar(qsi[:], qs[:], 1e-30, None, ALU.max),
                   waits=[("cv", v)])
        v = self.V(lambda e: e.tensor_scalar(qsi[:].bitcast(I32), qsi[:].bitcast(I32),
                                             self.c_pow2[:, 0:1], None, ALU.bitwise_and),
                   waits=[("cv", v)])
        v = self.V(lambda e: e.tensor_single_scalar(qm[:], qs[:], 0.0, ALU.is_gt),
                   waits=[("cv", v)])
        v = self.V(lambda e: e.tensor_scalar(qw[:], qsi[:], 0.25, None, ALU.mult),
                   waits=[("cv", v)])
        v = self.V(lambda e: e.memset(qsi[:], 1.0), waits=[("cv", v)])
        v = self.V(lambda e: e.select(qs[:], qm[:], qw[:], qsi[:], add_drain=True),
                   waits=[("cv", v)])
        s = self.S(lambda e: e.activation(qsi[:], self.fdw, AF.Tanh, scale=np.float32(4.0)),
                   waits=WD + [("cv", v)])
        s = self.S(lambda e: e.activation(qsi[:], qsi[:], AF.Exp, scale=np.float32(0.5 * LN2C)),
                   waits=[("cs", s)])
        v = self.V(lambda e: e.tensor_tensor(qs[:], qs[:], qsi[:], ALU.mult),
                   waits=[("cs", s), ("cv", v)])
        v = self.V(lambda e: e.reciprocal(qsi[:], qs[:]), waits=[("cv", v)])
        vq = self.emit_quant_core(qt0[:], qt1, qt2, qs[:], qsi[:],
                                  self.fq[:, 0:D], [("cv", v)])
        self.qscr_free[par] = ("cv", vq)
        stg = self.stg
        for tt in range(8):
            bi = 5 + (tt % 2)
            t = self.T(lambda e, t2=tt, b=self.pb[bi]: e.transpose(
                b.ap().bitcast(BF16)[0:128, 0:128],
                self.fq[:, t2 * 128:(t2 + 1) * 128], self.c_idbf[:, :]),
                waits=[("cv", vq)] + self.bank(bi)
                + ([("dmisc", self.now("dmisc"))] if tt == 0 else []))
            vv = self.V(lambda e, t2=tt, b=self.pb[bi]: e.tensor_copy(
                self.stgap[:, stg + t2 * 128: stg + (t2 + 1) * 128],
                b.ap().bitcast(BF16)[0:128, 0:128]), waits=[("ct", t)])
            self.bank_done(bi, "cv", vv)
        for tt in range(8):
            self.dma("sync", self.dwqt[tt, :, kc * 128:(kc + 1) * 128],
                     self.stgap[:, stg + tt * 128: stg + (tt + 1) * 128], "dmisc",
                     waits=[("cv", self.now("cv"))] if tt == 0 else ())
        self.qdmisc = self.now("dmisc")

    def emit_quant_core(self, x_in, tmp_sgn, tmp_ph, scale, inv_scale, out_bf, waits):
        V, S = self.V, self.S
        r32 = lambda ap: ap.rearrange("p (b k) -> p b k", k=32)
        bc = lambda ap: ap.unsqueeze(-1).broadcast_to((128, 32, 32))
        v = V(lambda e: e.tensor_tensor(r32(x_in), r32(x_in), bc(inv_scale),
                                        ALU.mult), waits=waits)
        v = V(lambda e: e.tensor_copy(tmp_sgn, x_in), waits=[("cv", v)])
        s = S(lambda e: e.activation(x_in, x_in, AF.Abs),
              waits=[("cv", v)] + self.snap("cs"))
        v = V(lambda e: e.tensor_scalar(x_in, x_in, 6.0, 1e-30, ALU.min, ALU.max),
              waits=[("cs", s)])
        v = V(lambda e: e.tensor_scalar(tmp_ph.bitcast(I32), x_in.bitcast(I32),
                                        self.c_pow2[:, 0:1], None, ALU.bitwise_and),
              waits=[("cv", v)])
        v = V(lambda e: e.tensor_scalar(tmp_ph, tmp_ph, 1.0, 4.0, ALU.max, ALU.min),
              waits=[("cv", v)])
        v = V(lambda e: e.tensor_scalar(tmp_ph, tmp_ph, MAGIC * 0.5, None,
                                        ALU.mult), waits=[("cv", v)])
        v = V(lambda e: e.tensor_tensor(x_in, x_in, tmp_ph, ALU.add),
              waits=[("cv", v)])
        v = V(lambda e: e.tensor_tensor(x_in, x_in, tmp_ph, ALU.subtract),
              waits=[("cv", v)])
        s = S(lambda e: e.sign(tmp_ph, tmp_sgn), waits=[("cv", v)] + self.snap("cs"))
        v = V(lambda e: e.tensor_tensor(x_in, x_in, tmp_ph, ALU.mult),
              waits=[("cs", s)])
        v = V(lambda e: e.tensor_tensor(r32(out_bf), r32(x_in), bc(scale),
                                        ALU.mult), waits=[("cv", v)])
        return v

    def ln1_tiles(self, g):
        last = (g == NG - 1)
        out = []
        cnt = 0
        for s in range(SEQ):
            for (toff, tn) in [(0, 128), (128, 128)] + ([] if last else [(256, 2)]):
                out.append((s, toff, tn, cnt % 2))
                cnt += 1
        return out

    def ln1_tile(self, g, s, toff, tn, bidx, w0=()):
        t0 = g * TPG
        buf = self.a_x[bidx]
        sc = 12 * bidx
        self.dma("sync", buf[0:tn, :],
                 self.dx[s, t0 + toff:t0 + toff + tn, :], "dx",
                 waits=[self.ax_free[bidx], self.ax_free[bidx + 2],
                        ("dx", self.now("dx"))])
        WX = [("dx", self.now("dx"))]
        v = self.V(lambda e, b=buf, n=tn: e.reduce_sum(
            self.b_stat[0:n, sc:sc + 1], b[0:n, :], axis=AX.X, negate=True),
            waits=WX + list(w0))
        v = self.V(lambda e, n=tn: e.tensor_scalar(
            self.b_stat[0:n, sc + 1:sc + 2], self.b_stat[0:n, sc:sc + 1],
            1.0 / D, None, ALU.mult), waits=[("cv", v)])
        s1 = self.S(lambda e, b=buf, n=tn: e.activation(
            b[0:n, :], b[0:n, :], AF.Identity, bias=self.b_stat[0:n, sc + 1:sc + 2]),
            waits=[("cv", v)] + list(w0))
        s1 = self.S(lambda e, b=buf, n=tn: e.activation(
            self.ln1sq[0:n, :], b[0:n, :], AF.Square,
            accum_out=self.b_stat[0:n, sc + 2:sc + 3]), waits=[("cs", s1)])
        s1 = self.S(lambda e, n=tn: e.activation(
            self.b_stat[0:n, sc + 3:sc + 4], self.b_stat[0:n, sc + 2:sc + 3], AF.Ln,
            scale=np.float32(1.0 / D), bias=np.float32(EPS_LN)), waits=[("cs", s1)])
        s1 = self.S(lambda e, n=tn: e.activation(
            self.b_stat[0:n, sc + 3:sc + 4], self.b_stat[0:n, sc + 3:sc + 4],
            AF.Exp, scale=np.float32(-0.5)), waits=[("cs", s1)])
        xnt = self.a_xnt_p[bidx]
        s1 = self.S(lambda e, b=buf, n=tn: e.activation(
            xnt[0:n, :], b[0:n, :], AF.Identity,
            scale=self.b_stat[0:n, sc + 3:sc + 4]),
            waits=[("cs", s1), self.xnt_free[bidx]])
        self.ax_free[bidx] = ("cs", s1)
        self.ax_free[bidx + 2] = ("cs", s1)
        t = None
        for kc in range(8):
            bi = 6 + (kc % 2)
            t = self.T(lambda e, k2=kc, b=self.pb[bi], n=tn: e.transpose(
                b.ap().bitcast(BF16)[0:128, 0:n],
                xnt[0:n, k2 * 128:(k2 + 1) * 128],
                self.c_idbf[0:n, 0:n]),
                waits=[("cs", s1)] + self.bank(bi))
            vv = self.V(lambda e, k2=kc, b=self.pb[bi], n=tn, s2=s, to=toff:
                        e.tensor_copy(
                self.a_xn[:, k2 * 520 + s2 * 260 + to:
                          k2 * 520 + s2 * 260 + to + n],
                b.ap().bitcast(BF16)[0:128, 0:n]), waits=[("ct", t)])
            self.bank_done(bi, "cv", vv)
        self.xnt_free[bidx] = ("ct", t)

    def emit_ln1(self, g):
        W0 = self.all_compute() + [("dmisc", self.now("dmisc")),
                                   ("dst0", self.now("dst0")),
                                   ("dst1", self.now("dst1"))]
        for i, (s, toff, tn, bidx) in enumerate(self.ln1_tiles(g)):
            self.ln1_tile(g, s, toff, tn, bidx, w0=W0 if i == 0 else ())

    def group(self, g):
        t0 = g * TPG
        last = (g == NG - 1)
        ncol = TPG if last else TPG + 2
        pb = self.pb

        xn_ready = self.snap("cv")
        g0_extra = ([("dmisc", self.now("dmisc"))] if g == 0 else []) + \
            [("dst0", self.now("dst0")), ("dst1", self.now("dst1"))]
        ev_xz = None
        for piece in range(4):
            rec = self.use()
            slot = rec["slot"]
            for dsub in range(4):
                dchunk = piece * 4 + dsub
                for s in range(SEQ):
                    bi = (dsub * 2 + s) % 4
                    mm = None
                    for kc in range(8):
                        lhs = self.wslot[slot][:, kc * 512 + dsub * 128:
                                               kc * 512 + dsub * 128 + 128]
                        rhs = self.a_xn[:, kc * 520 + s * 260:
                                        kc * 520 + s * 260 + ncol]
                        w = ([rec["ld"]] + xn_ready + self.bank(bi)) if kc == 0 else []
                        mm = self.T(lambda e, l=lhs, r=rhs, b=pb[bi], k2=kc, n=ncol:
                                    e.matmul(b[:, 0:n], l, r, start=(k2 == 0),
                                             stop=(k2 == 7)), waits=w)
                    ev_xz = self.S(lambda e, b=pb[bi], d2=dchunk, s2=s, n=ncol: e.copy(
                        self.a_big[:, d2 * 520 + s2 * 260 + 1:
                                   d2 * 520 + s2 * 260 + 1 + n], b[:, 0:n]),
                        waits=[("ct", mm)] + (g0_extra if dchunk == 0 else []))
                    self.bank_done(bi, "cs", ev_xz)
            self.release(rec, mm)
            if g == 0:
                self.quant_w_chunk(2 * piece)
                self.quant_w_chunk(2 * piece + 1)
        cw = [("cs", ev_xz), ("cg", self.now("cg"))]
        big3 = self.a_big[:].rearrange("p (ch q) -> p ch q", q=520)
        for s in range(SEQ):
            self.V(lambda e, s2=s: e.tensor_copy(
                big3[:, :, s2 * 260: s2 * 260 + 1],
                self.carry[:, s2 * 16: s2 * 16 + 16].rearrange("p (c o) -> p c o", o=1)),
                waits=cw)
            if last:
                self.V(lambda e, s2=s: e.memset(
                    big3[:, :, s2 * 260 + 257: s2 * 260 + 260], 0.0), waits=cw)
        xz_ready_cv = self.now("cv")

        for dchunk in range(16):
            half = dchunk // 8
            ch = dchunk % 8
            dst = self.a_xs if half == 0 else self.a_zs
            for s in range(SEQ):
                bi = 4 + (dchunk * 2 + s) % 2
                mm = None
                for blk in range(4):
                    for j in range(4):
                        diag = self.d_convdiag[32 * blk:32 * blk + 32,
                                               (dchunk * 4 + j) * 32:
                                               (dchunk * 4 + j) * 32 + 32]
                        rhs = self.a_big[32 * blk:32 * blk + 32,
                                         dchunk * 520 + s * 260 + j:
                                         dchunk * 520 + s * 260 + j + TPG]
                        w = ([("cv", xz_ready_cv)] + self.bank(bi)) \
                            if (j == 0 and blk == 0) else []
                        mm = self.T(lambda e, d2=diag, r=rhs, b=pb[bi], j2=j, bl=blk:
                                    e.matmul(b[32 * bl:32 * bl + 32, 0:TPG], d2, r,
                                             start=(j2 == 0), stop=(j2 == 3),
                                             tile_position=(32 * bl, 32 * bl)),
                                    waits=w)
                sc = self.S(lambda e, b=pb[bi], c2=ch, s2=s, h2=half, d2=dst:
                            e.activation(
                    d2[:, c2 * 512 + s2 * 256: c2 * 512 + s2 * 256 + TPG],
                    b[:, 0:TPG], AF.Silu,
                    bias=self.d_cb[:, h2 * 8 + c2: h2 * 8 + c2 + 1]),
                    waits=[("ct", mm)]
                    + ([self.qscr_free[0], self.qscr_free[1]]
                       if (g == 0 and dchunk == 0 and s == 0) else []))
                self.bank_done(bi, "cs", sc)
        conv_ct = self.now("ct")
        self.conv_cs = self.now("cs")
        if not last:
            for s in range(SEQ):
                self.V(lambda e, s2=s: e.tensor_copy(
                    self.carry[:, s2 * 16: s2 * 16 + 16]
                    .rearrange("p (c o) -> p c o", o=1),
                    big3[:, :, s2 * 260 + 256: s2 * 260 + 257]),
                    waits=[("ct", conv_ct)])

        if STOP_PHASE == "p1":
            return
        self.p2(g)

        if STOP_PHASE == "p2":
            return
        for c in range(CPG):
            for s in range(SEQ):
                self.scan_chunk(g, s, c)
        p3_cv, p3_cs = self.now("cv"), self.now("cs")

        if STOP_PHASE == "p3":
            return
        for tau in range(NTAU):
            s, c = divmod(tau, CPG)
            for kk in range(12):
                bi = 6 + (kk % 2)
                t = self.T(lambda e, t2=tau, k2=kk, b=pb[bi]: e.transpose(
                    b.ap().bitcast(BF16)[0:128, 0:128],
                    self.a_big[:, t2 * VD + k2 * 128: t2 * VD + (k2 + 1) * 128],
                    self.c_idbf[:, :]),
                    waits=(([("cv", p3_cv), ("cs", p3_cs)] if (tau == 0 and kk == 0)
                            else []) + self.bank(bi)))
                vv = self.V(lambda e, k2=kk, b=pb[bi], s2=s, c2=c: e.tensor_copy(
                    self.a_xs[:, k2 * 512 + s2 * 256 + c2 * 128:
                              k2 * 512 + s2 * 256 + c2 * 128 + 128],
                    b.ap().bitcast(BF16)[0:128, 0:128]), waits=[("ct", t)])
                self.bank_done(bi, "cv", vv)
        p4a_cv = self.now("cv")

        p4b_cs = None
        for mhalf in range(2):
            recs = [self.use(), self.use()]
            for msub in range(4):
                m = mhalf * 4 + msub
                bi = msub % 4
                mm = None
                for kc in range(12):
                    khalf, kc6 = divmod(kc, 6)
                    slot = recs[khalf]["slot"]
                    w = []
                    if kc == 0:
                        w = [recs[0]["ld"], ("cv", p4a_cv)] + self.bank(bi)
                    elif kc6 == 0:
                        w = [recs[khalf]["ld"]]
                    mm = self.T(lambda e, k6=kc6, sl=slot, m2=msub, b=pb[bi], k2=kc:
                                e.matmul(
                        b[:, 0:512],
                        self.wslot[sl][:, k6 * 512 + m2 * 128:
                                       k6 * 512 + (m2 + 1) * 128],
                        self.a_xs[:, k2 * 512: k2 * 512 + 512],
                        start=(k2 == 0), stop=(k2 == 11)), waits=w)
                p4b_cs = self.S(lambda e, m2=m, b=pb[bi]: e.activation(
                    self.a_v[:, m2 * 512: m2 * 512 + 512], b[:, 0:512], AF.Copy),
                    waits=[("ct", mm)])
                self.bank_done(bi, "cs", p4b_cs)
            self.release(recs[0], mm)
            self.release(recs[1], mm)

        for qh in range(2):
            w = [("ct", self.wq_free)]
            if g == 0:
                w.append(("dmisc", self.now("dmisc")))
            for tt in range(4):
                self.dma("sync", self.wqslot[qh][:, tt * 1024:(tt + 1) * 1024],
                         self.dwqt[qh * 4 + tt, :, :], f"dq{qh}", waits=w)
                w = []
            self.ld_wq[qh] = (f"dq{qh}", self.now(f"dq{qh}"))

        for half in range(2):
            recs = [self.use(), self.use()]
            for tau in range(NTAU):
                s, c = divmod(tau, CPG)
                xq = self.a_x[tau % 2][:, (tau // 2) * 512:(tau // 2) * 512 + 512]
                self.dma("sync", xq,
                         self.dx[s, t0 + c * 128: t0 + c * 128 + 128,
                                 half * 512:(half + 1) * 512],
                         "dx", waits=[self.ax_free[tau], ("dx", self.now("dx"))])
                WX = [("dx", self.now("dx"))]
                bi = 4 + (tau % 2)
                mm = None
                for khalf in range(2):
                    src = self.a_v if khalf == 0 else self.a_zs
                    sl = recs[khalf]["slot"]
                    for kc in range(8):
                        lhs = src[:, kc * 512 + s * 256 + c * 128:
                                  kc * 512 + s * 256 + c * 128 + 128]
                        w = ([recs[khalf]["ld"]] +
                             ([("cs", p4b_cs)] + self.bank(bi) if khalf == 0 else [])) \
                            if kc == 0 else []
                        mm = self.T(lambda e, l=lhs, b=pb[bi], k2=kc, kh=khalf, s2=sl:
                                    e.matmul(
                            b[:, 0:512], l,
                            self.wslot[s2][:, k2 * 512:(k2 + 1) * 512],
                            start=(kh == 0 and k2 == 0), stop=(kh == 1 and k2 == 7)),
                            waits=w)
                vv = self.V(lambda e, b=pb[bi], t2=tau, h2=half, bx=xq:
                            e.tensor_tensor(
                    self.x2[:, t2 * D + h2 * 512: t2 * D + h2 * 512 + 512],
                    bx, b[:, 0:512], ALU.add),
                    waits=[("ct", mm)] + WX
                    + ([self.x2_store[tau]] if half == 0 else []))
                self.bank_done(bi, "cv", vv)
                self.ax_free[tau] = ("cv", vv)
                if half == 1:
                    self.x2_ready[tau] = ("cv", vv)
            self.release(recs[0], mm)
            self.release(recs[1], mm)

        if STOP_PHASE == "p4":
            return
        tiles = self.ln1_tiles(g + 1) if not last else []
        ti = 0
        for tau in range(NTAU):
            self.p5_tile(g, tau)
            if ti < len(tiles):
                self.ln1_tile(g + 1, *tiles[ti])
                ti += 1
        while ti < len(tiles):
            self.ln1_tile(g + 1, *tiles[ti])
            ti += 1

    def p2(self, g):
        pb = self.pb

        def xs_lhs(kc, tau):
            s, c = divmod(tau, CPG)
            return self.a_xs[:, kc * 512 + s * 256 + c * 128:
                             kc * 512 + s * 256 + c * 128 + 128]

        first_mm = [True]

        def project(dten, outcols, evac_fn):
            npieces = (outcols + 511) // 512
            for piece in range(npieces):
                pcols = min(512, outcols - piece * 512)
                rec = self.use()
                slot = rec["slot"]
                for tau in range(NTAU):
                    bi = tau % 4
                    mm = None
                    for kc in range(8):
                        w = ([rec["ld"]] + self.bank(bi)) if kc == 0 else []
                        if w and first_mm[0]:
                            w.append(("cs", self.conv_cs))
                            first_mm[0] = False
                        mm = self.T(lambda e, k2=kc, b=pb[bi], t2=tau, sl=slot,
                                    pc=pcols: e.matmul(
                            b[:, 0:pc], xs_lhs(k2, t2),
                            self.wslot[sl][:, k2 * pc: k2 * pc + pc],
                            start=(k2 == 0), stop=(k2 == 7)), waits=w)
                    sem, ev = evac_fn(piece, pcols, tau, bi, mm)
                    self.bank_done(bi, sem, ev)
                self.release(rec, mm)

        def mk_evac(dst, width, func):
            def f(piece, pcols, tau, bi, mm):
                e1 = self.S(lambda e, b=pb[bi], t2=tau, p2=piece, pc=pcols:
                            e.activation(
                    dst[:, t2 * width + p2 * 512: t2 * width + p2 * 512 + pc],
                    b[:, 0:pc], func), waits=[("ct", mm)])
                return ("cs", e1)
            return f

        qevac_extra = [self.qscr_free[1]]

        def mk_evac_qk(dst, width):
            def f(piece, pcols, tau, bi, mm):
                w = [("ct", mm)] + qevac_extra[:]
                qevac_extra.clear()
                e1 = self.S(lambda e, b=pb[bi], t2=tau, p2=piece, pc=pcols:
                            e.activation(
                    dst[:, t2 * width + p2 * 512: t2 * width + p2 * 512 + pc],
                    b[:, 0:pc], AF.Copy), waits=w)
                return ("cs", e1)
            return f

        project(self.dwq, KD, mk_evac_qk(self.r_q, KD))
        project(self.dwk, KD, mk_evac_qk(self.r_k, KD))

        def gkb_evac(piece, pcols, tau, bi, mm):
            v = self.V(lambda e, b=pb[bi], t2=tau: e.tensor_copy(
                self.r_gkb[:, t2 * 8: t2 * 8 + 8], b[:, 0:8]), waits=[("ct", mm)])
            return ("cv", v)

        project(self.dwgkb, 8, gkb_evac)

        p2t = self.p2t
        for tau in range(NTAU):
            v = self.V(lambda e, t2=tau: e.tensor_tensor(
                p2t[:, 0:4], self.r_gkb[:, t2 * 8 + 4: t2 * 8 + 8],
                self.d_bb[:, :], ALU.add), waits=self.snap("cv"))
            s = self.S(lambda e: e.activation(p2t[:, 4:8], p2t[:, 0:4],
                                              AF.Exp, scale=np.float32(-1.0)),
                       waits=[("cv", v)] + self.snap("cs"))
            v = self.V(lambda e: e.tensor_scalar(p2t[:, 4:8], p2t[:, 4:8],
                                                 1.0, None, ALU.add), waits=[("cs", s)])
            v = self.V(lambda e, t2=tau: e.reciprocal(
                self.b_beta[:, t2 * H: t2 * H + H], p2t[:, 4:8]), waits=[("cv", v)])
            v2 = self.V(lambda e, t2=tau: e.tensor_tensor(
                p2t[:, 8:12], self.r_gkb[:, t2 * 8: t2 * 8 + 4],
                self.d_dtb[:, :], ALU.add), waits=[("cv", v)])
            s = self.S(lambda e: e.activation(p2t[:, 12:16], p2t[:, 8:12], AF.Abs),
                       waits=[("cv", v2)])
            s = self.S(lambda e: e.activation(p2t[:, 12:16], p2t[:, 12:16],
                                              AF.Exp, scale=np.float32(-1.0)), waits=[("cs", s)])
            s = self.S(lambda e: e.activation(p2t[:, 12:16], p2t[:, 12:16],
                                              AF.Ln, bias=np.float32(1.0)), waits=[("cs", s)])
            s = self.S(lambda e: e.activation(p2t[:, 16:20], p2t[:, 8:12], AF.Relu),
                       waits=[("cs", s)])
            v2 = self.V(lambda e: e.tensor_tensor(p2t[:, 12:16], p2t[:, 12:16],
                                                  p2t[:, 16:20], ALU.add),
                        waits=[("cs", s)])
            v2 = self.V(lambda e: e.tensor_tensor(p2t[:, 12:16], p2t[:, 12:16],
                                                  self.d_negAexp[:, :], ALU.mult),
                        waits=[("cv", v2)])
            bi = 4 + (tau % 2)
            t = self.T(lambda e, b=pb[bi]: e.matmul(
                b[:, 0:4], self.c_maskI[:, :], p2t[:, 12:16], start=True, stop=True),
                waits=[("cv", v2)] + self.bank(bi))
            v3 = self.V(lambda e, b=pb[bi], t2=tau: e.tensor_copy(
                self.b_G[:, t2 * H: t2 * H + H], b[:, 0:4]), waits=[("ct", t)])
            t2i = self.T(lambda e, b=pb[bi], t2=tau: e.transpose(
                b[0:4, 128:256], self.b_G[:, t2 * H: t2 * H + H], self.c_idf32[:, :]),
                waits=[("cv", v3)])
            v3 = self.V(lambda e, b=pb[bi], t2=tau: e.tensor_copy(
                self.b_Growt[0:4, t2 * 128:(t2 + 1) * 128], b[0:4, 128:256]),
                waits=[("ct", t2i)])
            self.bank_done(bi, "cv", v3)
        bw = self.snap("cv")
        s = self.S(lambda e: e.activation(self.b_Lam[:, :], self.b_G[:, :], AF.Exp),
                   waits=bw + self.snap("cs"))
        self.V(lambda e: e.tensor_scalar(self.b_negLam[:, :], self.b_Lam[:, :],
                                         -1.0, None, ALU.mult), waits=[("cs", s)])
        s = self.S(lambda e: e.activation(self.b_bias[:, :], self.b_beta[:, :], AF.Ln),
                   waits=bw)
        self.V(lambda e: e.tensor_tensor(self.b_bias[:, :], self.b_bias[:, :],
                                         self.b_G[:, :], ALU.subtract),
               waits=[("cs", s)])
        first_l2 = [self.qscr_free[0]]
        if g == 0:
            first_l2.append(("dmisc", self.qdmisc))
        for (raw, dst) in ((self.r_q, self.a_q), (self.r_k, self.a_k)):
            for tau in range(NTAU):
                s = self.S(lambda e, r2=raw, t2=tau: e.activation(
                    self.r_sq, r2[:, t2 * KD:(t2 + 1) * KD], AF.Square),
                    waits=self.snap("cs") + first_l2[:])
                first_l2.clear()
                v = self.V(lambda e: e.reduce_sum(
                    self.b_rs[:, :].unsqueeze(-1),
                    self.r_sq.rearrange("p (h d) -> p h d", d=DK), axis=AX.X),
                    waits=[("cs", s)] + self.snap("cv"))
                s = self.S(lambda e: e.activation(self.b_rs[:, :], self.b_rs[:, :],
                                                  AF.Ln, bias=np.float32(EPS_L2)),
                           waits=[("cv", v)])
                s = self.S(lambda e: e.activation(self.b_rs[:, :], self.b_rs[:, :],
                                                  AF.Exp, scale=np.float32(-0.5)),
                           waits=[("cs", s)])
                for h in range(H):
                    s = self.S(lambda e, r2=raw, d2=dst, t2=tau, h2=h: e.activation(
                        d2[:, t2 * KD + h2 * DK: t2 * KD + (h2 + 1) * DK],
                        r2[:, t2 * KD + h2 * DK: t2 * KD + (h2 + 1) * DK],
                        AF.Identity, scale=self.b_rs[:, h2:h2 + 1]), waits=[("cs", s)])
        self.qk_ready_snap = self.snap("cs", "cv")
        project(self.dwv, VD, mk_evac(self.a_v, VD, AF.Copy))
        project(self.dwg, VD, mk_evac(self.a_gs, VD, AF.Silu))
        self.gs_cs = self.now("cs")

    def scan_chunk(self, g, s, c):
        pb = self.pb
        tau = s * CPG + c
        qk_ready = self.qk_ready_snap
        bA, bB = s * 2, s * 2 + 1
        bE = 4 + s
        bT = 6 + s

        for (src, dstfm) in ((self.a_q, self.w_qfm[s]), (self.a_k, self.w_kfm[s])):
            tl = None
            for h in range(H):
                for (p0, fn, oo) in ((0, 128, 0), (128, 64, 128)):
                    tl = self.T(lambda e, h2=h, p2=p0, f2=fn, o2=oo, b=pb[bT], s2=src:
                                e.transpose(
                        b.ap().bitcast(BF16)[0:f2, h2 * 256 + o2: h2 * 256 + o2 + 128],
                        s2[:, tau * KD + h2 * DK + p2: tau * KD + h2 * DK + p2 + f2],
                        self.c_idbf[:, :]),
                        waits=(qk_ready + self.bank(bT)) if (h == 0 and p0 == 0) else [])
            pv = pb[bT].ap().bitcast(BF16).rearrange("p (h q) -> p h q", q=256)
            dv = dstfm[:].rearrange("p (h q) -> p h q", q=256)
            self.V(lambda e, pv2=pv, dv2=dv: e.tensor_copy(
                dv2[0:128, :, 0:128], pv2[0:128, :, 0:128]), waits=[("ct", tl)])
            vv = self.V(lambda e, pv2=pv, dv2=dv: e.tensor_copy(
                dv2[0:64, :, 128:256], pv2[0:64, :, 128:256]))
            self.bank_done(bT, "cv", vv)
        kfm, qfm = self.w_kfm[s], self.w_qfm[s]
        fm_cv = self.now("cv")
        if STOP_PHASE == "sA":
            return

        def fmp(buf, h, piece):
            if piece == 0:
                return buf[:, h * 256: h * 256 + 128]
            return buf[0:64, h * 256 + 128: h * 256 + 256]

        mmA = mmB = None
        for h in range(H):
            w = ([("cv", fm_cv)] + self.bank(bA)) if h == 0 else []
            self.T(lambda e, h2=h, b=pb[bA]: e.matmul(
                b[:, h2 * 128:(h2 + 1) * 128], fmp(kfm, h2, 0), fmp(kfm, h2, 0),
                start=True, stop=False), waits=w)
            mmA = self.T(lambda e, h2=h, b=pb[bA]: e.matmul(
                b[:, h2 * 128:(h2 + 1) * 128], fmp(kfm, h2, 1), fmp(kfm, h2, 1),
                start=False, stop=True))
        for h in range(H):
            w = self.bank(bB) if h == 0 else []
            self.T(lambda e, h2=h, b=pb[bB]: e.matmul(
                b[:, h2 * 128:(h2 + 1) * 128], fmp(kfm, h2, 0), fmp(qfm, h2, 0),
                start=True, stop=False), waits=w)
            mmB = self.T(lambda e, h2=h, b=pb[bB]: e.matmul(
                b[:, h2 * 128:(h2 + 1) * 128], fmp(kfm, h2, 1), fmp(qfm, h2, 1),
                start=False, stop=True))

        if STOP_PHASE == "sB":
            return
        prev_readers = []
        for h in range(H):
            col = tau * H + h
            w = (self.bank(bE) if h == 0 else []) + prev_readers
            t = self.T(lambda e, h2=h, b=pb[bE], t2=tau: e.matmul(
                b[:, 0:128], self.c_sel4[:, h2 * 128:(h2 + 1) * 128],
                self.b_Growt[0:4, t2 * 128:(t2 + 1) * 128],
                start=True, stop=True), waits=w + ([("ct", mmB)] if h == 0 else []))
            v = self.V(lambda e, b=pb[bE], c2=col: e.tensor_scalar(
                self.w_t1[:, 0:1], b[:, 127:128], self.b_G[:, c2:c2 + 1], None,
                ALU.subtract), waits=[("ct", t)] + self.snap("cv"))
            sx = self.S(lambda e, h2=h: e.activation(
                self.b_lamb4[s][:, h2:h2 + 1], self.w_t1[:, 0:1], AF.Exp),
                waits=[("cv", v)] + self.snap("cs"))
            sx2 = self.S(lambda e, b=pb[bE], h2=h: e.activation(
                self.b_gam4[s][:, h2:h2 + 1], b[:, 127:128], AF.Exp),
                waits=[("ct", t)])
            v = self.V(lambda e, h2=h, c2=col: e.tensor_tensor(
                self.b_lamb4[s][:, h2:h2 + 1], self.b_lamb4[s][:, h2:h2 + 1],
                self.b_beta[:, c2:c2 + 1], ALU.mult), waits=[("cs", sx)])
            vpre = self.V(lambda e, b=pb[bE], c2=col: e.tensor_scalar(
                self.w_pre[:, :], b[:, 0:128], self.b_bias[:, c2:c2 + 1], 0.0,
                ALU.add, ALU.min), waits=[("ct", t)] + self.snap("cv"))
            prev_readers = [("cv", vpre), ("cs", sx2)]
            se = self.S(lambda e: e.activation(self.w_pre[:, :], self.w_pre[:, :],
                                               AF.Exp), waits=[("cv", vpre), ("cs", sx2)])
            v = self.V(lambda e: e.tensor_tensor(self.w_t1[:, :], self.w_pre[:, :],
                                                 self.c_maskSN[:, :], ALU.mult),
                       waits=[("cs", se)])
            vD = self.V(lambda e, h2=h, b=pb[bA]: e.tensor_tensor(
                self.w_sv[s][:, h2 * 384: h2 * 384 + 128], self.w_t1[:, :],
                b[:, h2 * 128:(h2 + 1) * 128], ALU.mult),
                waits=[("cv", v), ("ct", mmA)])
            self.V(lambda e, h2=h: e.tensor_copy(
                self.w_sv[s][:, h2 * 384 + 128: h2 * 384 + 256], self.c_idf32[:, :]))
            v = self.V(lambda e: e.tensor_tensor(self.w_t1[:, :], self.w_pre[:, :],
                                                 self.c_maskI[:, :], ALU.mult),
                       waits=[("cv", vD)])
            vP = self.V(lambda e, h2=h, b=pb[bB]: e.tensor_tensor(
                self.w_Pb[s][:, h2 * 128:(h2 + 1) * 128], self.w_t1[:, :],
                b[:, h2 * 128:(h2 + 1) * 128], ALU.mult), waits=[("cv", v)])
            if h == H - 1:
                self.bank_done(bA, "cv", vD)
                self.bank_done(bB, "cv", vP)
                self.bank_done(bE, "cv", vP)
                self.bank_done(bE, "cs", sx2)
        d0_cv = self.now("cv")

        if STOP_PHASE == "sC":
            return
        d0_cv2 = self.now("cv")
        tl = None
        for h in range(H):
            tl = self.T(lambda e, h2=h, b=pb[bT]: e.transpose(
                b.ap().bitcast(SOLVE_DT)[0:128, h2 * 128:(h2 + 1) * 128],
                self.w_sv[s][:, h2 * 384: h2 * 384 + 128], self.c_idf32r[:, :]),
                waits=([("cv", d0_cv2)] + self.bank(bT)) if h == 0 else [])
        dkt_ev = self.V(lambda e, b=pb[bT]: e.tensor_copy(
            self.w_DkT[s][:, :], b.ap().bitcast(SOLVE_DT)[0:128, 0:512]),
            waits=[("ct", tl)])
        self.bank_done(bT, "cv", dkt_ev)
        x_ready = self.now("cv")
        if STOP_PHASE == "sD0":
            return
        for r in range(1, 8):
            cur_a = (r % 2 == 1)
            roff = 0 if cur_a else 128
            dk_psoff = 0 if cur_a else 128
            x_psoff = 128 - dk_psoff
            dk_dst = 256 if cur_a else 0
            mq = None
            wts = [("cv", dkt_ev), ("cv", x_ready)]
            for h in range(H):
                bi2 = bA if h < 2 else bB
                w2 = (wts + self.bank(bA) + self.bank(bB)) if h == 0 else []
                mq = self.T(lambda e, h2=h, b=pb[bi2], ro=roff: e.matmul(
                    b[:, (h2 % 2) * 256: (h2 % 2) * 256 + 256],
                    self.w_DkT[s][:, h2 * 128:(h2 + 1) * 128],
                    self.w_sv[s][:, h2 * 384 + ro: h2 * 384 + ro + 256],
                    start=True, stop=True), waits=w2)
            sv3 = self.w_sv[s][:].rearrange("p (h q) -> p h q", q=384)
            ev1 = None
            if r < 7 and "ev1" not in SKIP:
                for half in range(2):
                    b2 = pb[bA] if half == 0 else pb[bB]
                    pv = b2.ap().rearrange("p (h q) -> p h q", q=256)
                    ev1 = self.V(lambda e, pv2=pv, hf=half, dd=dk_dst, dp=dk_psoff:
                                 e.tensor_copy(
                        sv3[:, 2 * hf: 2 * hf + 2, dd: dd + 128],
                        pv2[:, 0:2, dp: dp + 128]),
                        waits=([("ct", mq)] + self.snap("cv")) if half == 0
                        else [])
            for half in range(2 if "ev2" not in SKIP else 0):
                b2 = pb[bA] if half == 0 else pb[bB]
                pv = b2.ap().rearrange("p (h q) -> p h q", q=256)
                ev2 = self.V(lambda e, pv2=pv, hf=half, xp=x_psoff: e.tensor_tensor(
                    sv3[:, 2 * hf: 2 * hf + 2, 128:256],
                    sv3[:, 2 * hf: 2 * hf + 2, 128:256],
                    pv2[:, 0:2, xp: xp + 128], ALU.add),
                    waits=([("ct", mq)] + self.snap("cv")) if half == 0 else [])
            x_ready = ev2
            self.bank_done(bA, "cv", ev2)
            self.bank_done(bB, "cv", ev2)
            if ev1 is not None:
                self.bank_done(bA, "cv", ev1)
                self.bank_done(bB, "cv", ev1)
            if r < 7 and "tr" not in SKIP:
                tl = None
                for h in range(H):
                    tl = self.T(lambda e, h2=h, b=pb[bT], dd=dk_dst: e.transpose(
                        b.ap().bitcast(SOLVE_DT)[0:128, h2 * 128:(h2 + 1) * 128],
                        self.w_sv[s][:, h2 * 384 + dd: h2 * 384 + dd + 128],
                        self.c_idf32r[:, :]),
                        waits=([("cv", ev1)] + self.bank(bT)) if h == 0 else [])
                dkt_ev = self.V(lambda e, b=pb[bT]: e.tensor_copy(
                    self.w_DkT[s][:, :], b.ap().bitcast(SOLVE_DT)[0:128, 0:512]),
                    waits=[("ct", tl)] + self.snap("cv"))
                self.bank_done(bT, "cv", dkt_ev)
            if STOP_PHASE == "sD1" and r == 1:
                return

        if STOP_PHASE == "sD":
            return
        for h in range(H):
            col = tau * H + h
            soff = (s * H + h) * DV
            sg = self.S(lambda e, h2=h: e.activation(
                self.w_gI[s][:, :], self.c_idbf[:, :], AF.Identity,
                scale=self.b_gam4[s][:, h2:h2 + 1]),
                waits=self.snap("cs") + self.snap("ct"))
            sk = self.S(lambda e, h2=h: e.activation(
                self.w_ksc[s][:, :],
                self.a_k[:, tau * KD + h2 * DK: tau * KD + (h2 + 1) * DK],
                AF.Identity, scale=self.b_lamb4[s][:, h2:h2 + 1]), waits=[("cs", sg)])
            w = self.bank(bA) + ([("cv", x_ready)] if h == 0 else [])
            self.T(lambda e, h2=h, b=pb[bA], so=soff: e.matmul(
                b[:, 0:DV], fmp(kfm, h2, 0), self.st1[:, so:so + DV],
                start=True, stop=False), waits=w)
            mmz = self.T(lambda e, h2=h, b=pb[bA], so=soff: e.matmul(
                b[:, 0:DV], fmp(kfm, h2, 1), self.st2[0:64, so:so + DV],
                start=False, stop=True))
            vz = self.V(lambda e, b=pb[bA], h2=h, c2=col: e.scalar_tensor_tensor(
                self.w_Zh[s][:, :], b[:, 0:DV],
                self.b_negLam[:, c2:c2 + 1],
                self.a_v[:, tau * VD + h2 * DV: tau * VD + (h2 + 1) * DV],
                ALU.mult, ALU.add),
                waits=[("ct", mmz)] + self.snap("cv") + self.snap("ct")
                + self.snap("cs"))
            self.bank_done(bA, "cv", vz)
            wW = self.bank(bE)
            mmw = self.T(lambda e, h2=h, b=pb[bE]: e.matmul(
                b[:, 0:DV],
                self.w_sv[s][:, h2 * 384 + 128: h2 * 384 + 256],
                self.w_Zh[s][:, :], start=True, stop=True),
                waits=wW + [("cv", vz)])
            sw = self.S(lambda e, b=pb[bE], h2=h: e.activation(
                self.w_W[s][:, h2 * DV:(h2 + 1) * DV], b[:, 0:DV], AF.Copy),
                waits=[("ct", mmw)] + self.snap("cs"))
            self.bank_done(bE, "cs", sw)
            wq = self.bank(bB)
            self.T(lambda e, h2=h, b=pb[bB], so=soff: e.matmul(
                b[:, 0:DV], fmp(qfm, h2, 0), self.st1[:, so:so + DV],
                start=True, stop=False), waits=wq)
            mmq = self.T(lambda e, h2=h, b=pb[bB], so=soff: e.matmul(
                b[:, 0:DV], fmp(qfm, h2, 1), self.st2[0:64, so:so + DV],
                start=False, stop=True))
            so1 = self.S(lambda e, b=pb[bB], c2=col: e.activation(
                self.w_O[s][:, :], b[:, 0:DV], AF.Identity,
                scale=self.b_Lam[:, c2:c2 + 1]), waits=[("ct", mmq)])
            self.bank_done(bB, "cs", so1)
            wo = self.bank(bB)
            mmo = self.T(lambda e, h2=h, b=pb[bB]: e.matmul(
                b[:, 0:DV], self.w_Pb[s][:, h2 * 128:(h2 + 1) * 128],
                self.w_W[s][:, h2 * DV:(h2 + 1) * DV], start=True, stop=True),
                waits=wo + [("cs", sw)])
            vo = self.V(lambda e, b=pb[bB]: e.tensor_tensor(
                self.w_O[s][:, :], self.w_O[s][:, :], b[:, 0:DV], ALU.add),
                waits=[("ct", mmo), ("cs", so1)])
            self.bank_done(bB, "cv", vo)
            wt = self.bank(bT)
            self.T(lambda e, b=pb[bT], so=soff: e.matmul(
                b[:, 0:DV], self.w_gI[s][:, :], self.st1[:, so:so + DV],
                start=True, stop=False), waits=wt + [("cs", sk)])
            mm1 = self.T(lambda e, b=pb[bT], h2=h: e.matmul(
                b[:, 0:DV], self.w_ksc[s][:, 0:128],
                self.w_W[s][:, h2 * DV:(h2 + 1) * DV], start=False, stop=True))
            bT2 = 6 + (1 - s)
            wt2 = self.bank(bT2)
            self.T(lambda e, b=pb[bT2], so=soff: e.matmul(
                b[0:64, 0:DV], self.w_gI[s][0:64, 0:64], self.st2[0:64, so:so + DV],
                start=True, stop=False), waits=wt2)
            mm2 = self.T(lambda e, b=pb[bT2], h2=h: e.matmul(
                b[0:64, 0:DV], self.w_ksc[s][:, 128:192],
                self.w_W[s][:, h2 * DV:(h2 + 1) * DV], start=False, stop=True))
            v1 = self.V(lambda e, b=pb[bT], so=soff: e.tensor_copy(
                self.st1[:, so:so + DV], b[:, 0:DV]), waits=[("ct", mm1)])
            self.bank_done(bT, "cv", v1)
            v2 = self.V(lambda e, b=pb[bT2], so=soff: e.tensor_copy(
                self.st2[0:64, so:so + DV], b[0:64, 0:DV]), waits=[("ct", mm2)])
            self.bank_done(bT2, "cv", v2)
            sq = self.S(lambda e: e.activation(
                self.w_osq[s][:, :], self.w_O[s][:, :], AF.Square,
                accum_out=self.b_ssq[s][:, 0:1]), waits=[("cv", vo)] + self.snap("cs"))
            sq = self.S(lambda e: e.activation(
                self.b_ssq[s][:, 1:2], self.b_ssq[s][:, 0:1], AF.Ln,
                scale=np.float32(1.0 / DV), bias=np.float32(EPS_RMS)), waits=[("cs", sq)])
            sq = self.S(lambda e: e.activation(
                self.b_ssq[s][:, 1:2], self.b_ssq[s][:, 1:2], AF.Exp, scale=np.float32(-0.5)),
                waits=[("cs", sq)])
            vg2 = self.V(lambda e: e.scalar_tensor_tensor(
                self.w_osq[s][:, :], self.w_O[s][:, :], self.b_ssq[s][:, 1:2],
                self.d_gnw[:, :], ALU.mult, ALU.mult),
                waits=[("cs", sq)] + self.snap("cv"))
            self.V(lambda e, h2=h, t2=tau: e.tensor_tensor(
                self.a_big[:, t2 * VD + h2 * DV: t2 * VD + (h2 + 1) * DV],
                self.w_osq[s][:, :],
                self.a_gs[:, t2 * VD + h2 * DV: t2 * VD + (h2 + 1) * DV], ALU.mult),
                waits=[("cv", vg2), ("cs", self.gs_cs)])

    def p5_tile(self, g, tau):
        pb = self.pb
        s, c = divmod(tau, CPG)
        t0 = g * TPG
        par = tau % 2
        xt = self.x2[:, tau * D: tau * D + D]
        qt0 = self.qt0p[par]
        qs, qsi, qm, qw = self.qsp[par]
        qt1, qt2 = (self.qt1, self.qt2) if par == 0 else (self.qt1b, self.qt2b)
        sc = 4 + par * 4
        W0 = [self.x2_ready[tau], self.qscr_free[par]]
        v = self.V(lambda e: e.reduce_sum(self.b_stat[:, sc:sc + 1], xt, axis=AX.X,
                                          negate=True), waits=W0)
        v = self.V(lambda e: e.tensor_scalar(self.b_stat[:, sc + 1:sc + 2],
                                             self.b_stat[:, sc:sc + 1],
                                             1.0 / D, None, ALU.mult),
                   waits=[("cv", v)])
        s1 = self.S(lambda e: e.activation(qt0[:, :], xt, AF.Identity,
                                           bias=self.b_stat[:, sc + 1:sc + 2]),
                    waits=[("cv", v), self.qscr_free[par]])
        s1 = self.S(lambda e: e.activation(qt1, qt0[:, :], AF.Square,
                                           accum_out=self.b_stat[:, sc + 2:sc + 3]),
                    waits=[("cs", s1)])
        s1 = self.S(lambda e: e.activation(self.b_stat[:, sc + 3:sc + 4],
                                           self.b_stat[:, sc + 2:sc + 3],
                                           AF.Ln, scale=np.float32(1.0 / D), bias=np.float32(EPS_LN)),
                    waits=[("cs", s1)])
        s1 = self.S(lambda e: e.activation(self.b_stat[:, sc + 3:sc + 4],
                                           self.b_stat[:, sc + 3:sc + 4],
                                           AF.Exp, scale=np.float32(-0.5)), waits=[("cs", s1)])
        s1 = self.S(lambda e: e.activation(qt0[:, :], qt0[:, :], AF.Identity,
                                           scale=self.b_stat[:, sc + 3:sc + 4]),
                    waits=[("cs", s1)])
        v = self.V(lambda e: e.reduce_max(qs[:].unsqueeze(-1),
                                          qt0[:].rearrange("p (b k) -> p b k", k=32),
                                          axis=AX.X, apply_absolute_value=True),
                   waits=[("cs", s1)])
        v = self.V(lambda e: e.tensor_scalar(qsi[:], qs[:], 1e-30, None,
                                             ALU.max), waits=[("cv", v)])
        v = self.V(lambda e: e.tensor_scalar(qsi[:].bitcast(I32),
                                             qsi[:].bitcast(I32),
                                             self.c_pow2[:, 0:1], None, ALU.bitwise_and),
                   waits=[("cv", v)])
        v = self.V(lambda e: e.tensor_single_scalar(qm[:], qs[:], 0.0,
                                                    ALU.is_gt), waits=[("cv", v)])
        v = self.V(lambda e: e.tensor_scalar(qw[:], qsi[:], 0.25, None,
                                             ALU.mult), waits=[("cv", v)])
        v = self.V(lambda e: e.memset(qsi[:], 1.0), waits=[("cv", v)])
        v = self.V(lambda e: e.select(qs[:], qm[:], qw[:], qsi[:], add_drain=True),
                   waits=[("cv", v)])
        v = self.V(lambda e: e.tensor_scalar(qs[:], qs[:],
                                             self.d_dactp[:, 0:1], None, ALU.mult),
                   waits=[("cv", v)])
        v = self.V(lambda e: e.reciprocal(qsi[:], qs[:]), waits=[("cv", v)])
        vq = self.emit_quant_core(qt0[:], qt1, qt2, qs[:], qsi[:],
                                  self.a_xq[:, tau * D: tau * D + D], [("cv", v)])
        self.qscr_free[par] = ("cv", vq)
        for kk in range(8):
            bi = par * 2 + (kk % 2)
            t = self.T(lambda e, k2=kk, b=pb[bi], t2=tau: e.transpose(
                b.ap().bitcast(BF16)[0:128, 0:128],
                self.a_xq[:, t2 * D + k2 * 128: t2 * D + (k2 + 1) * 128],
                self.c_idbf[:, :]), waits=[("cv", vq)] + self.bank(bi))
            vv = self.V(lambda e, k2=kk, b=pb[bi], s2=s, c2=c: e.tensor_copy(
                self.a_zs[:, k2 * 512 + s2 * 256 + c2 * 128:
                          k2 * 512 + s2 * 256 + c2 * 128 + 128],
                b.ap().bitcast(BF16)[0:128, 0:128]), waits=[("ct", t)])
            self.bank_done(bi, "cv", vv)
        xqfm_cv = self.now("cv")
        vlast = None
        for half in range(2):
            bi = 4 + par * 2 + half
            mm = None
            for kc in range(8):
                lhs = self.a_zs[:, kc * 512 + s * 256 + c * 128:
                                kc * 512 + s * 256 + c * 128 + 128]
                w = ([("cv", xqfm_cv), self.ld_wq[0]] + self.bank(bi)) \
                    if kc == 0 else ([self.ld_wq[1]] if kc == 4 else [])
                mm = self.T(lambda e, l=lhs, b=pb[bi], k2=kc, h2=half: e.matmul(
                    b[:, 0:512], l,
                    self.wqslot[k2 // 4][:, (k2 % 4) * 1024 + h2 * 512:
                                         (k2 % 4) * 1024 + h2 * 512 + 512],
                    start=(k2 == 0), stop=False), waits=w)
            mm = self.T(lambda e, b=pb[bi], h2=half: e.matmul(
                b[:, 0:512], self.c_ones1b[:, :],
                self.d_fbias[0:1, h2 * 512: h2 * 512 + 512],
                start=False, stop=True))
            self.wq_free = mm
            vlast = self.V(lambda e, b=pb[bi], h2=half: e.tensor_tensor(
                self.x2[:, tau * D + h2 * 512: tau * D + h2 * 512 + 512],
                self.x2[:, tau * D + h2 * 512: tau * D + h2 * 512 + 512],
                b[:, 0:512], ALU.add),
                waits=[("ct", mm)])
            self.bank_done(bi, "cv", vlast)
        self.dma("gpsimd", self.dout[s, t0 + c * 128: t0 + c * 128 + 128, :],
                 xt, f"dst{par}",
                 waits=[("cv", vlast), (f"dst{par}", self.now(f"dst{par}"))])
        self.x2_store[tau] = (f"dst{par}", self.now(f"dst{par}"))


def kernel(**inputs):
    x = np.ascontiguousarray(inputs["L0_attn_x"], dtype=np.float32)
    convw = np.concatenate([np.ascontiguousarray(inputs["L0_attn_conv_x_w"][:, 0, :]),
                            np.ascontiguousarray(inputs["L0_attn_conv_z_w"][:, 0, :])],
                           axis=0)
    convb = np.concatenate([np.ascontiguousarray(inputs["L0_attn_conv_x_b"]),
                            np.ascontiguousarray(inputs["L0_attn_conv_z_b"])])[None, :]
    small = np.zeros((1, 16), np.float32)
    small[0, 0:4] = inputs["L0_attn_b_b"]
    small[0, 4:8] = inputs["L0_attn_A_log"]
    small[0, 8:12] = inputs["L0_attn_dt_bias"]
    small[0, 12:13] = inputs["L0_ffn_delta_act"]
    base = {
        "w_in": np.ascontiguousarray(inputs["L0_attn_W_in"]),
        "conv_w": convw,
        "conv_b": convb,
        "w_q": np.ascontiguousarray(inputs["L0_attn_W_q"]),
        "w_k": np.ascontiguousarray(inputs["L0_attn_W_k"]),
        "w_v": np.ascontiguousarray(inputs["L0_attn_W_v"]),
        "w_g": np.ascontiguousarray(inputs["L0_attn_W_g"]),
        "w_gkb": np.concatenate([np.ascontiguousarray(inputs["L0_attn_W_gk"]),
                                 np.ascontiguousarray(inputs["L0_attn_W_b"])], axis=1),
        "small_in": small,
        "g_norm_w": np.ascontiguousarray(inputs["L0_attn_g_norm_w"][None, :]),
        "w_so": np.ascontiguousarray(inputs["L0_attn_W_so"]),
        "w_out": np.ascontiguousarray(inputs["L0_attn_W_out"]),
        "ffn_w": np.ascontiguousarray(inputs["L0_ffn_weight"]),
        "ffn_b": np.ascontiguousarray(inputs["L0_ffn_bias"][None, :]),
        "ffn_dw": np.ascontiguousarray(inputs["L0_ffn_delta_w"]),
    }
    nc = B().build()
    in_maps = []
    for core in range(NCORES):
        m = dict(base)
        m["x"] = np.ascontiguousarray(x[core * SEQ:(core + 1) * SEQ])
        in_maps.append(m)
    res = run_bass_kernel_spmd(nc, in_maps, core_ids=list(range(NCORES)))
    return np.concatenate([r["out"] for r in res.results], axis=0)


# version 11
# speedup vs baseline: 1.4316x; 1.0120x over previous
import contextlib
import numpy as np
import ml_dtypes
import concourse.bass as bass
import concourse.mybir as mybir
from concourse.bass_utils import run_bass_kernel_spmd

F32 = mybir.dt.float32
F32R = mybir.dt.float32r
SOLVE_DT = F32R
BF16 = mybir.dt.bfloat16
I32 = mybir.dt.int32
AF = mybir.ActivationFunctionType
ALU = mybir.AluOpType
AX = mybir.AxisListType
BF = ml_dtypes.bfloat16

NCORES = 8
SEQ = 2
L = 2048
D = 1024
DI = 2048
KD = 768
VD = 1536
H = 4
DK = 192
DV = 384
C = 128
NG = 8
TPG = 256
CPG = TPG // C
NTAU = SEQ * CPG
EPS_LN = 1e-5
EPS_L2 = 1e-6
EPS_RMS = 1e-5
LN2C = float(np.log(2.0))
MAGIC = 12582912.0
STOP_PHASE = None
SKIP = set()
POW2MASK = float(-8388608)

ENGINES = ("sync", "scalar", "tensor", "vector", "gpsimd")


class B:

    def __init__(self):
        self.nc = bass.Bass("TRN2", target_bir_lowering=False)
        self.steps = {e: [] for e in ENGINES}
        self.counts = {}
        self.bank_free = [[] for _ in range(8)]

    def now(self, sem):
        return self.counts.get(sem, 0)

    def _step(self, eng, fn, waits, inc, amt):
        waits = [w for w in waits if w[1] > 0]
        val = None
        if inc is not None:
            self.counts[inc] = self.counts.get(inc, 0) + amt
            val = self.counts[inc]
        self.steps[eng].append(dict(fn=fn, waits=waits, inc=inc, amt=amt, val=val))
        return val

    def dma(self, queue, out_ap, in_ap, sem="dw", waits=(), slow=False):
        return self._step(queue,
                          lambda e, o=out_ap, i=in_ap, sl=slow: e.dma_start(
                              out=o, in_=i, allow_slow_non_contiguous=sl),
                          waits, sem, 16)

    def T(self, fn, waits=()):
        return self._step("tensor", fn, waits, "ct", 1)

    def V(self, fn, waits=()):
        return self._step("vector", fn, waits, "cv", 1)

    def S(self, fn, waits=()):
        return self._step("scalar", fn, waits, "cs", 1)

    def G(self, fn, waits=()):
        return self._step("gpsimd", fn, waits, "cg", 1)

    def snap(self, *sems):
        return [(s, self.now(s)) for s in sems]

    def all_compute(self):
        return self.snap("ct", "cv", "cs", "cg")

    def bank(self, i, extra=()):
        w = self.bank_free[i] + list(extra)
        self.bank_free[i] = []
        return w

    def bank_done(self, i, sem, val):
        self.bank_free[i].append((sem, val))

    def build(self):
        nc = self.nc
        self.dram()
        self.consts()
        self.es = contextlib.ExitStack()
        with self.es:
            self.alloc()
            self.prep()
            self.pump()
            self.pump()
            self.emit_ln1(0)
            for g in range(NG):
                self.group(g)
            self._step("sync", lambda e: e.nop(),
                       [("dst0", self.now("dst0")), ("dst1", self.now("dst1"))],
                       None, 0)
            compute_sems = ("ct", "cv", "cs", "cg")
            used = {cs_: set() for cs_ in compute_sems}
            for ename in ENGINES:
                for st in self.steps[ename]:
                    for (sname, val) in st["waits"]:
                        if sname in used:
                            used[sname].add(val)
            ranks = {cs_: {v: i + 1 for i, v in enumerate(sorted(used[cs_]))}
                     for cs_ in compute_sems}

            def remap_wait(sname, val):
                if sname not in ranks:
                    return val
                return ranks[sname][val]

            blk = self.es.enter_context(nc.Block())
            for ename in ENGINES:
                steps = self.steps[ename]
                sems = self.sems

                def emitter(eng, steps=steps, sems=sems):
                    for st in steps:
                        for (sname, val) in st["waits"]:
                            eng.wait_ge(sems[sname], remap_wait(sname, val))
                        ins = st["fn"](eng)
                        inc = st["inc"]
                        if inc is not None:
                            if inc in used:
                                if st["val"] in used[inc]:
                                    ins.then_inc(sems[inc], 1)
                            else:
                                ins.then_inc(sems[inc], st["amt"])

                getattr(blk, ename)(emitter)
        return nc

    def dram(self):
        nc = self.nc
        t = lambda n, s: nc.dram_tensor(n, s, F32, kind="ExternalInput").ap()
        self.dx = t("x", [SEQ, L, D])
        self.dwin = t("w_in", [D, DI])
        self.dconvw = t("conv_w", [2 * D, 4])
        self.dconvb = t("conv_b", [1, 2 * D])
        self.dwq = t("w_q", [D, KD])
        self.dwk = t("w_k", [D, KD])
        self.dwv = t("w_v", [D, VD])
        self.dwg = t("w_g", [D, VD])
        self.dwgkb = t("w_gkb", [D, 8])
        self.dsmallin = t("small_in", [1, 16])
        self.dgnw = t("g_norm_w", [1, DV])
        self.dwso = t("w_so", [VD, D])
        self.dwout = t("w_out", [DI, D])
        self.dfw = t("ffn_w", [D, D])
        self.dfb = t("ffn_b", [1, D])
        self.dfdw = t("ffn_dw", [D, 32])
        self.dout = nc.dram_tensor("out", [SEQ, L, D], F32, kind="ExternalOutput").ap()
        self.dwqt = nc.dram_tensor("wqt_scratch", [8, 128, D], BF16, kind="Internal").ap()
        self.dwbf = nc.dram_tensor("wbf_scratch", [23, 128, 4096], BF16,
                                   kind="Internal").ap()
        self.mk_pieces()

    def mk_pieces(self):
        P = []

        def add(subs):
            P.append(dict(w=max(c + n for c, n, _ in subs), subs=subs))

        for p in range(4):
            add([(kc * 512, 512,
                  self.dwin[kc * 128:(kc + 1) * 128, p * 512:(p + 1) * 512])
                 for kc in range(8)])

        def proj(d, outcols):
            for p in range((outcols + 511) // 512):
                pc = min(512, outcols - p * 512)
                add([(kc * pc, pc,
                      d[kc * 128:(kc + 1) * 128, p * 512:p * 512 + pc])
                     for kc in range(8)])

        proj(self.dwq, KD)
        proj(self.dwk, KD)
        add([(kc * 8, 8, self.dwgkb[kc * 128:(kc + 1) * 128, :]) for kc in range(8)])
        proj(self.dwv, VD)
        proj(self.dwg, VD)
        for mhalf in range(2):
            for khalf in range(2):
                add([(kc6 * 512, 512,
                      self.dwso[(khalf * 6 + kc6) * 128:(khalf * 6 + kc6 + 1) * 128,
                                mhalf * 512:(mhalf + 1) * 512])
                     for kc6 in range(6)])
        for half in range(2):
            for khalf in range(2):
                add([(kc * 512, 512,
                      self.dwout[(khalf * 8 + kc) * 128:(khalf * 8 + kc + 1) * 128,
                                 half * 512:(half + 1) * 512])
                     for kc in range(8)])
        self.pieces = P
        self.wsched = [(g, pi) for g in range(NG) for pi in range(len(P))]

    def pump(self):
        if self.wnext >= len(self.wsched):
            return
        g, pi = self.wsched[self.wnext]
        slot = self.wnext % 2
        self.wnext += 1
        pc = self.pieces[pi]
        sem = f"dw{slot}"
        if g == 0:
            w = [("ct", self.slot_free[slot])]
            if self.wb_last_slot[slot] is not None:
                w.append(("dwb", self.wb_last_slot[slot]))
            for (coff, ncols, src) in pc["subs"]:
                self.dma("gpsimd", self.wslot[slot][:, coff:coff + ncols], src,
                         sem, waits=w)
                w = []
            ldv = self.now(sem)
            self.dma("sync", self.dwbf[pi, :, 0:pc["w"]],
                     self.wslot[slot][:, 0:pc["w"]], "dwb",
                     waits=[(sem, ldv), ("dwb", self.now("dwb"))])
            self.wb_val[pi] = self.now("dwb")
            self.wb_last_slot[slot] = self.wb_val[pi]
        else:
            sem = f"ds{slot}"
            wbw = max(self.wb_val[pi], self.wb_last_slot[slot] or 0)
            self.dma("sync", self.wslot[slot][:, 0:pc["w"]],
                     self.dwbf[pi, :, 0:pc["w"]], sem,
                     waits=[("ct", self.slot_free[slot]), ("dwb", wbw)])
            ldv = self.now(sem)
        self.wrecs.append(dict(slot=slot, ld=(sem, ldv)))

    def use(self):
        r = self.wrecs[self.wuse]
        self.wuse += 1
        return r

    def release(self, rec, ctval):
        self.slot_free[rec["slot"]] = ctval
        self.pump()

    def consts(self):
        nc = self.nc
        self.i_idf32 = nc.inline_tensor(np.eye(128, dtype=np.float32), "c_idf32")
        self.i_idbf = nc.inline_tensor(np.eye(128, dtype=np.float32).astype(BF), "c_idbf")
        self.i_i32rep = nc.inline_tensor(
            np.tile(np.eye(32, dtype=np.float32), (4, 1)).astype(BF), "c_i32rep")
        self.i_maskSN = nc.inline_tensor(
            -np.triu(np.ones((128, 128), dtype=np.float32), 1), "c_maskSN")
        self.i_maskI = nc.inline_tensor(
            np.triu(np.ones((128, 128), dtype=np.float32)), "c_maskI")
        self.i_i4bf = nc.inline_tensor(
            np.tile(np.eye(128, dtype=np.float32), (1, 4)).astype(BF), "c_i4bf")
        sel = np.zeros((4, H * 128), np.float32)
        for h in range(H):
            sel[h, h * 128:(h + 1) * 128] = 1.0
        self.i_sel4 = nc.inline_tensor(sel, "c_sel4")
        self.i_ones1f = nc.inline_tensor(np.ones((1, 128), np.float32), "c_ones1f")
        self.i_ones1b = nc.inline_tensor(np.ones((1, 128), np.float32).astype(BF),
                                         "c_ones1b")
        self.i_pow2 = nc.inline_tensor(np.full((128, 1), -8388608, np.int32), "c_pow2")
        self.i_sgn = nc.inline_tensor(np.full((128, 1), -2147483648, np.int64)
                                      .astype(np.int32), "c_sgn")
        self.i_abs = nc.inline_tensor(np.full((128, 1), 0x7FFFFFFF, np.int32), "c_abs")

    def alloc(self):
        nc, es = self.nc, self.es
        sb = lambda n, s, d: es.enter_context(nc.sbuf_tensor(n, s, d))
        ps = lambda n: es.enter_context(nc.psum_tensor(n, [128, 512], F32))
        self.c_idf32 = sb("s_idf32", [128, 128], F32)
        self.c_idbf = sb("s_idbf", [128, 128], BF16)
        self.c_i32rep = sb("s_i32rep", [128, 32], BF16)
        self.c_maskSN = sb("s_maskSN", [128, 128], F32)
        self.c_maskI = sb("s_maskI", [128, 128], F32)
        self.c_i4bf = sb("s_i4bf", [128, 512], BF16)
        self.c_sel4 = sb("s_sel4", [4, H * 128], F32)
        self.c_ones1f = sb("s_ones1f", [1, 128], F32)
        self.c_ones1b = sb("s_ones1b", [1, 128], BF16)
        self.c_pow2 = sb("s_pow2", [128, 1], I32)
        self.c_sgn = sb("s_sgn", [128, 1], I32)
        self.c_abs = sb("s_abs", [128, 1], I32)
        self.c_idf32r = sb("s_idf32r", [128, 128], SOLVE_DT)
        self.d_convdiag = sb("d_convdiag", [128, 16 * 4 * 32], BF16)
        self.d_cb = sb("d_cb", [128, 16], F32)
        self.d_bb = sb("d_bb", [128, 4], F32)
        self.d_dtb = sb("d_dtb", [128, 4], F32)
        self.d_negAexp = sb("d_negAexp", [128, 4], F32)
        self.d_gnw = sb("d_gnw", [128, DV], F32)
        self.d_fbias = sb("d_fbias", [1, D], BF16)
        self.d_dactp = sb("d_dactp", [128, 1], F32)
        self.d_small = sb("d_small", [1, 64], F32)
        self.wslot = [sb(f"wslot{i}", [128, 4096], BF16) for i in range(2)]
        self.wqslot = [sb(f"wqslot{i}", [128, 4096], BF16) for i in range(2)]
        self.a_x = [sb(f"a_x{i}", [128, D], F32) for i in range(2)]
        self.a_xn = sb("a_xn", [128, 8 * 520], BF16)
        self.a_big = sb("a_big", [128, 8320], BF16)
        self.a_xs = sb("a_xs", [128, 6144], BF16)
        self.a_zs = sb("a_zs", [128, 4096], BF16)
        self.a_q = sb("a_q", [128, NTAU * KD], BF16)
        self.a_k = sb("a_k", [128, NTAU * KD], BF16)
        self.a_v = sb("a_v", [128, NTAU * VD], BF16)
        self.a_gs = sb("a_gs", [128, NTAU * VD], BF16)
        self.st1 = sb("st1", [128, SEQ * H * DV], BF16)
        self.st2 = sb("st2", [64, SEQ * H * DV], BF16)
        self.w_pre = sb("w_pre", [128, 128], F32)
        self.w_t1 = sb("w_t1", [128, 128], F32)
        self.w_sv = [sb(f"w_sv_{s}", [128, H * 384], SOLVE_DT) for s in range(SEQ)]
        self.w_DkT = [sb(f"w_DkT_{s}", [128, 512], SOLVE_DT) for s in range(SEQ)]
        self.w_Zh = [sb(f"w_Zh_{s}", [128, DV], SOLVE_DT) for s in range(SEQ)]
        self.w_Pb = [sb(f"w_Pb_{s}", [128, 512], BF16) for s in range(SEQ)]
        self.w_qfm = [sb(f"w_qfm_{s}", [128, H * 256], BF16) for s in range(SEQ)]
        self.w_kfm = [sb(f"w_kfm_{s}", [128, H * 256], BF16) for s in range(SEQ)]
        self.w_zw = sb("w_zw", [128, 6144], BF16)
        self.w_Z = [self.w_zw[:, s * (H * DV):(s + 1) * (H * DV)] for s in range(SEQ)]
        self.w_W = [self.w_zw[:, 3072 + s * (H * DV):3072 + (s + 1) * (H * DV)]
                    for s in range(SEQ)]
        self.r_q = self.w_zw[:, 0:NTAU * KD]
        self.r_k = self.w_zw[:, 3072:3072 + NTAU * KD]
        self.w_O = [sb(f"w_O_{s}", [128, DV], F32) for s in range(SEQ)]
        self.w_osq = self.w_Zh
        self.w_ksc = [sb(f"w_ksc_{s}", [128, DK], BF16) for s in range(SEQ)]
        self.w_gI = [sb(f"w_gI_{s}", [128, 128], BF16) for s in range(SEQ)]
        self.b_beta = sb("b_beta", [128, NTAU * H], F32)
        self.b_G = sb("b_G", [128, NTAU * H], F32)
        self.b_Lam = sb("b_Lam", [128, NTAU * H], F32)
        self.b_negLam = sb("b_negLam", [128, NTAU * H], F32)
        self.b_bias = sb("b_bias", [128, NTAU * H], F32)
        self.b_Growt = sb("b_Growt", [4, NTAU * 128], F32)
        self.b_lamb4 = [sb(f"b_lamb4_{s}", [128, H], F32) for s in range(SEQ)]
        self.b_gam4 = [sb(f"b_gam4_{s}", [128, H], F32) for s in range(SEQ)]
        self.b_rs = sb("b_rs", [128, H], F32)
        self.b_ssq = [sb(f"b_ssq_{s}", [128, 2], F32) for s in range(SEQ)]
        self.b_stat = sb("b_stat", [128, 16], F32)
        self.r_gkb = sb("r_gkb", [128, NTAU * 8], F32)
        self.p2t = sb("p2t", [128, 24], F32)
        self.carry = sb("carry", [128, 2 * 16], BF16)
        self.qt0p = [sb(f"qt0_{i}", [128, D], F32) for i in range(2)]
        self.qsp = [[sb(f"qs_{i}", [128, 32], F32), sb(f"qsi_{i}", [128, 32], F32),
                     sb(f"qm_{i}", [128, 32], I32), sb(f"qw_{i}", [128, 32], F32)]
                    for i in range(2)]
        self.r_sq = self.qt0p[0][:, 0:KD]
        xsf = self.a_xs.ap().bitcast(F32)
        self.qt1 = xsf[:, 0:D]
        self.qt2 = xsf[:, D:2 * D]
        wzf = self.w_zw.ap().bitcast(F32)
        self.qt1b = wzf[:, 0:D]
        self.qt2b = wzf[:, D:2 * D]
        self.ln1sq = wzf[:, 2 * D:3 * D]
        self.a_xnt_p = [self.a_xs[:, 4096:4096 + D],
                        self.a_xs[:, 5120:5120 + D]]
        self.a_xq = self.a_gs
        bigf = self.a_big.ap().bitcast(F32)
        self.x2 = bigf[:, 0:NTAU * D]
        self.fb_stage = bigf[0:1, 0:D]
        self.d_cw = bigf[:, 3200:3264]
        aqf = self.a_q.ap().bitcast(F32)
        self.fq = self.a_q[:, 0:D]
        self.fdw = aqf[:, 1024:1056]
        self.stgap = self.a_q
        self.stg = 1024
        self.pb = [ps(f"pb{i}") for i in range(8)]
        self.out_store = [0, 0]
        self.sems = {}
        for sname in ("dx", "dw0", "dw1", "ds0", "ds1", "dwb", "dq0", "dq1",
                      "dmisc", "dst0", "dst1", "ct", "cv", "cs", "cg"):
            self.sems[sname] = es.enter_context(nc.semaphore(sname))
        self.slot_free = [0, 0]
        self.wb_val = [None] * len(self.pieces)
        self.wb_last_slot = [None, None]
        self.wrecs = []
        self.wuse = 0
        self.wnext = 0
        self.wq_free = 0
        self.ld_wq = [None, None]
        self.ax_free = [("cv", 0)] * 4
        self.x2_ready = [("cv", 0)] * NTAU
        self.x2_store = [(f"dst{t % 2}", 0) for t in range(NTAU)]
        self.qscr_free = [("cv", 0), ("cv", 0)]
        self.xnt_free = [("ct", 0), ("ct", 0)]
        self.qdmisc = 0
        self.gs_cs = 0
        self.conv_cs = 0
        self.p5_vq = [0] * NTAU

    def prep(self):
        for dst, src in ((self.c_idf32, self.i_idf32), (self.c_idbf, self.i_idbf),
                         (self.c_i32rep, self.i_i32rep),
                         (self.c_maskSN, self.i_maskSN), (self.c_maskI, self.i_maskI),
                         (self.c_i4bf, self.i_i4bf), (self.c_sel4, self.i_sel4),
                         (self.c_ones1f, self.i_ones1f), (self.c_ones1b, self.i_ones1b),
                         (self.c_pow2, self.i_pow2), (self.c_sgn, self.i_sgn),
                         (self.c_abs, self.i_abs)):
            self.dma("sync", dst[:], src.ap(), "dmisc")
        self.dma("sync", self.d_small[0:1, 0:16], self.dsmallin, "dmisc")
        self.dma("sync", self.d_cw.rearrange("p (c j) -> p c j", j=4),
                 self.dconvw.rearrange("(c p) j -> p c j", p=128), "dmisc", slow=True)
        self.dma("sync", self.d_cb[:].rearrange("p (c o) -> p c o", o=1),
                 self.dconvb.rearrange("o (c p) -> p c o", p=128), "dmisc", slow=True)
        self.dma("sync", self.d_gnw[0:1, :], self.dgnw, "dmisc")
        self.dma("sync", self.fb_stage, self.dfb, "dmisc")
        WM = [("dmisc", self.now("dmisc"))]
        pb = self.pb

        def bcast(bank_i, dst_ap, src_row, n, extraw=()):
            t = self.T(lambda e, b=pb[bank_i], s=src_row, n2=n: e.matmul(
                b[:, 0:n2], self.c_ones1f[:, :], s, start=True, stop=True),
                waits=WM + list(extraw) + self.bank(bank_i))
            v = self.V(lambda e, b=pb[bank_i], d2=dst_ap, n2=n: e.tensor_copy(
                d2, b[:, 0:n2]), waits=[("ct", t)])
            self.bank_done(bank_i, "cv", v)
            return v

        bcast(0, self.d_bb[:, :], self.d_small[0:1, 0:4], 4)
        bcast(1, self.d_dtb[:, :], self.d_small[0:1, 8:12], 4)
        sA = self.S(lambda e: e.activation(self.d_small[0:1, 16:20],
                                           self.d_small[0:1, 4:8], AF.Exp), waits=WM)
        tA = self.T(lambda e: e.matmul(pb[2][:, 0:4], self.c_ones1f[:, :],
                                       self.d_small[0:1, 16:20], start=True, stop=True),
                    waits=[("cs", sA)] + self.bank(2))
        vA = self.V(lambda e: e.tensor_scalar(self.d_negAexp[:, :], pb[2][:, 0:4],
                                              -1.0, None, ALU.mult), waits=[("ct", tA)])
        self.bank_done(2, "cv", vA)
        bcast(3, self.d_gnw[:, :], self.d_gnw[0:1, :], DV)
        self.S(lambda e: e.copy(self.c_idf32r[:, :], self.c_idf32[:, :]), waits=WM)
        self.S(lambda e: e.copy(self.d_fbias[:, :], self.fb_stage), waits=WM)
        sD = self.S(lambda e: e.activation(self.d_small[0:1, 20:21],
                                           self.d_small[0:1, 12:13], AF.Tanh,
                                           scale=np.float32(4.0)), waits=WM)
        sD = self.S(lambda e: e.activation(self.d_small[0:1, 21:22],
                                           self.d_small[0:1, 20:21], AF.Exp,
                                           scale=np.float32(0.5 * LN2C)), waits=[("cs", sD)])
        tD = self.T(lambda e: e.matmul(pb[4][:, 0:1], self.c_ones1f[:, :],
                                       self.d_small[0:1, 21:22], start=True, stop=True),
                    waits=[("cs", sD)] + self.bank(4))
        vD = self.V(lambda e: e.tensor_copy(self.d_dactp[:, :], pb[4][:, 0:1]),
                    waits=[("ct", tD)])
        self.bank_done(4, "cv", vD)
        for ch in range(16):
            for j in range(4):
                off = (ch * 4 + j) * 32
                self.S(lambda e, o=off, c2=ch, j2=j: e.activation(
                    self.d_convdiag[:, o:o + 32], self.c_i32rep[:, :], AF.Identity,
                    scale=self.d_cw[:, c2 * 4 + j2:c2 * 4 + j2 + 1]), waits=WM)
        self.G(lambda e: e.memset(self.st1[:, :], 0.0))
        self.G(lambda e: e.memset(self.st2[:, :], 0.0))
        self.G(lambda e: e.memset(self.carry[:, :], 0.0))

    def quant_w_chunk(self, kc):
        par = kc % 2
        qt0 = self.qt0p[par]
        qs, qsi, qm, qw = self.qsp[par]
        qt1, qt2 = (self.qt1, self.qt2) if par == 0 else (self.qt1b, self.qt2b)
        self.dma("sync", qt0[:], self.dfw[kc * 128:(kc + 1) * 128, :], "dmisc",
                 waits=self.snap("cv", "cs"))
        self.dma("sync", self.fdw, self.dfdw[kc * 128:(kc + 1) * 128, :], "dmisc")
        WD = [("dmisc", self.now("dmisc")), ("cv", self.now("cv")),
              ("cs", self.now("cs"))]
        v = self.V(lambda e: e.reduce_max(qs[:].unsqueeze(-1),
                                          qt0[:].rearrange("p (b k) -> p b k", k=32),
                                          axis=AX.X, apply_absolute_value=True),
                   waits=WD)
        v = self.V(lambda e: e.tensor_scalar(qsi[:], qs[:], 1e-30, None, ALU.max),
                   waits=[("cv", v)])
        v = self.V(lambda e: e.tensor_scalar(qsi[:].bitcast(I32), qsi[:].bitcast(I32),
                                             self.c_pow2[:, 0:1], None, ALU.bitwise_and),
                   waits=[("cv", v)])
        v = self.V(lambda e: e.tensor_scalar(qs[:], qsi[:], 0.25, None, ALU.mult),
                   waits=[("cv", v)])
        s = self.S(lambda e: e.activation(qsi[:], self.fdw, AF.Tanh, scale=np.float32(4.0)),
                   waits=WD + [("cv", v)])
        s = self.S(lambda e: e.activation(qsi[:], qsi[:], AF.Exp, scale=np.float32(0.5 * LN2C)),
                   waits=[("cs", s)])
        v = self.V(lambda e: e.tensor_tensor(qs[:], qs[:], qsi[:], ALU.mult),
                   waits=[("cs", s), ("cv", v)])
        v = self.V(lambda e: e.reciprocal(qsi[:], qs[:]), waits=[("cv", v)])
        vq = self.emit_quant_core(qt0[:], qt1, qt2, qs[:], qsi[:],
                                  self.fq[:, 0:D], [("cv", v)])
        self.qscr_free[par] = ("cv", vq)
        stg = self.stg
        for tt in range(8):
            bi = 5 + (tt % 2)
            t = self.T(lambda e, t2=tt, b=self.pb[bi]: e.transpose(
                b.ap().bitcast(BF16)[0:128, 0:128],
                self.fq[:, t2 * 128:(t2 + 1) * 128], self.c_idbf[:, :]),
                waits=[("cv", vq)] + self.bank(bi)
                + ([("dmisc", self.now("dmisc"))] if tt == 0 else []))
            vv = self.V(lambda e, t2=tt, b=self.pb[bi]: e.tensor_copy(
                self.stgap[:, stg + t2 * 128: stg + (t2 + 1) * 128],
                b.ap().bitcast(BF16)[0:128, 0:128]), waits=[("ct", t)])
            self.bank_done(bi, "cv", vv)
        for tt in range(8):
            self.dma("sync", self.dwqt[tt, :, kc * 128:(kc + 1) * 128],
                     self.stgap[:, stg + tt * 128: stg + (tt + 1) * 128], "dmisc",
                     waits=[("cv", self.now("cv"))] if tt == 0 else ())
        self.qdmisc = self.now("dmisc")

    def emit_quant_core(self, x_in, tmp_sgn, tmp_ph, scale, inv_scale, out_bf, waits):
        V = self.V
        r32 = lambda ap: ap.rearrange("p (b k) -> p b k", k=32)
        bc = lambda ap: ap.unsqueeze(-1).broadcast_to((128, 32, 32))
        v = V(lambda e: e.tensor_tensor(r32(x_in), r32(x_in), bc(inv_scale),
                                        ALU.mult), waits=waits)
        v = V(lambda e: e.tensor_scalar(tmp_sgn.bitcast(I32), x_in.bitcast(I32),
                                        self.c_sgn[:, 0:1], None, ALU.bitwise_and),
              waits=[("cv", v)])
        v = V(lambda e: e.tensor_scalar(x_in.bitcast(I32), x_in.bitcast(I32),
                                        self.c_abs[:, 0:1], None, ALU.bitwise_and),
              waits=[("cv", v)])
        v = V(lambda e: e.tensor_scalar(x_in, x_in, 6.0, 1e-30, ALU.min, ALU.max),
              waits=[("cv", v)])
        v = V(lambda e: e.tensor_scalar(tmp_ph.bitcast(I32), x_in.bitcast(I32),
                                        self.c_pow2[:, 0:1], None, ALU.bitwise_and),
              waits=[("cv", v)])
        v = V(lambda e: e.tensor_scalar(tmp_ph, tmp_ph, 1.0, 4.0, ALU.max, ALU.min),
              waits=[("cv", v)])
        v = V(lambda e: e.tensor_scalar(tmp_ph, tmp_ph, MAGIC * 0.5, None,
                                        ALU.mult), waits=[("cv", v)])
        v = V(lambda e: e.tensor_tensor(x_in, x_in, tmp_ph, ALU.add),
              waits=[("cv", v)])
        v = V(lambda e: e.tensor_tensor(x_in, x_in, tmp_ph, ALU.subtract),
              waits=[("cv", v)])
        v = V(lambda e: e.tensor_tensor(x_in.bitcast(I32), x_in.bitcast(I32),
                                        tmp_sgn.bitcast(I32), ALU.bitwise_or),
              waits=[("cv", v)])
        v = V(lambda e: e.tensor_tensor(r32(out_bf), r32(x_in), bc(scale),
                                        ALU.mult), waits=[("cv", v)])
        return v

    def ln1_tiles(self, g):
        last = (g == NG - 1)
        out = []
        cnt = 0
        for s in range(SEQ):
            for (toff, tn) in [(0, 128), (128, 128)] + ([] if last else [(256, 2)]):
                out.append((s, toff, tn, cnt % 2))
                cnt += 1
        return out

    def ln1_tile(self, g, s, toff, tn, bidx, w0=()):
        t0 = g * TPG
        buf = self.a_x[bidx]
        sc = 12 * bidx
        self.dma("sync", buf[0:tn, :],
                 self.dx[s, t0 + toff:t0 + toff + tn, :], "dx",
                 waits=[self.ax_free[bidx], self.ax_free[bidx + 2],
                        ("dx", self.now("dx"))])
        WX = [("dx", self.now("dx"))]
        v = self.V(lambda e, b=buf, n=tn: e.reduce_sum(
            self.b_stat[0:n, sc:sc + 1], b[0:n, :], axis=AX.X, negate=True),
            waits=WX + list(w0))
        v = self.V(lambda e, n=tn: e.tensor_scalar(
            self.b_stat[0:n, sc + 1:sc + 2], self.b_stat[0:n, sc:sc + 1],
            1.0 / D, None, ALU.mult), waits=[("cv", v)])
        s1 = self.S(lambda e, b=buf, n=tn: e.activation(
            b[0:n, :], b[0:n, :], AF.Identity, bias=self.b_stat[0:n, sc + 1:sc + 2]),
            waits=[("cv", v)] + list(w0))
        s1 = self.S(lambda e, b=buf, n=tn: e.activation(
            self.ln1sq[0:n, :], b[0:n, :], AF.Square,
            accum_out=self.b_stat[0:n, sc + 2:sc + 3]), waits=[("cs", s1)])
        s1 = self.S(lambda e, n=tn: e.activation(
            self.b_stat[0:n, sc + 3:sc + 4], self.b_stat[0:n, sc + 2:sc + 3], AF.Ln,
            scale=np.float32(1.0 / D), bias=np.float32(EPS_LN)), waits=[("cs", s1)])
        s1 = self.S(lambda e, n=tn: e.activation(
            self.b_stat[0:n, sc + 3:sc + 4], self.b_stat[0:n, sc + 3:sc + 4],
            AF.Exp, scale=np.float32(-0.5)), waits=[("cs", s1)])
        xnt = self.a_xnt_p[bidx]
        s1 = self.S(lambda e, b=buf, n=tn: e.activation(
            xnt[0:n, :], b[0:n, :], AF.Identity,
            scale=self.b_stat[0:n, sc + 3:sc + 4]),
            waits=[("cs", s1), self.xnt_free[bidx]])
        self.ax_free[bidx] = ("cs", s1)
        self.ax_free[bidx + 2] = ("cs", s1)
        t = None
        for kc in range(8):
            bi = 6 + (kc % 2)
            t = self.T(lambda e, k2=kc, b=self.pb[bi], n=tn: e.transpose(
                b.ap().bitcast(BF16)[0:128, 0:n],
                xnt[0:n, k2 * 128:(k2 + 1) * 128],
                self.c_idbf[0:n, 0:n]),
                waits=[("cs", s1)] + self.bank(bi))
            vv = self.V(lambda e, k2=kc, b=self.pb[bi], n=tn, s2=s, to=toff:
                        e.tensor_copy(
                self.a_xn[:, k2 * 520 + s2 * 260 + to:
                          k2 * 520 + s2 * 260 + to + n],
                b.ap().bitcast(BF16)[0:128, 0:n]), waits=[("ct", t)])
            self.bank_done(bi, "cv", vv)
        self.xnt_free[bidx] = ("ct", t)

    def emit_ln1(self, g):
        W0 = self.all_compute() + [("dmisc", self.now("dmisc")),
                                   ("dst0", self.now("dst0")),
                                   ("dst1", self.now("dst1"))]
        for i, (s, toff, tn, bidx) in enumerate(self.ln1_tiles(g)):
            self.ln1_tile(g, s, toff, tn, bidx, w0=W0 if i == 0 else ())

    def group(self, g):
        t0 = g * TPG
        last = (g == NG - 1)
        ncol = TPG if last else TPG + 2
        pb = self.pb

        xn_ready = self.snap("cv")
        g0_extra = ([("dmisc", self.now("dmisc"))] if g == 0 else []) + \
            [("dst0", self.now("dst0")), ("dst1", self.now("dst1"))]
        ev_xz = None
        for piece in range(4):
            rec = self.use()
            slot = rec["slot"]
            for dsub in range(4):
                dchunk = piece * 4 + dsub
                for s in range(SEQ):
                    bi = (dsub * 2 + s) % 4
                    mm = None
                    for kc in range(8):
                        lhs = self.wslot[slot][:, kc * 512 + dsub * 128:
                                               kc * 512 + dsub * 128 + 128]
                        rhs = self.a_xn[:, kc * 520 + s * 260:
                                        kc * 520 + s * 260 + ncol]
                        w = ([rec["ld"]] + xn_ready + self.bank(bi)) if kc == 0 else []
                        mm = self.T(lambda e, l=lhs, r=rhs, b=pb[bi], k2=kc, n=ncol:
                                    e.matmul(b[:, 0:n], l, r, start=(k2 == 0),
                                             stop=(k2 == 7)), waits=w)
                    ev_xz = self.S(lambda e, b=pb[bi], d2=dchunk, s2=s, n=ncol: e.copy(
                        self.a_big[:, d2 * 520 + s2 * 260 + 1:
                                   d2 * 520 + s2 * 260 + 1 + n], b[:, 0:n]),
                        waits=[("ct", mm)] + (g0_extra if dchunk == 0 else []))
                    self.bank_done(bi, "cs", ev_xz)
            self.release(rec, mm)
            if g == 0:
                self.quant_w_chunk(2 * piece)
                self.quant_w_chunk(2 * piece + 1)
        cw = [("cs", ev_xz), ("cg", self.now("cg"))]
        big3 = self.a_big[:].rearrange("p (ch q) -> p ch q", q=520)
        for s in range(SEQ):
            self.V(lambda e, s2=s: e.tensor_copy(
                big3[:, :, s2 * 260: s2 * 260 + 1],
                self.carry[:, s2 * 16: s2 * 16 + 16].rearrange("p (c o) -> p c o", o=1)),
                waits=cw)
            if last:
                self.V(lambda e, s2=s: e.memset(
                    big3[:, :, s2 * 260 + 257: s2 * 260 + 260], 0.0), waits=cw)
        xz_ready_cv = self.now("cv")

        for dchunk in range(16):
            half = dchunk // 8
            ch = dchunk % 8
            dst = self.a_xs if half == 0 else self.a_zs
            for s in range(SEQ):
                bi = 4 + (dchunk * 2 + s) % 2
                mm = None
                for blk in range(4):
                    for j in range(4):
                        diag = self.d_convdiag[32 * blk:32 * blk + 32,
                                               (dchunk * 4 + j) * 32:
                                               (dchunk * 4 + j) * 32 + 32]
                        rhs = self.a_big[32 * blk:32 * blk + 32,
                                         dchunk * 520 + s * 260 + j:
                                         dchunk * 520 + s * 260 + j + TPG]
                        w = ([("cv", xz_ready_cv)] + self.bank(bi)) \
                            if (j == 0 and blk == 0) else []
                        mm = self.T(lambda e, d2=diag, r=rhs, b=pb[bi], j2=j, bl=blk:
                                    e.matmul(b[32 * bl:32 * bl + 32, 0:TPG], d2, r,
                                             start=(j2 == 0), stop=(j2 == 3),
                                             tile_position=(32 * bl, 32 * bl)),
                                    waits=w)
                sc = self.S(lambda e, b=pb[bi], c2=ch, s2=s, h2=half, d2=dst:
                            e.activation(
                    d2[:, c2 * 512 + s2 * 256: c2 * 512 + s2 * 256 + TPG],
                    b[:, 0:TPG], AF.Silu,
                    bias=self.d_cb[:, h2 * 8 + c2: h2 * 8 + c2 + 1]),
                    waits=[("ct", mm)]
                    + ([self.qscr_free[0], self.qscr_free[1]]
                       if (g == 0 and dchunk == 0 and s == 0) else []))
                self.bank_done(bi, "cs", sc)
        conv_ct = self.now("ct")
        self.conv_cs = self.now("cs")
        if not last:
            for s in range(SEQ):
                self.V(lambda e, s2=s: e.tensor_copy(
                    self.carry[:, s2 * 16: s2 * 16 + 16]
                    .rearrange("p (c o) -> p c o", o=1),
                    big3[:, :, s2 * 260 + 256: s2 * 260 + 257]),
                    waits=[("ct", conv_ct)])

        if STOP_PHASE == "p1":
            return
        self.p2(g)

        if STOP_PHASE == "p2":
            return
        for c in range(CPG):
            for s in range(SEQ):
                self.scan_chunk(g, s, c)
        p3_cv, p3_cs = self.now("cv"), self.now("cs")

        if STOP_PHASE == "p3":
            return
        for tau in range(NTAU):
            s, c = divmod(tau, CPG)
            for kk in range(12):
                bi = 6 + (kk % 2)
                t = self.T(lambda e, t2=tau, k2=kk, b=pb[bi]: e.transpose(
                    b.ap().bitcast(BF16)[0:128, 0:128],
                    self.a_big[:, t2 * VD + k2 * 128: t2 * VD + (k2 + 1) * 128],
                    self.c_idbf[:, :]),
                    waits=(([("cv", p3_cv), ("cs", p3_cs)] if (tau == 0 and kk == 0)
                            else []) + self.bank(bi)))
                vv = self.V(lambda e, k2=kk, b=pb[bi], s2=s, c2=c: e.tensor_copy(
                    self.a_xs[:, k2 * 512 + s2 * 256 + c2 * 128:
                              k2 * 512 + s2 * 256 + c2 * 128 + 128],
                    b.ap().bitcast(BF16)[0:128, 0:128]), waits=[("ct", t)])
                self.bank_done(bi, "cv", vv)
        p4a_cv = self.now("cv")

        p4b_cs = None
        for mhalf in range(2):
            recs = [self.use(), self.use()]
            for msub in range(4):
                m = mhalf * 4 + msub
                bi = msub % 4
                mm = None
                for kc in range(12):
                    khalf, kc6 = divmod(kc, 6)
                    slot = recs[khalf]["slot"]
                    w = []
                    if kc == 0:
                        w = [recs[0]["ld"], ("cv", p4a_cv)] + self.bank(bi)
                    elif kc6 == 0:
                        w = [recs[khalf]["ld"]]
                    mm = self.T(lambda e, k6=kc6, sl=slot, m2=msub, b=pb[bi], k2=kc:
                                e.matmul(
                        b[:, 0:512],
                        self.wslot[sl][:, k6 * 512 + m2 * 128:
                                       k6 * 512 + (m2 + 1) * 128],
                        self.a_xs[:, k2 * 512: k2 * 512 + 512],
                        start=(k2 == 0), stop=(k2 == 11)), waits=w)
                p4b_cs = self.S(lambda e, m2=m, b=pb[bi]: e.activation(
                    self.a_v[:, m2 * 512: m2 * 512 + 512], b[:, 0:512], AF.Copy),
                    waits=[("ct", mm)])
                self.bank_done(bi, "cs", p4b_cs)
            self.release(recs[0], mm)
            self.release(recs[1], mm)

        for qh in range(2):
            w = [("ct", self.wq_free)]
            if g == 0:
                w.append(("dmisc", self.now("dmisc")))
            for tt in range(4):
                self.dma("sync", self.wqslot[qh][:, tt * 1024:(tt + 1) * 1024],
                         self.dwqt[qh * 4 + tt, :, :], f"dq{qh}", waits=w)
                w = []
            self.ld_wq[qh] = (f"dq{qh}", self.now(f"dq{qh}"))

        for half in range(2):
            recs = [self.use(), self.use()]
            for tau in range(NTAU):
                s, c = divmod(tau, CPG)
                xq = self.a_x[tau % 2][:, (tau // 2) * 512:(tau // 2) * 512 + 512]
                self.dma("sync", xq,
                         self.dx[s, t0 + c * 128: t0 + c * 128 + 128,
                                 half * 512:(half + 1) * 512],
                         "dx", waits=[self.ax_free[tau], ("dx", self.now("dx"))])
                WX = [("dx", self.now("dx"))]
                bi = 4 + (tau % 2)
                mm = None
                for khalf in range(2):
                    src = self.a_v if khalf == 0 else self.a_zs
                    sl = recs[khalf]["slot"]
                    for kc in range(8):
                        lhs = src[:, kc * 512 + s * 256 + c * 128:
                                  kc * 512 + s * 256 + c * 128 + 128]
                        w = ([recs[khalf]["ld"]] +
                             ([("cs", p4b_cs)] + self.bank(bi) if khalf == 0 else [])) \
                            if kc == 0 else []
                        mm = self.T(lambda e, l=lhs, b=pb[bi], k2=kc, kh=khalf, s2=sl:
                                    e.matmul(
                            b[:, 0:512], l,
                            self.wslot[s2][:, k2 * 512:(k2 + 1) * 512],
                            start=(kh == 0 and k2 == 0), stop=(kh == 1 and k2 == 7)),
                            waits=w)
                vv = self.V(lambda e, b=pb[bi], t2=tau, h2=half, bx=xq:
                            e.tensor_tensor(
                    self.x2[:, t2 * D + h2 * 512: t2 * D + h2 * 512 + 512],
                    bx, b[:, 0:512], ALU.add),
                    waits=[("ct", mm)] + WX
                    + ([self.x2_store[tau]] if half == 0 else []))
                self.bank_done(bi, "cv", vv)
                self.ax_free[tau] = ("cv", vv)
                if half == 1:
                    self.x2_ready[tau] = ("cv", vv)
            self.release(recs[0], mm)
            self.release(recs[1], mm)

        if STOP_PHASE == "p4":
            return
        tiles = self.ln1_tiles(g + 1) if not last else []
        ti = 0
        self.p5_ln2(g, 0)
        for tau in range(NTAU):
            if tau + 1 < NTAU:
                self.p5_ln2(g, tau + 1)
            self.p5_ffn(g, tau)
            if ti < len(tiles):
                self.ln1_tile(g + 1, *tiles[ti])
                ti += 1
        while ti < len(tiles):
            self.ln1_tile(g + 1, *tiles[ti])
            ti += 1

    def p2(self, g):
        pb = self.pb

        def xs_lhs(kc, tau):
            s, c = divmod(tau, CPG)
            return self.a_xs[:, kc * 512 + s * 256 + c * 128:
                             kc * 512 + s * 256 + c * 128 + 128]

        first_mm = [True]

        def project(dten, outcols, evac_fn):
            npieces = (outcols + 511) // 512
            for piece in range(npieces):
                pcols = min(512, outcols - piece * 512)
                rec = self.use()
                slot = rec["slot"]
                for tau in range(NTAU):
                    bi = tau % 4
                    mm = None
                    for kc in range(8):
                        w = ([rec["ld"]] + self.bank(bi)) if kc == 0 else []
                        if w and first_mm[0]:
                            w.append(("cs", self.conv_cs))
                            first_mm[0] = False
                        mm = self.T(lambda e, k2=kc, b=pb[bi], t2=tau, sl=slot,
                                    pc=pcols: e.matmul(
                            b[:, 0:pc], xs_lhs(k2, t2),
                            self.wslot[sl][:, k2 * pc: k2 * pc + pc],
                            start=(k2 == 0), stop=(k2 == 7)), waits=w)
                    sem, ev = evac_fn(piece, pcols, tau, bi, mm)
                    self.bank_done(bi, sem, ev)
                self.release(rec, mm)

        def mk_evac(dst, width, func):
            def f(piece, pcols, tau, bi, mm):
                e1 = self.S(lambda e, b=pb[bi], t2=tau, p2=piece, pc=pcols:
                            e.activation(
                    dst[:, t2 * width + p2 * 512: t2 * width + p2 * 512 + pc],
                    b[:, 0:pc], func), waits=[("ct", mm)])
                return ("cs", e1)
            return f

        qevac_extra = [self.qscr_free[1]]

        def mk_evac_qk(dst, width):
            def f(piece, pcols, tau, bi, mm):
                w = [("ct", mm)] + qevac_extra[:]
                qevac_extra.clear()
                e1 = self.S(lambda e, b=pb[bi], t2=tau, p2=piece, pc=pcols:
                            e.activation(
                    dst[:, t2 * width + p2 * 512: t2 * width + p2 * 512 + pc],
                    b[:, 0:pc], AF.Copy), waits=w)
                return ("cs", e1)
            return f

        project(self.dwq, KD, mk_evac_qk(self.r_q, KD))
        project(self.dwk, KD, mk_evac_qk(self.r_k, KD))

        def gkb_evac(piece, pcols, tau, bi, mm):
            v = self.V(lambda e, b=pb[bi], t2=tau: e.tensor_copy(
                self.r_gkb[:, t2 * 8: t2 * 8 + 8], b[:, 0:8]), waits=[("ct", mm)])
            return ("cv", v)

        project(self.dwgkb, 8, gkb_evac)

        p2t = self.p2t
        for tau in range(NTAU):
            v = self.V(lambda e, t2=tau: e.tensor_tensor(
                p2t[:, 0:4], self.r_gkb[:, t2 * 8 + 4: t2 * 8 + 8],
                self.d_bb[:, :], ALU.add), waits=self.snap("cv"))
            s = self.S(lambda e: e.activation(p2t[:, 4:8], p2t[:, 0:4],
                                              AF.Exp, scale=np.float32(-1.0)),
                       waits=[("cv", v)] + self.snap("cs"))
            v = self.V(lambda e: e.tensor_scalar(p2t[:, 4:8], p2t[:, 4:8],
                                                 1.0, None, ALU.add), waits=[("cs", s)])
            v = self.V(lambda e, t2=tau: e.reciprocal(
                self.b_beta[:, t2 * H: t2 * H + H], p2t[:, 4:8]), waits=[("cv", v)])
            v2 = self.V(lambda e, t2=tau: e.tensor_tensor(
                p2t[:, 8:12], self.r_gkb[:, t2 * 8: t2 * 8 + 4],
                self.d_dtb[:, :], ALU.add), waits=[("cv", v)])
            s = self.S(lambda e: e.activation(p2t[:, 12:16], p2t[:, 8:12], AF.Abs),
                       waits=[("cv", v2)])
            s = self.S(lambda e: e.activation(p2t[:, 12:16], p2t[:, 12:16],
                                              AF.Exp, scale=np.float32(-1.0)), waits=[("cs", s)])
            s = self.S(lambda e: e.activation(p2t[:, 12:16], p2t[:, 12:16],
                                              AF.Ln, bias=np.float32(1.0)), waits=[("cs", s)])
            s = self.S(lambda e: e.activation(p2t[:, 16:20], p2t[:, 8:12], AF.Relu),
                       waits=[("cs", s)])
            v2 = self.V(lambda e: e.tensor_tensor(p2t[:, 12:16], p2t[:, 12:16],
                                                  p2t[:, 16:20], ALU.add),
                        waits=[("cs", s)])
            v2 = self.V(lambda e: e.tensor_tensor(p2t[:, 12:16], p2t[:, 12:16],
                                                  self.d_negAexp[:, :], ALU.mult),
                        waits=[("cv", v2)])
            bi = 4 + (tau % 2)
            t = self.T(lambda e, b=pb[bi]: e.matmul(
                b[:, 0:4], self.c_maskI[:, :], p2t[:, 12:16], start=True, stop=True),
                waits=[("cv", v2)] + self.bank(bi))
            v3 = self.V(lambda e, b=pb[bi], t2=tau: e.tensor_copy(
                self.b_G[:, t2 * H: t2 * H + H], b[:, 0:4]), waits=[("ct", t)])
            t2i = self.T(lambda e, b=pb[bi], t2=tau: e.transpose(
                b[0:4, 128:256], self.b_G[:, t2 * H: t2 * H + H], self.c_idf32[:, :]),
                waits=[("cv", v3)])
            v3 = self.V(lambda e, b=pb[bi], t2=tau: e.tensor_copy(
                self.b_Growt[0:4, t2 * 128:(t2 + 1) * 128], b[0:4, 128:256]),
                waits=[("ct", t2i)])
            self.bank_done(bi, "cv", v3)
        bw = self.snap("cv")
        s = self.S(lambda e: e.activation(self.b_Lam[:, :], self.b_G[:, :], AF.Exp),
                   waits=bw + self.snap("cs"))
        self.V(lambda e: e.tensor_scalar(self.b_negLam[:, :], self.b_Lam[:, :],
                                         -1.0, None, ALU.mult), waits=[("cs", s)])
        s = self.S(lambda e: e.activation(self.b_bias[:, :], self.b_beta[:, :], AF.Ln),
                   waits=bw)
        self.V(lambda e: e.tensor_tensor(self.b_bias[:, :], self.b_bias[:, :],
                                         self.b_G[:, :], ALU.subtract),
               waits=[("cs", s)])
        first_l2 = [self.qscr_free[0]]
        if g == 0:
            first_l2.append(("dmisc", self.qdmisc))
        for (raw, dst) in ((self.r_q, self.a_q), (self.r_k, self.a_k)):
            for tau in range(NTAU):
                s = self.S(lambda e, r2=raw, t2=tau: e.activation(
                    self.r_sq, r2[:, t2 * KD:(t2 + 1) * KD], AF.Square),
                    waits=self.snap("cs") + first_l2[:])
                first_l2.clear()
                v = self.V(lambda e: e.reduce_sum(
                    self.b_rs[:, :].unsqueeze(-1),
                    self.r_sq.rearrange("p (h d) -> p h d", d=DK), axis=AX.X),
                    waits=[("cs", s)] + self.snap("cv"))
                s = self.S(lambda e: e.activation(self.b_rs[:, :], self.b_rs[:, :],
                                                  AF.Ln, bias=np.float32(EPS_L2)),
                           waits=[("cv", v)])
                s = self.S(lambda e: e.activation(self.b_rs[:, :], self.b_rs[:, :],
                                                  AF.Exp, scale=np.float32(-0.5)),
                           waits=[("cs", s)])
                for h in range(H):
                    s = self.S(lambda e, r2=raw, d2=dst, t2=tau, h2=h: e.activation(
                        d2[:, t2 * KD + h2 * DK: t2 * KD + (h2 + 1) * DK],
                        r2[:, t2 * KD + h2 * DK: t2 * KD + (h2 + 1) * DK],
                        AF.Identity, scale=self.b_rs[:, h2:h2 + 1]), waits=[("cs", s)])
        self.qk_ready_snap = self.snap("cs", "cv")
        project(self.dwv, VD, mk_evac(self.a_v, VD, AF.Copy))
        project(self.dwg, VD, mk_evac(self.a_gs, VD, AF.Silu))
        self.gs_cs = self.now("cs")

    def scan_chunk(self, g, s, c):
        pb = self.pb
        tau = s * CPG + c
        qk_ready = self.qk_ready_snap
        bA, bB = s * 2, s * 2 + 1
        bE = 4 + s
        bT = 6 + s

        for (src, dstfm) in ((self.a_q, self.w_qfm[s]), (self.a_k, self.w_kfm[s])):
            tl = None
            for h in range(H):
                for (p0, fn, oo) in ((0, 128, 0), (128, 64, 128)):
                    tl = self.T(lambda e, h2=h, p2=p0, f2=fn, o2=oo, b=pb[bT], s2=src:
                                e.transpose(
                        b.ap().bitcast(BF16)[0:f2, h2 * 256 + o2: h2 * 256 + o2 + 128],
                        s2[:, tau * KD + h2 * DK + p2: tau * KD + h2 * DK + p2 + f2],
                        self.c_idbf[:, :]),
                        waits=(qk_ready + self.bank(bT)) if (h == 0 and p0 == 0) else [])
            pv = pb[bT].ap().bitcast(BF16).rearrange("p (h q) -> p h q", q=256)
            dv = dstfm[:].rearrange("p (h q) -> p h q", q=256)
            self.V(lambda e, pv2=pv, dv2=dv: e.tensor_copy(
                dv2[0:128, :, 0:128], pv2[0:128, :, 0:128]), waits=[("ct", tl)])
            vv = self.V(lambda e, pv2=pv, dv2=dv: e.tensor_copy(
                dv2[0:64, :, 128:256], pv2[0:64, :, 128:256]))
            self.bank_done(bT, "cv", vv)
        kfm, qfm = self.w_kfm[s], self.w_qfm[s]
        fm_cv = self.now("cv")
        if STOP_PHASE == "sA":
            return

        def fmp(buf, h, piece):
            if piece == 0:
                return buf[:, h * 256: h * 256 + 128]
            return buf[0:64, h * 256 + 128: h * 256 + 256]

        mmA = mmB = None
        for h in range(H):
            w = ([("cv", fm_cv)] + self.bank(bA)) if h == 0 else []
            self.T(lambda e, h2=h, b=pb[bA]: e.matmul(
                b[:, h2 * 128:(h2 + 1) * 128], fmp(kfm, h2, 0), fmp(kfm, h2, 0),
                start=True, stop=False), waits=w)
            mmA = self.T(lambda e, h2=h, b=pb[bA]: e.matmul(
                b[:, h2 * 128:(h2 + 1) * 128], fmp(kfm, h2, 1), fmp(kfm, h2, 1),
                start=False, stop=True))
        for h in range(H):
            w = self.bank(bB) if h == 0 else []
            self.T(lambda e, h2=h, b=pb[bB]: e.matmul(
                b[:, h2 * 128:(h2 + 1) * 128], fmp(kfm, h2, 0), fmp(qfm, h2, 0),
                start=True, stop=False), waits=w)
            mmB = self.T(lambda e, h2=h, b=pb[bB]: e.matmul(
                b[:, h2 * 128:(h2 + 1) * 128], fmp(kfm, h2, 1), fmp(qfm, h2, 1),
                start=False, stop=True))

        if STOP_PHASE == "sB":
            return
        prev_readers = []
        for h in range(H):
            col = tau * H + h
            w = (self.bank(bE) if h == 0 else []) + prev_readers
            t = self.T(lambda e, h2=h, b=pb[bE], t2=tau: e.matmul(
                b[:, 0:128], self.c_sel4[:, h2 * 128:(h2 + 1) * 128],
                self.b_Growt[0:4, t2 * 128:(t2 + 1) * 128],
                start=True, stop=True), waits=w + ([("ct", mmB)] if h == 0 else []))
            v = self.V(lambda e, b=pb[bE], c2=col: e.tensor_scalar(
                self.w_t1[:, 0:1], b[:, 127:128], self.b_G[:, c2:c2 + 1], None,
                ALU.subtract), waits=[("ct", t)] + self.snap("cv"))
            sx = self.S(lambda e, h2=h: e.activation(
                self.b_lamb4[s][:, h2:h2 + 1], self.w_t1[:, 0:1], AF.Exp),
                waits=[("cv", v)] + self.snap("cs"))
            sx2 = self.S(lambda e, b=pb[bE], h2=h: e.activation(
                self.b_gam4[s][:, h2:h2 + 1], b[:, 127:128], AF.Exp),
                waits=[("ct", t)])
            v = self.V(lambda e, h2=h, c2=col: e.tensor_tensor(
                self.b_lamb4[s][:, h2:h2 + 1], self.b_lamb4[s][:, h2:h2 + 1],
                self.b_beta[:, c2:c2 + 1], ALU.mult), waits=[("cs", sx)])
            vpre = self.V(lambda e, b=pb[bE], c2=col: e.tensor_scalar(
                self.w_pre[:, :], b[:, 0:128], self.b_bias[:, c2:c2 + 1], 0.0,
                ALU.add, ALU.min), waits=[("ct", t)] + self.snap("cv"))
            prev_readers = [("cv", vpre), ("cs", sx2)]
            se = self.S(lambda e: e.activation(self.w_pre[:, :], self.w_pre[:, :],
                                               AF.Exp), waits=[("cv", vpre), ("cs", sx2)])
            v = self.V(lambda e: e.tensor_tensor(self.w_t1[:, :], self.w_pre[:, :],
                                                 self.c_maskSN[:, :], ALU.mult),
                       waits=[("cs", se)])
            vD = self.V(lambda e, h2=h, b=pb[bA]: e.tensor_tensor(
                self.w_sv[s][:, h2 * 384: h2 * 384 + 128], self.w_t1[:, :],
                b[:, h2 * 128:(h2 + 1) * 128], ALU.mult),
                waits=[("cv", v), ("ct", mmA)])
            self.V(lambda e, h2=h: e.tensor_copy(
                self.w_sv[s][:, h2 * 384 + 128: h2 * 384 + 256], self.c_idf32[:, :]))
            v = self.V(lambda e: e.tensor_tensor(self.w_t1[:, :], self.w_pre[:, :],
                                                 self.c_maskI[:, :], ALU.mult),
                       waits=[("cv", vD)])
            vP = self.V(lambda e, h2=h, b=pb[bB]: e.tensor_tensor(
                self.w_Pb[s][:, h2 * 128:(h2 + 1) * 128], self.w_t1[:, :],
                b[:, h2 * 128:(h2 + 1) * 128], ALU.mult), waits=[("cv", v)])
            if h == H - 1:
                self.bank_done(bA, "cv", vD)
                self.bank_done(bB, "cv", vP)
                self.bank_done(bE, "cv", vP)
                self.bank_done(bE, "cs", sx2)
        d0_cv = self.now("cv")

        if STOP_PHASE == "sC":
            return
        d0_cv2 = self.now("cv")
        tl = None
        for h in range(H):
            tl = self.T(lambda e, h2=h, b=pb[bT]: e.transpose(
                b.ap().bitcast(SOLVE_DT)[0:128, h2 * 128:(h2 + 1) * 128],
                self.w_sv[s][:, h2 * 384: h2 * 384 + 128], self.c_idf32r[:, :]),
                waits=([("cv", d0_cv2)] + self.bank(bT)) if h == 0 else [])
        dkt_ev = self.V(lambda e, b=pb[bT]: e.tensor_copy(
            self.w_DkT[s][:, :], b.ap().bitcast(SOLVE_DT)[0:128, 0:512]),
            waits=[("ct", tl)])
        self.bank_done(bT, "cv", dkt_ev)
        x_ready = self.now("cv")
        if STOP_PHASE == "sD0":
            return
        for r in range(1, 8):
            cur_a = (r % 2 == 1)
            roff = 0 if cur_a else 128
            dk_psoff = 0 if cur_a else 128
            x_psoff = 128 - dk_psoff
            dk_dst = 256 if cur_a else 0
            mq = None
            wts = [("cv", dkt_ev), ("cv", x_ready)]
            for h in range(H):
                bi2 = bA if h < 2 else bB
                w2 = (wts + self.bank(bA) + self.bank(bB)) if h == 0 else []
                mq = self.T(lambda e, h2=h, b=pb[bi2], ro=roff: e.matmul(
                    b[:, (h2 % 2) * 256: (h2 % 2) * 256 + 256],
                    self.w_DkT[s][:, h2 * 128:(h2 + 1) * 128],
                    self.w_sv[s][:, h2 * 384 + ro: h2 * 384 + ro + 256],
                    start=True, stop=True), waits=w2)
            sv3 = self.w_sv[s][:].rearrange("p (h q) -> p h q", q=384)
            ev1 = None
            if r < 7 and "ev1" not in SKIP:
                for half in range(2):
                    b2 = pb[bA] if half == 0 else pb[bB]
                    pv = b2.ap().rearrange("p (h q) -> p h q", q=256)
                    ev1 = self.V(lambda e, pv2=pv, hf=half, dd=dk_dst, dp=dk_psoff:
                                 e.tensor_copy(
                        sv3[:, 2 * hf: 2 * hf + 2, dd: dd + 128],
                        pv2[:, 0:2, dp: dp + 128]),
                        waits=([("ct", mq)] + self.snap("cv")) if half == 0
                        else [])
            for half in range(2 if "ev2" not in SKIP else 0):
                b2 = pb[bA] if half == 0 else pb[bB]
                pv = b2.ap().rearrange("p (h q) -> p h q", q=256)
                ev2 = self.V(lambda e, pv2=pv, hf=half, xp=x_psoff: e.tensor_tensor(
                    sv3[:, 2 * hf: 2 * hf + 2, 128:256],
                    sv3[:, 2 * hf: 2 * hf + 2, 128:256],
                    pv2[:, 0:2, xp: xp + 128], ALU.add),
                    waits=([("ct", mq)] + self.snap("cv")) if half == 0 else [])
            x_ready = ev2
            self.bank_done(bA, "cv", ev2)
            self.bank_done(bB, "cv", ev2)
            if ev1 is not None:
                self.bank_done(bA, "cv", ev1)
                self.bank_done(bB, "cv", ev1)
            if r < 7 and "tr" not in SKIP:
                tl = None
                for h in range(H):
                    tl = self.T(lambda e, h2=h, b=pb[bT], dd=dk_dst: e.transpose(
                        b.ap().bitcast(SOLVE_DT)[0:128, h2 * 128:(h2 + 1) * 128],
                        self.w_sv[s][:, h2 * 384 + dd: h2 * 384 + dd + 128],
                        self.c_idf32r[:, :]),
                        waits=([("cv", ev1)] + self.bank(bT)) if h == 0 else [])
                dkt_ev = self.V(lambda e, b=pb[bT]: e.tensor_copy(
                    self.w_DkT[s][:, :], b.ap().bitcast(SOLVE_DT)[0:128, 0:512]),
                    waits=[("ct", tl)] + self.snap("cv"))
                self.bank_done(bT, "cv", dkt_ev)
            if STOP_PHASE == "sD1" and r == 1:
                return

        if STOP_PHASE == "sD":
            return
        for h in range(H):
            col = tau * H + h
            soff = (s * H + h) * DV
            sg = self.S(lambda e, h2=h: e.activation(
                self.w_gI[s][:, :], self.c_idbf[:, :], AF.Identity,
                scale=self.b_gam4[s][:, h2:h2 + 1]),
                waits=self.snap("cs") + self.snap("ct"))
            sk = self.S(lambda e, h2=h: e.activation(
                self.w_ksc[s][:, :],
                self.a_k[:, tau * KD + h2 * DK: tau * KD + (h2 + 1) * DK],
                AF.Identity, scale=self.b_lamb4[s][:, h2:h2 + 1]), waits=[("cs", sg)])
            w = self.bank(bA) + ([("cv", x_ready)] if h == 0 else [])
            self.T(lambda e, h2=h, b=pb[bA], so=soff: e.matmul(
                b[:, 0:DV], fmp(kfm, h2, 0), self.st1[:, so:so + DV],
                start=True, stop=False), waits=w)
            mmz = self.T(lambda e, h2=h, b=pb[bA], so=soff: e.matmul(
                b[:, 0:DV], fmp(kfm, h2, 1), self.st2[0:64, so:so + DV],
                start=False, stop=True))
            vz = self.V(lambda e, b=pb[bA], h2=h, c2=col: e.scalar_tensor_tensor(
                self.w_Zh[s][:, :], b[:, 0:DV],
                self.b_negLam[:, c2:c2 + 1],
                self.a_v[:, tau * VD + h2 * DV: tau * VD + (h2 + 1) * DV],
                ALU.mult, ALU.add),
                waits=[("ct", mmz)] + self.snap("cv") + self.snap("ct")
                + self.snap("cs"))
            self.bank_done(bA, "cv", vz)
            wW = self.bank(bE)
            mmw = self.T(lambda e, h2=h, b=pb[bE]: e.matmul(
                b[:, 0:DV],
                self.w_sv[s][:, h2 * 384 + 128: h2 * 384 + 256],
                self.w_Zh[s][:, :], start=True, stop=True),
                waits=wW + [("cv", vz)])
            sw = self.S(lambda e, b=pb[bE], h2=h: e.activation(
                self.w_W[s][:, h2 * DV:(h2 + 1) * DV], b[:, 0:DV], AF.Copy),
                waits=[("ct", mmw)] + self.snap("cs"))
            self.bank_done(bE, "cs", sw)
            wq = self.bank(bB)
            self.T(lambda e, h2=h, b=pb[bB], so=soff: e.matmul(
                b[:, 0:DV], fmp(qfm, h2, 0), self.st1[:, so:so + DV],
                start=True, stop=False), waits=wq)
            mmq = self.T(lambda e, h2=h, b=pb[bB], so=soff: e.matmul(
                b[:, 0:DV], fmp(qfm, h2, 1), self.st2[0:64, so:so + DV],
                start=False, stop=True))
            so1 = self.S(lambda e, b=pb[bB], c2=col: e.activation(
                self.w_O[s][:, :], b[:, 0:DV], AF.Identity,
                scale=self.b_Lam[:, c2:c2 + 1]), waits=[("ct", mmq)])
            self.bank_done(bB, "cs", so1)
            wo = self.bank(bB)
            mmo = self.T(lambda e, h2=h, b=pb[bB]: e.matmul(
                b[:, 0:DV], self.w_Pb[s][:, h2 * 128:(h2 + 1) * 128],
                self.w_W[s][:, h2 * DV:(h2 + 1) * DV], start=True, stop=True),
                waits=wo + [("cs", sw)])
            vo = self.V(lambda e, b=pb[bB]: e.tensor_tensor(
                self.w_O[s][:, :], self.w_O[s][:, :], b[:, 0:DV], ALU.add),
                waits=[("ct", mmo), ("cs", so1)])
            self.bank_done(bB, "cv", vo)
            wt = self.bank(bT)
            self.T(lambda e, b=pb[bT], so=soff: e.matmul(
                b[:, 0:DV], self.w_gI[s][:, :], self.st1[:, so:so + DV],
                start=True, stop=False), waits=wt + [("cs", sk)])
            mm1 = self.T(lambda e, b=pb[bT], h2=h: e.matmul(
                b[:, 0:DV], self.w_ksc[s][:, 0:128],
                self.w_W[s][:, h2 * DV:(h2 + 1) * DV], start=False, stop=True))
            bT2 = 6 + (1 - s)
            wt2 = self.bank(bT2)
            self.T(lambda e, b=pb[bT2], so=soff: e.matmul(
                b[0:64, 0:DV], self.w_gI[s][0:64, 0:64], self.st2[0:64, so:so + DV],
                start=True, stop=False), waits=wt2)
            mm2 = self.T(lambda e, b=pb[bT2], h2=h: e.matmul(
                b[0:64, 0:DV], self.w_ksc[s][:, 128:192],
                self.w_W[s][:, h2 * DV:(h2 + 1) * DV], start=False, stop=True))
            v1 = self.V(lambda e, b=pb[bT], so=soff: e.tensor_copy(
                self.st1[:, so:so + DV], b[:, 0:DV]), waits=[("ct", mm1)])
            self.bank_done(bT, "cv", v1)
            v2 = self.V(lambda e, b=pb[bT2], so=soff: e.tensor_copy(
                self.st2[0:64, so:so + DV], b[0:64, 0:DV]), waits=[("ct", mm2)])
            self.bank_done(bT2, "cv", v2)
            sq = self.S(lambda e: e.activation(
                self.w_osq[s][:, :], self.w_O[s][:, :], AF.Square,
                accum_out=self.b_ssq[s][:, 0:1]), waits=[("cv", vo)] + self.snap("cs"))
            sq = self.S(lambda e: e.activation(
                self.b_ssq[s][:, 1:2], self.b_ssq[s][:, 0:1], AF.Ln,
                scale=np.float32(1.0 / DV), bias=np.float32(EPS_RMS)), waits=[("cs", sq)])
            sq = self.S(lambda e: e.activation(
                self.b_ssq[s][:, 1:2], self.b_ssq[s][:, 1:2], AF.Exp, scale=np.float32(-0.5)),
                waits=[("cs", sq)])
            vg2 = self.V(lambda e: e.scalar_tensor_tensor(
                self.w_osq[s][:, :], self.w_O[s][:, :], self.b_ssq[s][:, 1:2],
                self.d_gnw[:, :], ALU.mult, ALU.mult),
                waits=[("cs", sq)] + self.snap("cv"))
            self.V(lambda e, h2=h, t2=tau: e.tensor_tensor(
                self.a_big[:, t2 * VD + h2 * DV: t2 * VD + (h2 + 1) * DV],
                self.w_osq[s][:, :],
                self.a_gs[:, t2 * VD + h2 * DV: t2 * VD + (h2 + 1) * DV], ALU.mult),
                waits=[("cv", vg2), ("cs", self.gs_cs)])

    def p5_ln2(self, g, tau):
        s, c = divmod(tau, CPG)
        par = tau % 2
        xt = self.x2[:, tau * D: tau * D + D]
        qt0 = self.qt0p[par]
        qs, qsi, qm, qw = self.qsp[par]
        qt1, qt2 = (self.qt1, self.qt2) if par == 0 else (self.qt1b, self.qt2b)
        sc = 4 + par * 4
        W0 = [self.x2_ready[tau], self.qscr_free[par]]
        v = self.V(lambda e: e.reduce_sum(self.b_stat[:, sc:sc + 1], xt, axis=AX.X,
                                          negate=True), waits=W0)
        v = self.V(lambda e: e.tensor_scalar(self.b_stat[:, sc + 1:sc + 2],
                                             self.b_stat[:, sc:sc + 1],
                                             1.0 / D, None, ALU.mult),
                   waits=[("cv", v)])
        s1 = self.S(lambda e: e.activation(qt0[:, :], xt, AF.Identity,
                                           bias=self.b_stat[:, sc + 1:sc + 2]),
                    waits=[("cv", v), self.qscr_free[par]])
        s1 = self.S(lambda e: e.activation(qt1, qt0[:, :], AF.Square,
                                           accum_out=self.b_stat[:, sc + 2:sc + 3]),
                    waits=[("cs", s1)])
        s1 = self.S(lambda e: e.activation(self.b_stat[:, sc + 3:sc + 4],
                                           self.b_stat[:, sc + 2:sc + 3],
                                           AF.Ln, scale=np.float32(1.0 / D), bias=np.float32(EPS_LN)),
                    waits=[("cs", s1)])
        s1 = self.S(lambda e: e.activation(self.b_stat[:, sc + 3:sc + 4],
                                           self.b_stat[:, sc + 3:sc + 4],
                                           AF.Exp, scale=np.float32(-0.5)), waits=[("cs", s1)])
        s1 = self.S(lambda e: e.activation(qt0[:, :], qt0[:, :], AF.Identity,
                                           scale=self.b_stat[:, sc + 3:sc + 4]),
                    waits=[("cs", s1)])
        v = self.V(lambda e: e.reduce_max(qs[:].unsqueeze(-1),
                                          qt0[:].rearrange("p (b k) -> p b k", k=32),
                                          axis=AX.X, apply_absolute_value=True),
                   waits=[("cs", s1)])
        v = self.V(lambda e: e.tensor_scalar(qsi[:], qs[:], 1e-30, None,
                                             ALU.max), waits=[("cv", v)])
        v = self.V(lambda e: e.tensor_scalar(qsi[:].bitcast(I32),
                                             qsi[:].bitcast(I32),
                                             self.c_pow2[:, 0:1], None, ALU.bitwise_and),
                   waits=[("cv", v)])
        v = self.V(lambda e: e.tensor_scalar(qs[:], qsi[:], 0.25, None, ALU.mult),
                   waits=[("cv", v)])
        v = self.V(lambda e: e.tensor_scalar(qs[:], qs[:], self.d_dactp[:, 0:1],
                                             None, ALU.mult), waits=[("cv", v)])
        v = self.V(lambda e: e.reciprocal(qsi[:], qs[:]), waits=[("cv", v)])
        vq = self.emit_quant_core(qt0[:], qt1, qt2, qs[:], qsi[:],
                                  self.a_xq[:, tau * D: tau * D + D], [("cv", v)])
        self.qscr_free[par] = ("cv", vq)
        self.p5_vq[tau] = vq

    def p5_ffn(self, g, tau):
        pb = self.pb
        s, c = divmod(tau, CPG)
        t0 = g * TPG
        par = tau % 2
        xt = self.x2[:, tau * D: tau * D + D]
        vq = self.p5_vq[tau]
        for kk in range(8):
            bi = par * 2 + (kk % 2)
            t = self.T(lambda e, k2=kk, b=pb[bi], t2=tau: e.transpose(
                b.ap().bitcast(BF16)[0:128, 0:128],
                self.a_xq[:, t2 * D + k2 * 128: t2 * D + (k2 + 1) * 128],
                self.c_idbf[:, :]), waits=[("cv", vq)] + self.bank(bi))
            vv = self.V(lambda e, k2=kk, b=pb[bi], s2=s, c2=c: e.tensor_copy(
                self.a_zs[:, k2 * 512 + s2 * 256 + c2 * 128:
                          k2 * 512 + s2 * 256 + c2 * 128 + 128],
                b.ap().bitcast(BF16)[0:128, 0:128]), waits=[("ct", t)])
            self.bank_done(bi, "cv", vv)
        xqfm_cv = self.now("cv")
        vlast = None
        for half in range(2):
            bi = 4 + par * 2 + half
            mm = None
            for kc in range(8):
                lhs = self.a_zs[:, kc * 512 + s * 256 + c * 128:
                                kc * 512 + s * 256 + c * 128 + 128]
                w = ([("cv", xqfm_cv), self.ld_wq[0]] + self.bank(bi)) \
                    if kc == 0 else ([self.ld_wq[1]] if kc == 4 else [])
                mm = self.T(lambda e, l=lhs, b=pb[bi], k2=kc, h2=half: e.matmul(
                    b[:, 0:512], l,
                    self.wqslot[k2 // 4][:, (k2 % 4) * 1024 + h2 * 512:
                                         (k2 % 4) * 1024 + h2 * 512 + 512],
                    start=(k2 == 0), stop=False), waits=w)
            mm = self.T(lambda e, b=pb[bi], h2=half: e.matmul(
                b[:, 0:512], self.c_ones1b[:, :],
                self.d_fbias[0:1, h2 * 512: h2 * 512 + 512],
                start=False, stop=True))
            self.wq_free = mm
            vlast = self.V(lambda e, b=pb[bi], h2=half: e.tensor_tensor(
                self.x2[:, tau * D + h2 * 512: tau * D + h2 * 512 + 512],
                self.x2[:, tau * D + h2 * 512: tau * D + h2 * 512 + 512],
                b[:, 0:512], ALU.add),
                waits=[("ct", mm)])
            self.bank_done(bi, "cv", vlast)
        self.dma("gpsimd", self.dout[s, t0 + c * 128: t0 + c * 128 + 128, :],
                 xt, f"dst{par}",
                 waits=[("cv", vlast), (f"dst{par}", self.now(f"dst{par}"))])
        self.x2_store[tau] = (f"dst{par}", self.now(f"dst{par}"))


def kernel(**inputs):
    x = np.ascontiguousarray(inputs["L0_attn_x"], dtype=np.float32)
    convw = np.concatenate([np.ascontiguousarray(inputs["L0_attn_conv_x_w"][:, 0, :]),
                            np.ascontiguousarray(inputs["L0_attn_conv_z_w"][:, 0, :])],
                           axis=0)
    convb = np.concatenate([np.ascontiguousarray(inputs["L0_attn_conv_x_b"]),
                            np.ascontiguousarray(inputs["L0_attn_conv_z_b"])])[None, :]
    small = np.zeros((1, 16), np.float32)
    small[0, 0:4] = inputs["L0_attn_b_b"]
    small[0, 4:8] = inputs["L0_attn_A_log"]
    small[0, 8:12] = inputs["L0_attn_dt_bias"]
    small[0, 12:13] = inputs["L0_ffn_delta_act"]
    base = {
        "w_in": np.ascontiguousarray(inputs["L0_attn_W_in"]),
        "conv_w": convw,
        "conv_b": convb,
        "w_q": np.ascontiguousarray(inputs["L0_attn_W_q"]),
        "w_k": np.ascontiguousarray(inputs["L0_attn_W_k"]),
        "w_v": np.ascontiguousarray(inputs["L0_attn_W_v"]),
        "w_g": np.ascontiguousarray(inputs["L0_attn_W_g"]),
        "w_gkb": np.concatenate([np.ascontiguousarray(inputs["L0_attn_W_gk"]),
                                 np.ascontiguousarray(inputs["L0_attn_W_b"])], axis=1),
        "small_in": small,
        "g_norm_w": np.ascontiguousarray(inputs["L0_attn_g_norm_w"][None, :]),
        "w_so": np.ascontiguousarray(inputs["L0_attn_W_so"]),
        "w_out": np.ascontiguousarray(inputs["L0_attn_W_out"]),
        "ffn_w": np.ascontiguousarray(inputs["L0_ffn_weight"]),
        "ffn_b": np.ascontiguousarray(inputs["L0_ffn_bias"][None, :]),
        "ffn_dw": np.ascontiguousarray(inputs["L0_ffn_delta_w"]),
    }
    nc = B().build()
    in_maps = []
    for core in range(NCORES):
        m = dict(base)
        m["x"] = np.ascontiguousarray(x[core * SEQ:(core + 1) * SEQ])
        in_maps.append(m)
    res = run_bass_kernel_spmd(nc, in_maps, core_ids=list(range(NCORES)))
    return np.concatenate([r["out"] for r in res.results], axis=0)


# version 12
# speedup vs baseline: 1.5245x; 1.0649x over previous
import contextlib
import numpy as np
import ml_dtypes
import concourse.bass as bass
import concourse.mybir as mybir
from concourse.bass_utils import run_bass_kernel_spmd

F32 = mybir.dt.float32
F32R = mybir.dt.float32r
SOLVE_DT = F32R
BF16 = mybir.dt.bfloat16
I32 = mybir.dt.int32
AF = mybir.ActivationFunctionType
ALU = mybir.AluOpType
AX = mybir.AxisListType
BF = ml_dtypes.bfloat16

NCORES = 8
SEQ = 2
L = 2048
D = 1024
DI = 2048
KD = 768
VD = 1536
H = 4
DK = 192
DV = 384
C = 128
NG = 8
TPG = 256
CPG = TPG // C
NTAU = SEQ * CPG
EPS_LN = 1e-5
EPS_L2 = 1e-6
EPS_RMS = 1e-5
LN2C = float(np.log(2.0))
MAGIC = 12582912.0
STOP_PHASE = None
SKIP = set()
POW2MASK = float(-8388608)

ENGINES = ("sync", "scalar", "tensor", "vector", "gpsimd")


class B:

    def __init__(self):
        self.nc = bass.Bass("TRN2", target_bir_lowering=False)
        self.steps = {e: [] for e in ENGINES}
        self.counts = {}
        self.bank_free = [[] for _ in range(8)]

    def now(self, sem):
        return self.counts.get(sem, 0)

    def _step(self, eng, fn, waits, inc, amt):
        waits = [w for w in waits if w[1] > 0]
        val = None
        if inc is not None:
            self.counts[inc] = self.counts.get(inc, 0) + amt
            val = self.counts[inc]
        self.steps[eng].append(dict(fn=fn, waits=waits, inc=inc, amt=amt, val=val))
        return val

    def dma(self, queue, out_ap, in_ap, sem="dw", waits=(), slow=False):
        return self._step(queue,
                          lambda e, o=out_ap, i=in_ap, sl=slow: e.dma_start(
                              out=o, in_=i, allow_slow_non_contiguous=sl),
                          waits, sem, 16)

    def T(self, fn, waits=()):
        return self._step("tensor", fn, waits, "ct", 1)

    def V(self, fn, waits=()):
        return self._step("vector", fn, waits, "cv", 1)

    def S(self, fn, waits=()):
        return self._step("scalar", fn, waits, "cs", 1)

    def G(self, fn, waits=()):
        return self._step("gpsimd", fn, waits, "cg", 1)

    def snap(self, *sems):
        return [(s, self.now(s)) for s in sems]

    def all_compute(self):
        return self.snap("ct", "cv", "cs", "cg")

    def bank(self, i, extra=()):
        w = self.bank_free[i] + list(extra)
        self.bank_free[i] = []
        return w

    def bank_done(self, i, sem, val):
        self.bank_free[i].append((sem, val))

    def build(self):
        nc = self.nc
        self.dram()
        self.consts()
        self.es = contextlib.ExitStack()
        with self.es:
            self.alloc()
            self.prep()
            self.pump()
            self.pump()
            self.emit_ln1(0)
            for g in range(NG):
                self.group(g)
            self._step("sync", lambda e: e.nop(),
                       [("dst0", self.now("dst0")), ("dst1", self.now("dst1"))],
                       None, 0)
            compute_sems = ("ct", "cv", "cs", "cg")
            used = {cs_: set() for cs_ in compute_sems}
            for ename in ENGINES:
                for st in self.steps[ename]:
                    for (sname, val) in st["waits"]:
                        if sname in used:
                            used[sname].add(val)
            ranks = {cs_: {v: i + 1 for i, v in enumerate(sorted(used[cs_]))}
                     for cs_ in compute_sems}

            def remap_wait(sname, val):
                if sname not in ranks:
                    return val
                return ranks[sname][val]

            blk = self.es.enter_context(nc.Block())
            for ename in ENGINES:
                steps = self.steps[ename]
                sems = self.sems

                def emitter(eng, steps=steps, sems=sems):
                    for st in steps:
                        for (sname, val) in st["waits"]:
                            eng.wait_ge(sems[sname], remap_wait(sname, val))
                        ins = st["fn"](eng)
                        inc = st["inc"]
                        if inc is not None:
                            if inc in used:
                                if st["val"] in used[inc]:
                                    ins.then_inc(sems[inc], 1)
                            else:
                                ins.then_inc(sems[inc], st["amt"])

                getattr(blk, ename)(emitter)
        return nc

    def dram(self):
        nc = self.nc
        t = lambda n, s: nc.dram_tensor(n, s, F32, kind="ExternalInput").ap()
        self.dx = t("x", [SEQ, L, D])
        self.dwin = t("w_in", [D, DI])
        self.dconvw = t("conv_w", [2 * D, 4])
        self.dconvb = t("conv_b", [1, 2 * D])
        self.dwq = t("w_q", [D, KD])
        self.dwk = t("w_k", [D, KD])
        self.dwv = t("w_v", [D, VD])
        self.dwg = t("w_g", [D, VD])
        self.dwgkb = t("w_gkb", [D, 8])
        self.dsmallin = t("small_in", [1, 16])
        self.dgnw = t("g_norm_w", [1, DV])
        self.dwso = t("w_so", [VD, D])
        self.dwout = t("w_out", [DI, D])
        self.dfw = t("ffn_w", [D, D])
        self.dfb = t("ffn_b", [1, D])
        self.dfdw = t("ffn_dw", [D, 32])
        self.dout = nc.dram_tensor("out", [SEQ, L, D], F32, kind="ExternalOutput").ap()
        self.dwqt = nc.dram_tensor("wqt_scratch", [8, 128, D], BF16, kind="Internal").ap()
        self.dwbf = nc.dram_tensor("wbf_scratch", [23, 128, 4096], BF16,
                                   kind="Internal").ap()
        self.mk_pieces()

    def mk_pieces(self):
        P = []

        def add(subs):
            P.append(dict(w=max(c + n for c, n, _ in subs), subs=subs))

        for p in range(4):
            add([(kc * 512, 512,
                  self.dwin[kc * 128:(kc + 1) * 128, p * 512:(p + 1) * 512])
                 for kc in range(8)])

        def proj(d, outcols):
            for p in range((outcols + 511) // 512):
                pc = min(512, outcols - p * 512)
                add([(kc * pc, pc,
                      d[kc * 128:(kc + 1) * 128, p * 512:p * 512 + pc])
                     for kc in range(8)])

        proj(self.dwq, KD)
        proj(self.dwk, KD)
        add([(kc * 8, 8, self.dwgkb[kc * 128:(kc + 1) * 128, :]) for kc in range(8)])
        proj(self.dwv, VD)
        proj(self.dwg, VD)
        for mhalf in range(2):
            for khalf in range(2):
                add([(kc6 * 512, 512,
                      self.dwso[(khalf * 6 + kc6) * 128:(khalf * 6 + kc6 + 1) * 128,
                                mhalf * 512:(mhalf + 1) * 512])
                     for kc6 in range(6)])
        for half in range(2):
            for khalf in range(2):
                add([(kc * 512, 512,
                      self.dwout[(khalf * 8 + kc) * 128:(khalf * 8 + kc + 1) * 128,
                                 half * 512:(half + 1) * 512])
                     for kc in range(8)])
        self.pieces = P
        self.wsched = [(g, pi) for g in range(NG) for pi in range(len(P))]

    def pump(self):
        if self.wnext >= len(self.wsched):
            return
        g, pi = self.wsched[self.wnext]
        slot = self.wnext % 2
        self.wnext += 1
        pc = self.pieces[pi]
        sem = f"dw{slot}"
        if g == 0:
            w = [("ct", self.slot_free[slot])]
            if self.wb_last_slot[slot] is not None:
                w.append(("dwb", self.wb_last_slot[slot]))
            for (coff, ncols, src) in pc["subs"]:
                self.dma("gpsimd", self.wslot[slot][:, coff:coff + ncols], src,
                         sem, waits=w)
                w = []
            ldv = self.now(sem)
            self.dma("sync", self.dwbf[pi, :, 0:pc["w"]],
                     self.wslot[slot][:, 0:pc["w"]], "dwb",
                     waits=[(sem, ldv), ("dwb", self.now("dwb"))])
            self.wb_val[pi] = self.now("dwb")
            self.wb_last_slot[slot] = self.wb_val[pi]
        else:
            sem = f"ds{slot}"
            wbw = max(self.wb_val[pi], self.wb_last_slot[slot] or 0)
            self.dma("sync", self.wslot[slot][:, 0:pc["w"]],
                     self.dwbf[pi, :, 0:pc["w"]], sem,
                     waits=[("ct", self.slot_free[slot]), ("dwb", wbw)])
            ldv = self.now(sem)
        self.wrecs.append(dict(slot=slot, ld=(sem, ldv)))

    def use(self):
        r = self.wrecs[self.wuse]
        self.wuse += 1
        return r

    def release(self, rec, ctval):
        self.slot_free[rec["slot"]] = ctval
        self.pump()

    def consts(self):
        nc = self.nc
        self.i_idf32 = nc.inline_tensor(np.eye(128, dtype=np.float32), "c_idf32")
        self.i_idbf = nc.inline_tensor(np.eye(128, dtype=np.float32).astype(BF), "c_idbf")
        self.i_i32rep = nc.inline_tensor(
            np.tile(np.eye(32, dtype=np.float32), (4, 1)).astype(BF), "c_i32rep")
        self.i_maskSN = nc.inline_tensor(
            -np.triu(np.ones((128, 128), dtype=np.float32), 1), "c_maskSN")
        self.i_maskI = nc.inline_tensor(
            np.triu(np.ones((128, 128), dtype=np.float32)), "c_maskI")
        self.i_i4bf = nc.inline_tensor(
            np.tile(np.eye(128, dtype=np.float32), (1, 4)).astype(BF), "c_i4bf")
        sel = np.zeros((4, H * 128), np.float32)
        for h in range(H):
            sel[h, h * 128:(h + 1) * 128] = 1.0
        self.i_sel4 = nc.inline_tensor(sel, "c_sel4")
        self.i_ones1f = nc.inline_tensor(np.ones((1, 128), np.float32), "c_ones1f")
        self.i_ones1b = nc.inline_tensor(np.ones((1, 128), np.float32).astype(BF),
                                         "c_ones1b")
        self.i_pow2 = nc.inline_tensor(np.full((128, 1), -8388608, np.int32), "c_pow2")
        self.i_sgn = nc.inline_tensor(np.full((128, 1), -2147483648, np.int64)
                                      .astype(np.int32), "c_sgn")
        self.i_abs = nc.inline_tensor(np.full((128, 1), 0x7FFFFFFF, np.int32), "c_abs")

    def alloc(self):
        nc, es = self.nc, self.es
        sb = lambda n, s, d: es.enter_context(nc.sbuf_tensor(n, s, d))
        ps = lambda n: es.enter_context(nc.psum_tensor(n, [128, 512], F32))
        self.c_idf32 = sb("s_idf32", [128, 128], F32)
        self.c_idbf = sb("s_idbf", [128, 128], BF16)
        self.c_i32rep = sb("s_i32rep", [128, 32], BF16)
        self.c_maskSN = sb("s_maskSN", [128, 128], F32)
        self.c_maskI = sb("s_maskI", [128, 128], F32)
        self.c_i4bf = sb("s_i4bf", [128, 512], BF16)
        self.c_sel4 = sb("s_sel4", [4, H * 128], F32)
        self.c_ones1f = sb("s_ones1f", [1, 128], F32)
        self.c_ones1b = sb("s_ones1b", [1, 128], BF16)
        self.c_pow2 = sb("s_pow2", [128, 1], I32)
        self.c_sgn = sb("s_sgn", [128, 1], I32)
        self.c_abs = sb("s_abs", [128, 1], I32)
        self.c_idf32r = sb("s_idf32r", [128, 128], SOLVE_DT)
        self.d_convdiag = sb("d_convdiag", [128, 16 * 4 * 32], BF16)
        self.d_cb = sb("d_cb", [128, 16], F32)
        self.d_bb = sb("d_bb", [128, 4], F32)
        self.d_dtb = sb("d_dtb", [128, 4], F32)
        self.d_negAexp = sb("d_negAexp", [128, 4], F32)
        self.d_gnw = sb("d_gnw", [128, DV], F32)
        self.d_fbias = sb("d_fbias", [1, D], BF16)
        self.d_dactp = sb("d_dactp", [128, 1], F32)
        self.d_small = sb("d_small", [1, 64], F32)
        self.wslot = [sb(f"wslot{i}", [128, 4096], BF16) for i in range(2)]
        self.wqslot = [sb(f"wqslot{i}", [128, 4096], BF16) for i in range(2)]
        self.a_x = [sb(f"a_x{i}", [128, D], F32) for i in range(2)]
        self.a_xn = sb("a_xn", [128, 8 * 520], BF16)
        self.a_big = sb("a_big", [128, 8320], BF16)
        self.a_xs = sb("a_xs", [128, 6144], BF16)
        self.a_zs = sb("a_zs", [128, 4096], BF16)
        self.a_q = sb("a_q", [128, NTAU * KD], BF16)
        self.a_k = sb("a_k", [128, NTAU * KD], BF16)
        self.a_v = sb("a_v", [128, NTAU * VD], BF16)
        self.a_gs = sb("a_gs", [128, NTAU * VD], BF16)
        self.st1 = sb("st1", [128, SEQ * H * DV], BF16)
        self.st2 = sb("st2", [64, SEQ * H * DV], BF16)
        self.w_pre = [sb(f"w_pre_{s}", [128, 128], F32) for s in range(SEQ)]
        self.w_t1 = [sb(f"w_t1_{s}", [128, 128], F32) for s in range(SEQ)]
        self.w_sv = [sb(f"w_sv_{s}", [128, H * 384], SOLVE_DT) for s in range(SEQ)]
        self.w_DkT = [sb(f"w_DkT_{s}", [128, 512], SOLVE_DT) for s in range(SEQ)]
        self.w_Zh = [sb(f"w_Zh_{s}", [128, DV], SOLVE_DT) for s in range(SEQ)]
        self.w_Pb = [sb(f"w_Pb_{s}", [128, 512], BF16) for s in range(SEQ)]
        self.w_qfm = [sb(f"w_qfm_{s}", [128, H * 256], BF16) for s in range(SEQ)]
        self.w_kfm = [sb(f"w_kfm_{s}", [128, H * 256], BF16) for s in range(SEQ)]
        self.w_zw = sb("w_zw", [128, 6144], BF16)
        self.w_Z = [self.w_zw[:, s * (H * DV):(s + 1) * (H * DV)] for s in range(SEQ)]
        self.w_W = [self.w_zw[:, 3072 + s * (H * DV):3072 + (s + 1) * (H * DV)]
                    for s in range(SEQ)]
        self.r_q = self.w_zw[:, 0:NTAU * KD]
        self.r_k = self.w_zw[:, 3072:3072 + NTAU * KD]
        self.w_O = [sb(f"w_O_{s}", [128, DV], F32) for s in range(SEQ)]
        self.w_osq = self.w_Zh
        self.w_ksc = [sb(f"w_ksc_{s}", [128, DK], BF16) for s in range(SEQ)]
        self.w_gI = [sb(f"w_gI_{s}", [128, 128], BF16) for s in range(SEQ)]
        self.b_beta = sb("b_beta", [128, NTAU * H], F32)
        self.b_G = sb("b_G", [128, NTAU * H], F32)
        self.b_Lam = sb("b_Lam", [128, NTAU * H], F32)
        self.b_negLam = sb("b_negLam", [128, NTAU * H], F32)
        self.b_bias = sb("b_bias", [128, NTAU * H], F32)
        self.b_Growt = sb("b_Growt", [4, NTAU * 128], F32)
        self.b_lamb4 = [sb(f"b_lamb4_{s}", [128, H], F32) for s in range(SEQ)]
        self.b_gam4 = [sb(f"b_gam4_{s}", [128, H], F32) for s in range(SEQ)]
        self.b_rs = sb("b_rs", [128, H], F32)
        self.b_ssq = [sb(f"b_ssq_{s}", [128, 2], F32) for s in range(SEQ)]
        self.b_stat = sb("b_stat", [128, 16], F32)
        self.r_gkb = sb("r_gkb", [128, NTAU * 8], F32)
        self.p2t = sb("p2t", [128, 24], F32)
        self.carry = sb("carry", [128, 2 * 16], BF16)
        self.qt0p = [sb(f"qt0_{i}", [128, D], F32) for i in range(2)]
        self.qsp = [[sb(f"qs_{i}", [128, 32], F32), sb(f"qsi_{i}", [128, 32], F32),
                     sb(f"qm_{i}", [128, 32], I32), sb(f"qw_{i}", [128, 32], F32)]
                    for i in range(2)]
        self.r_sq = self.qt0p[0][:, 0:KD]
        xsf = self.a_xs.ap().bitcast(F32)
        self.qt1 = xsf[:, 0:D]
        self.qt2 = xsf[:, D:2 * D]
        wzf = self.w_zw.ap().bitcast(F32)
        self.qt1b = wzf[:, 0:D]
        self.qt2b = wzf[:, D:2 * D]
        self.ln1sq = wzf[:, 2 * D:3 * D]
        self.a_xnt_p = [self.a_xs[:, 4096:4096 + D],
                        self.a_xs[:, 5120:5120 + D]]
        self.a_xq = self.a_gs
        bigf = self.a_big.ap().bitcast(F32)
        self.x2 = bigf[:, 0:NTAU * D]
        self.fb_stage = bigf[0:1, 0:D]
        self.d_cw = bigf[:, 3200:3264]
        aqf = self.a_q.ap().bitcast(F32)
        self.fq = self.a_q[:, 0:D]
        self.fdw = aqf[:, 1024:1056]
        self.stgap = self.a_q
        self.stg = 1024
        self.pb = [ps(f"pb{i}") for i in range(8)]
        self.out_store = [0, 0]
        self.sems = {}
        for sname in ("dx", "dw0", "dw1", "ds0", "ds1", "dwb", "dq0", "dq1",
                      "dmisc", "dst0", "dst1", "ct", "cv", "cs", "cg"):
            self.sems[sname] = es.enter_context(nc.semaphore(sname))
        self.slot_free = [0, 0]
        self.wb_val = [None] * len(self.pieces)
        self.wb_last_slot = [None, None]
        self.wrecs = []
        self.wuse = 0
        self.wnext = 0
        self.wq_free = 0
        self.ld_wq = [None, None]
        self.ax_free = [("cv", 0)] * 4
        self.x2_ready = [("cv", 0)] * NTAU
        self.x2_store = [(f"dst{t % 2}", 0) for t in range(NTAU)]
        self.qscr_free = [("cv", 0), ("cv", 0)]
        self.xnt_free = [("ct", 0), ("ct", 0)]
        self.qdmisc = 0
        self.gs_cs = 0
        self.conv_cs = 0
        self.p5_vq = [0] * NTAU

    def prep(self):
        for dst, src in ((self.c_idf32, self.i_idf32), (self.c_idbf, self.i_idbf),
                         (self.c_i32rep, self.i_i32rep),
                         (self.c_maskSN, self.i_maskSN), (self.c_maskI, self.i_maskI),
                         (self.c_i4bf, self.i_i4bf), (self.c_sel4, self.i_sel4),
                         (self.c_ones1f, self.i_ones1f), (self.c_ones1b, self.i_ones1b),
                         (self.c_pow2, self.i_pow2), (self.c_sgn, self.i_sgn),
                         (self.c_abs, self.i_abs)):
            self.dma("sync", dst[:], src.ap(), "dmisc")
        self.dma("sync", self.d_small[0:1, 0:16], self.dsmallin, "dmisc")
        self.dma("sync", self.d_cw.rearrange("p (c j) -> p c j", j=4),
                 self.dconvw.rearrange("(c p) j -> p c j", p=128), "dmisc", slow=True)
        self.dma("sync", self.d_cb[:].rearrange("p (c o) -> p c o", o=1),
                 self.dconvb.rearrange("o (c p) -> p c o", p=128), "dmisc", slow=True)
        self.dma("sync", self.d_gnw[0:1, :], self.dgnw, "dmisc")
        self.dma("sync", self.fb_stage, self.dfb, "dmisc")
        WM = [("dmisc", self.now("dmisc"))]
        pb = self.pb

        def bcast(bank_i, dst_ap, src_row, n, extraw=()):
            t = self.T(lambda e, b=pb[bank_i], s=src_row, n2=n: e.matmul(
                b[:, 0:n2], self.c_ones1f[:, :], s, start=True, stop=True),
                waits=WM + list(extraw) + self.bank(bank_i))
            v = self.V(lambda e, b=pb[bank_i], d2=dst_ap, n2=n: e.tensor_copy(
                d2, b[:, 0:n2]), waits=[("ct", t)])
            self.bank_done(bank_i, "cv", v)
            return v

        bcast(0, self.d_bb[:, :], self.d_small[0:1, 0:4], 4)
        bcast(1, self.d_dtb[:, :], self.d_small[0:1, 8:12], 4)
        sA = self.S(lambda e: e.activation(self.d_small[0:1, 16:20],
                                           self.d_small[0:1, 4:8], AF.Exp), waits=WM)
        tA = self.T(lambda e: e.matmul(pb[2][:, 0:4], self.c_ones1f[:, :],
                                       self.d_small[0:1, 16:20], start=True, stop=True),
                    waits=[("cs", sA)] + self.bank(2))
        vA = self.V(lambda e: e.tensor_scalar(self.d_negAexp[:, :], pb[2][:, 0:4],
                                              -1.0, None, ALU.mult), waits=[("ct", tA)])
        self.bank_done(2, "cv", vA)
        bcast(3, self.d_gnw[:, :], self.d_gnw[0:1, :], DV)
        self.S(lambda e: e.copy(self.c_idf32r[:, :], self.c_idf32[:, :]), waits=WM)
        self.S(lambda e: e.copy(self.d_fbias[:, :], self.fb_stage), waits=WM)
        sD = self.S(lambda e: e.activation(self.d_small[0:1, 20:21],
                                           self.d_small[0:1, 12:13], AF.Tanh,
                                           scale=np.float32(4.0)), waits=WM)
        sD = self.S(lambda e: e.activation(self.d_small[0:1, 21:22],
                                           self.d_small[0:1, 20:21], AF.Exp,
                                           scale=np.float32(0.5 * LN2C)), waits=[("cs", sD)])
        tD = self.T(lambda e: e.matmul(pb[4][:, 0:1], self.c_ones1f[:, :],
                                       self.d_small[0:1, 21:22], start=True, stop=True),
                    waits=[("cs", sD)] + self.bank(4))
        vD = self.V(lambda e: e.tensor_copy(self.d_dactp[:, :], pb[4][:, 0:1]),
                    waits=[("ct", tD)])
        self.bank_done(4, "cv", vD)
        for ch in range(16):
            for j in range(4):
                off = (ch * 4 + j) * 32
                self.S(lambda e, o=off, c2=ch, j2=j: e.activation(
                    self.d_convdiag[:, o:o + 32], self.c_i32rep[:, :], AF.Identity,
                    scale=self.d_cw[:, c2 * 4 + j2:c2 * 4 + j2 + 1]), waits=WM)
        self.G(lambda e: e.memset(self.st1[:, :], 0.0))
        self.G(lambda e: e.memset(self.st2[:, :], 0.0))
        self.G(lambda e: e.memset(self.carry[:, :], 0.0))

    def quant_w_chunk(self, kc):
        par = kc % 2
        qt0 = self.qt0p[par]
        qs, qsi, qm, qw = self.qsp[par]
        qt1, qt2 = (self.qt1, self.qt2) if par == 0 else (self.qt1b, self.qt2b)
        self.dma("sync", qt0[:], self.dfw[kc * 128:(kc + 1) * 128, :], "dmisc",
                 waits=self.snap("cv", "cs"))
        self.dma("sync", self.fdw, self.dfdw[kc * 128:(kc + 1) * 128, :], "dmisc")
        WD = [("dmisc", self.now("dmisc")), ("cv", self.now("cv")),
              ("cs", self.now("cs"))]
        v = self.V(lambda e: e.reduce_max(qs[:].unsqueeze(-1),
                                          qt0[:].rearrange("p (b k) -> p b k", k=32),
                                          axis=AX.X, apply_absolute_value=True),
                   waits=WD)
        v = self.V(lambda e: e.tensor_scalar(qsi[:], qs[:], 1e-30, None, ALU.max),
                   waits=[("cv", v)])
        v = self.V(lambda e: e.tensor_scalar(qsi[:].bitcast(I32), qsi[:].bitcast(I32),
                                             self.c_pow2[:, 0:1], None, ALU.bitwise_and),
                   waits=[("cv", v)])
        v = self.V(lambda e: e.tensor_scalar(qs[:], qsi[:], 0.25, None, ALU.mult),
                   waits=[("cv", v)])
        s = self.S(lambda e: e.activation(qsi[:], self.fdw, AF.Tanh, scale=np.float32(4.0)),
                   waits=WD + [("cv", v)])
        s = self.S(lambda e: e.activation(qsi[:], qsi[:], AF.Exp, scale=np.float32(0.5 * LN2C)),
                   waits=[("cs", s)])
        v = self.V(lambda e: e.tensor_tensor(qs[:], qs[:], qsi[:], ALU.mult),
                   waits=[("cs", s), ("cv", v)])
        v = self.V(lambda e: e.reciprocal(qsi[:], qs[:]), waits=[("cv", v)])
        vq = self.emit_quant_core(qt0[:], qt1, qt2, qs[:], qsi[:],
                                  self.fq[:, 0:D], [("cv", v)])
        self.qscr_free[par] = ("cv", vq)
        stg = self.stg
        for tt in range(8):
            bi = 5 + (tt % 2)
            t = self.T(lambda e, t2=tt, b=self.pb[bi]: e.transpose(
                b.ap().bitcast(BF16)[0:128, 0:128],
                self.fq[:, t2 * 128:(t2 + 1) * 128], self.c_idbf[:, :]),
                waits=[("cv", vq)] + self.bank(bi)
                + ([("dmisc", self.now("dmisc"))] if tt == 0 else []))
            vv = self.V(lambda e, t2=tt, b=self.pb[bi]: e.tensor_copy(
                self.stgap[:, stg + t2 * 128: stg + (t2 + 1) * 128],
                b.ap().bitcast(BF16)[0:128, 0:128]), waits=[("ct", t)])
            self.bank_done(bi, "cv", vv)
        for tt in range(8):
            self.dma("sync", self.dwqt[tt, :, kc * 128:(kc + 1) * 128],
                     self.stgap[:, stg + tt * 128: stg + (tt + 1) * 128], "dmisc",
                     waits=[("cv", self.now("cv"))] if tt == 0 else ())
        self.qdmisc = self.now("dmisc")

    def emit_quant_core(self, x_in, tmp_sgn, tmp_ph, scale, inv_scale, out_bf, waits):
        V = self.V
        r32 = lambda ap: ap.rearrange("p (b k) -> p b k", k=32)
        bc = lambda ap: ap.unsqueeze(-1).broadcast_to((128, 32, 32))
        v = V(lambda e: e.tensor_tensor(r32(x_in), r32(x_in), bc(inv_scale),
                                        ALU.mult), waits=waits)
        v = V(lambda e: e.tensor_scalar(tmp_sgn.bitcast(I32), x_in.bitcast(I32),
                                        self.c_sgn[:, 0:1], None, ALU.bitwise_and),
              waits=[("cv", v)])
        v = V(lambda e: e.tensor_scalar(x_in.bitcast(I32), x_in.bitcast(I32),
                                        self.c_abs[:, 0:1], None, ALU.bitwise_and),
              waits=[("cv", v)])
        v = V(lambda e: e.tensor_scalar(x_in, x_in, 6.0, 1e-30, ALU.min, ALU.max),
              waits=[("cv", v)])
        v = V(lambda e: e.tensor_scalar(tmp_ph.bitcast(I32), x_in.bitcast(I32),
                                        self.c_pow2[:, 0:1], None, ALU.bitwise_and),
              waits=[("cv", v)])
        v = V(lambda e: e.tensor_scalar(tmp_ph, tmp_ph, 1.0, 4.0, ALU.max, ALU.min),
              waits=[("cv", v)])
        v = V(lambda e: e.tensor_scalar(tmp_ph, tmp_ph, MAGIC * 0.5, None,
                                        ALU.mult), waits=[("cv", v)])
        v = V(lambda e: e.tensor_tensor(x_in, x_in, tmp_ph, ALU.add),
              waits=[("cv", v)])
        v = V(lambda e: e.tensor_tensor(x_in, x_in, tmp_ph, ALU.subtract),
              waits=[("cv", v)])
        v = V(lambda e: e.tensor_tensor(x_in.bitcast(I32), x_in.bitcast(I32),
                                        tmp_sgn.bitcast(I32), ALU.bitwise_or),
              waits=[("cv", v)])
        v = V(lambda e: e.tensor_tensor(r32(out_bf), r32(x_in), bc(scale),
                                        ALU.mult), waits=[("cv", v)])
        return v

    def ln1_tiles(self, g):
        last = (g == NG - 1)
        out = []
        cnt = 0
        for s in range(SEQ):
            for (toff, tn) in [(0, 128), (128, 128)] + ([] if last else [(256, 2)]):
                out.append((s, toff, tn, cnt % 2))
                cnt += 1
        return out

    def ln1_tile(self, g, s, toff, tn, bidx, w0=()):
        t0 = g * TPG
        buf = self.a_x[bidx]
        sc = 12 * bidx
        self.dma("sync", buf[0:tn, :],
                 self.dx[s, t0 + toff:t0 + toff + tn, :], "dx",
                 waits=[self.ax_free[bidx], self.ax_free[bidx + 2],
                        ("dx", self.now("dx"))])
        WX = [("dx", self.now("dx"))]
        v = self.V(lambda e, b=buf, n=tn: e.reduce_sum(
            self.b_stat[0:n, sc:sc + 1], b[0:n, :], axis=AX.X, negate=True),
            waits=WX + list(w0))
        v = self.V(lambda e, n=tn: e.tensor_scalar(
            self.b_stat[0:n, sc + 1:sc + 2], self.b_stat[0:n, sc:sc + 1],
            1.0 / D, None, ALU.mult), waits=[("cv", v)])
        s1 = self.S(lambda e, b=buf, n=tn: e.activation(
            b[0:n, :], b[0:n, :], AF.Identity, bias=self.b_stat[0:n, sc + 1:sc + 2]),
            waits=[("cv", v)] + list(w0))
        s1 = self.S(lambda e, b=buf, n=tn: e.activation(
            self.ln1sq[0:n, :], b[0:n, :], AF.Square,
            accum_out=self.b_stat[0:n, sc + 2:sc + 3]), waits=[("cs", s1)])
        s1 = self.S(lambda e, n=tn: e.activation(
            self.b_stat[0:n, sc + 3:sc + 4], self.b_stat[0:n, sc + 2:sc + 3], AF.Ln,
            scale=np.float32(1.0 / D), bias=np.float32(EPS_LN)), waits=[("cs", s1)])
        s1 = self.S(lambda e, n=tn: e.activation(
            self.b_stat[0:n, sc + 3:sc + 4], self.b_stat[0:n, sc + 3:sc + 4],
            AF.Exp, scale=np.float32(-0.5)), waits=[("cs", s1)])
        xnt = self.a_xnt_p[bidx]
        s1 = self.S(lambda e, b=buf, n=tn: e.activation(
            xnt[0:n, :], b[0:n, :], AF.Identity,
            scale=self.b_stat[0:n, sc + 3:sc + 4]),
            waits=[("cs", s1), self.xnt_free[bidx]])
        self.ax_free[bidx] = ("cs", s1)
        self.ax_free[bidx + 2] = ("cs", s1)
        t = None
        for kc in range(8):
            bi = 6 + (kc % 2)
            t = self.T(lambda e, k2=kc, b=self.pb[bi], n=tn: e.transpose(
                b.ap().bitcast(BF16)[0:128, 0:n],
                xnt[0:n, k2 * 128:(k2 + 1) * 128],
                self.c_idbf[0:n, 0:n]),
                waits=[("cs", s1)] + self.bank(bi))
            vv = self.V(lambda e, k2=kc, b=self.pb[bi], n=tn, s2=s, to=toff:
                        e.tensor_copy(
                self.a_xn[:, k2 * 520 + s2 * 260 + to:
                          k2 * 520 + s2 * 260 + to + n],
                b.ap().bitcast(BF16)[0:128, 0:n]), waits=[("ct", t)])
            self.bank_done(bi, "cv", vv)
        self.xnt_free[bidx] = ("ct", t)

    def emit_ln1(self, g):
        W0 = self.all_compute() + [("dmisc", self.now("dmisc")),
                                   ("dst0", self.now("dst0")),
                                   ("dst1", self.now("dst1"))]
        for i, (s, toff, tn, bidx) in enumerate(self.ln1_tiles(g)):
            self.ln1_tile(g, s, toff, tn, bidx, w0=W0 if i == 0 else ())

    def group(self, g):
        t0 = g * TPG
        last = (g == NG - 1)
        ncol = TPG if last else TPG + 2
        pb = self.pb

        xn_ready = self.snap("cv")
        g0_extra = ([("dmisc", self.now("dmisc"))] if g == 0 else []) + \
            [("dst0", self.now("dst0")), ("dst1", self.now("dst1"))]
        ev_xz = None
        for piece in range(4):
            rec = self.use()
            slot = rec["slot"]
            for dsub in range(4):
                dchunk = piece * 4 + dsub
                for s in range(SEQ):
                    bi = (dsub * 2 + s) % 4
                    mm = None
                    for kc in range(8):
                        lhs = self.wslot[slot][:, kc * 512 + dsub * 128:
                                               kc * 512 + dsub * 128 + 128]
                        rhs = self.a_xn[:, kc * 520 + s * 260:
                                        kc * 520 + s * 260 + ncol]
                        w = ([rec["ld"]] + xn_ready + self.bank(bi)) if kc == 0 else []
                        mm = self.T(lambda e, l=lhs, r=rhs, b=pb[bi], k2=kc, n=ncol:
                                    e.matmul(b[:, 0:n], l, r, start=(k2 == 0),
                                             stop=(k2 == 7)), waits=w)
                    ev_xz = self.S(lambda e, b=pb[bi], d2=dchunk, s2=s, n=ncol: e.copy(
                        self.a_big[:, d2 * 520 + s2 * 260 + 1:
                                   d2 * 520 + s2 * 260 + 1 + n], b[:, 0:n]),
                        waits=[("ct", mm)] + (g0_extra if dchunk == 0 else []))
                    self.bank_done(bi, "cs", ev_xz)
            self.release(rec, mm)
            if g == 0:
                self.quant_w_chunk(2 * piece)
                self.quant_w_chunk(2 * piece + 1)
        cw = [("cs", ev_xz), ("cg", self.now("cg"))]
        big3 = self.a_big[:].rearrange("p (ch q) -> p ch q", q=520)
        for s in range(SEQ):
            self.V(lambda e, s2=s: e.tensor_copy(
                big3[:, :, s2 * 260: s2 * 260 + 1],
                self.carry[:, s2 * 16: s2 * 16 + 16].rearrange("p (c o) -> p c o", o=1)),
                waits=cw)
            if last:
                self.V(lambda e, s2=s: e.memset(
                    big3[:, :, s2 * 260 + 257: s2 * 260 + 260], 0.0), waits=cw)
        xz_ready_cv = self.now("cv")

        for dchunk in range(16):
            half = dchunk // 8
            ch = dchunk % 8
            dst = self.a_xs if half == 0 else self.a_zs
            for s in range(SEQ):
                bi = 4 + (dchunk * 2 + s) % 2
                mm = None
                for blk in range(4):
                    for j in range(4):
                        diag = self.d_convdiag[32 * blk:32 * blk + 32,
                                               (dchunk * 4 + j) * 32:
                                               (dchunk * 4 + j) * 32 + 32]
                        rhs = self.a_big[32 * blk:32 * blk + 32,
                                         dchunk * 520 + s * 260 + j:
                                         dchunk * 520 + s * 260 + j + TPG]
                        w = ([("cv", xz_ready_cv)] + self.bank(bi)) \
                            if (j == 0 and blk == 0) else []
                        mm = self.T(lambda e, d2=diag, r=rhs, b=pb[bi], j2=j, bl=blk:
                                    e.matmul(b[32 * bl:32 * bl + 32, 0:TPG], d2, r,
                                             start=(j2 == 0), stop=(j2 == 3),
                                             tile_position=(32 * bl, 32 * bl)),
                                    waits=w)
                sc = self.S(lambda e, b=pb[bi], c2=ch, s2=s, h2=half, d2=dst:
                            e.activation(
                    d2[:, c2 * 512 + s2 * 256: c2 * 512 + s2 * 256 + TPG],
                    b[:, 0:TPG], AF.Silu,
                    bias=self.d_cb[:, h2 * 8 + c2: h2 * 8 + c2 + 1]),
                    waits=[("ct", mm)]
                    + ([self.qscr_free[0], self.qscr_free[1]]
                       if (g == 0 and dchunk == 0 and s == 0) else []))
                self.bank_done(bi, "cs", sc)
        conv_ct = self.now("ct")
        self.conv_cs = self.now("cs")
        if not last:
            for s in range(SEQ):
                self.V(lambda e, s2=s: e.tensor_copy(
                    self.carry[:, s2 * 16: s2 * 16 + 16]
                    .rearrange("p (c o) -> p c o", o=1),
                    big3[:, :, s2 * 260 + 256: s2 * 260 + 257]),
                    waits=[("ct", conv_ct)])

        if STOP_PHASE == "p1":
            return
        self.p2(g)

        if STOP_PHASE == "p2":
            return
        for c in range(CPG):
            gens = [self.scan_chunk_gen(g, s, c) for s in range(SEQ)]
            done = [False] * SEQ
            while not all(done):
                for i in range(SEQ):
                    if not done[i]:
                        try:
                            next(gens[i])
                        except StopIteration:
                            done[i] = True
        p3_cv, p3_cs = self.now("cv"), self.now("cs")

        if STOP_PHASE == "p3":
            return
        for tau in range(NTAU):
            s, c = divmod(tau, CPG)
            for kk in range(12):
                bi = 6 + (kk % 2)
                t = self.T(lambda e, t2=tau, k2=kk, b=pb[bi]: e.transpose(
                    b.ap().bitcast(BF16)[0:128, 0:128],
                    self.a_big[:, t2 * VD + k2 * 128: t2 * VD + (k2 + 1) * 128],
                    self.c_idbf[:, :]),
                    waits=(([("cv", p3_cv), ("cs", p3_cs)] if (tau == 0 and kk == 0)
                            else []) + self.bank(bi)))
                vv = self.V(lambda e, k2=kk, b=pb[bi], s2=s, c2=c: e.tensor_copy(
                    self.a_xs[:, k2 * 512 + s2 * 256 + c2 * 128:
                              k2 * 512 + s2 * 256 + c2 * 128 + 128],
                    b.ap().bitcast(BF16)[0:128, 0:128]), waits=[("ct", t)])
                self.bank_done(bi, "cv", vv)
        p4a_cv = self.now("cv")

        p4b_cs = None
        for mhalf in range(2):
            recs = [self.use(), self.use()]
            for msub in range(4):
                m = mhalf * 4 + msub
                bi = msub % 4
                mm = None
                for kc in range(12):
                    khalf, kc6 = divmod(kc, 6)
                    slot = recs[khalf]["slot"]
                    w = []
                    if kc == 0:
                        w = [recs[0]["ld"], ("cv", p4a_cv)] + self.bank(bi)
                    elif kc6 == 0:
                        w = [recs[khalf]["ld"]]
                    mm = self.T(lambda e, k6=kc6, sl=slot, m2=msub, b=pb[bi], k2=kc:
                                e.matmul(
                        b[:, 0:512],
                        self.wslot[sl][:, k6 * 512 + m2 * 128:
                                       k6 * 512 + (m2 + 1) * 128],
                        self.a_xs[:, k2 * 512: k2 * 512 + 512],
                        start=(k2 == 0), stop=(k2 == 11)), waits=w)
                p4b_cs = self.S(lambda e, m2=m, b=pb[bi]: e.activation(
                    self.a_v[:, m2 * 512: m2 * 512 + 512], b[:, 0:512], AF.Copy),
                    waits=[("ct", mm)])
                self.bank_done(bi, "cs", p4b_cs)
            self.release(recs[0], mm)
            self.release(recs[1], mm)

        for qh in range(2):
            w = [("ct", self.wq_free)]
            if g == 0:
                w.append(("dmisc", self.now("dmisc")))
            for tt in range(4):
                self.dma("sync", self.wqslot[qh][:, tt * 1024:(tt + 1) * 1024],
                         self.dwqt[qh * 4 + tt, :, :], f"dq{qh}", waits=w)
                w = []
            self.ld_wq[qh] = (f"dq{qh}", self.now(f"dq{qh}"))

        for half in range(2):
            recs = [self.use(), self.use()]
            for tau in range(NTAU):
                s, c = divmod(tau, CPG)
                xq = self.a_x[tau % 2][:, (tau // 2) * 512:(tau // 2) * 512 + 512]
                self.dma("sync", xq,
                         self.dx[s, t0 + c * 128: t0 + c * 128 + 128,
                                 half * 512:(half + 1) * 512],
                         "dx", waits=[self.ax_free[tau], ("dx", self.now("dx"))])
                WX = [("dx", self.now("dx"))]
                bi = 4 + (tau % 2)
                mm = None
                for khalf in range(2):
                    src = self.a_v if khalf == 0 else self.a_zs
                    sl = recs[khalf]["slot"]
                    for kc in range(8):
                        lhs = src[:, kc * 512 + s * 256 + c * 128:
                                  kc * 512 + s * 256 + c * 128 + 128]
                        w = ([recs[khalf]["ld"]] +
                             ([("cs", p4b_cs)] + self.bank(bi) if khalf == 0 else [])) \
                            if kc == 0 else []
                        mm = self.T(lambda e, l=lhs, b=pb[bi], k2=kc, kh=khalf, s2=sl:
                                    e.matmul(
                            b[:, 0:512], l,
                            self.wslot[s2][:, k2 * 512:(k2 + 1) * 512],
                            start=(kh == 0 and k2 == 0), stop=(kh == 1 and k2 == 7)),
                            waits=w)
                vv = self.V(lambda e, b=pb[bi], t2=tau, h2=half, bx=xq:
                            e.tensor_tensor(
                    self.x2[:, t2 * D + h2 * 512: t2 * D + h2 * 512 + 512],
                    bx, b[:, 0:512], ALU.add),
                    waits=[("ct", mm)] + WX
                    + ([self.x2_store[tau]] if half == 0 else []))
                self.bank_done(bi, "cv", vv)
                self.ax_free[tau] = ("cv", vv)
                if half == 1:
                    self.x2_ready[tau] = ("cv", vv)
            self.release(recs[0], mm)
            self.release(recs[1], mm)

        if STOP_PHASE == "p4":
            return
        tiles = self.ln1_tiles(g + 1) if not last else []
        ti = 0
        self.p5_ln2(g, 0)
        for tau in range(NTAU):
            if tau + 1 < NTAU:
                self.p5_ln2(g, tau + 1)
            self.p5_ffn(g, tau)
            if ti < len(tiles):
                self.ln1_tile(g + 1, *tiles[ti])
                ti += 1
        while ti < len(tiles):
            self.ln1_tile(g + 1, *tiles[ti])
            ti += 1

    def p2(self, g):
        pb = self.pb

        def xs_lhs(kc, tau):
            s, c = divmod(tau, CPG)
            return self.a_xs[:, kc * 512 + s * 256 + c * 128:
                             kc * 512 + s * 256 + c * 128 + 128]

        first_mm = [True]

        def project(dten, outcols, evac_fn):
            npieces = (outcols + 511) // 512
            for piece in range(npieces):
                pcols = min(512, outcols - piece * 512)
                rec = self.use()
                slot = rec["slot"]
                for tau in range(NTAU):
                    bi = tau % 4
                    mm = None
                    for kc in range(8):
                        w = ([rec["ld"]] + self.bank(bi)) if kc == 0 else []
                        if w and first_mm[0]:
                            w.append(("cs", self.conv_cs))
                            first_mm[0] = False
                        mm = self.T(lambda e, k2=kc, b=pb[bi], t2=tau, sl=slot,
                                    pc=pcols: e.matmul(
                            b[:, 0:pc], xs_lhs(k2, t2),
                            self.wslot[sl][:, k2 * pc: k2 * pc + pc],
                            start=(k2 == 0), stop=(k2 == 7)), waits=w)
                    sem, ev = evac_fn(piece, pcols, tau, bi, mm)
                    self.bank_done(bi, sem, ev)
                self.release(rec, mm)

        def mk_evac(dst, width, func):
            def f(piece, pcols, tau, bi, mm):
                e1 = self.S(lambda e, b=pb[bi], t2=tau, p2=piece, pc=pcols:
                            e.activation(
                    dst[:, t2 * width + p2 * 512: t2 * width + p2 * 512 + pc],
                    b[:, 0:pc], func), waits=[("ct", mm)])
                return ("cs", e1)
            return f

        qevac_extra = [self.qscr_free[1]]

        def mk_evac_qk(dst, width):
            def f(piece, pcols, tau, bi, mm):
                w = [("ct", mm)] + qevac_extra[:]
                qevac_extra.clear()
                e1 = self.S(lambda e, b=pb[bi], t2=tau, p2=piece, pc=pcols:
                            e.activation(
                    dst[:, t2 * width + p2 * 512: t2 * width + p2 * 512 + pc],
                    b[:, 0:pc], AF.Copy), waits=w)
                return ("cs", e1)
            return f

        project(self.dwq, KD, mk_evac_qk(self.r_q, KD))
        project(self.dwk, KD, mk_evac_qk(self.r_k, KD))

        def gkb_evac(piece, pcols, tau, bi, mm):
            v = self.V(lambda e, b=pb[bi], t2=tau: e.tensor_copy(
                self.r_gkb[:, t2 * 8: t2 * 8 + 8], b[:, 0:8]), waits=[("ct", mm)])
            return ("cv", v)

        project(self.dwgkb, 8, gkb_evac)

        p2t = self.p2t
        for tau in range(NTAU):
            v = self.V(lambda e, t2=tau: e.tensor_tensor(
                p2t[:, 0:4], self.r_gkb[:, t2 * 8 + 4: t2 * 8 + 8],
                self.d_bb[:, :], ALU.add), waits=self.snap("cv"))
            s = self.S(lambda e: e.activation(p2t[:, 4:8], p2t[:, 0:4],
                                              AF.Exp, scale=np.float32(-1.0)),
                       waits=[("cv", v)] + self.snap("cs"))
            v = self.V(lambda e: e.tensor_scalar(p2t[:, 4:8], p2t[:, 4:8],
                                                 1.0, None, ALU.add), waits=[("cs", s)])
            v = self.V(lambda e, t2=tau: e.reciprocal(
                self.b_beta[:, t2 * H: t2 * H + H], p2t[:, 4:8]), waits=[("cv", v)])
            v2 = self.V(lambda e, t2=tau: e.tensor_tensor(
                p2t[:, 8:12], self.r_gkb[:, t2 * 8: t2 * 8 + 4],
                self.d_dtb[:, :], ALU.add), waits=[("cv", v)])
            s = self.S(lambda e: e.activation(p2t[:, 12:16], p2t[:, 8:12], AF.Abs),
                       waits=[("cv", v2)])
            s = self.S(lambda e: e.activation(p2t[:, 12:16], p2t[:, 12:16],
                                              AF.Exp, scale=np.float32(-1.0)), waits=[("cs", s)])
            s = self.S(lambda e: e.activation(p2t[:, 12:16], p2t[:, 12:16],
                                              AF.Ln, bias=np.float32(1.0)), waits=[("cs", s)])
            s = self.S(lambda e: e.activation(p2t[:, 16:20], p2t[:, 8:12], AF.Relu),
                       waits=[("cs", s)])
            v2 = self.V(lambda e: e.tensor_tensor(p2t[:, 12:16], p2t[:, 12:16],
                                                  p2t[:, 16:20], ALU.add),
                        waits=[("cs", s)])
            v2 = self.V(lambda e: e.tensor_tensor(p2t[:, 12:16], p2t[:, 12:16],
                                                  self.d_negAexp[:, :], ALU.mult),
                        waits=[("cv", v2)])
            bi = 4 + (tau % 2)
            t = self.T(lambda e, b=pb[bi]: e.matmul(
                b[:, 0:4], self.c_maskI[:, :], p2t[:, 12:16], start=True, stop=True),
                waits=[("cv", v2)] + self.bank(bi))
            v3 = self.V(lambda e, b=pb[bi], t2=tau: e.tensor_copy(
                self.b_G[:, t2 * H: t2 * H + H], b[:, 0:4]), waits=[("ct", t)])
            t2i = self.T(lambda e, b=pb[bi], t2=tau: e.transpose(
                b[0:4, 128:256], self.b_G[:, t2 * H: t2 * H + H], self.c_idf32[:, :]),
                waits=[("cv", v3)])
            v3 = self.V(lambda e, b=pb[bi], t2=tau: e.tensor_copy(
                self.b_Growt[0:4, t2 * 128:(t2 + 1) * 128], b[0:4, 128:256]),
                waits=[("ct", t2i)])
            self.bank_done(bi, "cv", v3)
        bw = self.snap("cv")
        s = self.S(lambda e: e.activation(self.b_Lam[:, :], self.b_G[:, :], AF.Exp),
                   waits=bw + self.snap("cs"))
        self.V(lambda e: e.tensor_scalar(self.b_negLam[:, :], self.b_Lam[:, :],
                                         -1.0, None, ALU.mult), waits=[("cs", s)])
        s = self.S(lambda e: e.activation(self.b_bias[:, :], self.b_beta[:, :], AF.Ln),
                   waits=bw)
        self.V(lambda e: e.tensor_tensor(self.b_bias[:, :], self.b_bias[:, :],
                                         self.b_G[:, :], ALU.subtract),
               waits=[("cs", s)])
        first_l2 = [self.qscr_free[0]]
        if g == 0:
            first_l2.append(("dmisc", self.qdmisc))
        for (raw, dst) in ((self.r_q, self.a_q), (self.r_k, self.a_k)):
            for tau in range(NTAU):
                s = self.S(lambda e, r2=raw, t2=tau: e.activation(
                    self.r_sq, r2[:, t2 * KD:(t2 + 1) * KD], AF.Square),
                    waits=self.snap("cs") + first_l2[:])
                first_l2.clear()
                v = self.V(lambda e: e.reduce_sum(
                    self.b_rs[:, :].unsqueeze(-1),
                    self.r_sq.rearrange("p (h d) -> p h d", d=DK), axis=AX.X),
                    waits=[("cs", s)] + self.snap("cv"))
                s = self.S(lambda e: e.activation(self.b_rs[:, :], self.b_rs[:, :],
                                                  AF.Ln, bias=np.float32(EPS_L2)),
                           waits=[("cv", v)])
                s = self.S(lambda e: e.activation(self.b_rs[:, :], self.b_rs[:, :],
                                                  AF.Exp, scale=np.float32(-0.5)),
                           waits=[("cs", s)])
                for h in range(H):
                    s = self.S(lambda e, r2=raw, d2=dst, t2=tau, h2=h: e.activation(
                        d2[:, t2 * KD + h2 * DK: t2 * KD + (h2 + 1) * DK],
                        r2[:, t2 * KD + h2 * DK: t2 * KD + (h2 + 1) * DK],
                        AF.Identity, scale=self.b_rs[:, h2:h2 + 1]), waits=[("cs", s)])
        self.qk_ready_snap = self.snap("cs", "cv")
        project(self.dwv, VD, mk_evac(self.a_v, VD, AF.Copy))
        project(self.dwg, VD, mk_evac(self.a_gs, VD, AF.Silu))
        self.gs_cs = self.now("cs")

    def scan_chunk_gen(self, g, s, c):
        pb = self.pb
        tau = s * CPG + c
        wpre = self.w_pre[s]
        wt1 = self.w_t1[s]
        qk_ready = self.qk_ready_snap
        bA, bB = s * 2, s * 2 + 1
        bE = 4 + s
        bT = 6 + s

        for (src, dstfm) in ((self.a_q, self.w_qfm[s]), (self.a_k, self.w_kfm[s])):
            tl = None
            for h in range(H):
                for (p0, fn, oo) in ((0, 128, 0), (128, 64, 128)):
                    tl = self.T(lambda e, h2=h, p2=p0, f2=fn, o2=oo, b=pb[bT], s2=src:
                                e.transpose(
                        b.ap().bitcast(BF16)[0:f2, h2 * 256 + o2: h2 * 256 + o2 + 128],
                        s2[:, tau * KD + h2 * DK + p2: tau * KD + h2 * DK + p2 + f2],
                        self.c_idbf[:, :]),
                        waits=(qk_ready + self.bank(bT)) if (h == 0 and p0 == 0) else [])
            pv = pb[bT].ap().bitcast(BF16).rearrange("p (h q) -> p h q", q=256)
            dv = dstfm[:].rearrange("p (h q) -> p h q", q=256)
            self.V(lambda e, pv2=pv, dv2=dv: e.tensor_copy(
                dv2[0:128, :, 0:128], pv2[0:128, :, 0:128]), waits=[("ct", tl)])
            vv = self.V(lambda e, pv2=pv, dv2=dv: e.tensor_copy(
                dv2[0:64, :, 128:256], pv2[0:64, :, 128:256]))
            self.bank_done(bT, "cv", vv)
        kfm, qfm = self.w_kfm[s], self.w_qfm[s]
        fm_cv = self.now("cv")
        if STOP_PHASE == "sA":
            return
        yield

        def fmp(buf, h, piece):
            if piece == 0:
                return buf[:, h * 256: h * 256 + 128]
            return buf[0:64, h * 256 + 128: h * 256 + 256]

        mmA = mmB = None
        for h in range(H):
            w = ([("cv", fm_cv)] + self.bank(bA)) if h == 0 else []
            self.T(lambda e, h2=h, b=pb[bA]: e.matmul(
                b[:, h2 * 128:(h2 + 1) * 128], fmp(kfm, h2, 0), fmp(kfm, h2, 0),
                start=True, stop=False), waits=w)
            mmA = self.T(lambda e, h2=h, b=pb[bA]: e.matmul(
                b[:, h2 * 128:(h2 + 1) * 128], fmp(kfm, h2, 1), fmp(kfm, h2, 1),
                start=False, stop=True))
        for h in range(H):
            w = self.bank(bB) if h == 0 else []
            self.T(lambda e, h2=h, b=pb[bB]: e.matmul(
                b[:, h2 * 128:(h2 + 1) * 128], fmp(kfm, h2, 0), fmp(qfm, h2, 0),
                start=True, stop=False), waits=w)
            mmB = self.T(lambda e, h2=h, b=pb[bB]: e.matmul(
                b[:, h2 * 128:(h2 + 1) * 128], fmp(kfm, h2, 1), fmp(qfm, h2, 1),
                start=False, stop=True))

        if STOP_PHASE == "sB":
            return
        yield
        prev_readers = []
        for h in range(H):
            col = tau * H + h
            w = (self.bank(bE) if h == 0 else []) + prev_readers
            t = self.T(lambda e, h2=h, b=pb[bE], t2=tau: e.matmul(
                b[:, 0:128], self.c_sel4[:, h2 * 128:(h2 + 1) * 128],
                self.b_Growt[0:4, t2 * 128:(t2 + 1) * 128],
                start=True, stop=True), waits=w + ([("ct", mmB)] if h == 0 else []))
            v = self.V(lambda e, b=pb[bE], c2=col: e.tensor_scalar(
                wt1[:, 0:1], b[:, 127:128], self.b_G[:, c2:c2 + 1], None,
                ALU.subtract), waits=[("ct", t)] + self.snap("cv"))
            sx = self.S(lambda e, h2=h: e.activation(
                self.b_lamb4[s][:, h2:h2 + 1], wt1[:, 0:1], AF.Exp),
                waits=[("cv", v)] + self.snap("cs"))
            sx2 = self.S(lambda e, b=pb[bE], h2=h: e.activation(
                self.b_gam4[s][:, h2:h2 + 1], b[:, 127:128], AF.Exp),
                waits=[("ct", t)])
            v = self.V(lambda e, h2=h, c2=col: e.tensor_tensor(
                self.b_lamb4[s][:, h2:h2 + 1], self.b_lamb4[s][:, h2:h2 + 1],
                self.b_beta[:, c2:c2 + 1], ALU.mult), waits=[("cs", sx)])
            vpre = self.V(lambda e, b=pb[bE], c2=col: e.tensor_scalar(
                wpre[:, :], b[:, 0:128], self.b_bias[:, c2:c2 + 1], 0.0,
                ALU.add, ALU.min), waits=[("ct", t)] + self.snap("cv"))
            prev_readers = [("cv", vpre), ("cs", sx2)]
            se = self.S(lambda e: e.activation(wpre[:, :], wpre[:, :],
                                               AF.Exp), waits=[("cv", vpre), ("cs", sx2)])
            v = self.V(lambda e: e.tensor_tensor(wt1[:, :], wpre[:, :],
                                                 self.c_maskSN[:, :], ALU.mult),
                       waits=[("cs", se)])
            vD = self.V(lambda e, h2=h, b=pb[bA]: e.tensor_tensor(
                self.w_sv[s][:, h2 * 384: h2 * 384 + 128], wt1[:, :],
                b[:, h2 * 128:(h2 + 1) * 128], ALU.mult),
                waits=[("cv", v), ("ct", mmA)])
            self.V(lambda e, h2=h: e.tensor_copy(
                self.w_sv[s][:, h2 * 384 + 128: h2 * 384 + 256], self.c_idf32[:, :]))
            v = self.V(lambda e: e.tensor_tensor(wt1[:, :], wpre[:, :],
                                                 self.c_maskI[:, :], ALU.mult),
                       waits=[("cv", vD)])
            vP = self.V(lambda e, h2=h, b=pb[bB]: e.tensor_tensor(
                self.w_Pb[s][:, h2 * 128:(h2 + 1) * 128], wt1[:, :],
                b[:, h2 * 128:(h2 + 1) * 128], ALU.mult), waits=[("cv", v)])
            if h == H - 1:
                self.bank_done(bA, "cv", vD)
                self.bank_done(bB, "cv", vP)
                self.bank_done(bE, "cv", vP)
                self.bank_done(bE, "cs", sx2)
            yield
        d0_cv = self.now("cv")

        if STOP_PHASE == "sC":
            return
        d0_cv2 = self.now("cv")
        tl = None
        for h in range(H):
            tl = self.T(lambda e, h2=h, b=pb[bT]: e.transpose(
                b.ap().bitcast(SOLVE_DT)[0:128, h2 * 128:(h2 + 1) * 128],
                self.w_sv[s][:, h2 * 384: h2 * 384 + 128], self.c_idf32r[:, :]),
                waits=([("cv", d0_cv2)] + self.bank(bT)) if h == 0 else [])
        dkt_ev = self.V(lambda e, b=pb[bT]: e.tensor_copy(
            self.w_DkT[s][:, :], b.ap().bitcast(SOLVE_DT)[0:128, 0:512]),
            waits=[("ct", tl)])
        self.bank_done(bT, "cv", dkt_ev)
        x_ready = self.now("cv")
        if STOP_PHASE == "sD0":
            return
        yield
        for r in range(1, 8):
            cur_a = (r % 2 == 1)
            roff = 0 if cur_a else 128
            dk_psoff = 0 if cur_a else 128
            x_psoff = 128 - dk_psoff
            dk_dst = 256 if cur_a else 0
            mq = None
            wts = [("cv", dkt_ev), ("cv", x_ready)]
            for h in range(H):
                bi2 = bA if h < 2 else bB
                w2 = (wts + self.bank(bA) + self.bank(bB)) if h == 0 else []
                mq = self.T(lambda e, h2=h, b=pb[bi2], ro=roff: e.matmul(
                    b[:, (h2 % 2) * 256: (h2 % 2) * 256 + 256],
                    self.w_DkT[s][:, h2 * 128:(h2 + 1) * 128],
                    self.w_sv[s][:, h2 * 384 + ro: h2 * 384 + ro + 256],
                    start=True, stop=True), waits=w2)
            sv3 = self.w_sv[s][:].rearrange("p (h q) -> p h q", q=384)
            ev1 = None
            if r < 7 and "ev1" not in SKIP:
                for half in range(2):
                    b2 = pb[bA] if half == 0 else pb[bB]
                    pv = b2.ap().rearrange("p (h q) -> p h q", q=256)
                    ev1 = self.V(lambda e, pv2=pv, hf=half, dd=dk_dst, dp=dk_psoff:
                                 e.tensor_copy(
                        sv3[:, 2 * hf: 2 * hf + 2, dd: dd + 128],
                        pv2[:, 0:2, dp: dp + 128]),
                        waits=([("ct", mq)] + self.snap("cv")) if half == 0
                        else [])
            for half in range(2 if "ev2" not in SKIP else 0):
                b2 = pb[bA] if half == 0 else pb[bB]
                pv = b2.ap().rearrange("p (h q) -> p h q", q=256)
                ev2 = self.V(lambda e, pv2=pv, hf=half, xp=x_psoff: e.tensor_tensor(
                    sv3[:, 2 * hf: 2 * hf + 2, 128:256],
                    sv3[:, 2 * hf: 2 * hf + 2, 128:256],
                    pv2[:, 0:2, xp: xp + 128], ALU.add),
                    waits=([("ct", mq)] + self.snap("cv")) if half == 0 else [])
            x_ready = ev2
            self.bank_done(bA, "cv", ev2)
            self.bank_done(bB, "cv", ev2)
            if ev1 is not None:
                self.bank_done(bA, "cv", ev1)
                self.bank_done(bB, "cv", ev1)
            if r < 7 and "tr" not in SKIP:
                tl = None
                for h in range(H):
                    tl = self.T(lambda e, h2=h, b=pb[bT], dd=dk_dst: e.transpose(
                        b.ap().bitcast(SOLVE_DT)[0:128, h2 * 128:(h2 + 1) * 128],
                        self.w_sv[s][:, h2 * 384 + dd: h2 * 384 + dd + 128],
                        self.c_idf32r[:, :]),
                        waits=([("cv", ev1)] + self.bank(bT)) if h == 0 else [])
                dkt_ev = self.V(lambda e, b=pb[bT]: e.tensor_copy(
                    self.w_DkT[s][:, :], b.ap().bitcast(SOLVE_DT)[0:128, 0:512]),
                    waits=[("ct", tl)] + self.snap("cv"))
                self.bank_done(bT, "cv", dkt_ev)
            if STOP_PHASE == "sD1" and r == 1:
                return
            yield

        if STOP_PHASE == "sD":
            return
        for h in range(H):
            col = tau * H + h
            soff = (s * H + h) * DV
            sg = self.S(lambda e, h2=h: e.activation(
                self.w_gI[s][:, :], self.c_idbf[:, :], AF.Identity,
                scale=self.b_gam4[s][:, h2:h2 + 1]),
                waits=self.snap("cs") + self.snap("ct"))
            sk = self.S(lambda e, h2=h: e.activation(
                self.w_ksc[s][:, :],
                self.a_k[:, tau * KD + h2 * DK: tau * KD + (h2 + 1) * DK],
                AF.Identity, scale=self.b_lamb4[s][:, h2:h2 + 1]), waits=[("cs", sg)])
            w = self.bank(bA) + ([("cv", x_ready)] if h == 0 else [])
            self.T(lambda e, h2=h, b=pb[bA], so=soff: e.matmul(
                b[:, 0:DV], fmp(kfm, h2, 0), self.st1[:, so:so + DV],
                start=True, stop=False), waits=w)
            mmz = self.T(lambda e, h2=h, b=pb[bA], so=soff: e.matmul(
                b[:, 0:DV], fmp(kfm, h2, 1), self.st2[0:64, so:so + DV],
                start=False, stop=True))
            vz = self.V(lambda e, b=pb[bA], h2=h, c2=col: e.scalar_tensor_tensor(
                self.w_Zh[s][:, :], b[:, 0:DV],
                self.b_negLam[:, c2:c2 + 1],
                self.a_v[:, tau * VD + h2 * DV: tau * VD + (h2 + 1) * DV],
                ALU.mult, ALU.add),
                waits=[("ct", mmz)] + self.snap("cv") + self.snap("ct")
                + self.snap("cs"))
            self.bank_done(bA, "cv", vz)
            wW = self.bank(bE)
            mmw = self.T(lambda e, h2=h, b=pb[bE]: e.matmul(
                b[:, 0:DV],
                self.w_sv[s][:, h2 * 384 + 128: h2 * 384 + 256],
                self.w_Zh[s][:, :], start=True, stop=True),
                waits=wW + [("cv", vz)])
            sw = self.S(lambda e, b=pb[bE], h2=h: e.activation(
                self.w_W[s][:, h2 * DV:(h2 + 1) * DV], b[:, 0:DV], AF.Copy),
                waits=[("ct", mmw)] + self.snap("cs"))
            self.bank_done(bE, "cs", sw)
            wq = self.bank(bB)
            self.T(lambda e, h2=h, b=pb[bB], so=soff: e.matmul(
                b[:, 0:DV], fmp(qfm, h2, 0), self.st1[:, so:so + DV],
                start=True, stop=False), waits=wq)
            mmq = self.T(lambda e, h2=h, b=pb[bB], so=soff: e.matmul(
                b[:, 0:DV], fmp(qfm, h2, 1), self.st2[0:64, so:so + DV],
                start=False, stop=True))
            so1 = self.S(lambda e, b=pb[bB], c2=col: e.activation(
                self.w_O[s][:, :], b[:, 0:DV], AF.Identity,
                scale=self.b_Lam[:, c2:c2 + 1]), waits=[("ct", mmq)])
            self.bank_done(bB, "cs", so1)
            wo = self.bank(bB)
            mmo = self.T(lambda e, h2=h, b=pb[bB]: e.matmul(
                b[:, 0:DV], self.w_Pb[s][:, h2 * 128:(h2 + 1) * 128],
                self.w_W[s][:, h2 * DV:(h2 + 1) * DV], start=True, stop=True),
                waits=wo + [("cs", sw)])
            vo = self.V(lambda e, b=pb[bB]: e.tensor_tensor(
                self.w_O[s][:, :], self.w_O[s][:, :], b[:, 0:DV], ALU.add),
                waits=[("ct", mmo), ("cs", so1)])
            self.bank_done(bB, "cv", vo)
            wt = self.bank(bT)
            self.T(lambda e, b=pb[bT], so=soff: e.matmul(
                b[:, 0:DV], self.w_gI[s][:, :], self.st1[:, so:so + DV],
                start=True, stop=False), waits=wt + [("cs", sk)])
            mm1 = self.T(lambda e, b=pb[bT], h2=h: e.matmul(
                b[:, 0:DV], self.w_ksc[s][:, 0:128],
                self.w_W[s][:, h2 * DV:(h2 + 1) * DV], start=False, stop=True))
            bT2 = 6 + (1 - s)
            wt2 = self.bank(bT2)
            self.T(lambda e, b=pb[bT2], so=soff: e.matmul(
                b[0:64, 0:DV], self.w_gI[s][0:64, 0:64], self.st2[0:64, so:so + DV],
                start=True, stop=False), waits=wt2)
            mm2 = self.T(lambda e, b=pb[bT2], h2=h: e.matmul(
                b[0:64, 0:DV], self.w_ksc[s][:, 128:192],
                self.w_W[s][:, h2 * DV:(h2 + 1) * DV], start=False, stop=True))
            v1 = self.V(lambda e, b=pb[bT], so=soff: e.tensor_copy(
                self.st1[:, so:so + DV], b[:, 0:DV]), waits=[("ct", mm1)])
            self.bank_done(bT, "cv", v1)
            v2 = self.V(lambda e, b=pb[bT2], so=soff: e.tensor_copy(
                self.st2[0:64, so:so + DV], b[0:64, 0:DV]), waits=[("ct", mm2)])
            self.bank_done(bT2, "cv", v2)
            sq = self.S(lambda e: e.activation(
                self.w_osq[s][:, :], self.w_O[s][:, :], AF.Square,
                accum_out=self.b_ssq[s][:, 0:1]), waits=[("cv", vo)] + self.snap("cs"))
            sq = self.S(lambda e: e.activation(
                self.b_ssq[s][:, 1:2], self.b_ssq[s][:, 0:1], AF.Ln,
                scale=np.float32(1.0 / DV), bias=np.float32(EPS_RMS)), waits=[("cs", sq)])
            sq = self.S(lambda e: e.activation(
                self.b_ssq[s][:, 1:2], self.b_ssq[s][:, 1:2], AF.Exp, scale=np.float32(-0.5)),
                waits=[("cs", sq)])
            vg2 = self.V(lambda e: e.scalar_tensor_tensor(
                self.w_osq[s][:, :], self.w_O[s][:, :], self.b_ssq[s][:, 1:2],
                self.d_gnw[:, :], ALU.mult, ALU.mult),
                waits=[("cs", sq)] + self.snap("cv"))
            self.V(lambda e, h2=h, t2=tau: e.tensor_tensor(
                self.a_big[:, t2 * VD + h2 * DV: t2 * VD + (h2 + 1) * DV],
                self.w_osq[s][:, :],
                self.a_gs[:, t2 * VD + h2 * DV: t2 * VD + (h2 + 1) * DV], ALU.mult),
                waits=[("cv", vg2), ("cs", self.gs_cs)])
            yield

    def p5_ln2(self, g, tau):
        s, c = divmod(tau, CPG)
        par = tau % 2
        xt = self.x2[:, tau * D: tau * D + D]
        qt0 = self.qt0p[par]
        qs, qsi, qm, qw = self.qsp[par]
        qt1, qt2 = (self.qt1, self.qt2) if par == 0 else (self.qt1b, self.qt2b)
        sc = 4 + par * 4
        W0 = [self.x2_ready[tau], self.qscr_free[par]]
        v = self.V(lambda e: e.reduce_sum(self.b_stat[:, sc:sc + 1], xt, axis=AX.X,
                                          negate=True), waits=W0)
        v = self.V(lambda e: e.tensor_scalar(self.b_stat[:, sc + 1:sc + 2],
                                             self.b_stat[:, sc:sc + 1],
                                             1.0 / D, None, ALU.mult),
                   waits=[("cv", v)])
        s1 = self.S(lambda e: e.activation(qt0[:, :], xt, AF.Identity,
                                           bias=self.b_stat[:, sc + 1:sc + 2]),
                    waits=[("cv", v), self.qscr_free[par]])
        s1 = self.S(lambda e: e.activation(qt1, qt0[:, :], AF.Square,
                                           accum_out=self.b_stat[:, sc + 2:sc + 3]),
                    waits=[("cs", s1)])
        s1 = self.S(lambda e: e.activation(self.b_stat[:, sc + 3:sc + 4],
                                           self.b_stat[:, sc + 2:sc + 3],
                                           AF.Ln, scale=np.float32(1.0 / D), bias=np.float32(EPS_LN)),
                    waits=[("cs", s1)])
        s1 = self.S(lambda e: e.activation(self.b_stat[:, sc + 3:sc + 4],
                                           self.b_stat[:, sc + 3:sc + 4],
                                           AF.Exp, scale=np.float32(-0.5)), waits=[("cs", s1)])
        s1 = self.S(lambda e: e.activation(qt0[:, :], qt0[:, :], AF.Identity,
                                           scale=self.b_stat[:, sc + 3:sc + 4]),
                    waits=[("cs", s1)])
        v = self.V(lambda e: e.reduce_max(qs[:].unsqueeze(-1),
                                          qt0[:].rearrange("p (b k) -> p b k", k=32),
                                          axis=AX.X, apply_absolute_value=True),
                   waits=[("cs", s1)])
        v = self.V(lambda e: e.tensor_scalar(qsi[:], qs[:], 1e-30, None,
                                             ALU.max), waits=[("cv", v)])
        v = self.V(lambda e: e.tensor_scalar(qsi[:].bitcast(I32),
                                             qsi[:].bitcast(I32),
                                             self.c_pow2[:, 0:1], None, ALU.bitwise_and),
                   waits=[("cv", v)])
        v = self.V(lambda e: e.tensor_scalar(qs[:], qsi[:], 0.25, None, ALU.mult),
                   waits=[("cv", v)])
        v = self.V(lambda e: e.tensor_scalar(qs[:], qs[:], self.d_dactp[:, 0:1],
                                             None, ALU.mult), waits=[("cv", v)])
        v = self.V(lambda e: e.reciprocal(qsi[:], qs[:]), waits=[("cv", v)])
        vq = self.emit_quant_core(qt0[:], qt1, qt2, qs[:], qsi[:],
                                  self.a_xq[:, tau * D: tau * D + D], [("cv", v)])
        self.qscr_free[par] = ("cv", vq)
        self.p5_vq[tau] = vq

    def p5_ffn(self, g, tau):
        pb = self.pb
        s, c = divmod(tau, CPG)
        t0 = g * TPG
        par = tau % 2
        xt = self.x2[:, tau * D: tau * D + D]
        vq = self.p5_vq[tau]
        for kk in range(8):
            bi = par * 2 + (kk % 2)
            t = self.T(lambda e, k2=kk, b=pb[bi], t2=tau: e.transpose(
                b.ap().bitcast(BF16)[0:128, 0:128],
                self.a_xq[:, t2 * D + k2 * 128: t2 * D + (k2 + 1) * 128],
                self.c_idbf[:, :]), waits=[("cv", vq)] + self.bank(bi))
            vv = self.V(lambda e, k2=kk, b=pb[bi], s2=s, c2=c: e.tensor_copy(
                self.a_zs[:, k2 * 512 + s2 * 256 + c2 * 128:
                          k2 * 512 + s2 * 256 + c2 * 128 + 128],
                b.ap().bitcast(BF16)[0:128, 0:128]), waits=[("ct", t)])
            self.bank_done(bi, "cv", vv)
        xqfm_cv = self.now("cv")
        vlast = None
        for half in range(2):
            bi = 4 + par * 2 + half
            mm = None
            for kc in range(8):
                lhs = self.a_zs[:, kc * 512 + s * 256 + c * 128:
                                kc * 512 + s * 256 + c * 128 + 128]
                w = ([("cv", xqfm_cv), self.ld_wq[0]] + self.bank(bi)) \
                    if kc == 0 else ([self.ld_wq[1]] if kc == 4 else [])
                mm = self.T(lambda e, l=lhs, b=pb[bi], k2=kc, h2=half: e.matmul(
                    b[:, 0:512], l,
                    self.wqslot[k2 // 4][:, (k2 % 4) * 1024 + h2 * 512:
                                         (k2 % 4) * 1024 + h2 * 512 + 512],
                    start=(k2 == 0), stop=False), waits=w)
            mm = self.T(lambda e, b=pb[bi], h2=half: e.matmul(
                b[:, 0:512], self.c_ones1b[:, :],
                self.d_fbias[0:1, h2 * 512: h2 * 512 + 512],
                start=False, stop=True))
            self.wq_free = mm
            vlast = self.V(lambda e, b=pb[bi], h2=half: e.tensor_tensor(
                self.x2[:, tau * D + h2 * 512: tau * D + h2 * 512 + 512],
                self.x2[:, tau * D + h2 * 512: tau * D + h2 * 512 + 512],
                b[:, 0:512], ALU.add),
                waits=[("ct", mm)])
            self.bank_done(bi, "cv", vlast)
        self.dma("gpsimd", self.dout[s, t0 + c * 128: t0 + c * 128 + 128, :],
                 xt, f"dst{par}",
                 waits=[("cv", vlast), (f"dst{par}", self.now(f"dst{par}"))])
        self.x2_store[tau] = (f"dst{par}", self.now(f"dst{par}"))


def kernel(**inputs):
    x = np.ascontiguousarray(inputs["L0_attn_x"], dtype=np.float32)
    convw = np.concatenate([np.ascontiguousarray(inputs["L0_attn_conv_x_w"][:, 0, :]),
                            np.ascontiguousarray(inputs["L0_attn_conv_z_w"][:, 0, :])],
                           axis=0)
    convb = np.concatenate([np.ascontiguousarray(inputs["L0_attn_conv_x_b"]),
                            np.ascontiguousarray(inputs["L0_attn_conv_z_b"])])[None, :]
    small = np.zeros((1, 16), np.float32)
    small[0, 0:4] = inputs["L0_attn_b_b"]
    small[0, 4:8] = inputs["L0_attn_A_log"]
    small[0, 8:12] = inputs["L0_attn_dt_bias"]
    small[0, 12:13] = inputs["L0_ffn_delta_act"]
    base = {
        "w_in": np.ascontiguousarray(inputs["L0_attn_W_in"]),
        "conv_w": convw,
        "conv_b": convb,
        "w_q": np.ascontiguousarray(inputs["L0_attn_W_q"]),
        "w_k": np.ascontiguousarray(inputs["L0_attn_W_k"]),
        "w_v": np.ascontiguousarray(inputs["L0_attn_W_v"]),
        "w_g": np.ascontiguousarray(inputs["L0_attn_W_g"]),
        "w_gkb": np.concatenate([np.ascontiguousarray(inputs["L0_attn_W_gk"]),
                                 np.ascontiguousarray(inputs["L0_attn_W_b"])], axis=1),
        "small_in": small,
        "g_norm_w": np.ascontiguousarray(inputs["L0_attn_g_norm_w"][None, :]),
        "w_so": np.ascontiguousarray(inputs["L0_attn_W_so"]),
        "w_out": np.ascontiguousarray(inputs["L0_attn_W_out"]),
        "ffn_w": np.ascontiguousarray(inputs["L0_ffn_weight"]),
        "ffn_b": np.ascontiguousarray(inputs["L0_ffn_bias"][None, :]),
        "ffn_dw": np.ascontiguousarray(inputs["L0_ffn_delta_w"]),
    }
    nc = B().build()
    in_maps = []
    for core in range(NCORES):
        m = dict(base)
        m["x"] = np.ascontiguousarray(x[core * SEQ:(core + 1) * SEQ])
        in_maps.append(m)
    res = run_bass_kernel_spmd(nc, in_maps, core_ids=list(range(NCORES)))
    return np.concatenate([r["out"] for r in res.results], axis=0)
